# version 09
# baseline (speedup 1.0000x reference)
from contextlib import ExitStack

import numpy as np
import ml_dtypes
import concourse.bass as bass
import concourse.mybir as mybir
from concourse.bass_utils import run_bass_kernel_spmd

F32 = mybir.dt.float32
BF16 = mybir.dt.bfloat16
I32 = mybir.dt.int32
AF = mybir.ActivationFunctionType
ALU = mybir.AluOpType
AX = mybir.AxisListType

NCORES = 8
S_TOK = 16384
D_ = 1024
DI_ = 2816
T_ = 2048
NQT = 16
NPAIR = 8
NKV = 4
NCH = 4
HC = 8
IC = 22
EPS = 1e-5
NWU = 2 * IC + 8

PAIR_T = [16 * (u + 1) for u in range(NPAIR)]
CUM_T = [0]
for _u in range(NPAIR):
    CUM_T.append(CUM_T[-1] + PAIR_T[_u])
PAIR_G = [4 * (u + 1) for u in range(NPAIR)]
CUM_G = [0]
for _u in range(NPAIR):
    CUM_G.append(CUM_G[-1] + PAIR_G[_u])

WQ_PROD = 0
PROD_STAGES = []
for _j in range(4):
    PROD_STAGES.append(("kv", _j))
    PROD_STAGES.append(("q", _j))
PROD_TILES = []
for _st, _i in PROD_STAGES:
    for _k in range(4):
        PROD_TILES.append((_st, 4 * _i + _k))
PT_I = {t: i for i, t in enumerate(PROD_TILES)}
WD_SUBW = [1024, 1024, 768]
WU_KIND = ([(0, i, 0, 1) for i in range(IC)] + [(1, i, 0, 1) for i in range(IC)]
           + [(2, rb, s, p) for rb in range(8) for p in (0, 1) for s in range(3)])
NWU = len(WU_KIND)

CHUNKS = 2
SC_S0 = [0, 8]
SC_NSEG = [8, 16]
SEG_LIST = [(v, g) for v in range(CHUNKS) for g in range(SC_NSEG[v])]
SEG_I = {key: i for i, key in enumerate(SEG_LIST)}


def seg_qlo(v, g):
    return max(SC_S0[v], g)


def seg_ns(v, g):
    return SC_S0[v] + 8 - seg_qlo(v, g)


def seg_subs(v, g):
    W = seg_ns(v, g) * 128
    subs = []
    off = 0
    while off < W:
        w = min(512, W - off)
        subs.append((off, w))
        off += w
    return subs


STILE_LIST = [(v, g, t2, si2) for (v, g) in SEG_LIST for t2 in range(8)
              for si2 in range(len(seg_subs(v, g)))]
STILE_I = {key: i for i, key in enumerate(STILE_LIST)}
SWEEP_LIST = [(v, g, si) for (v, g) in SEG_LIST for si in range(seg_ns(v, g))]
SWEEP_I = {key: i for i, key in enumerate(SWEEP_LIST)}
LOAD_LIST = [(v, gi2) for v in range(CHUNKS) for gi2 in range(2 * SC_NSEG[v])]
LOAD_I = {key: i for i, key in enumerate(LOAD_LIST)}

WQ_SEG = {}
_k = WQ_PROD
_segs = list(SEG_LIST)
_i = 0
while _k < NWU:
    _sg = _segs[2 + _i % (len(_segs) - 2)]
    WQ_SEG[_sg] = WQ_SEG.get(_sg, 0) + 1
    _k += 1
    _i += 1



def _wu_w(k):
    kind, chunk, sub, phase = WU_KIND[k]
    return D_ if kind < 2 else WD_SUBW[sub]


def _wu_spills(k):
    kind, chunk, sub, phase = WU_KIND[k]
    return kind < 2 or phase == 1


def _prev_spill(k):
    for k2 in range(k - 1, -1, -1):
        if _wu_spills(k2) and k2 % 2 == k % 2:
            return k2
    return None


def ti(u, t):
    return CUM_T[u] + t


def gi(u, g):
    return CUM_G[u] + g


INV_G = {}
for _u in range(NPAIR):
    for _g in range(PAIR_G[_u]):
        INV_G[gi(_u, _g)] = (_u, _g)


def mki(u, t):
    return 16 * u + (t - 16 * u)


class Ctr:
    def __init__(self, es, nc, name):
        self.h = es.enter_context(nc.semaphore(name))
        self.name = name
        self.n = 0

    def add(self, k=1):
        self.n += k
        return self.n


def build_nc():
    nc = bass.Bass("TRN2", target_bir_lowering=False, debug=False)

    xq_in = nc.dram_tensor("xq", [T_, D_], F32, kind="ExternalInput").ap()
    xkv_in = nc.dram_tensor("xkv", [T_, D_], F32, kind="ExternalInput").ap()
    wq_in = nc.dram_tensor("wq", [D_, D_], F32, kind="ExternalInput").ap()
    wk_in = nc.dram_tensor("wk", [D_, D_], F32, kind="ExternalInput").ap()
    wv_in = nc.dram_tensor("wv", [D_, D_], F32, kind="ExternalInput").ap()
    wg_in = nc.dram_tensor("wg", [DI_, D_], F32, kind="ExternalInput").ap()
    wu_in = nc.dram_tensor("wu", [DI_, D_], F32, kind="ExternalInput").ap()
    wd_in = nc.dram_tensor("wd", [D_, DI_], F32, kind="ExternalInput").ap()
    clipx_in = nc.dram_tensor("clipx", [128, 1], F32, kind="ExternalInput").ap()
    clipd_in = nc.dram_tensor("clipd", [128, 1], F32, kind="ExternalInput").ap()
    masks_in = nc.dram_tensor("masks", [8 * 128, 128], BF16, kind="ExternalInput").ap()
    identf_in = nc.dram_tensor("identf", [128, 128], F32, kind="ExternalInput").ap()
    identb_in = nc.dram_tensor("identb", [128, 128], BF16, kind="ExternalInput").ap()
    out_t = nc.dram_tensor("out", [T_, D_], F32, kind="ExternalOutput")

    ksrc = [nc.dram_tensor(f"ksrc{j}", [128, HC * 512], BF16) for j in range(NCH)]
    vsrc = [nc.dram_tensor(f"vsrc{j}", [128, 4 * D_], BF16) for j in range(NCH)]
    kall = [nc.dram_tensor(f"kall{j}", [NCORES * 128, HC * 512], BF16,
                           addr_space="Shared") for j in range(NCH)]
    vall = [nc.dram_tensor(f"vall{j}", [NCORES * 128, 4 * D_], BF16,
                           addr_space="Shared") for j in range(NCH)]
    wq_sp = [nc.dram_tensor("wgq_s", [DI_, D_], BF16),
             nc.dram_tensor("wuq_s", [DI_, D_], BF16),
             nc.dram_tensor("wdq_s", [D_, DI_], BF16)]
    ln1_sp = nc.dram_tensor("ln1_s", [T_, D_], F32)
    h_spd = nc.dram_tensor("h_s", [128, IC * T_], F32)
    ar_i = [nc.dram_tensor(f"ar_i{i}", [128, 1], F32) for i in range(3)]
    ar_o = [nc.dram_tensor(f"ar_o{i}", [128, 1], F32) for i in range(3)]

    RG = [list(range(NCORES))]
    es = ExitStack()

    def sbg(name, shape, dt):
        return es.enter_context(nc.sbuf_tensor(name, shape, dt))

    identf = sbg("identf_sb", [128, 128], F32)
    identb = sbg("identb_sb", [128, 128], BF16)
    zeros = sbg("zeros_sb", [128, 128], F32)
    ones_bf = sbg("ones_bf", [128, 1], BF16)
    clipx = sbg("clipx_sb", [128, 1], F32)
    clipd = sbg("clipd_sb", [128, 1], F32)
    sgl = sbg("sgl", [128, 32], F32)
    stat = sbg("stat", [128, 32], F32)
    mx_x = sbg("mx_x", [128, 1], F32)
    mx_h = sbg("mx_h", [128, 1], F32)
    junk = sbg("junk", [128, D_], F32)
    wtmp = sbg("wtmp", [128, D_], F32)
    wtmpi = sbg("wtmpi", [128, D_], I32)
    wqbf = [sbg(f"wqbf{i}", [128, D_], BF16) for i in range(2)]
    negm = sbg("negm_sb", [128, 1], F32)
    eps_sb = sbg("eps_sb", [128, 1], F32)
    arx_sb = sbg("arx_sb", [128, 1], F32)
    arh_sb = sbg("arh_sb", [128, 1], F32)

    es_a1 = ExitStack()

    def sba1(name, shape, dt):
        return es_a1.enter_context(nc.sbuf_tensor(name, shape, dt))

    masks = sba1("masks_sb", [128, 8 * 128], BF16)
    q_fm = sba1("q_fm", [128, HC * T_], BF16)
    ksamp = sba1("ksamp", [128, HC * 128], BF16)
    xqr = [sba1(f"xqr{i}", [128, D_], F32) for i in range(2)]
    wraw = [sba1(f"wraw{i}", [128, D_], F32) for i in range(2)]
    kstage = sba1("kstage", [128, HC * 512], BF16)
    vstage = sba1("vstage", [128, 4 * D_], BF16)

    es_pr = ExitStack()
    xstage = [es_pr.enter_context(nc.sbuf_tensor(f"xstage{i}", [128, D_], BF16)) for i in range(2)]
    xfm = [es_pr.enter_context(nc.sbuf_tensor(f"xfm{i}", [128, HC * 512], BF16)) for i in range(2)]
    wkv_bf = [es_pr.enter_context(nc.sbuf_tensor(f"wkv{i}", [128, HC * D_], BF16)) for i in range(2)]
    wq_bf = es_pr.enter_context(nc.sbuf_tensor("wq_bf", [128, HC * D_], BF16))
    ps_xt = es_pr.enter_context(nc.psum_tensor("ps_xt", [128, 2048], BF16))
    ps_pa = es_pr.enter_context(nc.psum_tensor("ps_pa", [128, 512], F32))
    ps_pb = es_pr.enter_context(nc.psum_tensor("ps_pb", [128, 512], F32))
    ps_v = es_pr.enter_context(nc.psum_tensor("ps_v", [128, 2048], F32))

    AT = {}

    BT = {}
    CT = {}

    gx = [Ctr(es, nc, f"gx{i}") for i in range(2)]
    g_id = Ctr(es, nc, "g_id")
    gw_k = Ctr(es, nc, "gw_k")
    gw_v = Ctr(es, nc, "gw_v")
    gw_q = Ctr(es, nc, "gw_q")
    g_ksp = Ctr(es, nc, "g_ksp")
    g_vsp = Ctr(es, nc, "g_vsp")
    g_wsp = [Ctr(es, nc, f"g_wsp{i}") for i in range(2)]
    g_lnsp = [Ctr(es, nc, f"g_lnsp{i}") for i in range(2)]
    g_hsp = [Ctr(es, nc, f"g_hsp{i}") for i in range(2)]
    g_ar = Ctr(es, nc, "g_ar")
    g_cc = Ctr(es, nc, "g_cc")
    s_cfg = Ctr(es, nc, "s_cfg")
    kg = [Ctr(es, nc, f"kg{i}") for i in range(3)]
    vg = [Ctr(es, nc, f"vg{i}") for i in range(3)]
    d_oa = Ctr(es, nc, "d_oa")
    sxq = [Ctr(es, nc, f"sxq{i}") for i in range(2)]
    sw = [Ctr(es, nc, f"sw{i}") for i in range(2)]
    s_arx = Ctr(es, nc, "s_arx")
    s_lnr = [Ctr(es, nc, f"s_lnr{i}") for i in range(4)]
    s_wgu = Ctr(es, nc, "s_wgu")
    s_arh = Ctr(es, nc, "s_arh")
    s_qwd = Ctr(es, nc, "s_qwd")
    s_hld = [Ctr(es, nc, f"s_hld{i}") for i in range(2)]
    s_clnr = [Ctr(es, nc, f"s_clnr{i}") for i in range(2)]
    s_out = [Ctr(es, nc, f"s_out{i}") for i in range(2)]
    p_pr = Ctr(es, nc, "p_pr")
    p_s = Ctr(es, nc, "p_s")
    p_pv = Ctr(es, nc, "p_pv")
    p_m = Ctr(es, nc, "p_m")
    p_bc = Ctr(es, nc, "p_bc")
    a_ev = Ctr(es, nc, "a_ev")
    a_exp = Ctr(es, nc, "a_exp")
    a_m = Ctr(es, nc, "a_m")
    d_mk = Ctr(es, nc, "d_mk")
    d_ep = Ctr(es, nc, "d_ep")
    d_q = Ctr(es, nc, "d_q")
    d_m = Ctr(es, nc, "d_m")
    d_bc = Ctr(es, nc, "d_bc")
    ctrs = (gx + [g_id, gw_k, gw_v, gw_q, g_ksp, g_vsp] + g_wsp + g_lnsp
            + g_hsp + [g_ar, g_cc, s_cfg] + kg + vg + sxq + sw
            + [s_arx] + s_lnr + [s_wgu, s_arh, s_qwd] + s_hld + s_clnr
            + s_out + [p_pr, p_s, p_pv, p_m, p_bc, a_ev, a_exp, a_m, d_mk,
                       d_ep, d_q, d_m, d_bc, d_oa])

    CK = {}

    def mark(name, ctr):
        if name in CK:
            assert CK[name] == (ctr, ctr.n), (name, CK[name][1], ctr.n)
        CK[name] = (ctr, ctr.n)

    def wt(eng, name):
        ctr, val = CK[name]
        if val > 0:
            eng.wait_ge(ctr.h, val)


    def A_gpsimd(E):
        em = E is not None
        if em:
            E.memset(sgl[:, 31:32], 0.0)
            E.drain()
            E.dma_start(out=ar_i[2].ap(), in_=sgl[:, 31:32]).then_inc(g_ar.h, 16)
        g_ar.add(16)
        if em:
            E.wait_ge(g_ar.h, g_ar.n)

        if em:
            E.dma_start(out=identb[:], in_=identb_in).then_inc(g_id.h, 16)
        g_id.add(16)
        mark("identb", g_id)

        def xload(i):
            st, m = PROD_TILES[i]
            srcten = xkv_in if st == "kv" else xq_in
            if em:
                if i >= 2:
                    wt(E, f"pt_t{i - 2}")
                E.dma_start(out=xstage[i % 2][:],
                            in_=srcten[m * 128:(m + 1) * 128, :]).then_inc(gx[i % 2].h, 16)
            gx[i % 2].add(16)
            mark(f"ldt{i}", gx[i % 2])

        for i in range(4):
            xload(i)
        for d in range(HC):
            if em:
                E.dma_start(out=wkv_bf[0][:, d * D_:(d + 1) * D_],
                            in_=wk_in[d * 128:(d + 1) * 128, :]).then_inc(gw_k.h, 16)
            gw_k.add(16)
        mark("wk_all", gw_k)
        for d in range(HC):
            if em:
                E.dma_start(out=wkv_bf[1][:, d * D_:(d + 1) * D_],
                            in_=wv_in[d * 128:(d + 1) * 128, :]).then_inc(gw_v.h, 16)
            gw_v.add(16)
        mark("wv_all", gw_v)
        for d in range(HC):
            if em:
                E.dma_start(out=wq_bf[:, d * D_:(d + 1) * D_],
                            in_=wq_in[d * 128:(d + 1) * 128, :]).then_inc(gw_q.h, 16)
            gw_q.add(16)
        mark("wq_all", gw_q)

        def wraw_load(k):
            kind, chunk, sub, phase = WU_KIND[k]
            w = _wu_w(k)
            b = k % 2
            if em:
                if k >= 2:
                    wt(E, f"wq_u{k - 2}")
                if kind < 2:
                    ap_src = [wg_in, wu_in][kind][chunk * 128:(chunk + 1) * 128, :]
                else:
                    colo = sum(WD_SUBW[:sub])
                    ap_src = wd_in[chunk * 128:(chunk + 1) * 128, colo:colo + w]
                E.dma_start(out=wraw[b][:, 0:w], in_=ap_src).then_inc(sw[b].h, 16)
            sw[b].add(16)
            mark(f"wl{k}", sw[b])

        def spill_k(j):
            if em:
                wt(E, f"kev{j}")
                E.dma_start(out=ksrc[j].ap(), in_=kstage[:]).then_inc(g_ksp.h, 16)
            g_ksp.add(16)
            mark(f"ksp{j}", g_ksp)
            if em:
                E.wait_ge(g_ksp.h, g_ksp.n)
                E.collective_compute("AllGather", ALU.bypass, replica_groups=RG,
                                     ins=[ksrc[j].ap().opt()],
                                     outs=[kall[j].ap().opt()]).then_inc(g_cc.h, 1)
            g_cc.add(1)
            mark(f"cc_k{j}", g_cc)

        def spill_v(j):
            if em:
                wt(E, f"vev{j}")
                E.dma_start(out=vsrc[j].ap(), in_=vstage[:]).then_inc(g_vsp.h, 16)
            g_vsp.add(16)
            mark(f"vsp{j}", g_vsp)
            if em:
                E.wait_ge(g_vsp.h, g_vsp.n)
                E.collective_compute("AllGather", ALU.bypass, replica_groups=RG,
                                     ins=[vsrc[j].ap().opt()],
                                     outs=[vall[j].ap().opt()]).then_inc(g_cc.h, 1)
            g_cc.add(1)
            mark(f"cc_ch{j}", g_cc)

        for i in range(4, 8):
            xload(i)
        spill_k(0)
        for i in range(8, 12):
            xload(i)
        spill_v(0)
        for i in range(12, 16):
            xload(i)
        spill_k(1)
        spill_v(1)
        for i in range(16, 20):
            xload(i)
        for i in range(20, 24):
            xload(i)
        spill_k(2)
        spill_v(2)
        for i in range(24, 28):
            xload(i)
        for i in range(28, 32):
            xload(i)
        spill_k(3)
        spill_v(3)

        def wspill(k):
            if not _wu_spills(k):
                return
            kind, chunk, sub, phase = WU_KIND[k]
            w = _wu_w(k)
            b = k % 2
            if em:
                wt(E, f"wq_u{k}")
                if kind < 2:
                    dst = wq_sp[kind].ap()[chunk * 128:(chunk + 1) * 128, :]
                else:
                    colo = sum(WD_SUBW[:sub])
                    dst = wq_sp[2].ap()[chunk * 128:(chunk + 1) * 128, colo:colo + w]
                E.dma_start(out=dst, in_=wqbf[b][:, 0:w]).then_inc(g_wsp[b].h, 16)
            g_wsp[b].add(16)
            mark(f"wsp{k}", g_wsp[b])

        kld_done = 0
        ksp_done = 0

        def wq_step(n):
            nonlocal kld_done, ksp_done
            for _ in range(n):
                if kld_done < NWU:
                    wraw_load(kld_done)
                    kld_done += 1
                if kld_done >= 3 and ksp_done < kld_done - 3:
                    wspill(ksp_done)
                    ksp_done += 1

        wq_step(WQ_PROD)
        for v in range(CHUNKS):
            for g in range(SC_NSEG[v]):
                wq_step(WQ_SEG.get((v, g), 0))
                if (v, g) in ((1, 0), (1, 5), (1, 10)):
                    if em:
                        wt(E, f"segdone{v}_{g}")
                        E.collective_compute("AllReduce", ALU.max, replica_groups=RG,
                                             ins=[ar_i[2].ap().opt()],
                                             outs=[ar_o[2].ap().opt()]).then_inc(g_cc.h, 1)
                    g_cc.add(1)
                if g >= SC_S0[v]:
                    while ksp_done < kld_done:
                        wspill(ksp_done)
                        ksp_done += 1
                    s = g
                    if em:
                        wt(E, f"epi_s{s}")
                        E.dma_start(out=ln1_sp.ap()[s * 128:(s + 1) * 128, :],
                                    in_=AT["lnst"][:, (s % 2) * D_:(s % 2 + 1) * D_]).then_inc(g_lnsp[s % 2].h, 16)
                    g_lnsp[s % 2].add(16)
                    mark(f"lnsp_s{s}", g_lnsp[s % 2])
        while ksp_done < NWU:
            wspill(ksp_done)
            ksp_done += 1
        assert ksp_done == NWU and kld_done == NWU
        mark("lnsp_all0", g_lnsp[0])
        mark("lnsp_all1", g_lnsp[1])
        mark("wsp_all0", g_wsp[0])
        mark("wsp_all1", g_wsp[1])
        if em:
            wt(E, "epic1")
            E.dma_start(out=ar_i[0].ap(), in_=mx_x[:]).then_inc(g_ar.h, 16)
        g_ar.add(16)
        if em:
            E.wait_ge(g_ar.h, g_ar.n)
            E.collective_compute("AllReduce", ALU.max, replica_groups=RG,
                                 ins=[ar_i[0].ap().opt()],
                                 outs=[ar_o[0].ap().opt()]).then_inc(g_cc.h, 1)
        g_cc.add(1)
        mark("cc_arx", g_cc)

    def A_pe(E):
        em = E is not None

        def xtile(i):
            if em:
                wt(E, f"ldt{i}")
                if i < 2:
                    wt(E, "identb")
                else:
                    wt(E, f"ptev{i - 2}")
                ins = None
                for d in range(HC):
                    ins = E.transpose(
                        ps_xt[:, (i % 2) * 1024 + d * 128:(i % 2) * 1024 + (d + 1) * 128],
                        xstage[i % 2][:, d * 128:(d + 1) * 128], identb[:])
                ins.then_inc(p_pr.h, 1)
            p_pr.add(1)
            mark(f"pt_t{i}", p_pr)

        for st, idx in PROD_STAGES:
            base = PT_I[(st, 4 * idx)]
            for k in range(4):
                xtile(base + k)
            if st == "kv":
                j = idx
                if em:
                    wt(E, "wk_all")
                    wt(E, f"xfmkv{j}")
                for d in range(HC):
                    if em:
                        if d >= 2:
                            wt(E, f"kevd{j}_{d - 2}")
                        if j == 1 and d < 2:
                            wt(E, ["mrep", "negm"][d])
                        pb = [ps_pa, ps_pb][d % 2]
                        ins = None
                        for hch in range(HC):
                            ins = E.matmul(pb[:, 0:512],
                                           wkv_bf[0][:, hch * D_ + d * 128:hch * D_ + (d + 1) * 128],
                                           xfm[0][:, hch * 512:(hch + 1) * 512],
                                           start=hch == 0, stop=hch == HC - 1)
                        ins.then_inc(p_pr.h, 1)
                    p_pr.add(1)
                    mark(f"kmm{j}_{d}", p_pr)
                if em:
                    wt(E, "wv_all")
                for ksub in range(4):
                    if em:
                        if ksub >= 2:
                            wt(E, f"vevk{j}_{ksub - 2}")
                        vb0 = (ksub % 2) * 1024
                        ins = None
                        for hch in range(HC):
                            for dvh in range(2):
                                ins = E.matmul(
                                    ps_v[:, vb0 + dvh * 512:vb0 + (dvh + 1) * 512],
                                    xfm[0][:, hch * 512 + ksub * 128:hch * 512 + ksub * 128 + 128],
                                    wkv_bf[1][:, hch * D_ + dvh * 512:hch * D_ + (dvh + 1) * 512],
                                    start=hch == 0, stop=hch == HC - 1)
                        ins.then_inc(p_pr.h, 1)
                    p_pr.add(1)
                    mark(f"vmm{j}_{ksub}", p_pr)
            else:
                r = idx
                if em:
                    wt(E, "wq_all")
                    wt(E, f"xfmq{r}")
                for d in range(HC):
                    if em:
                        if d >= 2:
                            wt(E, f"qevd{r}_{d - 2}")
                        pb = [ps_pa, ps_pb][d % 2]
                        ins = None
                        for hch in range(HC):
                            ins = E.matmul(pb[:, 0:512],
                                           wq_bf[:, hch * D_ + d * 128:hch * D_ + (d + 1) * 128],
                                           xfm[1][:, hch * 512:(hch + 1) * 512],
                                           start=hch == 0, stop=hch == HC - 1)
                        ins.then_inc(p_pr.h, 1)
                    p_pr.add(1)
                    mark(f"qmm{r}_{d}", p_pr)
                if r == 0:
                    if em:
                        wt(E, "ksampc")
                        wt(E, "qevd0_7")
                        ins = None
                        for d in range(HC):
                            ins = E.matmul(ps_pa[:, 0:128], ksamp[:, d * 128:(d + 1) * 128],
                                           q_fm[:, d * T_:d * T_ + 128],
                                           start=d == 0, stop=d == HC - 1)
                        ins.then_inc(p_m.h, 1)
                    p_m.add(1)
                    mark("samp_mm", p_m)
                    if em:
                        wt(E, "mrep")
                        wt(E, "cfg")
                        E.transpose(ps_pb[:, 0:128], junk[:, 0:128], identf[:]).then_inc(p_m.h, 1)
                    p_m.add(1)
                    mark("mrep_t", p_m)

        ps_s, ps_ot, ps_r = AT["ps_s"], AT["ps_ot"], AT["ps_r"]

        for v in range(CHUNKS):
            for g in range(SC_NSEG[v]):
                qlo = seg_qlo(v, g)
                subs = seg_subs(v, g)
                for t2 in range(8):
                    t = 8 * g + t2
                    L = LOAD_I[(v, 2 * g + t2 // 4)]
                    buf = L % 3
                    for si2, (off, w) in enumerate(subs):
                        STI = STILE_I[(v, g, t2, si2)]
                        if em:
                            wt(E, f"kgl{L}")
                            if STI == 0:
                                wt(E, "prod_ev")
                                wt(E, "negm")
                            if STI >= 2:
                                E.wait_ge(a_exp.h, STI - 1)
                            slot = ps_s[:, (STI % 2) * 512:(STI % 2) * 512 + w]
                            ins = None
                            for d in range(HC):
                                ins = E.matmul(
                                    slot,
                                    AT["kbuf"][buf][:, d * 512 + (t % 4) * 128:d * 512 + (t % 4) * 128 + 128],
                                    q_fm[:, d * T_ + 128 * qlo + off:d * T_ + 128 * qlo + off + w],
                                    start=d == 0, stop=d == HC - 1)
                            ins.then_inc(p_s.h, 1)
                        p_s.add(1)
                        assert p_s.n == STI + 1
                for si in range(seg_ns(v, g)):
                    SW = SWEEP_I[(v, g, si)]
                    if em:
                        if SW == 0:
                            wt(E, "prod_ev")
                        if si == 0:
                            wt(E, f"vgl{LOAD_I[(v, 2 * g)]}")
                            wt(E, f"vgl{LOAD_I[(v, 2 * g + 1)]}")
                        E.wait_ge(a_exp.h, STILE_I[(v, g, 7, len(subs) - 1)] + 1)
                        if g >= SC_S0[v]:
                            wt(E, f"mseg{v}_{g}")
                        if SW >= 2:
                            E.wait_ge(d_oa.h, SW - 1)
                        par = SW % 2
                        ins = None
                        for t2 in range(8):
                            buf = LOAD_I[(v, 2 * g + t2 // 4)] % 3
                            lh = AT["pseg"][:, t2 * 1024 + si * 128:t2 * 1024 + si * 128 + 128]
                            for dvh in range(2):
                                ins = E.matmul(
                                    ps_ot[:, par * 1024 + dvh * 512:par * 1024 + (dvh + 1) * 512],
                                    lh,
                                    AT["vbuf"][buf][:, (8 * g + t2) % 4 * 1024 + dvh * 512:(8 * g + t2) % 4 * 1024 + (dvh + 1) * 512],
                                    start=t2 == 0, stop=t2 == 7)
                            ins = E.matmul(ps_r[:, par * 512:par * 512 + 1], lh, ones_bf[:],
                                           start=t2 == 0, stop=t2 == 7)
                        ins.then_inc(p_pv.h, 1)
                    p_pv.add(1)
                    assert p_pv.n == SW + 1
                mark(f"segdone{v}_{g}", p_pv)

    def A_act(E):
        em = E is not None

        def evac(waits, out_fn, in_fn, name):
            if em:
                for w in waits:
                    wt(E, w)
                E.activation(out=out_fn(), in_=in_fn(), func=AF.Copy).then_inc(a_ev.h, 1)
            a_ev.add(1)
            mark(name, a_ev)

        for st, idx in PROD_STAGES:
            base = PT_I[(st, 4 * idx)]
            buf = 0 if st == "kv" else 1
            for k in range(4):
                i = base + k
                extra = []
                if k == 0:
                    if st == "kv" and idx >= 1:
                        extra = [f"vmm{idx - 1}_3"]
                    if st == "q" and idx >= 1:
                        extra = [f"qmm{idx - 1}_7"]
                evac([f"pt_t{i}"] + extra,
                     lambda buf=buf, k=k: xfm[buf][:].rearrange(
                         "p (h s k) -> p h s k", h=HC, s=4)[:, :, k, :],
                     lambda i=i: ps_xt[:, (i % 2) * 1024:(i % 2) * 1024 + 1024].rearrange(
                         "p (h k) -> p h k", h=HC),
                     f"ptev{i}")
            if st == "kv":
                j = idx
                mark(f"xfmkv{j}", a_ev)
                for d in range(HC):
                    extra = []
                    if d == 0 and j >= 1:
                        extra = [f"ksp{j - 1}"]
                        if j == 1:
                            extra.append("ksampc")
                    evac([f"kmm{j}_{d}"] + extra,
                         lambda d=d: kstage[:, d * 512:(d + 1) * 512],
                         lambda d=d: [ps_pa, ps_pb][d % 2][:, 0:512],
                         f"kevd{j}_{d}")
                mark(f"kev{j}", a_ev)
                for ksub in range(4):
                    extra = [f"vsp{j - 1}"] if (ksub == 0 and j >= 1) else []
                    evac([f"vmm{j}_{ksub}"] + extra,
                         lambda k=ksub: vstage[:, k * 1024:(k + 1) * 1024],
                         lambda k=ksub: ps_v[:, (k % 2) * 1024:(k % 2) * 1024 + 1024],
                         f"vevk{j}_{ksub}")
                mark(f"vev{j}", a_ev)
            else:
                r = idx
                mark(f"xfmq{r}", a_ev)
                for d in range(HC):
                    evac([f"qmm{r}_{d}"],
                         lambda r=r, d=d: q_fm[:, d * T_ + r * 512:d * T_ + (r + 1) * 512],
                         lambda d=d: [ps_pa, ps_pb][d % 2][:, 0:512],
                         f"qevd{r}_{d}")
        mark("prod_ev", a_ev)

        ps_s = AT["ps_s"]
        for v in range(CHUNKS):
            for g in range(SC_NSEG[v]):
                subs = seg_subs(v, g)
                SI = SEG_I[(v, g)]
                for gi2 in (2 * g, 2 * g + 1):
                    L = LOAD_I[(v, gi2)]
                    j = gi2 // 8
                    r = gi2 % 8
                    if em:
                        wt(E, f"cc_ch{j}")
                        if L < 3:
                            wt(E, "qmm3_7")
                            wt(E, "prod_ev")
                            wt(E, "ksp3")
                            wt(E, "vsp3")
                        else:
                            vp, gp = LOAD_LIST[L - 3]
                            wt(E, f"segdone{vp}_{gp // 2}")
                        E.dma_start(out=AT["vbuf"][L % 3][:],
                                    in_=vall[j].ap()[r * 128:(r + 1) * 128, :]).then_inc(vg[L % 3].h, 16)
                    vg[L % 3].add(16)
                    mark(f"vgl{L}", vg[L % 3])
                if g >= SC_S0[v]:
                    s = g
                    if em:
                        if s >= 2:
                            wt(E, f"vstt{s - 2}")
                        E.dma_start(out=xqr[s % 2][:],
                                    in_=xq_in[s * 128:(s + 1) * 128, :]).then_inc(sxq[s % 2].h, 16)
                    sxq[s % 2].add(16)
                    mark(f"xq{s}", sxq[s % 2])
                for t2 in range(8):
                    for si2, (off, w) in enumerate(subs):
                        STI = STILE_I[(v, g, t2, si2)]
                        if em:
                            E.wait_ge(p_s.h, STI + 1)
                            if STI == 0:
                                wt(E, "negm")
                            if SI >= 1:
                                vp, gp = SEG_LIST[SI - 1]
                                wt(E, f"segdone{vp}_{gp}")
                            E.activation(out=AT["pseg"][:, t2 * 1024 + off:t2 * 1024 + off + w],
                                         in_=ps_s[:, (STI % 2) * 512:(STI % 2) * 512 + w],
                                         func=AF.Exp, bias=negm[:],
                                         scale=1.0 / 32.0).then_inc(a_exp.h, 1)
                        a_exp.add(1)
                if g >= SC_S0[v]:
                    si = g - SC_S0[v]
                    s = g
                    if em:
                        wt(E, f"var_s{s}")
                        E.activation(out=sgl[:, 24:25], in_=stat[:, 8 + si:9 + si],
                                     func=AF.Ln, bias=eps_sb[:], scale=1.0)
                        E.drain()
                        E.activation(out=stat[:, 16 + si:17 + si], in_=sgl[:, 24:25],
                                     func=AF.Exp, bias=0.0, scale=-0.5).then_inc(a_m.h, 1)
                    a_m.add(1)
                    mark(f"rstd_s{s}", a_m)

    def A_dve(E):
        em = E is not None
        if em:
            E.memset(zeros[:], 0.0)
            E.memset(ones_bf[:], 1.0)
            E.memset(mx_x[:], 0.0)
            E.memset(mx_h[:], 0.0)
            E.memset(eps_sb[:], EPS)
            E.engine_nop().then_inc(d_m.h, 1)
        d_m.add(1)
        mark("dve_init", d_m)

        if em:
            wt(E, "kev0")
            E.tensor_copy(
                ksamp[:].rearrange("p (d k) -> p d k", d=HC),
                kstage[:].rearrange("p (d k) -> p d k", d=HC)[:, :, 0:128],
            ).then_inc(d_m.h, 1)
        d_m.add(1)
        mark("ksampc", d_m)

        def wq_unit(k):
            kind, chunk, sub, phase = WU_KIND[k]
            w = _wu_w(k)
            b = k % 2
            if em:
                wt(E, f"wl{k}")
                buf = wraw[b]
                if kind < 2:
                    c0, c1 = 8 + b * 2, 9 + b * 2
                    kp = _prev_spill(k)
                    if kp is not None:
                        wt(E, f"wsp{kp}")
                    E.tensor_reduce(sgl[:, c0:c0 + 1], buf[:, 0:w], AX.X, ALU.max,
                                    apply_absolute_value=True)
                    E.drain()
                    E.reciprocal(sgl[:, c1:c1 + 1], sgl[:, c0:c0 + 1])
                    E.drain()
                    E.tensor_scalar(out=sgl[:, c1:c1 + 1], in0=sgl[:, c1:c1 + 1],
                                    scalar1=127.0, scalar2=None, op0=ALU.mult)
                    E.tensor_scalar(out=sgl[:, c0:c0 + 1], in0=sgl[:, c0:c0 + 1],
                                    scalar1=1.0 / 127.0, scalar2=None, op0=ALU.mult)
                    E.drain()
                    E.tensor_scalar(out=wtmp[:, 0:w], in0=buf[:, 0:w],
                                    scalar1=sgl[:, c1:c1 + 1], scalar2=127.49,
                                    op0=ALU.mult, op1=ALU.min)
                    E.drain()
                    E.tensor_scalar(out=wtmpi[:, 0:w], in0=wtmp[:, 0:w],
                                    scalar1=-128.49, scalar2=None, op0=ALU.max)
                    E.drain()
                    E.tensor_scalar(out=wqbf[b][:, 0:w], in0=wtmpi[:, 0:w],
                                    scalar1=sgl[:, c0:c0 + 1], scalar2=None,
                                    op0=ALU.mult).then_inc(d_q.h, 1)
                elif phase == 0:
                    E.tensor_reduce(sgl[:, 13:14], buf[:, 0:w], AX.X, ALU.max,
                                    apply_absolute_value=True)
                    E.drain()
                    if sub == 0:
                        E.tensor_copy(sgl[:, 12:13], sgl[:, 13:14])
                    else:
                        E.tensor_max(sgl[:, 12:13], sgl[:, 12:13], sgl[:, 13:14])
                    E.drain()
                    if sub == 2:
                        E.reciprocal(sgl[:, 14:15], sgl[:, 12:13])
                        E.drain()
                        E.tensor_scalar(out=sgl[:, 14:15], in0=sgl[:, 14:15],
                                        scalar1=127.0, scalar2=None, op0=ALU.mult)
                        E.tensor_scalar(out=sgl[:, 12:13], in0=sgl[:, 12:13],
                                        scalar1=1.0 / 127.0, scalar2=None, op0=ALU.mult)
                        E.drain()
                    E.engine_nop().then_inc(d_q.h, 1)
                else:
                    kp = _prev_spill(k)
                    if kp is not None:
                        wt(E, f"wsp{kp}")
                    E.tensor_scalar(out=wtmp[:, 0:w], in0=buf[:, 0:w],
                                    scalar1=sgl[:, 14:15], scalar2=127.49,
                                    op0=ALU.mult, op1=ALU.min)
                    E.drain()
                    E.tensor_scalar(out=wtmpi[:, 0:w], in0=wtmp[:, 0:w],
                                    scalar1=-128.49, scalar2=None, op0=ALU.max)
                    E.drain()
                    E.tensor_scalar(out=wqbf[b][:, 0:w], in0=wtmpi[:, 0:w],
                                    scalar1=sgl[:, 12:13], scalar2=None,
                                    op0=ALU.mult).then_inc(d_q.h, 1)
            d_q.add(1)
            mark(f"wq_u{k}", d_q)

        kdone = 0
        for k in range(WQ_PROD):
            wq_unit(k)
            kdone += 1

        if em:
            wt(E, "samp_mm")
            E.tensor_reduce(sgl[:, 0:1], ps_pa[:, 0:128], AX.X, ALU.max)
            E.drain()
            E.tensor_scalar(out=junk[:, 0:128], in0=zeros[:],
                            scalar1=sgl[:, 0:1], scalar2=None, op0=ALU.add)
            E.engine_nop().then_inc(d_m.h, 1)
        d_m.add(1)
        mark("mrep", d_m)
        if em:
            wt(E, "mrep_t")
            E.tensor_reduce(sgl[:, 1:2], ps_pb[:, 0:128], AX.X, ALU.max)
            E.drain()
            E.tensor_scalar(out=negm[:], in0=sgl[:, 1:2], scalar1=-1.0 / 32.0,
                            scalar2=-2.0, op0=ALU.mult, op1=ALU.add).then_inc(d_m.h, 1)
        d_m.add(1)
        mark("negm", d_m)

        ps_ot, ps_r = AT.get("ps_ot"), AT.get("ps_r")
        for v in range(CHUNKS):
            for g in range(SC_NSEG[v]):
                qlo = seg_qlo(v, g)
                subs = seg_subs(v, g)
                if g >= SC_S0[v]:
                    for t2 in range(8):
                        STI = STILE_I[(v, g, t2, 0)]
                        if em:
                            if d_mk.n == 0:
                                wt(E, "cfg")
                            E.wait_ge(a_exp.h, STI + 1)
                            E.tensor_tensor(AT["pseg"][:, t2 * 1024:t2 * 1024 + 128],
                                            AT["pseg"][:, t2 * 1024:t2 * 1024 + 128],
                                            masks[:, t2 * 128:(t2 + 1) * 128],
                                            ALU.mult).then_inc(d_mk.h, 1)
                        d_mk.add(1)
                    mark(f"mseg{v}_{g}", d_mk)
                for si in range(seg_ns(v, g)):
                    SW = SWEEP_I[(v, g, si)]
                    par = SW % 2
                    osl = qlo + si - SC_S0[v]
                    first = g == 0
                    if em:
                        E.wait_ge(p_pv.h, SW + 1)
                        if first:
                            E.tensor_copy(AT["osb"][:, osl * D_:(osl + 1) * D_],
                                          ps_ot[:, par * 1024:par * 1024 + 1024])
                            E.drain()
                            E.tensor_copy(AT["rsb"][:, osl:osl + 1],
                                          ps_r[:, par * 512:par * 512 + 1])
                            E.drain()
                        else:
                            E.tensor_add(AT["osb"][:, osl * D_:(osl + 1) * D_],
                                         AT["osb"][:, osl * D_:(osl + 1) * D_],
                                         ps_ot[:, par * 1024:par * 1024 + 1024])
                            E.drain()
                            E.tensor_add(AT["rsb"][:, osl:osl + 1],
                                         AT["rsb"][:, osl:osl + 1],
                                         ps_r[:, par * 512:par * 512 + 1])
                            E.drain()
                        E.engine_nop().then_inc(d_oa.h, 1)
                    d_oa.add(1)
                    assert d_oa.n == SW + 1
                for _ in range(WQ_SEG.get((v, g), 0)):
                    if kdone < NWU:
                        wq_unit(kdone)
                        kdone += 1
                if g >= SC_S0[v]:
                    si = g - SC_S0[v]
                    s = g
                    if em:
                        wt(E, f"xq{s}")
                        if s >= 2:
                            wt(E, f"lnsp_s{s - 2}")
                        E.reciprocal(sgl[:, 2:3], AT["rsb"][:, si:si + 1])
                        E.drain()
                        E.scalar_tensor_tensor(
                            out=AT["resid"][:], in0=AT["osb"][:, si * D_:(si + 1) * D_],
                            scalar=sgl[:, 2:3], in1=xqr[s % 2][:],
                            op0=ALU.mult, op1=ALU.add, accum_out=stat[:, 24:25])
                        E.engine_nop().then_inc(d_ep.h, 1)
                    d_ep.add(1)
                    mark(f"vstt{s}", d_ep)
                    if em:
                        E.drain()
                        E.scalar_tensor_tensor(
                            out=junk[:, 0:D_], in0=AT["resid"][:], scalar=1.0,
                            in1=AT["resid"][:], op0=ALU.mult, op1=ALU.mult,
                            accum_out=stat[:, 25:26])
                        E.drain()
                        E.tensor_scalar(out=stat[:, si:si + 1], in0=stat[:, 24:25],
                                        scalar1=1.0 / D_, scalar2=None, op0=ALU.mult)
                        E.drain()
                        E.tensor_tensor(stat[:, 26:27], stat[:, si:si + 1],
                                        stat[:, si:si + 1], ALU.mult)
                        E.tensor_scalar(out=stat[:, 8 + si:9 + si], in0=stat[:, 25:26],
                                        scalar1=1.0 / D_, scalar2=None, op0=ALU.mult)
                        E.drain()
                        E.tensor_sub(stat[:, 8 + si:9 + si], stat[:, 8 + si:9 + si],
                                     stat[:, 26:27])
                        E.engine_nop().then_inc(d_ep.h, 1)
                    d_ep.add(1)
                    mark(f"var_s{s}", d_ep)
                    if em:
                        wt(E, f"rstd_s{s}")
                        E.tensor_scalar(out=AT["lnst"][:, (s % 2) * D_:(s % 2 + 1) * D_],
                                        in0=AT["resid"][:],
                                        scalar1=stat[:, si:si + 1],
                                        scalar2=stat[:, 16 + si:17 + si],
                                        op0=ALU.subtract, op1=ALU.mult)
                        E.drain()
                        E.tensor_reduce(stat[:, 27:28],
                                        AT["lnst"][:, (s % 2) * D_:(s % 2 + 1) * D_],
                                        AX.X, ALU.max, apply_absolute_value=True)
                        E.drain()
                        E.tensor_tensor(mx_x[:], mx_x[:], stat[:, 27:28], ALU.max)
                        E.engine_nop().then_inc(d_ep.h, 1)
                    d_ep.add(1)
                    mark(f"epi_s{s}", d_ep)
            mark(f"epic{v}", d_ep)
        assert kdone == NWU

    def A_sync(E):
        em = E is not None
        if em:
            E.dma_start(out=identf[:], in_=identf_in).then_inc(s_cfg.h, 16)
            E.dma_start(out=clipx[:], in_=clipx_in).then_inc(s_cfg.h, 16)
            E.dma_start(out=clipd[:], in_=clipd_in).then_inc(s_cfg.h, 16)
            for tt in range(8):
                E.dma_start(out=masks[:, tt * 128:(tt + 1) * 128],
                            in_=masks_in[tt * 128:(tt + 1) * 128, :]).then_inc(s_cfg.h, 16)
        s_cfg.add(16 * 11)
        mark("cfg", s_cfg)


        for v in range(CHUNKS):
            for g in range(SC_NSEG[v]):
                for gi2 in (2 * g, 2 * g + 1):
                    L = LOAD_I[(v, gi2)]
                    j = gi2 // 8
                    r = gi2 % 8
                    if em:
                        wt(E, f"cc_ch{j}")
                        if L < 3:
                            wt(E, "qmm3_7")
                            wt(E, "prod_ev")
                            wt(E, "ksp3")
                            wt(E, "vsp3")
                        else:
                            vp, gp = LOAD_LIST[L - 3]
                            wt(E, f"segdone{vp}_{gp // 2}")
                        E.dma_start(out=AT["kbuf"][L % 3][:],
                                    in_=kall[j].ap()[r * 128:(r + 1) * 128, :]).then_inc(kg[L % 3].h, 16)
                    kg[L % 3].add(16)
                    mark(f"kgl{L}", kg[L % 3])

    def B_sync(E):
        em = E is not None

        def lnrd_load(n):
            if em:
                if n == 0:
                    wt(E, "lnsp_all0")
                    wt(E, "lnsp_all1")
                if n >= 4:
                    wt(E, f"qx{n - 4}q")
                E.dma_start(out=BT["lnrd"][n % 4][:],
                            in_=ln1_sp.ap()[n * 128:(n + 1) * 128, :]).then_inc(s_lnr[n % 4].h, 16)
            s_lnr[n % 4].add(16)
            mark(f"lnrd{n}", s_lnr[n % 4])

        for n in range(4):
            lnrd_load(n)
        if em:
            wt(E, "cc_arx")
            E.dma_start(out=arx_sb[:], in_=ar_o[0].ap()).then_inc(s_arx.h, 16)
        s_arx.add(16)
        mark("arx_ld", s_arx)
        for n in range(4, NQT):
            lnrd_load(n)

    def B_pe(E):
        em = E is not None
        if em:
            wt(E, "bx_rep")
            E.transpose(BT["ps_t"][:, 0:128], junk[:, 0:128], identf[:]).then_inc(p_bc.h, 1)
        p_bc.add(1)
        mark("bx_tr", p_bc)
        for n in range(NQT):
            if em:
                wt(E, f"qx{n}q")
                if n >= 2:
                    wt(E, f"qxev{n - 2}")
                ins = None
                for d in range(HC):
                    ins = E.transpose(
                        BT["ps_xtb"][:, (n % 2) * 1024 + d * 128:(n % 2) * 1024 + (d + 1) * 128],
                        BT["qxst"][n % 2][:, d * 128:(d + 1) * 128], identb[:])
                ins.then_inc(p_bc.h, 1)
            p_bc.add(1)
            mark(f"qxt{n}", p_bc)
        nu = 0
        for icx in range(IC):
            for tc in range(4):
                par = nu % 2
                if em:
                    if nu == 0:
                        wt(E, "wguT")
                    if icx == 0:
                        wt(E, f"qxev{4 * tc + 3}")
                    if nu >= 2:
                        wt(E, f"hdone{nu - 2}")
                    ins = None
                    for d in range(HC):
                        ins = E.matmul(BT["ps_g"][:, par * 512:par * 512 + 512],
                                       BT["wg_T"][:, d * DI_ + icx * 128:d * DI_ + (icx + 1) * 128],
                                       BT["qx_fm"][:, d * T_ + tc * 512:d * T_ + (tc + 1) * 512],
                                       start=d == 0, stop=d == HC - 1)
                    ins.then_inc(p_bc.h, 1)
                p_bc.add(1)
                mark(f"g_mm{nu}", p_bc)
                if em:
                    ins = None
                    for d in range(HC):
                        ins = E.matmul(BT["ps_u"][:, par * 512:par * 512 + 512],
                                       BT["wu_T"][:, d * DI_ + icx * 128:d * DI_ + (icx + 1) * 128],
                                       BT["qx_fm"][:, d * T_ + tc * 512:d * T_ + (tc + 1) * 512],
                                       start=d == 0, stop=d == HC - 1)
                    ins.then_inc(p_bc.h, 1)
                p_bc.add(1)
                mark(f"u_mm{nu}", p_bc)
                nu += 1

    def B_act(E):
        em = E is not None
        for d in range(HC):
            if em:
                if d == 0:
                    wt(E, "wsp_all0")
                    wt(E, "wsp_all1")
                E.dma_start(out=BT["wg_T"][:, d * DI_:(d + 1) * DI_],
                            in_=wq_sp[0].ap()[:, d * 128:(d + 1) * 128],
                            transpose=True).then_inc(s_wgu.h, 16)
            s_wgu.add(16)
        for d in range(HC):
            if em:
                E.dma_start(out=BT["wu_T"][:, d * DI_:(d + 1) * DI_],
                            in_=wq_sp[1].ap()[:, d * 128:(d + 1) * 128],
                            transpose=True).then_inc(s_wgu.h, 16)
            s_wgu.add(16)
        mark("wguT", s_wgu)
        if em:
            wt(E, "bx_tr")
            wt(E, "cfg")
            E.activation(out=sgl[:, 16:17], in_=clipx[:], func=AF.Sigmoid).then_inc(a_m.h, 1)
        a_m.add(1)
        mark("sigx", a_m)
        for n in range(NQT):
            if em:
                wt(E, f"qxt{n}")
                E.activation(
                    out=BT["qx_fm"][:].rearrange("p (h s k) -> p h s k", h=HC, s=NQT)[:, :, n, :],
                    in_=BT["ps_xtb"][:, (n % 2) * 1024:(n % 2) * 1024 + 1024].rearrange(
                        "p (h k) -> p h k", h=HC),
                    func=AF.Copy).then_inc(a_m.h, 1)
            a_m.add(1)
            mark(f"qxev{n}", a_m)
        for nu in range(4 * IC):
            par = nu % 2
            if em:
                wt(E, f"g_mm{nu}")
                if nu >= 2:
                    wt(E, f"hdone{nu - 2}")
                E.activation(out=BT["silu"][par][:],
                             in_=BT["ps_g"][:, par * 512:par * 512 + 512],
                             func=AF.Silu).then_inc(a_m.h, 1)
            a_m.add(1)
            mark(f"silu{nu}", a_m)

    def B_dve(E):
        em = E is not None
        if em:
            wt(E, "arx_ld")
            E.tensor_scalar(out=junk[:, 0:128], in0=zeros[:], scalar1=arx_sb[:],
                            scalar2=None, op0=ALU.add)
            E.engine_nop().then_inc(d_bc.h, 1)
        d_bc.add(1)
        mark("bx_rep", d_bc)
        if em:
            wt(E, "bx_tr")
            E.tensor_reduce(sgl[:, 17:18], BT["ps_t"][:, 0:128], AX.X, ALU.max)
            wt(E, "sigx")
            E.drain()
            E.tensor_tensor(sgl[:, 18:19], sgl[:, 16:17], sgl[:, 17:18], ALU.mult)
            E.drain()
            E.tensor_scalar(out=sgl[:, 18:19], in0=sgl[:, 18:19],
                            scalar1=1.0 / 127.0, scalar2=None, op0=ALU.mult)
            E.drain()
            E.reciprocal(sgl[:, 19:20], sgl[:, 18:19])
            E.engine_nop().then_inc(d_bc.h, 1)
        d_bc.add(1)
        mark("sx", d_bc)

        for n in range(NQT):
            if em:
                if n == 0:
                    wt(E, "sx")
                wt(E, f"lnrd{n}")
                if n >= 2:
                    wt(E, f"qxt{n - 2}")
                E.tensor_scalar(out=junk[:, 0:D_], in0=BT["lnrd"][n % 4][:],
                                scalar1=sgl[:, 19:20], scalar2=127.49,
                                op0=ALU.mult, op1=ALU.min)
                E.drain()
                E.tensor_scalar(out=wtmpi[:, 0:D_], in0=junk[:, 0:D_],
                                scalar1=-128.49, scalar2=None, op0=ALU.max)
                E.drain()
                E.tensor_scalar(out=BT["qxst"][n % 2][:], in0=wtmpi[:, 0:D_],
                                scalar1=sgl[:, 18:19], scalar2=None,
                                op0=ALU.mult).then_inc(d_bc.h, 1)
            d_bc.add(1)
            mark(f"qx{n}q", d_bc)

        for nu in range(4 * IC):
            par = nu % 2
            if em:
                wt(E, f"silu{nu}")
                wt(E, f"u_mm{nu}")
                if nu >= 2:
                    wt(E, f"hsp{nu - 2}")
                E.tensor_tensor(BT["hst"][par][:], BT["silu"][par][:],
                                BT["ps_u"][:, par * 512:par * 512 + 512], ALU.mult)
                E.drain()
                E.tensor_reduce(stat[:, 14:15], BT["hst"][par][:], AX.X, ALU.max,
                                apply_absolute_value=True)
                E.drain()
                E.tensor_tensor(mx_h[:], mx_h[:], stat[:, 14:15], ALU.max)
                E.engine_nop().then_inc(d_bc.h, 1)
            d_bc.add(1)
            mark(f"hdone{nu}", d_bc)

    def B_gpsimd(E):
        em = E is not None
        for nu in range(4 * IC):
            icx, tc = nu // 4, nu % 4
            par = nu % 2
            if em:
                wt(E, f"hdone{nu}")
                E.dma_start(out=h_spd.ap()[:, icx * T_ + tc * 512:icx * T_ + (tc + 1) * 512],
                            in_=BT["hst"][par][:]).then_inc(g_hsp[par].h, 16)
            g_hsp[par].add(16)
            mark(f"hsp{nu}", g_hsp[par])
        mark("hsp_all0", g_hsp[0])
        mark("hsp_all1", g_hsp[1])
        if em:
            wt(E, f"hdone{60}")
            E.collective_compute("AllReduce", ALU.max, replica_groups=RG,
                                 ins=[ar_i[2].ap().opt()],
                                 outs=[ar_o[2].ap().opt()]).then_inc(g_cc.h, 1)
        g_cc.add(1)
        if em:
            wt(E, f"hdone{4 * IC - 1}")
            E.dma_start(out=ar_i[1].ap(), in_=mx_h[:]).then_inc(g_ar.h, 16)
        g_ar.add(16)
        if em:
            E.wait_ge(g_ar.h, g_ar.n)
            E.collective_compute("AllReduce", ALU.max, replica_groups=RG,
                                 ins=[ar_i[1].ap().opt()],
                                 outs=[ar_o[1].ap().opt()]).then_inc(g_cc.h, 1)
        g_cc.add(1)
        mark("cc_arh", g_cc)

    def C_sync(E):
        em = E is not None

        def hld_load(n, EE):
            if EE is not None:
                if n < 2:
                    wt(EE, "hsp_all0")
                    wt(EE, "hsp_all1")
                else:
                    wt(EE, f"hq{n - 2}")
                EE.dma_start(
                    out=CT["hbuf"][n % 2][:].rearrange("p (i k) -> p i k", i=IC),
                    in_=h_spd.ap()[:, :].rearrange("p (i t) -> p i t", i=IC)[:, :, n * 128:(n + 1) * 128],
                ).then_inc(s_hld[n % 2].h, 16)
            s_hld[n % 2].add(16)
            mark(f"hld{n}", s_hld[n % 2])

        C_sync.hld_load = hld_load
        hld_load(0, E)
        if em:
            wt(E, "cc_arh")
            E.dma_start(out=arh_sb[:], in_=ar_o[1].ap()).then_inc(s_arh.h, 16)
        s_arh.add(16)
        mark("arh_ld", s_arh)
        for n in range(1, NQT):
            if n % 2 == 0:
                hld_load(n, E)
            else:
                if em:
                    if n >= 2:
                        wt(E, f"cepi{n - 2}")
                    E.dma_start(out=CT["lnr"][n % 2][:],
                                in_=ln1_sp.ap()[n * 128:(n + 1) * 128, :]).then_inc(s_clnr[n % 2].h, 16)
                s_clnr[n % 2].add(16)
                mark(f"lnr{n}", s_clnr[n % 2])
        if em:
            E.wait_ge(s_out[0].h, 16 * NQT // 2)
            E.wait_ge(s_out[1].h, 16 * NQT // 2)

    def C_pe(E):
        em = E is not None
        if em:
            wt(E, "ch_rep")
            E.transpose(CT["ps_ct"][:, 0:128], junk[:, 0:128], identf[:]).then_inc(p_bc.h, 1)
        p_bc.add(1)
        mark("ch_tr", p_bc)
        for n in range(NQT):
            par = n % 2
            if em:
                if n == 0:
                    wt(E, "qwdT")
                wt(E, f"hq{n}")
                if n >= 2:
                    wt(E, f"cepi{n - 2}")
                ins = None
                for icx in range(IC):
                    for hoh in range(2):
                        ins = E.matmul(
                            CT["ps_d"][:, par * 1024 + hoh * 512:par * 1024 + (hoh + 1) * 512],
                            CT["hq"][par][:, icx * 128:(icx + 1) * 128],
                            CT["qwd_T"][:, icx * D_ + hoh * 512:icx * D_ + (hoh + 1) * 512],
                            start=icx == 0, stop=icx == IC - 1)
                ins.then_inc(p_bc.h, 1)
            p_bc.add(1)
            mark(f"dn{n}", p_bc)

    def C_act(E):
        em = E is not None
        for icx in range(IC):
            if em:
                if icx == 0:
                    wt(E, "wsp_all0")
                    wt(E, "wsp_all1")
                E.dma_start(out=CT["qwd_T"][:, icx * D_:(icx + 1) * D_],
                            in_=wq_sp[2].ap()[:, icx * 128:(icx + 1) * 128],
                            transpose=True).then_inc(s_qwd.h, 16)
            s_qwd.add(16)
        mark("qwdT", s_qwd)
        if em:
            wt(E, "ch_tr")
            E.activation(out=sgl[:, 20:21], in_=clipd[:], func=AF.Sigmoid).then_inc(a_m.h, 1)
        a_m.add(1)
        mark("sigd", a_m)
        def act_load(m, EE):
            if m % 2 == 1:
                C_sync.hld_load(m, EE)
            else:
                if EE is not None:
                    if m == 0:
                        wt(EE, "lnsp_all0")
                        wt(EE, "lnsp_all1")
                    if m >= 2:
                        wt(EE, f"cvar{m - 2}")
                    EE.dma_start(out=CT["lnr"][m % 2][:],
                                 in_=ln1_sp.ap()[m * 128:(m + 1) * 128, :]).then_inc(s_clnr[m % 2].h, 16)
                s_clnr[m % 2].add(16)
                mark(f"lnr{m}", s_clnr[m % 2])

        act_load(0, E)
        act_load(1, E)
        for n in range(NQT):
            if n + 2 < NQT:
                act_load(n + 2, E)
            if em:
                wt(E, f"cvar{n}")
                E.activation(out=stat[:, 4:5], in_=stat[:, 2:3], func=AF.Ln,
                             bias=eps_sb[:], scale=1.0)
                E.drain()
                E.activation(out=stat[:, 6:7], in_=stat[:, 4:5], func=AF.Exp,
                             bias=0.0, scale=-0.5).then_inc(a_m.h, 1)
            a_m.add(1)
            mark(f"crstd{n}", a_m)

    def C_dve(E):
        em = E is not None
        if em:
            wt(E, "arh_ld")
            E.tensor_scalar(out=junk[:, 0:128], in0=zeros[:], scalar1=arh_sb[:],
                            scalar2=None, op0=ALU.add)
            E.engine_nop().then_inc(d_bc.h, 1)
        d_bc.add(1)
        mark("ch_rep", d_bc)
        if em:
            wt(E, "ch_tr")
            E.tensor_reduce(sgl[:, 21:22], CT["ps_ct"][:, 0:128], AX.X, ALU.max)
            wt(E, "sigd")
            E.drain()
            E.tensor_tensor(sgl[:, 22:23], sgl[:, 20:21], sgl[:, 21:22], ALU.mult)
            E.drain()
            E.tensor_scalar(out=sgl[:, 22:23], in0=sgl[:, 22:23],
                            scalar1=1.0 / 127.0, scalar2=None, op0=ALU.mult)
            E.drain()
            E.reciprocal(sgl[:, 23:24], sgl[:, 22:23])
            E.engine_nop().then_inc(d_bc.h, 1)
        d_bc.add(1)
        mark("sh", d_bc)

        for n in range(NQT):
            par = n % 2
            if em:
                if n == 0:
                    wt(E, "sh")
                wt(E, f"hld{n}")
                E.tensor_scalar(out=CT["ctmp"][:], in0=CT["hbuf"][par][:],
                                scalar1=sgl[:, 23:24], scalar2=127.49,
                                op0=ALU.mult, op1=ALU.min)
                E.drain()
                E.tensor_scalar(out=CT["ctmpi"][:], in0=CT["ctmp"][:],
                                scalar1=-128.49, scalar2=None, op0=ALU.max)
                E.drain()
                E.tensor_scalar(out=CT["hq"][par][:], in0=CT["ctmpi"][:],
                                scalar1=sgl[:, 22:23], scalar2=None,
                                op0=ALU.mult).then_inc(d_bc.h, 1)
            d_bc.add(1)
            mark(f"hq{n}", d_bc)
            if em:
                wt(E, f"dn{n}")
                wt(E, f"lnr{n}")
                E.scalar_tensor_tensor(
                    out=CT["r2"][par][:], in0=CT["ps_d"][:, par * 1024:par * 1024 + 1024],
                    scalar=1.0, in1=CT["lnr"][par][:],
                    op0=ALU.mult, op1=ALU.add, accum_out=stat[:, 8:9])
                E.drain()
                E.scalar_tensor_tensor(
                    out=junk[:, 0:D_], in0=CT["r2"][par][:], scalar=1.0,
                    in1=CT["r2"][par][:], op0=ALU.mult, op1=ALU.mult,
                    accum_out=stat[:, 10:11])
                E.drain()
                E.tensor_scalar(out=stat[:, 0:1], in0=stat[:, 8:9],
                                scalar1=1.0 / D_, scalar2=None, op0=ALU.mult)
                E.drain()
                E.tensor_tensor(stat[:, 12:13], stat[:, 0:1], stat[:, 0:1], ALU.mult)
                E.tensor_scalar(out=stat[:, 2:3], in0=stat[:, 10:11],
                                scalar1=1.0 / D_, scalar2=None, op0=ALU.mult)
                E.drain()
                E.tensor_sub(stat[:, 2:3], stat[:, 2:3], stat[:, 12:13])
                E.engine_nop().then_inc(d_bc.h, 1)
            d_bc.add(1)
            mark(f"cvar{n}", d_bc)
            if em:
                wt(E, f"crstd{n}")
                if n >= 2:
                    wt(E, f"out{n - 2}")
                E.tensor_scalar(out=CT["ot"][par][:], in0=CT["r2"][par][:],
                                scalar1=stat[:, 0:1], scalar2=stat[:, 6:7],
                                op0=ALU.subtract, op1=ALU.mult).then_inc(d_bc.h, 1)
            d_bc.add(1)
            mark(f"cepi{n}", d_bc)

    def C_gpsimd(E):
        em = E is not None
        for n in range(NQT):
            if em:
                wt(E, f"cepi{n}")
                E.dma_start(out=out_t.ap()[n * 128:(n + 1) * 128, :],
                            in_=CT["ot"][n % 2][:]).then_inc(s_out[n % 2].h, 16)
            s_out[n % 2].add(16)
            mark(f"out{n}", s_out[n % 2])

    for _k in ["kbuf", "vbuf", "pseg", "osb", "rsb", "resid", "lnst",
               "ps_s", "ps_ot", "ps_r"]:
        AT[_k] = None

    A_gpsimd(None)
    A_pe(None)
    A_act(None)
    A_dve(None)
    A_sync(None)
    B_sync(None)
    B_pe(None)
    B_act(None)
    B_dve(None)
    B_gpsimd(None)
    C_sync(None)
    C_pe(None)
    C_act(None)
    C_dve(None)
    C_gpsimd(None)
    dry = {id(c): c.n for c in ctrs}
    for c in ctrs:
        c.n = 0

    es_pr.close()
    es_a2 = ExitStack()
    AT["kbuf"] = [es_a2.enter_context(nc.sbuf_tensor(f"kbuf{i}", [128, HC * 512], BF16)) for i in range(3)]
    AT["vbuf"] = [es_a2.enter_context(nc.sbuf_tensor(f"vbuf{i}", [128, 4 * D_], BF16)) for i in range(3)]
    AT["pseg"] = es_a2.enter_context(nc.sbuf_tensor("pseg", [128, 8 * 1024], BF16))
    AT["osb"] = es_a2.enter_context(nc.sbuf_tensor("osb", [128, 8 * D_], F32))
    AT["rsb"] = es_a2.enter_context(nc.sbuf_tensor("rsb", [128, 8], F32))
    AT["resid"] = es_a2.enter_context(nc.sbuf_tensor("resid", [128, D_], F32))
    AT["lnst"] = es_a2.enter_context(nc.sbuf_tensor("lnst", [128, 2 * D_], F32))
    AT["ps_s"] = es_a2.enter_context(nc.psum_tensor("ps_s", [128, 1024], F32))
    AT["ps_ot"] = es_a2.enter_context(nc.psum_tensor("ps_ot", [128, 2048], F32))
    AT["ps_r"] = es_a2.enter_context(nc.psum_tensor("ps_r", [128, 1024], F32))

    with nc.Block("A") as blk:
        @blk.gpsimd
        def _(E):
            A_gpsimd(E)

        @blk.tensor
        def _(E):
            A_pe(E)

        @blk.scalar
        def _(E):
            A_act(E)

        @blk.vector
        def _(E):
            A_dve(E)

        @blk.sync
        def _(E):
            A_sync(E)

    es_a2.close()
    es_a1.close()
    es_b = ExitStack()
    BT["qx_fm"] = es_b.enter_context(nc.sbuf_tensor("qx_fm", [128, HC * T_], BF16))
    BT["wg_T"] = es_b.enter_context(nc.sbuf_tensor("wg_T", [128, HC * DI_], BF16))
    BT["wu_T"] = es_b.enter_context(nc.sbuf_tensor("wu_T", [128, HC * DI_], BF16))
    BT["silu"] = [es_b.enter_context(nc.sbuf_tensor(f"silu{i}", [128, 512], F32)) for i in range(2)]
    BT["hst"] = [es_b.enter_context(nc.sbuf_tensor(f"hst{i}", [128, 512], F32)) for i in range(2)]
    BT["qxst"] = [es_b.enter_context(nc.sbuf_tensor(f"qxst{i}", [128, D_], BF16)) for i in range(2)]
    BT["lnrd"] = [es_b.enter_context(nc.sbuf_tensor(f"lnrd{i}", [128, D_], F32)) for i in range(4)]
    BT["ps_xtb"] = es_b.enter_context(nc.psum_tensor("ps_xtb", [128, 2048], BF16))
    BT["ps_g"] = es_b.enter_context(nc.psum_tensor("ps_g", [128, 1024], F32))
    BT["ps_u"] = es_b.enter_context(nc.psum_tensor("ps_u", [128, 1024], F32))
    BT["ps_t"] = es_b.enter_context(nc.psum_tensor("ps_t", [128, 512], F32))

    with nc.Block("B") as blk:
        @blk.gpsimd
        def _(E):
            B_gpsimd(E)

        @blk.tensor
        def _(E):
            B_pe(E)

        @blk.scalar
        def _(E):
            B_act(E)

        @blk.vector
        def _(E):
            B_dve(E)

        @blk.sync
        def _(E):
            B_sync(E)

    es_b.close()
    es_c = ExitStack()
    CT["qwd_T"] = es_c.enter_context(nc.sbuf_tensor("qwd_T", [128, IC * D_], BF16))
    CT["hbuf"] = [es_c.enter_context(nc.sbuf_tensor(f"hbuf{i}", [128, DI_], F32)) for i in range(2)]
    CT["hq"] = [es_c.enter_context(nc.sbuf_tensor(f"hq{i}", [128, DI_], BF16)) for i in range(2)]
    CT["lnr"] = [es_c.enter_context(nc.sbuf_tensor(f"lnr{i}", [128, D_], F32)) for i in range(2)]
    CT["r2"] = [es_c.enter_context(nc.sbuf_tensor(f"r2_{i}", [128, D_], F32)) for i in range(2)]
    CT["ot"] = [es_c.enter_context(nc.sbuf_tensor(f"ot{i}", [128, D_], F32)) for i in range(2)]
    CT["ctmp"] = es_c.enter_context(nc.sbuf_tensor("ctmp", [128, DI_], F32))
    CT["ctmpi"] = es_c.enter_context(nc.sbuf_tensor("ctmpi", [128, DI_], I32))
    CT["ps_d"] = es_c.enter_context(nc.psum_tensor("ps_d", [128, 2048], F32))
    CT["ps_ct"] = es_c.enter_context(nc.psum_tensor("ps_ct", [128, 512], F32))

    with nc.Block("C") as blk:
        @blk.gpsimd
        def _(E):
            C_gpsimd(E)

        @blk.tensor
        def _(E):
            C_pe(E)

        @blk.scalar
        def _(E):
            C_act(E)

        @blk.vector
        def _(E):
            C_dve(E)

        @blk.sync
        def _(E):
            C_sync(E)

    for c in ctrs:
        assert c.n == dry[id(c)], (c.name, c.n, dry[id(c)])

    es_c.close()
    es.close()
    return nc


_NC_CACHE = {}


def _get_nc():
    if "nc" not in _NC_CACHE:
        _NC_CACHE["nc"] = build_nc()
    return _NC_CACHE["nc"]


def kernel(L0_attn_embeddings, L0_attn_W_Q, L0_attn_W_K, L0_attn_W_V,
           L0_ffn_clip_x, L0_ffn_clip_down, L0_ffn_w_gate, L0_ffn_w_up,
           L0_ffn_w_down):
    x = np.asarray(L0_attn_embeddings, dtype=np.float32).reshape(S_TOK, D_)
    wq = np.ascontiguousarray(np.asarray(L0_attn_W_Q, dtype=np.float32))
    wk = np.ascontiguousarray(np.asarray(L0_attn_W_K, dtype=np.float32))
    wv = np.ascontiguousarray(np.asarray(L0_attn_W_V, dtype=np.float32))
    wg = np.ascontiguousarray(np.asarray(L0_ffn_w_gate, dtype=np.float32))
    wu = np.ascontiguousarray(np.asarray(L0_ffn_w_up, dtype=np.float32))
    wd = np.ascontiguousarray(np.asarray(L0_ffn_w_down, dtype=np.float32))
    cx = float(np.asarray(L0_ffn_clip_x).reshape(-1)[0])
    cd = float(np.asarray(L0_ffn_clip_down).reshape(-1)[0])
    cx_t = np.full((128, 1), cx, dtype=np.float32)
    cd_t = np.full((128, 1), cd, dtype=np.float32)
    identf = np.eye(128, dtype=np.float32)
    identb = np.eye(128, dtype=ml_dtypes.bfloat16)

    in_maps = []
    tri = np.triu(np.ones((128, 128), dtype=np.float32)).astype(ml_dtypes.bfloat16)
    one_m = np.ones((128, 128), dtype=ml_dtypes.bfloat16)
    zero_m = np.zeros((128, 128), dtype=ml_dtypes.bfloat16)
    for c in range(NCORES):
        qrows = np.concatenate(
            [x[(8 * s + c) * 128:(8 * s + c + 1) * 128] for s in range(NQT)], axis=0)
        kvrows = np.concatenate(
            [x[(8 * j + c) * 512:(8 * j + c + 1) * 512] for j in range(NKV)], axis=0)
        mk = np.empty((8, 128, 128), dtype=ml_dtypes.bfloat16)
        for w in range(8):
            mk[w] = one_m if w < c else (tri if w == c else zero_m)
        in_maps.append({
            "xq": np.ascontiguousarray(qrows),
            "xkv": np.ascontiguousarray(kvrows),
            "wq": wq, "wk": wk, "wv": wv,
            "wg": wg, "wu": wu, "wd": wd,
            "clipx": cx_t, "clipd": cd_t,
            "masks": np.ascontiguousarray(mk.reshape(8 * 128, 128)),
            "identf": identf, "identb": identb,
        })

    nc = _get_nc()
    res = run_bass_kernel_spmd(nc, in_maps, core_ids=list(range(NCORES)))

    out = np.empty((S_TOK, D_), dtype=np.float32)
    for c in range(NCORES):
        oc = res.results[c]["out"]
        for s in range(NQT):
            out[(8 * s + c) * 128:(8 * s + c + 1) * 128] = oc[s * 128:(s + 1) * 128]
    return out.reshape(1, S_TOK, D_)


# version 11
# speedup vs baseline: 1.0650x; 1.0650x over previous
from contextlib import ExitStack

import numpy as np
import ml_dtypes
import concourse.bass as bass
import concourse.mybir as mybir
from concourse.bass_utils import run_bass_kernel_spmd

F32 = mybir.dt.float32
BF16 = mybir.dt.bfloat16
I32 = mybir.dt.int32
AF = mybir.ActivationFunctionType
ALU = mybir.AluOpType
AX = mybir.AxisListType

NCORES = 8
S_TOK = 16384
D_ = 1024
DI_ = 2816
T_ = 2048
NQT = 16
NPAIR = 8
NKV = 4
NCH = 4
HC = 8
IC = 22
EPS = 1e-5
NWU = 2 * IC + 8

PAIR_T = [16 * (u + 1) for u in range(NPAIR)]
CUM_T = [0]
for _u in range(NPAIR):
    CUM_T.append(CUM_T[-1] + PAIR_T[_u])
PAIR_G = [4 * (u + 1) for u in range(NPAIR)]
CUM_G = [0]
for _u in range(NPAIR):
    CUM_G.append(CUM_G[-1] + PAIR_G[_u])

WQ_PROD = 0
PROD_STAGES = []
for _j in range(4):
    PROD_STAGES.append(("kv", _j))
    PROD_STAGES.append(("q", _j))
PROD_TILES = []
for _st, _i in PROD_STAGES:
    for _k in range(4):
        PROD_TILES.append((_st, 4 * _i + _k))
PT_I = {t: i for i, t in enumerate(PROD_TILES)}
WD_SUBW = [1024, 1024, 768]
WU_KIND = ([(0, i, 0, 1) for i in range(IC)] + [(1, i, 0, 1) for i in range(IC)]
           + [(2, rb, s, p) for rb in range(8) for p in (0, 1) for s in range(3)])
NWU = len(WU_KIND)

CHUNKS = 2
SC_S0 = [0, 8]
SC_NSEG = [8, 16]
SEG_LIST = [(v, g) for v in range(CHUNKS) for g in range(SC_NSEG[v])]
SEG_I = {key: i for i, key in enumerate(SEG_LIST)}


def seg_qlo(v, g):
    return max(SC_S0[v], g)


def seg_ns(v, g):
    return SC_S0[v] + 8 - seg_qlo(v, g)


def seg_subs(v, g):
    W = seg_ns(v, g) * 128
    subs = []
    off = 0
    while off < W:
        w = min(512, W - off)
        subs.append((off, w))
        off += w
    return subs


STILE_LIST = [(v, g, t2, si2) for (v, g) in SEG_LIST for t2 in range(8)
              for si2 in range(len(seg_subs(v, g)))]
STILE_I = {key: i for i, key in enumerate(STILE_LIST)}
SWEEP_LIST = [(v, g, si) for (v, g) in SEG_LIST for si in range(seg_ns(v, g))]
SWEEP_I = {key: i for i, key in enumerate(SWEEP_LIST)}
LOAD_LIST = [(v, gi2) for v in range(CHUNKS) for gi2 in range(2 * SC_NSEG[v])]
LOAD_I = {key: i for i, key in enumerate(LOAD_LIST)}

WQ_SEG = {}
_k = WQ_PROD
_segs = list(SEG_LIST)
_i = 0
while _k < NWU:
    _sg = _segs[2 + _i % (len(_segs) - 2)]
    WQ_SEG[_sg] = WQ_SEG.get(_sg, 0) + 1
    _k += 1
    _i += 1



def _wu_w(k):
    kind, chunk, sub, phase = WU_KIND[k]
    return D_ if kind < 2 else WD_SUBW[sub]


def _wu_spills(k):
    kind, chunk, sub, phase = WU_KIND[k]
    return kind < 2 or phase == 1


def _prev_spill(k):
    for k2 in range(k - 1, -1, -1):
        if _wu_spills(k2) and k2 % 2 == k % 2:
            return k2
    return None


def ti(u, t):
    return CUM_T[u] + t


def gi(u, g):
    return CUM_G[u] + g


INV_G = {}
for _u in range(NPAIR):
    for _g in range(PAIR_G[_u]):
        INV_G[gi(_u, _g)] = (_u, _g)


def mki(u, t):
    return 16 * u + (t - 16 * u)


class Ctr:
    def __init__(self, es, nc, name):
        self.h = es.enter_context(nc.semaphore(name))
        self.name = name
        self.n = 0

    def add(self, k=1):
        self.n += k
        return self.n


def build_nc():
    nc = bass.Bass("TRN2", target_bir_lowering=False, debug=False)

    xq_in = nc.dram_tensor("xq", [T_, D_], F32, kind="ExternalInput").ap()
    xkv_in = nc.dram_tensor("xkv", [T_, D_], F32, kind="ExternalInput").ap()
    wq_in = nc.dram_tensor("wq", [D_, D_], F32, kind="ExternalInput").ap()
    wk_in = nc.dram_tensor("wk", [D_, D_], F32, kind="ExternalInput").ap()
    wv_in = nc.dram_tensor("wv", [D_, D_], F32, kind="ExternalInput").ap()
    wg_in = nc.dram_tensor("wg", [DI_, D_], F32, kind="ExternalInput").ap()
    wu_in = nc.dram_tensor("wu", [DI_, D_], F32, kind="ExternalInput").ap()
    wd_in = nc.dram_tensor("wd", [D_, DI_], F32, kind="ExternalInput").ap()
    clipx_in = nc.dram_tensor("clipx", [128, 1], F32, kind="ExternalInput").ap()
    clipd_in = nc.dram_tensor("clipd", [128, 1], F32, kind="ExternalInput").ap()
    masks_in = nc.dram_tensor("masks", [8 * 128, 128], BF16, kind="ExternalInput").ap()
    identf_in = nc.dram_tensor("identf", [128, 128], F32, kind="ExternalInput").ap()
    identb_in = nc.dram_tensor("identb", [128, 128], BF16, kind="ExternalInput").ap()
    out_t = nc.dram_tensor("out", [T_, D_], F32, kind="ExternalOutput")

    ksrc = [nc.dram_tensor(f"ksrc{j}", [128, HC * 512], BF16) for j in range(NCH)]
    vsrc = [nc.dram_tensor(f"vsrc{j}", [128, 4 * D_], BF16) for j in range(NCH)]
    kall = [nc.dram_tensor(f"kall{j}", [NCORES * 128, HC * 512], BF16,
                           addr_space="Shared") for j in range(NCH)]
    vall = [nc.dram_tensor(f"vall{j}", [NCORES * 128, 4 * D_], BF16,
                           addr_space="Shared") for j in range(NCH)]
    wq_sp = [nc.dram_tensor("wgq_s", [DI_, D_], BF16),
             nc.dram_tensor("wuq_s", [DI_, D_], BF16),
             nc.dram_tensor("wdq_s", [D_, DI_], BF16)]
    ln1_sp = nc.dram_tensor("ln1_s", [T_, D_], F32)
    h_spd = nc.dram_tensor("h_s", [128, IC * T_], F32)
    ar_i = [nc.dram_tensor(f"ar_i{i}", [128, 1], F32) for i in range(3)]
    ar_o = [nc.dram_tensor(f"ar_o{i}", [128, 1], F32) for i in range(3)]

    RG = [list(range(NCORES))]
    es = ExitStack()

    def sbg(name, shape, dt):
        return es.enter_context(nc.sbuf_tensor(name, shape, dt))

    identf = sbg("identf_sb", [128, 128], F32)
    identb = sbg("identb_sb", [128, 128], BF16)
    zeros = sbg("zeros_sb", [128, 128], F32)
    ones_bf = sbg("ones_bf", [128, 1], BF16)
    clipx = sbg("clipx_sb", [128, 1], F32)
    clipd = sbg("clipd_sb", [128, 1], F32)
    sgl = sbg("sgl", [128, 32], F32)
    stat = sbg("stat", [128, 32], F32)
    mx_x = sbg("mx_x", [128, 1], F32)
    mx_h = sbg("mx_h", [128, 1], F32)
    junk = sbg("junk", [128, D_], F32)
    wtmp = sbg("wtmp", [128, D_], F32)
    wtmpi = sbg("wtmpi", [128, D_], I32)
    wqbf = [sbg(f"wqbf{i}", [128, D_], BF16) for i in range(2)]
    negm = sbg("negm_sb", [128, 1], F32)
    eps_sb = sbg("eps_sb", [128, 1], F32)
    arx_sb = sbg("arx_sb", [128, 1], F32)
    arh_sb = sbg("arh_sb", [128, 1], F32)

    es_a1 = ExitStack()

    def sba1(name, shape, dt):
        return es_a1.enter_context(nc.sbuf_tensor(name, shape, dt))

    masks = sba1("masks_sb", [128, 8 * 128], BF16)
    q_fm = sba1("q_fm", [128, HC * T_], BF16)
    ksamp = sba1("ksamp", [128, HC * 128], BF16)
    xqr = [sba1(f"xqr{i}", [128, D_], F32) for i in range(2)]
    wraw = [sba1(f"wraw{i}", [128, D_], F32) for i in range(2)]
    kstage = sba1("kstage", [128, HC * 512], BF16)
    vstage = sba1("vstage", [128, 4 * D_], BF16)

    es_pr = ExitStack()
    xstage = [es_pr.enter_context(nc.sbuf_tensor(f"xstage{i}", [128, D_], BF16)) for i in range(2)]
    xfm = [es_pr.enter_context(nc.sbuf_tensor(f"xfm{i}", [128, HC * 512], BF16)) for i in range(2)]
    wkv_bf = [es_pr.enter_context(nc.sbuf_tensor(f"wkv{i}", [128, HC * D_], BF16)) for i in range(2)]
    wq_bf = es_pr.enter_context(nc.sbuf_tensor("wq_bf", [128, HC * D_], BF16))
    ps_xt = es_pr.enter_context(nc.psum_tensor("ps_xt", [128, 2048], BF16))
    ps_pa = es_pr.enter_context(nc.psum_tensor("ps_pa", [128, 512], F32))
    ps_pb = es_pr.enter_context(nc.psum_tensor("ps_pb", [128, 512], F32))
    ps_v = es_pr.enter_context(nc.psum_tensor("ps_v", [128, 2048], F32))

    AT = {}

    BT = {}
    CT = {}

    gx = [Ctr(es, nc, f"gx{i}") for i in range(2)]
    g_id = Ctr(es, nc, "g_id")
    gw_k = Ctr(es, nc, "gw_k")
    gw_v = Ctr(es, nc, "gw_v")
    gw_q = Ctr(es, nc, "gw_q")
    g_ksp = Ctr(es, nc, "g_ksp")
    g_vsp = Ctr(es, nc, "g_vsp")
    g_wsp = [Ctr(es, nc, f"g_wsp{i}") for i in range(2)]
    g_lnsp = [Ctr(es, nc, f"g_lnsp{i}") for i in range(2)]
    g_hsp = [Ctr(es, nc, f"g_hsp{i}") for i in range(2)]
    g_ar = Ctr(es, nc, "g_ar")
    g_cc = Ctr(es, nc, "g_cc")
    s_cfg = Ctr(es, nc, "s_cfg")
    kg = [Ctr(es, nc, f"kg{i}") for i in range(3)]
    vg = [Ctr(es, nc, f"vg{i}") for i in range(3)]
    d_oa = Ctr(es, nc, "d_oa")
    sxq = [Ctr(es, nc, f"sxq{i}") for i in range(2)]
    sw = [Ctr(es, nc, f"sw{i}") for i in range(2)]
    s_arx = Ctr(es, nc, "s_arx")
    s_lnr = [Ctr(es, nc, f"s_lnr{i}") for i in range(4)]
    s_wgu = Ctr(es, nc, "s_wgu")
    s_arh = Ctr(es, nc, "s_arh")
    s_qwd = Ctr(es, nc, "s_qwd")
    s_hld = [Ctr(es, nc, f"s_hld{i}") for i in range(2)]
    s_clnr = [Ctr(es, nc, f"s_clnr{i}") for i in range(4)]
    s_out = [Ctr(es, nc, f"s_out{i}") for i in range(2)]
    p_pr = Ctr(es, nc, "p_pr")
    p_s = Ctr(es, nc, "p_s")
    p_pv = Ctr(es, nc, "p_pv")
    p_m = Ctr(es, nc, "p_m")
    p_bc = Ctr(es, nc, "p_bc")
    a_ev = Ctr(es, nc, "a_ev")
    a_exp = Ctr(es, nc, "a_exp")
    a_m = Ctr(es, nc, "a_m")
    d_mk = Ctr(es, nc, "d_mk")
    d_ep = Ctr(es, nc, "d_ep")
    d_q = Ctr(es, nc, "d_q")
    d_m = Ctr(es, nc, "d_m")
    d_bc = Ctr(es, nc, "d_bc")
    ctrs = (gx + [g_id, gw_k, gw_v, gw_q, g_ksp, g_vsp] + g_wsp + g_lnsp
            + g_hsp + [g_ar, g_cc, s_cfg] + kg + vg + sxq + sw
            + [s_arx] + s_lnr + [s_wgu, s_arh, s_qwd] + s_hld + s_clnr
            + s_out + [p_pr, p_s, p_pv, p_m, p_bc, a_ev, a_exp, a_m, d_mk,
                       d_ep, d_q, d_m, d_bc, d_oa])

    CK = {}

    def mark(name, ctr):
        if name in CK:
            assert CK[name] == (ctr, ctr.n), (name, CK[name][1], ctr.n)
        CK[name] = (ctr, ctr.n)

    def wt(eng, name):
        ctr, val = CK[name]
        if val > 0:
            eng.wait_ge(ctr.h, val)


    def A_gpsimd(E):
        em = E is not None
        if em:
            E.memset(sgl[:, 31:32], 0.0)
            E.drain()
            E.dma_start(out=ar_i[2].ap(), in_=sgl[:, 31:32]).then_inc(g_ar.h, 16)
        g_ar.add(16)
        if em:
            E.wait_ge(g_ar.h, g_ar.n)

        if em:
            E.dma_start(out=identb[:], in_=identb_in).then_inc(g_id.h, 16)
        g_id.add(16)
        mark("identb", g_id)

        def xload(i):
            st, m = PROD_TILES[i]
            srcten = xkv_in if st == "kv" else xq_in
            if em:
                if i >= 2:
                    wt(E, f"pt_t{i - 2}")
                E.dma_start(out=xstage[i % 2][:],
                            in_=srcten[m * 128:(m + 1) * 128, :]).then_inc(gx[i % 2].h, 16)
            gx[i % 2].add(16)
            mark(f"ldt{i}", gx[i % 2])

        for i in range(4):
            xload(i)
        for d in range(HC):
            if em:
                E.dma_start(out=wkv_bf[0][:, d * D_:(d + 1) * D_],
                            in_=wk_in[d * 128:(d + 1) * 128, :]).then_inc(gw_k.h, 16)
            gw_k.add(16)
        mark("wk_all", gw_k)
        for d in range(HC):
            if em:
                E.dma_start(out=wkv_bf[1][:, d * D_:(d + 1) * D_],
                            in_=wv_in[d * 128:(d + 1) * 128, :]).then_inc(gw_v.h, 16)
            gw_v.add(16)
        mark("wv_all", gw_v)
        for d in range(HC):
            if em:
                E.dma_start(out=wq_bf[:, d * D_:(d + 1) * D_],
                            in_=wq_in[d * 128:(d + 1) * 128, :]).then_inc(gw_q.h, 16)
            gw_q.add(16)
        mark("wq_all", gw_q)

        def wraw_load(k):
            kind, chunk, sub, phase = WU_KIND[k]
            w = _wu_w(k)
            b = k % 2
            if em:
                if k >= 2:
                    wt(E, f"wq_u{k - 2}")
                if kind < 2:
                    ap_src = [wg_in, wu_in][kind][chunk * 128:(chunk + 1) * 128, :]
                else:
                    colo = sum(WD_SUBW[:sub])
                    ap_src = wd_in[chunk * 128:(chunk + 1) * 128, colo:colo + w]
                E.dma_start(out=wraw[b][:, 0:w], in_=ap_src).then_inc(sw[b].h, 16)
            sw[b].add(16)
            mark(f"wl{k}", sw[b])

        def spill_k(j):
            if em:
                wt(E, f"kev{j}")
                E.dma_start(out=ksrc[j].ap(), in_=kstage[:]).then_inc(g_ksp.h, 16)
            g_ksp.add(16)
            mark(f"ksp{j}", g_ksp)
            if em:
                E.wait_ge(g_ksp.h, g_ksp.n)
                E.collective_compute("AllGather", ALU.bypass, replica_groups=RG,
                                     ins=[ksrc[j].ap().opt()],
                                     outs=[kall[j].ap().opt()]).then_inc(g_cc.h, 1)
            g_cc.add(1)
            mark(f"cc_k{j}", g_cc)

        def spill_v(j):
            if em:
                wt(E, f"vev{j}")
                E.dma_start(out=vsrc[j].ap(), in_=vstage[:]).then_inc(g_vsp.h, 16)
            g_vsp.add(16)
            mark(f"vsp{j}", g_vsp)
            if em:
                E.wait_ge(g_vsp.h, g_vsp.n)
                E.collective_compute("AllGather", ALU.bypass, replica_groups=RG,
                                     ins=[vsrc[j].ap().opt()],
                                     outs=[vall[j].ap().opt()]).then_inc(g_cc.h, 1)
            g_cc.add(1)
            mark(f"cc_ch{j}", g_cc)

        for i in range(4, 8):
            xload(i)
        spill_k(0)
        for i in range(8, 12):
            xload(i)
        spill_v(0)
        for i in range(12, 16):
            xload(i)
        spill_k(1)
        spill_v(1)
        for i in range(16, 20):
            xload(i)
        for i in range(20, 24):
            xload(i)
        spill_k(2)
        spill_v(2)
        for i in range(24, 28):
            xload(i)
        for i in range(28, 32):
            xload(i)
        spill_k(3)
        spill_v(3)

        def wspill(k):
            if not _wu_spills(k):
                return
            kind, chunk, sub, phase = WU_KIND[k]
            w = _wu_w(k)
            b = k % 2
            if em:
                wt(E, f"wq_u{k}")
                if kind < 2:
                    dst = wq_sp[kind].ap()[chunk * 128:(chunk + 1) * 128, :]
                else:
                    colo = sum(WD_SUBW[:sub])
                    dst = wq_sp[2].ap()[chunk * 128:(chunk + 1) * 128, colo:colo + w]
                E.dma_start(out=dst, in_=wqbf[b][:, 0:w]).then_inc(g_wsp[b].h, 16)
            g_wsp[b].add(16)
            mark(f"wsp{k}", g_wsp[b])

        kld_done = 0
        ksp_done = 0

        def wq_step(n):
            nonlocal kld_done, ksp_done
            for _ in range(n):
                if kld_done < NWU:
                    wraw_load(kld_done)
                    kld_done += 1
                if kld_done >= 3 and ksp_done < kld_done - 3:
                    wspill(ksp_done)
                    ksp_done += 1

        wq_step(WQ_PROD)
        for v in range(CHUNKS):
            for g in range(SC_NSEG[v]):
                wq_step(WQ_SEG.get((v, g), 0))
                if (v, g) in ((1, 0), (1, 5), (1, 10)):
                    if em:
                        wt(E, f"segdone{v}_{g}")
                        E.collective_compute("AllReduce", ALU.max, replica_groups=RG,
                                             ins=[ar_i[2].ap().opt()],
                                             outs=[ar_o[2].ap().opt()]).then_inc(g_cc.h, 1)
                    g_cc.add(1)
                if g >= SC_S0[v]:
                    while ksp_done < kld_done:
                        wspill(ksp_done)
                        ksp_done += 1
                    s = g
                    if em:
                        wt(E, f"epi_s{s}")
                        E.dma_start(out=ln1_sp.ap()[s * 128:(s + 1) * 128, :],
                                    in_=AT["lnst"][:, (s % 2) * D_:(s % 2 + 1) * D_]).then_inc(g_lnsp[s % 2].h, 16)
                    g_lnsp[s % 2].add(16)
                    mark(f"lnsp_s{s}", g_lnsp[s % 2])
        while ksp_done < NWU:
            wspill(ksp_done)
            ksp_done += 1
        assert ksp_done == NWU and kld_done == NWU
        mark("lnsp_all0", g_lnsp[0])
        mark("lnsp_all1", g_lnsp[1])
        mark("wsp_all0", g_wsp[0])
        mark("wsp_all1", g_wsp[1])
        if em:
            wt(E, "epic1")
            E.dma_start(out=ar_i[0].ap(), in_=mx_x[:]).then_inc(g_ar.h, 16)
        g_ar.add(16)
        if em:
            E.wait_ge(g_ar.h, g_ar.n)
            E.collective_compute("AllReduce", ALU.max, replica_groups=RG,
                                 ins=[ar_i[0].ap().opt()],
                                 outs=[ar_o[0].ap().opt()]).then_inc(g_cc.h, 1)
        g_cc.add(1)
        mark("cc_arx", g_cc)

    def A_pe(E):
        em = E is not None

        def xtile(i):
            if em:
                wt(E, f"ldt{i}")
                if i < 2:
                    wt(E, "identb")
                else:
                    wt(E, f"ptev{i - 2}")
                ins = None
                for d in range(HC):
                    ins = E.transpose(
                        ps_xt[:, (i % 2) * 1024 + d * 128:(i % 2) * 1024 + (d + 1) * 128],
                        xstage[i % 2][:, d * 128:(d + 1) * 128], identb[:])
                ins.then_inc(p_pr.h, 1)
            p_pr.add(1)
            mark(f"pt_t{i}", p_pr)

        for st, idx in PROD_STAGES:
            base = PT_I[(st, 4 * idx)]
            for k in range(4):
                xtile(base + k)
            if st == "kv":
                j = idx
                if em:
                    wt(E, "wk_all")
                    wt(E, f"xfmkv{j}")
                for d in range(HC):
                    if em:
                        if d >= 2:
                            wt(E, f"kevd{j}_{d - 2}")
                        if j == 1 and d < 2:
                            wt(E, ["mrep", "negm"][d])
                        pb = [ps_pa, ps_pb][d % 2]
                        ins = None
                        for hch in range(HC):
                            ins = E.matmul(pb[:, 0:512],
                                           wkv_bf[0][:, hch * D_ + d * 128:hch * D_ + (d + 1) * 128],
                                           xfm[0][:, hch * 512:(hch + 1) * 512],
                                           start=hch == 0, stop=hch == HC - 1)
                        ins.then_inc(p_pr.h, 1)
                    p_pr.add(1)
                    mark(f"kmm{j}_{d}", p_pr)
                if em:
                    wt(E, "wv_all")
                for ksub in range(4):
                    if em:
                        if ksub >= 2:
                            wt(E, f"vevk{j}_{ksub - 2}")
                        vb0 = (ksub % 2) * 1024
                        ins = None
                        for hch in range(HC):
                            for dvh in range(2):
                                ins = E.matmul(
                                    ps_v[:, vb0 + dvh * 512:vb0 + (dvh + 1) * 512],
                                    xfm[0][:, hch * 512 + ksub * 128:hch * 512 + ksub * 128 + 128],
                                    wkv_bf[1][:, hch * D_ + dvh * 512:hch * D_ + (dvh + 1) * 512],
                                    start=hch == 0, stop=hch == HC - 1)
                        ins.then_inc(p_pr.h, 1)
                    p_pr.add(1)
                    mark(f"vmm{j}_{ksub}", p_pr)
            else:
                r = idx
                if em:
                    wt(E, "wq_all")
                    wt(E, f"xfmq{r}")
                for d in range(HC):
                    if em:
                        if d >= 2:
                            wt(E, f"qevd{r}_{d - 2}")
                        pb = [ps_pa, ps_pb][d % 2]
                        ins = None
                        for hch in range(HC):
                            ins = E.matmul(pb[:, 0:512],
                                           wq_bf[:, hch * D_ + d * 128:hch * D_ + (d + 1) * 128],
                                           xfm[1][:, hch * 512:(hch + 1) * 512],
                                           start=hch == 0, stop=hch == HC - 1)
                        ins.then_inc(p_pr.h, 1)
                    p_pr.add(1)
                    mark(f"qmm{r}_{d}", p_pr)
                if r == 0:
                    if em:
                        wt(E, "ksampc")
                        wt(E, "qevd0_7")
                        ins = None
                        for d in range(HC):
                            ins = E.matmul(ps_pa[:, 0:128], ksamp[:, d * 128:(d + 1) * 128],
                                           q_fm[:, d * T_:d * T_ + 128],
                                           start=d == 0, stop=d == HC - 1)
                        ins.then_inc(p_m.h, 1)
                    p_m.add(1)
                    mark("samp_mm", p_m)
                    if em:
                        wt(E, "mrep")
                        wt(E, "cfg")
                        E.transpose(ps_pb[:, 0:128], junk[:, 0:128], identf[:]).then_inc(p_m.h, 1)
                    p_m.add(1)
                    mark("mrep_t", p_m)

        ps_s, ps_ot, ps_r = AT["ps_s"], AT["ps_ot"], AT["ps_r"]

        for v in range(CHUNKS):
            for g in range(SC_NSEG[v]):
                qlo = seg_qlo(v, g)
                subs = seg_subs(v, g)
                for t2 in range(8):
                    t = 8 * g + t2
                    L = LOAD_I[(v, 2 * g + t2 // 4)]
                    buf = L % 3
                    for si2, (off, w) in enumerate(subs):
                        STI = STILE_I[(v, g, t2, si2)]
                        if em:
                            wt(E, f"kgl{L}")
                            if STI == 0:
                                wt(E, "prod_ev")
                                wt(E, "negm")
                            if STI >= 2:
                                E.wait_ge(a_exp.h, STI - 1)
                            slot = ps_s[:, (STI % 2) * 512:(STI % 2) * 512 + w]
                            ins = None
                            for d in range(HC):
                                ins = E.matmul(
                                    slot,
                                    AT["kbuf"][buf][:, d * 512 + (t % 4) * 128:d * 512 + (t % 4) * 128 + 128],
                                    q_fm[:, d * T_ + 128 * qlo + off:d * T_ + 128 * qlo + off + w],
                                    start=d == 0, stop=d == HC - 1)
                            ins.then_inc(p_s.h, 1)
                        p_s.add(1)
                        assert p_s.n == STI + 1
                for si in range(seg_ns(v, g)):
                    SW = SWEEP_I[(v, g, si)]
                    if em:
                        if SW == 0:
                            wt(E, "prod_ev")
                        if si == 0:
                            wt(E, f"vgl{LOAD_I[(v, 2 * g)]}")
                            wt(E, f"vgl{LOAD_I[(v, 2 * g + 1)]}")
                        E.wait_ge(a_exp.h, STILE_I[(v, g, 7, len(subs) - 1)] + 1)
                        if g >= SC_S0[v]:
                            wt(E, f"mseg{v}_{g}")
                        if SW >= 2:
                            E.wait_ge(d_oa.h, SW - 1)
                        par = SW % 2
                        ins = None
                        for t2 in range(8):
                            buf = LOAD_I[(v, 2 * g + t2 // 4)] % 3
                            lh = AT["pseg"][:, t2 * 1024 + si * 128:t2 * 1024 + si * 128 + 128]
                            for dvh in range(2):
                                ins = E.matmul(
                                    ps_ot[:, par * 1024 + dvh * 512:par * 1024 + (dvh + 1) * 512],
                                    lh,
                                    AT["vbuf"][buf][:, (8 * g + t2) % 4 * 1024 + dvh * 512:(8 * g + t2) % 4 * 1024 + (dvh + 1) * 512],
                                    start=t2 == 0, stop=t2 == 7)
                            ins = E.matmul(ps_r[:, par * 512:par * 512 + 1], lh, ones_bf[:],
                                           start=t2 == 0, stop=t2 == 7)
                        ins.then_inc(p_pv.h, 1)
                    p_pv.add(1)
                    assert p_pv.n == SW + 1
                mark(f"segdone{v}_{g}", p_pv)

    def A_act(E):
        em = E is not None

        def evac(waits, out_fn, in_fn, name):
            if em:
                for w in waits:
                    wt(E, w)
                E.activation(out=out_fn(), in_=in_fn(), func=AF.Copy).then_inc(a_ev.h, 1)
            a_ev.add(1)
            mark(name, a_ev)

        for st, idx in PROD_STAGES:
            base = PT_I[(st, 4 * idx)]
            buf = 0 if st == "kv" else 1
            for k in range(4):
                i = base + k
                extra = []
                if k == 0:
                    if st == "kv" and idx >= 1:
                        extra = [f"vmm{idx - 1}_3"]
                    if st == "q" and idx >= 1:
                        extra = [f"qmm{idx - 1}_7"]
                evac([f"pt_t{i}"] + extra,
                     lambda buf=buf, k=k: xfm[buf][:].rearrange(
                         "p (h s k) -> p h s k", h=HC, s=4)[:, :, k, :],
                     lambda i=i: ps_xt[:, (i % 2) * 1024:(i % 2) * 1024 + 1024].rearrange(
                         "p (h k) -> p h k", h=HC),
                     f"ptev{i}")
            if st == "kv":
                j = idx
                mark(f"xfmkv{j}", a_ev)
                for d in range(HC):
                    extra = []
                    if d == 0 and j >= 1:
                        extra = [f"ksp{j - 1}"]
                        if j == 1:
                            extra.append("ksampc")
                    evac([f"kmm{j}_{d}"] + extra,
                         lambda d=d: kstage[:, d * 512:(d + 1) * 512],
                         lambda d=d: [ps_pa, ps_pb][d % 2][:, 0:512],
                         f"kevd{j}_{d}")
                mark(f"kev{j}", a_ev)
                for ksub in range(4):
                    extra = [f"vsp{j - 1}"] if (ksub == 0 and j >= 1) else []
                    evac([f"vmm{j}_{ksub}"] + extra,
                         lambda k=ksub: vstage[:, k * 1024:(k + 1) * 1024],
                         lambda k=ksub: ps_v[:, (k % 2) * 1024:(k % 2) * 1024 + 1024],
                         f"vevk{j}_{ksub}")
                mark(f"vev{j}", a_ev)
            else:
                r = idx
                mark(f"xfmq{r}", a_ev)
                for d in range(HC):
                    evac([f"qmm{r}_{d}"],
                         lambda r=r, d=d: q_fm[:, d * T_ + r * 512:d * T_ + (r + 1) * 512],
                         lambda d=d: [ps_pa, ps_pb][d % 2][:, 0:512],
                         f"qevd{r}_{d}")
        mark("prod_ev", a_ev)

        ps_s = AT["ps_s"]
        for v in range(CHUNKS):
            for g in range(SC_NSEG[v]):
                subs = seg_subs(v, g)
                SI = SEG_I[(v, g)]
                for gi2 in (2 * g, 2 * g + 1):
                    L = LOAD_I[(v, gi2)]
                    j = gi2 // 8
                    r = gi2 % 8
                    if em:
                        wt(E, f"cc_ch{j}")
                        if L < 3:
                            wt(E, "qmm3_7")
                            wt(E, "prod_ev")
                            wt(E, "ksp3")
                            wt(E, "vsp3")
                        else:
                            vp, gp = LOAD_LIST[L - 3]
                            wt(E, f"segdone{vp}_{gp // 2}")
                        E.dma_start(out=AT["vbuf"][L % 3][:],
                                    in_=vall[j].ap()[r * 128:(r + 1) * 128, :]).then_inc(vg[L % 3].h, 16)
                    vg[L % 3].add(16)
                    mark(f"vgl{L}", vg[L % 3])
                if g >= SC_S0[v]:
                    s = g
                    if em:
                        if s >= 2:
                            wt(E, f"vstt{s - 2}")
                        E.dma_start(out=xqr[s % 2][:],
                                    in_=xq_in[s * 128:(s + 1) * 128, :]).then_inc(sxq[s % 2].h, 16)
                    sxq[s % 2].add(16)
                    mark(f"xq{s}", sxq[s % 2])
                for t2 in range(8):
                    for si2, (off, w) in enumerate(subs):
                        STI = STILE_I[(v, g, t2, si2)]
                        if em:
                            E.wait_ge(p_s.h, STI + 1)
                            if STI == 0:
                                wt(E, "negm")
                            if SI >= 1:
                                vp, gp = SEG_LIST[SI - 1]
                                wt(E, f"segdone{vp}_{gp}")
                            E.activation(out=AT["pseg"][:, t2 * 1024 + off:t2 * 1024 + off + w],
                                         in_=ps_s[:, (STI % 2) * 512:(STI % 2) * 512 + w],
                                         func=AF.Exp, bias=negm[:],
                                         scale=1.0 / 32.0).then_inc(a_exp.h, 1)
                        a_exp.add(1)
                if g >= SC_S0[v]:
                    si = g - SC_S0[v]
                    s = g
                    if em:
                        wt(E, f"var_s{s}")
                        E.activation(out=sgl[:, 24:25], in_=stat[:, 8 + si:9 + si],
                                     func=AF.Ln, bias=eps_sb[:], scale=1.0)
                        E.drain()
                        E.activation(out=stat[:, 16 + si:17 + si], in_=sgl[:, 24:25],
                                     func=AF.Exp, bias=0.0, scale=-0.5).then_inc(a_m.h, 1)
                    a_m.add(1)
                    mark(f"rstd_s{s}", a_m)

    def A_dve(E):
        em = E is not None
        if em:
            E.memset(zeros[:], 0.0)
            E.memset(ones_bf[:], 1.0)
            E.memset(mx_x[:], 0.0)
            E.memset(mx_h[:], 0.0)
            E.memset(eps_sb[:], EPS)
            E.engine_nop().then_inc(d_m.h, 1)
        d_m.add(1)
        mark("dve_init", d_m)

        if em:
            wt(E, "kev0")
            E.tensor_copy(
                ksamp[:].rearrange("p (d k) -> p d k", d=HC),
                kstage[:].rearrange("p (d k) -> p d k", d=HC)[:, :, 0:128],
            ).then_inc(d_m.h, 1)
        d_m.add(1)
        mark("ksampc", d_m)

        def wq_unit(k):
            kind, chunk, sub, phase = WU_KIND[k]
            w = _wu_w(k)
            b = k % 2
            if em:
                wt(E, f"wl{k}")
                buf = wraw[b]
                if kind < 2:
                    c0, c1 = 8 + b * 2, 9 + b * 2
                    kp = _prev_spill(k)
                    if kp is not None:
                        wt(E, f"wsp{kp}")
                    E.tensor_reduce(sgl[:, c0:c0 + 1], buf[:, 0:w], AX.X, ALU.max,
                                    apply_absolute_value=True)
                    E.drain()
                    E.reciprocal(sgl[:, c1:c1 + 1], sgl[:, c0:c0 + 1])
                    E.drain()
                    E.tensor_scalar(out=sgl[:, c1:c1 + 1], in0=sgl[:, c1:c1 + 1],
                                    scalar1=127.0, scalar2=None, op0=ALU.mult)
                    E.tensor_scalar(out=sgl[:, c0:c0 + 1], in0=sgl[:, c0:c0 + 1],
                                    scalar1=1.0 / 127.0, scalar2=None, op0=ALU.mult)
                    E.drain()
                    E.tensor_scalar(out=wtmp[:, 0:w], in0=buf[:, 0:w],
                                    scalar1=sgl[:, c1:c1 + 1], scalar2=127.49,
                                    op0=ALU.mult, op1=ALU.min)
                    E.drain()
                    E.tensor_scalar(out=wtmpi[:, 0:w], in0=wtmp[:, 0:w],
                                    scalar1=-128.49, scalar2=None, op0=ALU.max)
                    E.drain()
                    E.tensor_scalar(out=wqbf[b][:, 0:w], in0=wtmpi[:, 0:w],
                                    scalar1=sgl[:, c0:c0 + 1], scalar2=None,
                                    op0=ALU.mult).then_inc(d_q.h, 1)
                elif phase == 0:
                    E.tensor_reduce(sgl[:, 13:14], buf[:, 0:w], AX.X, ALU.max,
                                    apply_absolute_value=True)
                    E.drain()
                    if sub == 0:
                        E.tensor_copy(sgl[:, 12:13], sgl[:, 13:14])
                    else:
                        E.tensor_max(sgl[:, 12:13], sgl[:, 12:13], sgl[:, 13:14])
                    E.drain()
                    if sub == 2:
                        E.reciprocal(sgl[:, 14:15], sgl[:, 12:13])
                        E.drain()
                        E.tensor_scalar(out=sgl[:, 14:15], in0=sgl[:, 14:15],
                                        scalar1=127.0, scalar2=None, op0=ALU.mult)
                        E.tensor_scalar(out=sgl[:, 12:13], in0=sgl[:, 12:13],
                                        scalar1=1.0 / 127.0, scalar2=None, op0=ALU.mult)
                        E.drain()
                    E.engine_nop().then_inc(d_q.h, 1)
                else:
                    kp = _prev_spill(k)
                    if kp is not None:
                        wt(E, f"wsp{kp}")
                    E.tensor_scalar(out=wtmp[:, 0:w], in0=buf[:, 0:w],
                                    scalar1=sgl[:, 14:15], scalar2=127.49,
                                    op0=ALU.mult, op1=ALU.min)
                    E.drain()
                    E.tensor_scalar(out=wtmpi[:, 0:w], in0=wtmp[:, 0:w],
                                    scalar1=-128.49, scalar2=None, op0=ALU.max)
                    E.drain()
                    E.tensor_scalar(out=wqbf[b][:, 0:w], in0=wtmpi[:, 0:w],
                                    scalar1=sgl[:, 12:13], scalar2=None,
                                    op0=ALU.mult).then_inc(d_q.h, 1)
            d_q.add(1)
            mark(f"wq_u{k}", d_q)

        kdone = 0
        for k in range(WQ_PROD):
            wq_unit(k)
            kdone += 1

        if em:
            wt(E, "samp_mm")
            E.tensor_reduce(sgl[:, 0:1], ps_pa[:, 0:128], AX.X, ALU.max)
            E.drain()
            E.tensor_scalar(out=junk[:, 0:128], in0=zeros[:],
                            scalar1=sgl[:, 0:1], scalar2=None, op0=ALU.add)
            E.engine_nop().then_inc(d_m.h, 1)
        d_m.add(1)
        mark("mrep", d_m)
        if em:
            wt(E, "mrep_t")
            E.tensor_reduce(sgl[:, 1:2], ps_pb[:, 0:128], AX.X, ALU.max)
            E.drain()
            E.tensor_scalar(out=negm[:], in0=sgl[:, 1:2], scalar1=-1.0 / 32.0,
                            scalar2=-2.0, op0=ALU.mult, op1=ALU.add).then_inc(d_m.h, 1)
        d_m.add(1)
        mark("negm", d_m)

        ps_ot, ps_r = AT.get("ps_ot"), AT.get("ps_r")
        for v in range(CHUNKS):
            for g in range(SC_NSEG[v]):
                qlo = seg_qlo(v, g)
                subs = seg_subs(v, g)
                if g >= SC_S0[v]:
                    for t2 in range(8):
                        STI = STILE_I[(v, g, t2, 0)]
                        if em:
                            if d_mk.n == 0:
                                wt(E, "cfg")
                            E.wait_ge(a_exp.h, STI + 1)
                            E.tensor_tensor(AT["pseg"][:, t2 * 1024:t2 * 1024 + 128],
                                            AT["pseg"][:, t2 * 1024:t2 * 1024 + 128],
                                            masks[:, t2 * 128:(t2 + 1) * 128],
                                            ALU.mult).then_inc(d_mk.h, 1)
                        d_mk.add(1)
                    mark(f"mseg{v}_{g}", d_mk)
                for si in range(seg_ns(v, g)):
                    SW = SWEEP_I[(v, g, si)]
                    par = SW % 2
                    osl = qlo + si - SC_S0[v]
                    first = g == 0
                    if em:
                        E.wait_ge(p_pv.h, SW + 1)
                        if first:
                            E.tensor_copy(AT["osb"][:, osl * D_:(osl + 1) * D_],
                                          ps_ot[:, par * 1024:par * 1024 + 1024])
                            E.drain()
                            E.tensor_copy(AT["rsb"][:, osl:osl + 1],
                                          ps_r[:, par * 512:par * 512 + 1])
                            E.drain()
                        else:
                            E.tensor_add(AT["osb"][:, osl * D_:(osl + 1) * D_],
                                         AT["osb"][:, osl * D_:(osl + 1) * D_],
                                         ps_ot[:, par * 1024:par * 1024 + 1024])
                            E.drain()
                            E.tensor_add(AT["rsb"][:, osl:osl + 1],
                                         AT["rsb"][:, osl:osl + 1],
                                         ps_r[:, par * 512:par * 512 + 1])
                            E.drain()
                        E.engine_nop().then_inc(d_oa.h, 1)
                    d_oa.add(1)
                    assert d_oa.n == SW + 1
                for _ in range(WQ_SEG.get((v, g), 0)):
                    if kdone < NWU:
                        wq_unit(kdone)
                        kdone += 1
                if g >= SC_S0[v]:
                    si = g - SC_S0[v]
                    s = g
                    if em:
                        wt(E, f"xq{s}")
                        if s >= 2:
                            wt(E, f"lnsp_s{s - 2}")
                        E.reciprocal(sgl[:, 2:3], AT["rsb"][:, si:si + 1])
                        E.drain()
                        E.scalar_tensor_tensor(
                            out=AT["resid"][:], in0=AT["osb"][:, si * D_:(si + 1) * D_],
                            scalar=sgl[:, 2:3], in1=xqr[s % 2][:],
                            op0=ALU.mult, op1=ALU.add, accum_out=stat[:, 24:25])
                        E.engine_nop().then_inc(d_ep.h, 1)
                    d_ep.add(1)
                    mark(f"vstt{s}", d_ep)
                    if em:
                        E.drain()
                        E.scalar_tensor_tensor(
                            out=junk[:, 0:D_], in0=AT["resid"][:], scalar=1.0,
                            in1=AT["resid"][:], op0=ALU.mult, op1=ALU.mult,
                            accum_out=stat[:, 25:26])
                        E.drain()
                        E.tensor_scalar(out=stat[:, si:si + 1], in0=stat[:, 24:25],
                                        scalar1=1.0 / D_, scalar2=None, op0=ALU.mult)
                        E.drain()
                        E.tensor_tensor(stat[:, 26:27], stat[:, si:si + 1],
                                        stat[:, si:si + 1], ALU.mult)
                        E.tensor_scalar(out=stat[:, 8 + si:9 + si], in0=stat[:, 25:26],
                                        scalar1=1.0 / D_, scalar2=None, op0=ALU.mult)
                        E.drain()
                        E.tensor_sub(stat[:, 8 + si:9 + si], stat[:, 8 + si:9 + si],
                                     stat[:, 26:27])
                        E.engine_nop().then_inc(d_ep.h, 1)
                    d_ep.add(1)
                    mark(f"var_s{s}", d_ep)
                    if em:
                        wt(E, f"rstd_s{s}")
                        E.tensor_scalar(out=AT["lnst"][:, (s % 2) * D_:(s % 2 + 1) * D_],
                                        in0=AT["resid"][:],
                                        scalar1=stat[:, si:si + 1],
                                        scalar2=stat[:, 16 + si:17 + si],
                                        op0=ALU.subtract, op1=ALU.mult)
                        E.drain()
                        E.tensor_reduce(stat[:, 27:28],
                                        AT["lnst"][:, (s % 2) * D_:(s % 2 + 1) * D_],
                                        AX.X, ALU.max, apply_absolute_value=True)
                        E.drain()
                        E.tensor_tensor(mx_x[:], mx_x[:], stat[:, 27:28], ALU.max)
                        E.engine_nop().then_inc(d_ep.h, 1)
                    d_ep.add(1)
                    mark(f"epi_s{s}", d_ep)
            mark(f"epic{v}", d_ep)
        assert kdone == NWU

    def A_sync(E):
        em = E is not None
        if em:
            E.dma_start(out=identf[:], in_=identf_in).then_inc(s_cfg.h, 16)
            E.dma_start(out=clipx[:], in_=clipx_in).then_inc(s_cfg.h, 16)
            E.dma_start(out=clipd[:], in_=clipd_in).then_inc(s_cfg.h, 16)
            for tt in range(8):
                E.dma_start(out=masks[:, tt * 128:(tt + 1) * 128],
                            in_=masks_in[tt * 128:(tt + 1) * 128, :]).then_inc(s_cfg.h, 16)
        s_cfg.add(16 * 11)
        mark("cfg", s_cfg)


        for v in range(CHUNKS):
            for g in range(SC_NSEG[v]):
                for gi2 in (2 * g, 2 * g + 1):
                    L = LOAD_I[(v, gi2)]
                    j = gi2 // 8
                    r = gi2 % 8
                    if em:
                        wt(E, f"cc_ch{j}")
                        if L < 3:
                            wt(E, "qmm3_7")
                            wt(E, "prod_ev")
                            wt(E, "ksp3")
                            wt(E, "vsp3")
                        else:
                            vp, gp = LOAD_LIST[L - 3]
                            wt(E, f"segdone{vp}_{gp // 2}")
                        E.dma_start(out=AT["kbuf"][L % 3][:],
                                    in_=kall[j].ap()[r * 128:(r + 1) * 128, :]).then_inc(kg[L % 3].h, 16)
                    kg[L % 3].add(16)
                    mark(f"kgl{L}", kg[L % 3])

    def B_sync(E):
        em = E is not None

        def lnrd_load(n):
            if em:
                if n == 0:
                    wt(E, "lnsp_all0")
                    wt(E, "lnsp_all1")
                if n >= 4:
                    wt(E, f"qx{n - 4}q")
                E.dma_start(out=BT["lnrd"][n % 4][:],
                            in_=ln1_sp.ap()[n * 128:(n + 1) * 128, :]).then_inc(s_lnr[n % 4].h, 16)
            s_lnr[n % 4].add(16)
            mark(f"lnrd{n}", s_lnr[n % 4])

        for n in range(4):
            lnrd_load(n)
        if em:
            wt(E, "cc_arx")
            E.dma_start(out=arx_sb[:], in_=ar_o[0].ap()).then_inc(s_arx.h, 16)
        s_arx.add(16)
        mark("arx_ld", s_arx)
        for n in range(4, NQT):
            lnrd_load(n)

    def B_pe(E):
        em = E is not None
        if em:
            wt(E, "bx_rep")
            E.transpose(BT["ps_t"][:, 0:128], junk[:, 0:128], identf[:]).then_inc(p_bc.h, 1)
        p_bc.add(1)
        mark("bx_tr", p_bc)
        for n in range(NQT):
            if em:
                wt(E, f"qx{n}q")
                if n >= 2:
                    wt(E, f"qxev{n - 2}")
                ins = None
                for d in range(HC):
                    ins = E.transpose(
                        BT["ps_xtb"][:, (n % 2) * 1024 + d * 128:(n % 2) * 1024 + (d + 1) * 128],
                        BT["qxst"][n % 2][:, d * 128:(d + 1) * 128], identb[:])
                ins.then_inc(p_bc.h, 1)
            p_bc.add(1)
            mark(f"qxt{n}", p_bc)
        nu = 0
        for icx in range(IC):
            for tc in range(4):
                par = nu % 2
                if em:
                    if nu == 0:
                        wt(E, "wguT")
                    if icx == 0:
                        wt(E, f"qxev{4 * tc + 3}")
                    if nu >= 2:
                        wt(E, f"hdone{nu - 2}")
                    ins = None
                    for d in range(HC):
                        ins = E.matmul(BT["ps_g"][:, par * 512:par * 512 + 512],
                                       BT["wg_T"][:, d * DI_ + icx * 128:d * DI_ + (icx + 1) * 128],
                                       BT["qx_fm"][:, d * T_ + tc * 512:d * T_ + (tc + 1) * 512],
                                       start=d == 0, stop=d == HC - 1)
                    ins.then_inc(p_bc.h, 1)
                p_bc.add(1)
                mark(f"g_mm{nu}", p_bc)
                if em:
                    ins = None
                    for d in range(HC):
                        ins = E.matmul(BT["ps_u"][:, par * 512:par * 512 + 512],
                                       BT["wu_T"][:, d * DI_ + icx * 128:d * DI_ + (icx + 1) * 128],
                                       BT["qx_fm"][:, d * T_ + tc * 512:d * T_ + (tc + 1) * 512],
                                       start=d == 0, stop=d == HC - 1)
                    ins.then_inc(p_bc.h, 1)
                p_bc.add(1)
                mark(f"u_mm{nu}", p_bc)
                nu += 1

    def B_act(E):
        em = E is not None
        for d in range(HC):
            if em:
                if d == 0:
                    wt(E, "wsp_all0")
                    wt(E, "wsp_all1")
                E.dma_start(out=BT["wg_T"][:, d * DI_:(d + 1) * DI_],
                            in_=wq_sp[0].ap()[:, d * 128:(d + 1) * 128],
                            transpose=True).then_inc(s_wgu.h, 16)
            s_wgu.add(16)
        for d in range(HC):
            if em:
                E.dma_start(out=BT["wu_T"][:, d * DI_:(d + 1) * DI_],
                            in_=wq_sp[1].ap()[:, d * 128:(d + 1) * 128],
                            transpose=True).then_inc(s_wgu.h, 16)
            s_wgu.add(16)
        mark("wguT", s_wgu)
        if em:
            wt(E, "bx_tr")
            wt(E, "cfg")
            E.activation(out=sgl[:, 16:17], in_=clipx[:], func=AF.Sigmoid).then_inc(a_m.h, 1)
        a_m.add(1)
        mark("sigx", a_m)
        for n in range(NQT):
            if em:
                wt(E, f"qxt{n}")
                E.activation(
                    out=BT["qx_fm"][:].rearrange("p (h s k) -> p h s k", h=HC, s=NQT)[:, :, n, :],
                    in_=BT["ps_xtb"][:, (n % 2) * 1024:(n % 2) * 1024 + 1024].rearrange(
                        "p (h k) -> p h k", h=HC),
                    func=AF.Copy).then_inc(a_m.h, 1)
            a_m.add(1)
            mark(f"qxev{n}", a_m)
        for nu in range(4 * IC):
            par = nu % 2
            if em:
                wt(E, f"g_mm{nu}")
                if nu >= 2:
                    wt(E, f"hdone{nu - 2}")
                E.activation(out=BT["silu"][par][:],
                             in_=BT["ps_g"][:, par * 512:par * 512 + 512],
                             func=AF.Silu).then_inc(a_m.h, 1)
            a_m.add(1)
            mark(f"silu{nu}", a_m)

    def B_dve(E):
        em = E is not None
        if em:
            wt(E, "arx_ld")
            E.tensor_scalar(out=junk[:, 0:128], in0=zeros[:], scalar1=arx_sb[:],
                            scalar2=None, op0=ALU.add)
            E.engine_nop().then_inc(d_bc.h, 1)
        d_bc.add(1)
        mark("bx_rep", d_bc)
        if em:
            wt(E, "bx_tr")
            E.tensor_reduce(sgl[:, 17:18], BT["ps_t"][:, 0:128], AX.X, ALU.max)
            wt(E, "sigx")
            E.drain()
            E.tensor_tensor(sgl[:, 18:19], sgl[:, 16:17], sgl[:, 17:18], ALU.mult)
            E.drain()
            E.tensor_scalar(out=sgl[:, 18:19], in0=sgl[:, 18:19],
                            scalar1=1.0 / 127.0, scalar2=None, op0=ALU.mult)
            E.drain()
            E.reciprocal(sgl[:, 19:20], sgl[:, 18:19])
            E.engine_nop().then_inc(d_bc.h, 1)
        d_bc.add(1)
        mark("sx", d_bc)

        for n in range(NQT):
            if em:
                if n == 0:
                    wt(E, "sx")
                wt(E, f"lnrd{n}")
                if n >= 2:
                    wt(E, f"qxt{n - 2}")
                E.tensor_scalar(out=junk[:, 0:D_], in0=BT["lnrd"][n % 4][:],
                                scalar1=sgl[:, 19:20], scalar2=127.49,
                                op0=ALU.mult, op1=ALU.min)
                E.drain()
                E.tensor_scalar(out=wtmpi[:, 0:D_], in0=junk[:, 0:D_],
                                scalar1=-128.49, scalar2=None, op0=ALU.max)
                E.drain()
                E.tensor_scalar(out=BT["qxst"][n % 2][:], in0=wtmpi[:, 0:D_],
                                scalar1=sgl[:, 18:19], scalar2=None,
                                op0=ALU.mult).then_inc(d_bc.h, 1)
            d_bc.add(1)
            mark(f"qx{n}q", d_bc)

        for nu in range(4 * IC):
            par = nu % 2
            if em:
                wt(E, f"silu{nu}")
                wt(E, f"u_mm{nu}")
                if nu >= 2:
                    wt(E, f"hsp{nu - 2}")
                E.tensor_tensor(BT["hst"][par][:], BT["silu"][par][:],
                                BT["ps_u"][:, par * 512:par * 512 + 512], ALU.mult)
                E.drain()
                E.tensor_reduce(stat[:, 14:15], BT["hst"][par][:], AX.X, ALU.max,
                                apply_absolute_value=True)
                E.drain()
                E.tensor_tensor(mx_h[:], mx_h[:], stat[:, 14:15], ALU.max)
                E.engine_nop().then_inc(d_bc.h, 1)
            d_bc.add(1)
            mark(f"hdone{nu}", d_bc)

    def B_gpsimd(E):
        em = E is not None
        for nu in range(4 * IC):
            icx, tc = nu // 4, nu % 4
            par = nu % 2
            if em:
                wt(E, f"hdone{nu}")
                E.dma_start(out=h_spd.ap()[:, icx * T_ + tc * 512:icx * T_ + (tc + 1) * 512],
                            in_=BT["hst"][par][:]).then_inc(g_hsp[par].h, 16)
            g_hsp[par].add(16)
            mark(f"hsp{nu}", g_hsp[par])
        mark("hsp_all0", g_hsp[0])
        mark("hsp_all1", g_hsp[1])
        if em:
            wt(E, f"hdone{60}")
            E.collective_compute("AllReduce", ALU.max, replica_groups=RG,
                                 ins=[ar_i[2].ap().opt()],
                                 outs=[ar_o[2].ap().opt()]).then_inc(g_cc.h, 1)
        g_cc.add(1)
        if em:
            wt(E, f"hdone{4 * IC - 1}")
            E.dma_start(out=ar_i[1].ap(), in_=mx_h[:]).then_inc(g_ar.h, 16)
        g_ar.add(16)
        if em:
            E.wait_ge(g_ar.h, g_ar.n)
            E.collective_compute("AllReduce", ALU.max, replica_groups=RG,
                                 ins=[ar_i[1].ap().opt()],
                                 outs=[ar_o[1].ap().opt()]).then_inc(g_cc.h, 1)
        g_cc.add(1)
        mark("cc_arh", g_cc)

    def C_sync(E):
        em = E is not None

        def hld_load(n, EE):
            if EE is not None:
                if n < 2:
                    wt(EE, "hsp_all0")
                    wt(EE, "hsp_all1")
                else:
                    wt(EE, f"hq{n - 2}")
                EE.dma_start(
                    out=CT["hbuf"][n % 2][:].rearrange("p (i k) -> p i k", i=IC),
                    in_=h_spd.ap()[:, :].rearrange("p (i t) -> p i t", i=IC)[:, :, n * 128:(n + 1) * 128],
                ).then_inc(s_hld[n % 2].h, 16)
            s_hld[n % 2].add(16)
            mark(f"hld{n}", s_hld[n % 2])

        def lnr_load(m, EE):
            if EE is not None:
                if m == 0:
                    wt(EE, "lnsp_all0")
                    wt(EE, "lnsp_all1")
                if m >= 4:
                    wt(EE, f"r2d{m - 4}")
                EE.dma_start(out=CT["lnr"][m % 4][:],
                             in_=ln1_sp.ap()[m * 128:(m + 1) * 128, :]).then_inc(s_clnr[m % 4].h, 16)
            s_clnr[m % 4].add(16)
            mark(f"lnr{m}", s_clnr[m % 4])

        C_sync.hld_load = hld_load
        C_sync.lnr_load = lnr_load
        hld_load(0, E)
        if em:
            wt(E, "cc_arh")
            E.dma_start(out=arh_sb[:], in_=ar_o[1].ap()).then_inc(s_arh.h, 16)
        s_arh.add(16)
        mark("arh_ld", s_arh)
        for n in range(1, NQT):
            if n % 2 == 0:
                hld_load(n, E)
            else:
                lnr_load(n, E)
        if em:
            E.wait_ge(s_out[0].h, 16 * NQT // 2)
            E.wait_ge(s_out[1].h, 16 * NQT // 2)

    def C_pe(E):
        em = E is not None
        if em:
            wt(E, "ch_rep")
            E.transpose(CT["ps_ct"][:, 0:128], junk[:, 0:128], identf[:]).then_inc(p_bc.h, 1)
        p_bc.add(1)
        mark("ch_tr", p_bc)
        for n in range(NQT):
            par = n % 2
            if em:
                if n == 0:
                    wt(E, "qwdT")
                wt(E, f"hq{n}")
                if n >= 2:
                    wt(E, f"r2d{n - 2}")
                ins = None
                for icx in range(IC):
                    for hoh in range(2):
                        ins = E.matmul(
                            CT["ps_d"][:, par * 1024 + hoh * 512:par * 1024 + (hoh + 1) * 512],
                            CT["hq"][par][:, icx * 128:(icx + 1) * 128],
                            CT["qwd_T"][:, icx * D_ + hoh * 512:icx * D_ + (hoh + 1) * 512],
                            start=icx == 0, stop=icx == IC - 1)
                ins.then_inc(p_bc.h, 1)
            p_bc.add(1)
            mark(f"dn{n}", p_bc)

    def C_act(E):
        em = E is not None
        for icx in range(IC):
            if em:
                if icx == 0:
                    wt(E, "wsp_all0")
                    wt(E, "wsp_all1")
                E.dma_start(out=CT["qwd_T"][:, icx * D_:(icx + 1) * D_],
                            in_=wq_sp[2].ap()[:, icx * 128:(icx + 1) * 128],
                            transpose=True).then_inc(s_qwd.h, 16)
            s_qwd.add(16)
        mark("qwdT", s_qwd)
        if em:
            wt(E, "ch_tr")
            E.activation(out=sgl[:, 20:21], in_=clipd[:], func=AF.Sigmoid).then_inc(a_m.h, 1)
        a_m.add(1)
        mark("sigd", a_m)

        def act_load(m, EE):
            if m % 2 == 1:
                C_sync.hld_load(m, EE)
            else:
                C_sync.lnr_load(m, EE)

        act_load(0, E)
        act_load(1, E)
        for n in range(NQT + 3):
            if n + 2 < NQT:
                act_load(n + 2, E)
            if 3 <= n <= NQT + 2:
                m = n - 3
                p = m % 2
                if em:
                    wt(E, f"bias{m}")
                    if m >= 2:
                        wt(E, f"out{m - 2}")
                    E.drain()
                    E.activation(out=CT["ot"][p][:], in_=CT["r2"][p][:],
                                 func=AF.Identity, bias=stat[:, 14 + p:15 + p],
                                 scale=stat[:, 6 + p:7 + p]).then_inc(a_m.h, 1)
                a_m.add(1)
                mark(f"cote{m}", a_m)
            if 1 <= n <= NQT:
                m = n - 1
                p = m % 2
                if em:
                    wt(E, f"r2d{m}")
                    E.drain()
                    E.activation(out=junk[:, 0:D_], in_=CT["r2"][p][:],
                                 func=AF.Square,
                                 accum_out=stat[:, 10 + p:11 + p]).then_inc(a_m.h, 1)
                a_m.add(1)
                mark(f"sqa{m}", a_m)
            if 2 <= n <= NQT + 1:
                m = n - 2
                p = m % 2
                if em:
                    wt(E, f"vard{m}")
                    E.activation(out=stat[:, 4 + p:5 + p], in_=stat[:, 2 + p:3 + p],
                                 func=AF.Ln, bias=eps_sb[:], scale=1.0)
                    E.drain()
                    E.activation(out=stat[:, 6 + p:7 + p], in_=stat[:, 4 + p:5 + p],
                                 func=AF.Exp, bias=0.0, scale=-0.5).then_inc(a_m.h, 1)
                a_m.add(1)
                mark(f"crstd{m}", a_m)

    def C_dve(E):
        em = E is not None
        if em:
            wt(E, "arh_ld")
            E.tensor_scalar(out=junk[:, 0:128], in0=zeros[:], scalar1=arh_sb[:],
                            scalar2=None, op0=ALU.add)
            E.engine_nop().then_inc(d_bc.h, 1)
        d_bc.add(1)
        mark("ch_rep", d_bc)
        if em:
            wt(E, "ch_tr")
            E.tensor_reduce(sgl[:, 21:22], CT["ps_ct"][:, 0:128], AX.X, ALU.max)
            wt(E, "sigd")
            E.drain()
            E.tensor_tensor(sgl[:, 22:23], sgl[:, 20:21], sgl[:, 21:22], ALU.mult)
            E.drain()
            E.tensor_scalar(out=sgl[:, 22:23], in0=sgl[:, 22:23],
                            scalar1=1.0 / 127.0, scalar2=None, op0=ALU.mult)
            E.drain()
            E.reciprocal(sgl[:, 23:24], sgl[:, 22:23])
            E.engine_nop().then_inc(d_bc.h, 1)
        d_bc.add(1)
        mark("sh", d_bc)

        for n in range(NQT + 3):
            if n < NQT:
                par = n % 2
                if em:
                    if n == 0:
                        wt(E, "sh")
                    wt(E, f"hld{n}")
                    E.tensor_scalar(out=CT["ctmp"][:], in0=CT["hbuf"][par][:],
                                    scalar1=sgl[:, 23:24], scalar2=127.49,
                                    op0=ALU.mult, op1=ALU.min)
                    E.drain()
                    E.tensor_scalar(out=CT["ctmpi"][:], in0=CT["ctmp"][:],
                                    scalar1=-128.49, scalar2=None, op0=ALU.max)
                    E.drain()
                    E.tensor_scalar(out=CT["hq"][par][:], in0=CT["ctmpi"][:],
                                    scalar1=sgl[:, 22:23], scalar2=None,
                                    op0=ALU.mult).then_inc(d_bc.h, 1)
                d_bc.add(1)
                mark(f"hq{n}", d_bc)
            if 3 <= n <= NQT + 2:
                m = n - 3
                p = m % 2
                if em:
                    wt(E, f"crstd{m}")
                    E.tensor_scalar(out=stat[:, 14 + p:15 + p], in0=stat[:, 0 + p:1 + p],
                                    scalar1=stat[:, 6 + p:7 + p], scalar2=-1.0,
                                    op0=ALU.mult, op1=ALU.mult).then_inc(d_bc.h, 1)
                d_bc.add(1)
                mark(f"bias{m}", d_bc)
            if 1 <= n <= NQT:
                m = n - 1
                p = m % 2
                if em:
                    wt(E, f"dn{m}")
                    wt(E, f"lnr{m}")
                    if m >= 2:
                        wt(E, f"cote{m - 2}")
                    E.scalar_tensor_tensor(
                        out=CT["r2"][p][:], in0=CT["ps_d"][:, p * 1024:p * 1024 + 1024],
                        scalar=1.0, in1=CT["lnr"][m % 4][:],
                        op0=ALU.mult, op1=ALU.add, accum_out=stat[:, 8 + p:9 + p])
                    E.drain()
                    E.tensor_scalar(out=stat[:, 0 + p:1 + p], in0=stat[:, 8 + p:9 + p],
                                    scalar1=1.0 / D_, scalar2=None,
                                    op0=ALU.mult).then_inc(d_bc.h, 1)
                d_bc.add(1)
                mark(f"r2d{m}", d_bc)
            if 2 <= n <= NQT + 1:
                m = n - 2
                p = m % 2
                if em:
                    wt(E, f"sqa{m}")
                    E.drain()
                    E.tensor_scalar(out=stat[:, 2 + p:3 + p], in0=stat[:, 10 + p:11 + p],
                                    scalar1=1.0 / D_, scalar2=None, op0=ALU.mult)
                    E.tensor_tensor(stat[:, 12 + p:13 + p], stat[:, 0 + p:1 + p],
                                    stat[:, 0 + p:1 + p], ALU.mult)
                    E.drain()
                    E.tensor_sub(stat[:, 2 + p:3 + p], stat[:, 2 + p:3 + p],
                                 stat[:, 12 + p:13 + p])
                    E.engine_nop().then_inc(d_bc.h, 1)
                d_bc.add(1)
                mark(f"vard{m}", d_bc)

    def C_gpsimd(E):
        em = E is not None
        for n in range(NQT):
            if em:
                wt(E, f"cote{n}")
                E.dma_start(out=out_t.ap()[n * 128:(n + 1) * 128, :],
                            in_=CT["ot"][n % 2][:]).then_inc(s_out[n % 2].h, 16)
            s_out[n % 2].add(16)
            mark(f"out{n}", s_out[n % 2])

    for _k in ["kbuf", "vbuf", "pseg", "osb", "rsb", "resid", "lnst",
               "ps_s", "ps_ot", "ps_r"]:
        AT[_k] = None

    A_gpsimd(None)
    A_pe(None)
    A_act(None)
    A_dve(None)
    A_sync(None)
    B_sync(None)
    B_pe(None)
    B_act(None)
    B_dve(None)
    B_gpsimd(None)
    C_sync(None)
    C_pe(None)
    C_act(None)
    C_dve(None)
    C_gpsimd(None)
    dry = {id(c): c.n for c in ctrs}
    for c in ctrs:
        c.n = 0

    es_pr.close()
    es_a2 = ExitStack()
    AT["kbuf"] = [es_a2.enter_context(nc.sbuf_tensor(f"kbuf{i}", [128, HC * 512], BF16)) for i in range(3)]
    AT["vbuf"] = [es_a2.enter_context(nc.sbuf_tensor(f"vbuf{i}", [128, 4 * D_], BF16)) for i in range(3)]
    AT["pseg"] = es_a2.enter_context(nc.sbuf_tensor("pseg", [128, 8 * 1024], BF16))
    AT["osb"] = es_a2.enter_context(nc.sbuf_tensor("osb", [128, 8 * D_], F32))
    AT["rsb"] = es_a2.enter_context(nc.sbuf_tensor("rsb", [128, 8], F32))
    AT["resid"] = es_a2.enter_context(nc.sbuf_tensor("resid", [128, D_], F32))
    AT["lnst"] = es_a2.enter_context(nc.sbuf_tensor("lnst", [128, 2 * D_], F32))
    AT["ps_s"] = es_a2.enter_context(nc.psum_tensor("ps_s", [128, 1024], F32))
    AT["ps_ot"] = es_a2.enter_context(nc.psum_tensor("ps_ot", [128, 2048], F32))
    AT["ps_r"] = es_a2.enter_context(nc.psum_tensor("ps_r", [128, 1024], F32))

    with nc.Block("A") as blk:
        @blk.gpsimd
        def _(E):
            A_gpsimd(E)

        @blk.tensor
        def _(E):
            A_pe(E)

        @blk.scalar
        def _(E):
            A_act(E)

        @blk.vector
        def _(E):
            A_dve(E)

        @blk.sync
        def _(E):
            A_sync(E)

    es_a2.close()
    es_a1.close()
    es_b = ExitStack()
    BT["qx_fm"] = es_b.enter_context(nc.sbuf_tensor("qx_fm", [128, HC * T_], BF16))
    BT["wg_T"] = es_b.enter_context(nc.sbuf_tensor("wg_T", [128, HC * DI_], BF16))
    BT["wu_T"] = es_b.enter_context(nc.sbuf_tensor("wu_T", [128, HC * DI_], BF16))
    BT["silu"] = [es_b.enter_context(nc.sbuf_tensor(f"silu{i}", [128, 512], F32)) for i in range(2)]
    BT["hst"] = [es_b.enter_context(nc.sbuf_tensor(f"hst{i}", [128, 512], F32)) for i in range(2)]
    BT["qxst"] = [es_b.enter_context(nc.sbuf_tensor(f"qxst{i}", [128, D_], BF16)) for i in range(2)]
    BT["lnrd"] = [es_b.enter_context(nc.sbuf_tensor(f"lnrd{i}", [128, D_], F32)) for i in range(4)]
    BT["ps_xtb"] = es_b.enter_context(nc.psum_tensor("ps_xtb", [128, 2048], BF16))
    BT["ps_g"] = es_b.enter_context(nc.psum_tensor("ps_g", [128, 1024], F32))
    BT["ps_u"] = es_b.enter_context(nc.psum_tensor("ps_u", [128, 1024], F32))
    BT["ps_t"] = es_b.enter_context(nc.psum_tensor("ps_t", [128, 512], F32))

    with nc.Block("B") as blk:
        @blk.gpsimd
        def _(E):
            B_gpsimd(E)

        @blk.tensor
        def _(E):
            B_pe(E)

        @blk.scalar
        def _(E):
            B_act(E)

        @blk.vector
        def _(E):
            B_dve(E)

        @blk.sync
        def _(E):
            B_sync(E)

    es_b.close()
    es_c = ExitStack()
    CT["qwd_T"] = es_c.enter_context(nc.sbuf_tensor("qwd_T", [128, IC * D_], BF16))
    CT["hbuf"] = [es_c.enter_context(nc.sbuf_tensor(f"hbuf{i}", [128, DI_], F32)) for i in range(2)]
    CT["hq"] = [es_c.enter_context(nc.sbuf_tensor(f"hq{i}", [128, DI_], BF16)) for i in range(2)]
    CT["lnr"] = [es_c.enter_context(nc.sbuf_tensor(f"lnr{i}", [128, D_], F32)) for i in range(4)]
    CT["r2"] = [es_c.enter_context(nc.sbuf_tensor(f"r2_{i}", [128, D_], F32)) for i in range(2)]
    CT["ot"] = [es_c.enter_context(nc.sbuf_tensor(f"ot{i}", [128, D_], F32)) for i in range(2)]
    CT["ctmp"] = es_c.enter_context(nc.sbuf_tensor("ctmp", [128, DI_], F32))
    CT["ctmpi"] = es_c.enter_context(nc.sbuf_tensor("ctmpi", [128, DI_], I32))
    CT["ps_d"] = es_c.enter_context(nc.psum_tensor("ps_d", [128, 2048], F32))
    CT["ps_ct"] = es_c.enter_context(nc.psum_tensor("ps_ct", [128, 512], F32))

    with nc.Block("C") as blk:
        @blk.gpsimd
        def _(E):
            C_gpsimd(E)

        @blk.tensor
        def _(E):
            C_pe(E)

        @blk.scalar
        def _(E):
            C_act(E)

        @blk.vector
        def _(E):
            C_dve(E)

        @blk.sync
        def _(E):
            C_sync(E)

    for c in ctrs:
        assert c.n == dry[id(c)], (c.name, c.n, dry[id(c)])

    es_c.close()
    es.close()
    return nc


_NC_CACHE = {}


def _get_nc():
    if "nc" not in _NC_CACHE:
        _NC_CACHE["nc"] = build_nc()
    return _NC_CACHE["nc"]


def kernel(L0_attn_embeddings, L0_attn_W_Q, L0_attn_W_K, L0_attn_W_V,
           L0_ffn_clip_x, L0_ffn_clip_down, L0_ffn_w_gate, L0_ffn_w_up,
           L0_ffn_w_down):
    x = np.asarray(L0_attn_embeddings, dtype=np.float32).reshape(S_TOK, D_)
    wq = np.ascontiguousarray(np.asarray(L0_attn_W_Q, dtype=np.float32))
    wk = np.ascontiguousarray(np.asarray(L0_attn_W_K, dtype=np.float32))
    wv = np.ascontiguousarray(np.asarray(L0_attn_W_V, dtype=np.float32))
    wg = np.ascontiguousarray(np.asarray(L0_ffn_w_gate, dtype=np.float32))
    wu = np.ascontiguousarray(np.asarray(L0_ffn_w_up, dtype=np.float32))
    wd = np.ascontiguousarray(np.asarray(L0_ffn_w_down, dtype=np.float32))
    cx = float(np.asarray(L0_ffn_clip_x).reshape(-1)[0])
    cd = float(np.asarray(L0_ffn_clip_down).reshape(-1)[0])
    cx_t = np.full((128, 1), cx, dtype=np.float32)
    cd_t = np.full((128, 1), cd, dtype=np.float32)
    identf = np.eye(128, dtype=np.float32)
    identb = np.eye(128, dtype=ml_dtypes.bfloat16)

    in_maps = []
    tri = np.triu(np.ones((128, 128), dtype=np.float32)).astype(ml_dtypes.bfloat16)
    one_m = np.ones((128, 128), dtype=ml_dtypes.bfloat16)
    zero_m = np.zeros((128, 128), dtype=ml_dtypes.bfloat16)
    for c in range(NCORES):
        qrows = np.concatenate(
            [x[(8 * s + c) * 128:(8 * s + c + 1) * 128] for s in range(NQT)], axis=0)
        kvrows = np.concatenate(
            [x[(8 * j + c) * 512:(8 * j + c + 1) * 512] for j in range(NKV)], axis=0)
        mk = np.empty((8, 128, 128), dtype=ml_dtypes.bfloat16)
        for w in range(8):
            mk[w] = one_m if w < c else (tri if w == c else zero_m)
        in_maps.append({
            "xq": np.ascontiguousarray(qrows),
            "xkv": np.ascontiguousarray(kvrows),
            "wq": wq, "wk": wk, "wv": wv,
            "wg": wg, "wu": wu, "wd": wd,
            "clipx": cx_t, "clipd": cd_t,
            "masks": np.ascontiguousarray(mk.reshape(8 * 128, 128)),
            "identf": identf, "identb": identb,
        })

    nc = _get_nc()
    res = run_bass_kernel_spmd(nc, in_maps, core_ids=list(range(NCORES)))

    out = np.empty((S_TOK, D_), dtype=np.float32)
    for c in range(NCORES):
        oc = res.results[c]["out"]
        for s in range(NQT):
            out[(8 * s + c) * 128:(8 * s + c + 1) * 128] = oc[s * 128:(s + 1) * 128]
    return out.reshape(1, S_TOK, D_)


# version 12
# speedup vs baseline: 1.0745x; 1.0089x over previous
from contextlib import ExitStack

import numpy as np
import ml_dtypes
import concourse.bass as bass
import concourse.mybir as mybir
from concourse.bass_utils import run_bass_kernel_spmd

F32 = mybir.dt.float32
BF16 = mybir.dt.bfloat16
I32 = mybir.dt.int32
AF = mybir.ActivationFunctionType
ALU = mybir.AluOpType
AX = mybir.AxisListType

NCORES = 8
S_TOK = 16384
D_ = 1024
DI_ = 2816
T_ = 2048
NQT = 16
NPAIR = 8
NKV = 4
NCH = 4
HC = 8
IC = 22
EPS = 1e-5
NWU = 2 * IC + 8

PAIR_T = [16 * (u + 1) for u in range(NPAIR)]
CUM_T = [0]
for _u in range(NPAIR):
    CUM_T.append(CUM_T[-1] + PAIR_T[_u])
PAIR_G = [4 * (u + 1) for u in range(NPAIR)]
CUM_G = [0]
for _u in range(NPAIR):
    CUM_G.append(CUM_G[-1] + PAIR_G[_u])

WQ_PROD = 0
PROD_STAGES = []
for _j in range(4):
    PROD_STAGES.append(("kv", _j))
    PROD_STAGES.append(("q", _j))
PROD_TILES = []
for _st, _i in PROD_STAGES:
    for _k in range(4):
        PROD_TILES.append((_st, 4 * _i + _k))
PT_I = {t: i for i, t in enumerate(PROD_TILES)}
WD_SUBW = [1024, 1024, 768]
WU_KIND = ([(0, i, 0, 1) for i in range(IC)] + [(1, i, 0, 1) for i in range(IC)]
           + [(2, rb, s, p) for rb in range(8) for p in (0, 1) for s in range(3)])
NWU = len(WU_KIND)

CHUNKS = 2
SC_S0 = [0, 8]
SC_NSEG = [8, 16]
SEG_LIST = [(v, g) for v in range(CHUNKS) for g in range(SC_NSEG[v])]
SEG_I = {key: i for i, key in enumerate(SEG_LIST)}


def seg_qlo(v, g):
    return max(SC_S0[v], g)


def seg_ns(v, g):
    return SC_S0[v] + 8 - seg_qlo(v, g)


def seg_subs(v, g):
    W = seg_ns(v, g) * 128
    subs = []
    off = 0
    while off < W:
        w = min(512, W - off)
        subs.append((off, w))
        off += w
    return subs


STILE_LIST = [(v, g, t2, si2) for (v, g) in SEG_LIST for t2 in range(8)
              for si2 in range(len(seg_subs(v, g)))]
STILE_I = {key: i for i, key in enumerate(STILE_LIST)}
SWEEP_LIST = [(v, g, si) for (v, g) in SEG_LIST for si in range(seg_ns(v, g))]
SWEEP_I = {key: i for i, key in enumerate(SWEEP_LIST)}
LOAD_LIST = [(v, gi2) for v in range(CHUNKS) for gi2 in range(2 * SC_NSEG[v])]
LOAD_I = {key: i for i, key in enumerate(LOAD_LIST)}

_WQ_QUOTA = {(0, 0): 3, (0, 1): 3, (0, 2): 3, (0, 3): 3, (0, 4): 2, (0, 5): 2,
             (0, 6): 1, (0, 7): 1}
for _g in range(8):
    _WQ_QUOTA[(1, _g)] = 6
for _g, _q in zip(range(8, 16), (4, 4, 4, 4, 3, 3, 2, 2)):
    _WQ_QUOTA[(1, _g)] = _q
WQ_SEG = {}
_k = WQ_PROD
_i = 0
while _k < NWU:
    _sg = SEG_LIST[_i % len(SEG_LIST)]
    if WQ_SEG.get(_sg, 0) < _WQ_QUOTA.get(_sg, 0):
        WQ_SEG[_sg] = WQ_SEG.get(_sg, 0) + 1
        _k += 1
    _i += 1
    assert _i < 10000



def _wu_w(k):
    kind, chunk, sub, phase = WU_KIND[k]
    return D_ if kind < 2 else WD_SUBW[sub]


def _wu_spills(k):
    kind, chunk, sub, phase = WU_KIND[k]
    return kind < 2 or phase == 1


def _prev_spill(k):
    for k2 in range(k - 1, -1, -1):
        if _wu_spills(k2) and k2 % 2 == k % 2:
            return k2
    return None


def ti(u, t):
    return CUM_T[u] + t


def gi(u, g):
    return CUM_G[u] + g


INV_G = {}
for _u in range(NPAIR):
    for _g in range(PAIR_G[_u]):
        INV_G[gi(_u, _g)] = (_u, _g)


def mki(u, t):
    return 16 * u + (t - 16 * u)


class Ctr:
    def __init__(self, es, nc, name):
        self.h = es.enter_context(nc.semaphore(name))
        self.name = name
        self.n = 0

    def add(self, k=1):
        self.n += k
        return self.n


def build_nc():
    nc = bass.Bass("TRN2", target_bir_lowering=False, debug=False)

    xq_in = nc.dram_tensor("xq", [T_, D_], F32, kind="ExternalInput").ap()
    xkv_in = nc.dram_tensor("xkv", [T_, D_], F32, kind="ExternalInput").ap()
    wq_in = nc.dram_tensor("wq", [D_, D_], F32, kind="ExternalInput").ap()
    wk_in = nc.dram_tensor("wk", [D_, D_], F32, kind="ExternalInput").ap()
    wv_in = nc.dram_tensor("wv", [D_, D_], F32, kind="ExternalInput").ap()
    wg_in = nc.dram_tensor("wg", [DI_, D_], F32, kind="ExternalInput").ap()
    wu_in = nc.dram_tensor("wu", [DI_, D_], F32, kind="ExternalInput").ap()
    wd_in = nc.dram_tensor("wd", [D_, DI_], F32, kind="ExternalInput").ap()
    clipx_in = nc.dram_tensor("clipx", [128, 1], F32, kind="ExternalInput").ap()
    clipd_in = nc.dram_tensor("clipd", [128, 1], F32, kind="ExternalInput").ap()
    masks_in = nc.dram_tensor("masks", [8 * 128, 128], BF16, kind="ExternalInput").ap()
    identf_in = nc.dram_tensor("identf", [128, 128], F32, kind="ExternalInput").ap()
    identb_in = nc.dram_tensor("identb", [128, 128], BF16, kind="ExternalInput").ap()
    out_t = nc.dram_tensor("out", [T_, D_], F32, kind="ExternalOutput")

    ksrc = [nc.dram_tensor(f"ksrc{j}", [128, HC * 512], BF16) for j in range(NCH)]
    vsrc = [nc.dram_tensor(f"vsrc{j}", [128, 4 * D_], BF16) for j in range(NCH)]
    kall = [nc.dram_tensor(f"kall{j}", [NCORES * 128, HC * 512], BF16,
                           addr_space="Shared") for j in range(NCH)]
    vall = [nc.dram_tensor(f"vall{j}", [NCORES * 128, 4 * D_], BF16,
                           addr_space="Shared") for j in range(NCH)]
    wq_sp = [nc.dram_tensor("wgq_s", [DI_, D_], BF16),
             nc.dram_tensor("wuq_s", [DI_, D_], BF16),
             nc.dram_tensor("wdq_s", [D_, DI_], BF16)]
    ln1_sp = nc.dram_tensor("ln1_s", [T_, D_], F32)
    h_spd = nc.dram_tensor("h_s", [128, IC * T_], F32)
    ar_i = [nc.dram_tensor(f"ar_i{i}", [128, 1], F32) for i in range(3)]
    ar_o = [nc.dram_tensor(f"ar_o{i}", [128, 1], F32) for i in range(3)]

    RG = [list(range(NCORES))]
    es = ExitStack()

    def sbg(name, shape, dt):
        return es.enter_context(nc.sbuf_tensor(name, shape, dt))

    identf = sbg("identf_sb", [128, 128], F32)
    identb = sbg("identb_sb", [128, 128], BF16)
    zeros = sbg("zeros_sb", [128, 128], F32)
    ones_bf = sbg("ones_bf", [128, 1], BF16)
    clipx = sbg("clipx_sb", [128, 1], F32)
    clipd = sbg("clipd_sb", [128, 1], F32)
    sgl = sbg("sgl", [128, 32], F32)
    stat = sbg("stat", [128, 32], F32)
    mx_x = sbg("mx_x", [128, 1], F32)
    mx_h = sbg("mx_h", [128, 1], F32)
    junk = sbg("junk", [128, D_], F32)
    wtmp = sbg("wtmp", [128, D_], F32)
    wtmpi = sbg("wtmpi", [128, D_], I32)
    wqbf = [sbg(f"wqbf{i}", [128, D_], BF16) for i in range(2)]
    negm = sbg("negm_sb", [128, 1], F32)
    eps_sb = sbg("eps_sb", [128, 1], F32)
    arx_sb = sbg("arx_sb", [128, 1], F32)
    arh_sb = sbg("arh_sb", [128, 1], F32)

    es_a1 = ExitStack()

    def sba1(name, shape, dt):
        return es_a1.enter_context(nc.sbuf_tensor(name, shape, dt))

    masks = sba1("masks_sb", [128, 8 * 128], BF16)
    q_fm = sba1("q_fm", [128, HC * T_], BF16)
    ksamp = sba1("ksamp", [128, HC * 128], BF16)
    xqr = [sba1(f"xqr{i}", [128, D_], F32) for i in range(2)]
    wraw = [sba1(f"wraw{i}", [128, D_], F32) for i in range(2)]
    kstage = sba1("kstage", [128, HC * 512], BF16)
    vstage = sba1("vstage", [128, 4 * D_], BF16)

    es_pr = ExitStack()
    xstage = [es_pr.enter_context(nc.sbuf_tensor(f"xstage{i}", [128, D_], BF16)) for i in range(2)]
    xfm = [es_pr.enter_context(nc.sbuf_tensor(f"xfm{i}", [128, HC * 512], BF16)) for i in range(2)]
    wkv_bf = [es_pr.enter_context(nc.sbuf_tensor(f"wkv{i}", [128, HC * D_], BF16)) for i in range(2)]
    wq_bf = es_pr.enter_context(nc.sbuf_tensor("wq_bf", [128, HC * D_], BF16))
    ps_xt = es_pr.enter_context(nc.psum_tensor("ps_xt", [128, 2048], BF16))
    ps_pa = es_pr.enter_context(nc.psum_tensor("ps_pa", [128, 512], F32))
    ps_pb = es_pr.enter_context(nc.psum_tensor("ps_pb", [128, 512], F32))
    ps_v = es_pr.enter_context(nc.psum_tensor("ps_v", [128, 2048], F32))

    AT = {}

    BT = {}
    CT = {}

    gx = [Ctr(es, nc, f"gx{i}") for i in range(2)]
    g_id = Ctr(es, nc, "g_id")
    gw_k = Ctr(es, nc, "gw_k")
    gw_v = Ctr(es, nc, "gw_v")
    gw_q = Ctr(es, nc, "gw_q")
    g_ksp = Ctr(es, nc, "g_ksp")
    g_vsp = Ctr(es, nc, "g_vsp")
    g_wsp = [Ctr(es, nc, f"g_wsp{i}") for i in range(2)]
    g_lnsp = [Ctr(es, nc, f"g_lnsp{i}") for i in range(2)]
    g_hsp = [Ctr(es, nc, f"g_hsp{i}") for i in range(2)]
    g_ar = Ctr(es, nc, "g_ar")
    g_cc = Ctr(es, nc, "g_cc")
    s_cfg = Ctr(es, nc, "s_cfg")
    kg = [Ctr(es, nc, f"kg{i}") for i in range(3)]
    vg = [Ctr(es, nc, f"vg{i}") for i in range(3)]
    d_oa = Ctr(es, nc, "d_oa")
    sxq = [Ctr(es, nc, f"sxq{i}") for i in range(2)]
    sw = [Ctr(es, nc, f"sw{i}") for i in range(2)]
    s_arx = Ctr(es, nc, "s_arx")
    s_lnr = [Ctr(es, nc, f"s_lnr{i}") for i in range(4)]
    s_wgu = Ctr(es, nc, "s_wgu")
    s_arh = Ctr(es, nc, "s_arh")
    s_qwd = Ctr(es, nc, "s_qwd")
    s_hld = [Ctr(es, nc, f"s_hld{i}") for i in range(2)]
    s_clnr = [Ctr(es, nc, f"s_clnr{i}") for i in range(4)]
    s_out = [Ctr(es, nc, f"s_out{i}") for i in range(2)]
    p_pr = Ctr(es, nc, "p_pr")
    p_s = Ctr(es, nc, "p_s")
    p_pv = Ctr(es, nc, "p_pv")
    p_m = Ctr(es, nc, "p_m")
    p_bc = Ctr(es, nc, "p_bc")
    a_ev = Ctr(es, nc, "a_ev")
    a_exp = Ctr(es, nc, "a_exp")
    a_m = Ctr(es, nc, "a_m")
    d_mk = Ctr(es, nc, "d_mk")
    d_ep = Ctr(es, nc, "d_ep")
    d_q = Ctr(es, nc, "d_q")
    d_m = Ctr(es, nc, "d_m")
    d_bc = Ctr(es, nc, "d_bc")
    ctrs = (gx + [g_id, gw_k, gw_v, gw_q, g_ksp, g_vsp] + g_wsp + g_lnsp
            + g_hsp + [g_ar, g_cc, s_cfg] + kg + vg + sxq + sw
            + [s_arx] + s_lnr + [s_wgu, s_arh, s_qwd] + s_hld + s_clnr
            + s_out + [p_pr, p_s, p_pv, p_m, p_bc, a_ev, a_exp, a_m, d_mk,
                       d_ep, d_q, d_m, d_bc, d_oa])

    CK = {}

    def mark(name, ctr):
        if name in CK:
            assert CK[name] == (ctr, ctr.n), (name, CK[name][1], ctr.n)
        CK[name] = (ctr, ctr.n)

    def wt(eng, name):
        ctr, val = CK[name]
        if val > 0:
            eng.wait_ge(ctr.h, val)


    def A_gpsimd(E):
        em = E is not None
        if em:
            E.memset(sgl[:, 31:32], 0.0)
            E.drain()
            E.dma_start(out=ar_i[2].ap(), in_=sgl[:, 31:32]).then_inc(g_ar.h, 16)
        g_ar.add(16)
        if em:
            E.wait_ge(g_ar.h, g_ar.n)

        if em:
            E.dma_start(out=identb[:], in_=identb_in).then_inc(g_id.h, 16)
        g_id.add(16)
        mark("identb", g_id)

        def xload(i):
            st, m = PROD_TILES[i]
            srcten = xkv_in if st == "kv" else xq_in
            if em:
                if i >= 2:
                    wt(E, f"pt_t{i - 2}")
                E.dma_start(out=xstage[i % 2][:],
                            in_=srcten[m * 128:(m + 1) * 128, :]).then_inc(gx[i % 2].h, 16)
            gx[i % 2].add(16)
            mark(f"ldt{i}", gx[i % 2])

        for i in range(4):
            xload(i)
        for d in range(HC):
            if em:
                E.dma_start(out=wkv_bf[0][:, d * D_:(d + 1) * D_],
                            in_=wk_in[d * 128:(d + 1) * 128, :]).then_inc(gw_k.h, 16)
            gw_k.add(16)
        mark("wk_all", gw_k)
        for d in range(HC):
            if em:
                E.dma_start(out=wkv_bf[1][:, d * D_:(d + 1) * D_],
                            in_=wv_in[d * 128:(d + 1) * 128, :]).then_inc(gw_v.h, 16)
            gw_v.add(16)
        mark("wv_all", gw_v)
        for d in range(HC):
            if em:
                E.dma_start(out=wq_bf[:, d * D_:(d + 1) * D_],
                            in_=wq_in[d * 128:(d + 1) * 128, :]).then_inc(gw_q.h, 16)
            gw_q.add(16)
        mark("wq_all", gw_q)

        def wraw_load(k):
            kind, chunk, sub, phase = WU_KIND[k]
            w = _wu_w(k)
            b = k % 2
            if em:
                if k >= 2:
                    wt(E, f"wq_u{k - 2}")
                if kind < 2:
                    ap_src = [wg_in, wu_in][kind][chunk * 128:(chunk + 1) * 128, :]
                else:
                    colo = sum(WD_SUBW[:sub])
                    ap_src = wd_in[chunk * 128:(chunk + 1) * 128, colo:colo + w]
                E.dma_start(out=wraw[b][:, 0:w], in_=ap_src).then_inc(sw[b].h, 16)
            sw[b].add(16)
            mark(f"wl{k}", sw[b])

        def spill_k(j):
            if em:
                wt(E, f"kev{j}")
                E.dma_start(out=ksrc[j].ap(), in_=kstage[:]).then_inc(g_ksp.h, 16)
            g_ksp.add(16)
            mark(f"ksp{j}", g_ksp)
            if em:
                E.wait_ge(g_ksp.h, g_ksp.n)
                E.collective_compute("AllGather", ALU.bypass, replica_groups=RG,
                                     ins=[ksrc[j].ap().opt()],
                                     outs=[kall[j].ap().opt()]).then_inc(g_cc.h, 1)
            g_cc.add(1)
            mark(f"cc_k{j}", g_cc)

        def spill_v(j):
            if em:
                wt(E, f"vev{j}")
                E.dma_start(out=vsrc[j].ap(), in_=vstage[:]).then_inc(g_vsp.h, 16)
            g_vsp.add(16)
            mark(f"vsp{j}", g_vsp)
            if em:
                E.wait_ge(g_vsp.h, g_vsp.n)
                E.collective_compute("AllGather", ALU.bypass, replica_groups=RG,
                                     ins=[vsrc[j].ap().opt()],
                                     outs=[vall[j].ap().opt()]).then_inc(g_cc.h, 1)
            g_cc.add(1)
            mark(f"cc_ch{j}", g_cc)

        for i in range(4, 8):
            xload(i)
        spill_k(0)
        for i in range(8, 12):
            xload(i)
        spill_v(0)
        for i in range(12, 16):
            xload(i)
        spill_k(1)
        spill_v(1)
        for i in range(16, 20):
            xload(i)
        for i in range(20, 24):
            xload(i)
        spill_k(2)
        spill_v(2)
        for i in range(24, 28):
            xload(i)
        for i in range(28, 32):
            xload(i)
        spill_k(3)
        spill_v(3)

        def wspill(k):
            if not _wu_spills(k):
                return
            kind, chunk, sub, phase = WU_KIND[k]
            w = _wu_w(k)
            b = k % 2
            if em:
                wt(E, f"wq_u{k}")
                if kind < 2:
                    dst = wq_sp[kind].ap()[chunk * 128:(chunk + 1) * 128, :]
                else:
                    colo = sum(WD_SUBW[:sub])
                    dst = wq_sp[2].ap()[chunk * 128:(chunk + 1) * 128, colo:colo + w]
                E.dma_start(out=dst, in_=wqbf[b][:, 0:w]).then_inc(g_wsp[b].h, 16)
            g_wsp[b].add(16)
            mark(f"wsp{k}", g_wsp[b])

        kld_done = 0
        ksp_done = 0

        def wq_step(n):
            nonlocal kld_done, ksp_done
            for _ in range(n):
                if kld_done < NWU:
                    wraw_load(kld_done)
                    kld_done += 1
                if kld_done >= 3 and ksp_done < kld_done - 3:
                    wspill(ksp_done)
                    ksp_done += 1

        wq_step(WQ_PROD)
        for v in range(CHUNKS):
            for g in range(SC_NSEG[v]):
                wq_step(WQ_SEG.get((v, g), 0))
                if (v, g) in ((1, 0), (1, 5), (1, 10)):
                    if em:
                        wt(E, f"segdone{v}_{g}")
                        E.collective_compute("AllReduce", ALU.max, replica_groups=RG,
                                             ins=[ar_i[2].ap().opt()],
                                             outs=[ar_o[2].ap().opt()]).then_inc(g_cc.h, 1)
                    g_cc.add(1)
                if g >= SC_S0[v]:
                    while ksp_done < kld_done:
                        wspill(ksp_done)
                        ksp_done += 1
                    s = g
                    if em:
                        wt(E, f"epi_s{s}")
                        E.dma_start(out=ln1_sp.ap()[s * 128:(s + 1) * 128, :],
                                    in_=AT["lnst"][:, (s % 2) * D_:(s % 2 + 1) * D_]).then_inc(g_lnsp[s % 2].h, 16)
                    g_lnsp[s % 2].add(16)
                    mark(f"lnsp_s{s}", g_lnsp[s % 2])
        while ksp_done < NWU:
            wspill(ksp_done)
            ksp_done += 1
        assert ksp_done == NWU and kld_done == NWU
        mark("lnsp_all0", g_lnsp[0])
        mark("lnsp_all1", g_lnsp[1])
        mark("wsp_all0", g_wsp[0])
        mark("wsp_all1", g_wsp[1])
        if em:
            wt(E, "epic1")
            E.dma_start(out=ar_i[0].ap(), in_=mx_x[:]).then_inc(g_ar.h, 16)
        g_ar.add(16)
        if em:
            E.wait_ge(g_ar.h, g_ar.n)
            E.collective_compute("AllReduce", ALU.max, replica_groups=RG,
                                 ins=[ar_i[0].ap().opt()],
                                 outs=[ar_o[0].ap().opt()]).then_inc(g_cc.h, 1)
        g_cc.add(1)
        mark("cc_arx", g_cc)

    def A_pe(E):
        em = E is not None

        def xtile(i):
            if em:
                wt(E, f"ldt{i}")
                if i < 2:
                    wt(E, "identb")
                else:
                    wt(E, f"ptev{i - 2}")
                ins = None
                for d in range(HC):
                    ins = E.transpose(
                        ps_xt[:, (i % 2) * 1024 + d * 128:(i % 2) * 1024 + (d + 1) * 128],
                        xstage[i % 2][:, d * 128:(d + 1) * 128], identb[:])
                ins.then_inc(p_pr.h, 1)
            p_pr.add(1)
            mark(f"pt_t{i}", p_pr)

        for st, idx in PROD_STAGES:
            base = PT_I[(st, 4 * idx)]
            for k in range(4):
                xtile(base + k)
            if st == "kv":
                j = idx
                if em:
                    wt(E, "wk_all")
                    wt(E, f"xfmkv{j}")
                for d in range(HC):
                    if em:
                        if d >= 2:
                            wt(E, f"kevd{j}_{d - 2}")
                        if j == 1 and d < 2:
                            wt(E, ["mrep", "negm"][d])
                        pb = [ps_pa, ps_pb][d % 2]
                        ins = None
                        for hch in range(HC):
                            ins = E.matmul(pb[:, 0:512],
                                           wkv_bf[0][:, hch * D_ + d * 128:hch * D_ + (d + 1) * 128],
                                           xfm[0][:, hch * 512:(hch + 1) * 512],
                                           start=hch == 0, stop=hch == HC - 1)
                        ins.then_inc(p_pr.h, 1)
                    p_pr.add(1)
                    mark(f"kmm{j}_{d}", p_pr)
                if em:
                    wt(E, "wv_all")
                for ksub in range(4):
                    if em:
                        if ksub >= 2:
                            wt(E, f"vevk{j}_{ksub - 2}")
                        vb0 = (ksub % 2) * 1024
                        ins = None
                        for hch in range(HC):
                            for dvh in range(2):
                                ins = E.matmul(
                                    ps_v[:, vb0 + dvh * 512:vb0 + (dvh + 1) * 512],
                                    xfm[0][:, hch * 512 + ksub * 128:hch * 512 + ksub * 128 + 128],
                                    wkv_bf[1][:, hch * D_ + dvh * 512:hch * D_ + (dvh + 1) * 512],
                                    start=hch == 0, stop=hch == HC - 1)
                        ins.then_inc(p_pr.h, 1)
                    p_pr.add(1)
                    mark(f"vmm{j}_{ksub}", p_pr)
            else:
                r = idx
                if em:
                    wt(E, "wq_all")
                    wt(E, f"xfmq{r}")
                for d in range(HC):
                    if em:
                        if d >= 2:
                            wt(E, f"qevd{r}_{d - 2}")
                        pb = [ps_pa, ps_pb][d % 2]
                        ins = None
                        for hch in range(HC):
                            ins = E.matmul(pb[:, 0:512],
                                           wq_bf[:, hch * D_ + d * 128:hch * D_ + (d + 1) * 128],
                                           xfm[1][:, hch * 512:(hch + 1) * 512],
                                           start=hch == 0, stop=hch == HC - 1)
                        ins.then_inc(p_pr.h, 1)
                    p_pr.add(1)
                    mark(f"qmm{r}_{d}", p_pr)
                if r == 0:
                    if em:
                        wt(E, "ksampc")
                        wt(E, "qevd0_7")
                        ins = None
                        for d in range(HC):
                            ins = E.matmul(ps_pa[:, 0:128], ksamp[:, d * 128:(d + 1) * 128],
                                           q_fm[:, d * T_:d * T_ + 128],
                                           start=d == 0, stop=d == HC - 1)
                        ins.then_inc(p_m.h, 1)
                    p_m.add(1)
                    mark("samp_mm", p_m)
                    if em:
                        wt(E, "mrep")
                        wt(E, "cfg")
                        E.transpose(ps_pb[:, 0:128], junk[:, 0:128], identf[:]).then_inc(p_m.h, 1)
                    p_m.add(1)
                    mark("mrep_t", p_m)

        ps_s, ps_ot, ps_r = AT["ps_s"], AT["ps_ot"], AT["ps_r"]

        for v in range(CHUNKS):
            for g in range(SC_NSEG[v]):
                qlo = seg_qlo(v, g)
                subs = seg_subs(v, g)
                for t2 in range(8):
                    t = 8 * g + t2
                    L = LOAD_I[(v, 2 * g + t2 // 4)]
                    buf = L % 3
                    for si2, (off, w) in enumerate(subs):
                        STI = STILE_I[(v, g, t2, si2)]
                        if em:
                            wt(E, f"kgl{L}")
                            if STI == 0:
                                wt(E, "prod_ev")
                                wt(E, "negm")
                            if STI >= 2:
                                E.wait_ge(a_exp.h, STI - 1)
                            slot = ps_s[:, (STI % 2) * 512:(STI % 2) * 512 + w]
                            ins = None
                            for d in range(HC):
                                ins = E.matmul(
                                    slot,
                                    AT["kbuf"][buf][:, d * 512 + (t % 4) * 128:d * 512 + (t % 4) * 128 + 128],
                                    q_fm[:, d * T_ + 128 * qlo + off:d * T_ + 128 * qlo + off + w],
                                    start=d == 0, stop=d == HC - 1)
                            ins.then_inc(p_s.h, 1)
                        p_s.add(1)
                        assert p_s.n == STI + 1
                for si in range(seg_ns(v, g)):
                    SW = SWEEP_I[(v, g, si)]
                    if em:
                        if SW == 0:
                            wt(E, "prod_ev")
                        if si == 0:
                            wt(E, f"vgl{LOAD_I[(v, 2 * g)]}")
                            wt(E, f"vgl{LOAD_I[(v, 2 * g + 1)]}")
                        E.wait_ge(a_exp.h, STILE_I[(v, g, 7, len(subs) - 1)] + 1)
                        if g >= SC_S0[v]:
                            wt(E, f"mseg{v}_{g}")
                        if SW >= 2:
                            E.wait_ge(d_oa.h, SW - 1)
                        par = SW % 2
                        ins = None
                        for t2 in range(8):
                            buf = LOAD_I[(v, 2 * g + t2 // 4)] % 3
                            lh = AT["pseg"][:, t2 * 1024 + si * 128:t2 * 1024 + si * 128 + 128]
                            for dvh in range(2):
                                ins = E.matmul(
                                    ps_ot[:, par * 1024 + dvh * 512:par * 1024 + (dvh + 1) * 512],
                                    lh,
                                    AT["vbuf"][buf][:, (8 * g + t2) % 4 * 1024 + dvh * 512:(8 * g + t2) % 4 * 1024 + (dvh + 1) * 512],
                                    start=t2 == 0, stop=t2 == 7)
                            ins = E.matmul(ps_r[:, par * 512:par * 512 + 1], lh, ones_bf[:],
                                           start=t2 == 0, stop=t2 == 7)
                        ins.then_inc(p_pv.h, 1)
                    p_pv.add(1)
                    assert p_pv.n == SW + 1
                mark(f"segdone{v}_{g}", p_pv)

    def A_act(E):
        em = E is not None

        def evac(waits, out_fn, in_fn, name):
            if em:
                for w in waits:
                    wt(E, w)
                E.activation(out=out_fn(), in_=in_fn(), func=AF.Copy).then_inc(a_ev.h, 1)
            a_ev.add(1)
            mark(name, a_ev)

        for st, idx in PROD_STAGES:
            base = PT_I[(st, 4 * idx)]
            buf = 0 if st == "kv" else 1
            for k in range(4):
                i = base + k
                extra = []
                if k == 0:
                    if st == "kv" and idx >= 1:
                        extra = [f"vmm{idx - 1}_3"]
                    if st == "q" and idx >= 1:
                        extra = [f"qmm{idx - 1}_7"]
                evac([f"pt_t{i}"] + extra,
                     lambda buf=buf, k=k: xfm[buf][:].rearrange(
                         "p (h s k) -> p h s k", h=HC, s=4)[:, :, k, :],
                     lambda i=i: ps_xt[:, (i % 2) * 1024:(i % 2) * 1024 + 1024].rearrange(
                         "p (h k) -> p h k", h=HC),
                     f"ptev{i}")
            if st == "kv":
                j = idx
                mark(f"xfmkv{j}", a_ev)
                for d in range(HC):
                    extra = []
                    if d == 0 and j >= 1:
                        extra = [f"ksp{j - 1}"]
                        if j == 1:
                            extra.append("ksampc")
                    evac([f"kmm{j}_{d}"] + extra,
                         lambda d=d: kstage[:, d * 512:(d + 1) * 512],
                         lambda d=d: [ps_pa, ps_pb][d % 2][:, 0:512],
                         f"kevd{j}_{d}")
                mark(f"kev{j}", a_ev)
                for ksub in range(4):
                    extra = [f"vsp{j - 1}"] if (ksub == 0 and j >= 1) else []
                    evac([f"vmm{j}_{ksub}"] + extra,
                         lambda k=ksub: vstage[:, k * 1024:(k + 1) * 1024],
                         lambda k=ksub: ps_v[:, (k % 2) * 1024:(k % 2) * 1024 + 1024],
                         f"vevk{j}_{ksub}")
                mark(f"vev{j}", a_ev)
            else:
                r = idx
                mark(f"xfmq{r}", a_ev)
                for d in range(HC):
                    evac([f"qmm{r}_{d}"],
                         lambda r=r, d=d: q_fm[:, d * T_ + r * 512:d * T_ + (r + 1) * 512],
                         lambda d=d: [ps_pa, ps_pb][d % 2][:, 0:512],
                         f"qevd{r}_{d}")
        mark("prod_ev", a_ev)

        ps_s = AT["ps_s"]
        for v in range(CHUNKS):
            for g in range(SC_NSEG[v]):
                subs = seg_subs(v, g)
                SI = SEG_I[(v, g)]
                for gi2 in (2 * g, 2 * g + 1):
                    L = LOAD_I[(v, gi2)]
                    j = gi2 // 8
                    r = gi2 % 8
                    if em:
                        wt(E, f"cc_ch{j}")
                        if L < 3:
                            wt(E, "qmm3_7")
                            wt(E, "prod_ev")
                            wt(E, "ksp3")
                            wt(E, "vsp3")
                        else:
                            vp, gp = LOAD_LIST[L - 3]
                            wt(E, f"segdone{vp}_{gp // 2}")
                        E.dma_start(out=AT["vbuf"][L % 3][:],
                                    in_=vall[j].ap()[r * 128:(r + 1) * 128, :]).then_inc(vg[L % 3].h, 16)
                    vg[L % 3].add(16)
                    mark(f"vgl{L}", vg[L % 3])
                if g >= SC_S0[v]:
                    s = g
                    if em:
                        if s >= 2:
                            wt(E, f"vstt{s - 2}")
                        E.dma_start(out=xqr[s % 2][:],
                                    in_=xq_in[s * 128:(s + 1) * 128, :]).then_inc(sxq[s % 2].h, 16)
                    sxq[s % 2].add(16)
                    mark(f"xq{s}", sxq[s % 2])
                for t2 in range(8):
                    for si2, (off, w) in enumerate(subs):
                        STI = STILE_I[(v, g, t2, si2)]
                        if em:
                            E.wait_ge(p_s.h, STI + 1)
                            if STI == 0:
                                wt(E, "negm")
                            if SI >= 1:
                                vp, gp = SEG_LIST[SI - 1]
                                wt(E, f"segdone{vp}_{gp}")
                            E.activation(out=AT["pseg"][:, t2 * 1024 + off:t2 * 1024 + off + w],
                                         in_=ps_s[:, (STI % 2) * 512:(STI % 2) * 512 + w],
                                         func=AF.Exp, bias=negm[:],
                                         scale=1.0 / 32.0).then_inc(a_exp.h, 1)
                        a_exp.add(1)
                if g >= SC_S0[v]:
                    si = g - SC_S0[v]
                    s = g
                    if em:
                        wt(E, f"var_s{s}")
                        E.activation(out=sgl[:, 24:25], in_=stat[:, 8 + si:9 + si],
                                     func=AF.Ln, bias=eps_sb[:], scale=1.0)
                        E.drain()
                        E.activation(out=stat[:, 16 + si:17 + si], in_=sgl[:, 24:25],
                                     func=AF.Exp, bias=0.0, scale=-0.5).then_inc(a_m.h, 1)
                    a_m.add(1)
                    mark(f"rstd_s{s}", a_m)

    def A_dve(E):
        em = E is not None
        if em:
            E.memset(zeros[:], 0.0)
            E.memset(ones_bf[:], 1.0)
            E.memset(mx_x[:], 0.0)
            E.memset(mx_h[:], 0.0)
            E.memset(eps_sb[:], EPS)
            E.engine_nop().then_inc(d_m.h, 1)
        d_m.add(1)
        mark("dve_init", d_m)

        if em:
            wt(E, "kev0")
            E.tensor_copy(
                ksamp[:].rearrange("p (d k) -> p d k", d=HC),
                kstage[:].rearrange("p (d k) -> p d k", d=HC)[:, :, 0:128],
            ).then_inc(d_m.h, 1)
        d_m.add(1)
        mark("ksampc", d_m)

        def wq_unit(k):
            kind, chunk, sub, phase = WU_KIND[k]
            w = _wu_w(k)
            b = k % 2
            if em:
                wt(E, f"wl{k}")
                buf = wraw[b]
                if kind < 2:
                    c0, c1 = 8 + b * 2, 9 + b * 2
                    kp = _prev_spill(k)
                    if kp is not None:
                        wt(E, f"wsp{kp}")
                    E.tensor_reduce(sgl[:, c0:c0 + 1], buf[:, 0:w], AX.X, ALU.max,
                                    apply_absolute_value=True)
                    E.drain()
                    E.reciprocal(sgl[:, c1:c1 + 1], sgl[:, c0:c0 + 1])
                    E.drain()
                    E.tensor_scalar(out=sgl[:, c1:c1 + 1], in0=sgl[:, c1:c1 + 1],
                                    scalar1=127.0, scalar2=None, op0=ALU.mult)
                    E.tensor_scalar(out=sgl[:, c0:c0 + 1], in0=sgl[:, c0:c0 + 1],
                                    scalar1=1.0 / 127.0, scalar2=None, op0=ALU.mult)
                    E.drain()
                    E.tensor_scalar(out=wtmp[:, 0:w], in0=buf[:, 0:w],
                                    scalar1=sgl[:, c1:c1 + 1], scalar2=127.49,
                                    op0=ALU.mult, op1=ALU.min)
                    E.drain()
                    E.tensor_scalar(out=wtmpi[:, 0:w], in0=wtmp[:, 0:w],
                                    scalar1=-128.49, scalar2=None, op0=ALU.max)
                    E.drain()
                    E.tensor_scalar(out=wqbf[b][:, 0:w], in0=wtmpi[:, 0:w],
                                    scalar1=sgl[:, c0:c0 + 1], scalar2=None,
                                    op0=ALU.mult).then_inc(d_q.h, 1)
                elif phase == 0:
                    E.tensor_reduce(sgl[:, 13:14], buf[:, 0:w], AX.X, ALU.max,
                                    apply_absolute_value=True)
                    E.drain()
                    if sub == 0:
                        E.tensor_copy(sgl[:, 12:13], sgl[:, 13:14])
                    else:
                        E.tensor_max(sgl[:, 12:13], sgl[:, 12:13], sgl[:, 13:14])
                    E.drain()
                    if sub == 2:
                        E.reciprocal(sgl[:, 14:15], sgl[:, 12:13])
                        E.drain()
                        E.tensor_scalar(out=sgl[:, 14:15], in0=sgl[:, 14:15],
                                        scalar1=127.0, scalar2=None, op0=ALU.mult)
                        E.tensor_scalar(out=sgl[:, 12:13], in0=sgl[:, 12:13],
                                        scalar1=1.0 / 127.0, scalar2=None, op0=ALU.mult)
                        E.drain()
                    E.engine_nop().then_inc(d_q.h, 1)
                else:
                    kp = _prev_spill(k)
                    if kp is not None:
                        wt(E, f"wsp{kp}")
                    E.tensor_scalar(out=wtmp[:, 0:w], in0=buf[:, 0:w],
                                    scalar1=sgl[:, 14:15], scalar2=127.49,
                                    op0=ALU.mult, op1=ALU.min)
                    E.drain()
                    E.tensor_scalar(out=wtmpi[:, 0:w], in0=wtmp[:, 0:w],
                                    scalar1=-128.49, scalar2=None, op0=ALU.max)
                    E.drain()
                    E.tensor_scalar(out=wqbf[b][:, 0:w], in0=wtmpi[:, 0:w],
                                    scalar1=sgl[:, 12:13], scalar2=None,
                                    op0=ALU.mult).then_inc(d_q.h, 1)
            d_q.add(1)
            mark(f"wq_u{k}", d_q)

        kdone = 0
        for k in range(WQ_PROD):
            wq_unit(k)
            kdone += 1

        if em:
            wt(E, "samp_mm")
            E.tensor_reduce(sgl[:, 0:1], ps_pa[:, 0:128], AX.X, ALU.max)
            E.drain()
            E.tensor_scalar(out=junk[:, 0:128], in0=zeros[:],
                            scalar1=sgl[:, 0:1], scalar2=None, op0=ALU.add)
            E.engine_nop().then_inc(d_m.h, 1)
        d_m.add(1)
        mark("mrep", d_m)
        if em:
            wt(E, "mrep_t")
            E.tensor_reduce(sgl[:, 1:2], ps_pb[:, 0:128], AX.X, ALU.max)
            E.drain()
            E.tensor_scalar(out=negm[:], in0=sgl[:, 1:2], scalar1=-1.0 / 32.0,
                            scalar2=-2.0, op0=ALU.mult, op1=ALU.add).then_inc(d_m.h, 1)
        d_m.add(1)
        mark("negm", d_m)

        ps_ot, ps_r = AT.get("ps_ot"), AT.get("ps_r")
        for v in range(CHUNKS):
            for g in range(SC_NSEG[v]):
                qlo = seg_qlo(v, g)
                subs = seg_subs(v, g)
                if g >= SC_S0[v]:
                    for t2 in range(8):
                        STI = STILE_I[(v, g, t2, 0)]
                        if em:
                            if d_mk.n == 0:
                                wt(E, "cfg")
                            E.wait_ge(a_exp.h, STI + 1)
                            E.tensor_tensor(AT["pseg"][:, t2 * 1024:t2 * 1024 + 128],
                                            AT["pseg"][:, t2 * 1024:t2 * 1024 + 128],
                                            masks[:, t2 * 128:(t2 + 1) * 128],
                                            ALU.mult).then_inc(d_mk.h, 1)
                        d_mk.add(1)
                    mark(f"mseg{v}_{g}", d_mk)
                for si in range(seg_ns(v, g)):
                    SW = SWEEP_I[(v, g, si)]
                    par = SW % 2
                    osl = qlo + si - SC_S0[v]
                    first = g == 0
                    if em:
                        E.wait_ge(p_pv.h, SW + 1)
                        if first:
                            E.tensor_copy(AT["osb"][:, osl * D_:(osl + 1) * D_],
                                          ps_ot[:, par * 1024:par * 1024 + 1024])
                            E.drain()
                            E.tensor_copy(AT["rsb"][:, osl:osl + 1],
                                          ps_r[:, par * 512:par * 512 + 1])
                            E.drain()
                        else:
                            E.tensor_add(AT["osb"][:, osl * D_:(osl + 1) * D_],
                                         AT["osb"][:, osl * D_:(osl + 1) * D_],
                                         ps_ot[:, par * 1024:par * 1024 + 1024])
                            E.drain()
                            E.tensor_add(AT["rsb"][:, osl:osl + 1],
                                         AT["rsb"][:, osl:osl + 1],
                                         ps_r[:, par * 512:par * 512 + 1])
                            E.drain()
                        E.engine_nop().then_inc(d_oa.h, 1)
                    d_oa.add(1)
                    assert d_oa.n == SW + 1
                for _ in range(WQ_SEG.get((v, g), 0)):
                    if kdone < NWU:
                        wq_unit(kdone)
                        kdone += 1
                if g >= SC_S0[v]:
                    si = g - SC_S0[v]
                    s = g
                    if em:
                        wt(E, f"xq{s}")
                        if s >= 2:
                            wt(E, f"lnsp_s{s - 2}")
                        E.reciprocal(sgl[:, 2:3], AT["rsb"][:, si:si + 1])
                        E.drain()
                        E.scalar_tensor_tensor(
                            out=AT["resid"][:], in0=AT["osb"][:, si * D_:(si + 1) * D_],
                            scalar=sgl[:, 2:3], in1=xqr[s % 2][:],
                            op0=ALU.mult, op1=ALU.add, accum_out=stat[:, 24:25])
                        E.engine_nop().then_inc(d_ep.h, 1)
                    d_ep.add(1)
                    mark(f"vstt{s}", d_ep)
                    if em:
                        E.drain()
                        E.scalar_tensor_tensor(
                            out=junk[:, 0:D_], in0=AT["resid"][:], scalar=1.0,
                            in1=AT["resid"][:], op0=ALU.mult, op1=ALU.mult,
                            accum_out=stat[:, 25:26])
                        E.drain()
                        E.tensor_scalar(out=stat[:, si:si + 1], in0=stat[:, 24:25],
                                        scalar1=1.0 / D_, scalar2=None, op0=ALU.mult)
                        E.drain()
                        E.tensor_tensor(stat[:, 26:27], stat[:, si:si + 1],
                                        stat[:, si:si + 1], ALU.mult)
                        E.tensor_scalar(out=stat[:, 8 + si:9 + si], in0=stat[:, 25:26],
                                        scalar1=1.0 / D_, scalar2=None, op0=ALU.mult)
                        E.drain()
                        E.tensor_sub(stat[:, 8 + si:9 + si], stat[:, 8 + si:9 + si],
                                     stat[:, 26:27])
                        E.engine_nop().then_inc(d_ep.h, 1)
                    d_ep.add(1)
                    mark(f"var_s{s}", d_ep)
                    if em:
                        wt(E, f"rstd_s{s}")
                        E.tensor_scalar(out=AT["lnst"][:, (s % 2) * D_:(s % 2 + 1) * D_],
                                        in0=AT["resid"][:],
                                        scalar1=stat[:, si:si + 1],
                                        scalar2=stat[:, 16 + si:17 + si],
                                        op0=ALU.subtract, op1=ALU.mult)
                        E.drain()
                        E.tensor_reduce(stat[:, 27:28],
                                        AT["lnst"][:, (s % 2) * D_:(s % 2 + 1) * D_],
                                        AX.X, ALU.max, apply_absolute_value=True)
                        E.drain()
                        E.tensor_tensor(mx_x[:], mx_x[:], stat[:, 27:28], ALU.max)
                        E.engine_nop().then_inc(d_ep.h, 1)
                    d_ep.add(1)
                    mark(f"epi_s{s}", d_ep)
            mark(f"epic{v}", d_ep)
        assert kdone == NWU

    def A_sync(E):
        em = E is not None
        if em:
            E.dma_start(out=identf[:], in_=identf_in).then_inc(s_cfg.h, 16)
            E.dma_start(out=clipx[:], in_=clipx_in).then_inc(s_cfg.h, 16)
            E.dma_start(out=clipd[:], in_=clipd_in).then_inc(s_cfg.h, 16)
            for tt in range(8):
                E.dma_start(out=masks[:, tt * 128:(tt + 1) * 128],
                            in_=masks_in[tt * 128:(tt + 1) * 128, :]).then_inc(s_cfg.h, 16)
        s_cfg.add(16 * 11)
        mark("cfg", s_cfg)


        for v in range(CHUNKS):
            for g in range(SC_NSEG[v]):
                for gi2 in (2 * g, 2 * g + 1):
                    L = LOAD_I[(v, gi2)]
                    j = gi2 // 8
                    r = gi2 % 8
                    if em:
                        wt(E, f"cc_ch{j}")
                        if L < 3:
                            wt(E, "qmm3_7")
                            wt(E, "prod_ev")
                            wt(E, "ksp3")
                            wt(E, "vsp3")
                        else:
                            vp, gp = LOAD_LIST[L - 3]
                            wt(E, f"segdone{vp}_{gp // 2}")
                        E.dma_start(out=AT["kbuf"][L % 3][:],
                                    in_=kall[j].ap()[r * 128:(r + 1) * 128, :]).then_inc(kg[L % 3].h, 16)
                    kg[L % 3].add(16)
                    mark(f"kgl{L}", kg[L % 3])

    def B_sync(E):
        em = E is not None

        def lnrd_load(n):
            if em:
                if n == 0:
                    wt(E, "lnsp_all0")
                    wt(E, "lnsp_all1")
                if n >= 4:
                    wt(E, f"qx{n - 4}q")
                E.dma_start(out=BT["lnrd"][n % 4][:],
                            in_=ln1_sp.ap()[n * 128:(n + 1) * 128, :]).then_inc(s_lnr[n % 4].h, 16)
            s_lnr[n % 4].add(16)
            mark(f"lnrd{n}", s_lnr[n % 4])

        for n in range(4):
            lnrd_load(n)
        if em:
            wt(E, "cc_arx")
            E.dma_start(out=arx_sb[:], in_=ar_o[0].ap()).then_inc(s_arx.h, 16)
        s_arx.add(16)
        mark("arx_ld", s_arx)
        for n in range(4, NQT):
            lnrd_load(n)

    def B_pe(E):
        em = E is not None
        if em:
            wt(E, "bx_rep")
            E.transpose(BT["ps_t"][:, 0:128], junk[:, 0:128], identf[:]).then_inc(p_bc.h, 1)
        p_bc.add(1)
        mark("bx_tr", p_bc)
        for n in range(NQT):
            if em:
                wt(E, f"qx{n}q")
                if n >= 2:
                    wt(E, f"qxev{n - 2}")
                ins = None
                for d in range(HC):
                    ins = E.transpose(
                        BT["ps_xtb"][:, (n % 2) * 1024 + d * 128:(n % 2) * 1024 + (d + 1) * 128],
                        BT["qxst"][n % 2][:, d * 128:(d + 1) * 128], identb[:])
                ins.then_inc(p_bc.h, 1)
            p_bc.add(1)
            mark(f"qxt{n}", p_bc)
        nu = 0
        for icx in range(IC):
            for tc in range(4):
                par = nu % 2
                if em:
                    if nu == 0:
                        wt(E, "wguT")
                    if icx == 0:
                        wt(E, f"qxev{4 * tc + 3}")
                    if nu >= 2:
                        wt(E, f"hdone{nu - 2}")
                    ins = None
                    for d in range(HC):
                        ins = E.matmul(BT["ps_g"][:, par * 512:par * 512 + 512],
                                       BT["wg_T"][:, d * DI_ + icx * 128:d * DI_ + (icx + 1) * 128],
                                       BT["qx_fm"][:, d * T_ + tc * 512:d * T_ + (tc + 1) * 512],
                                       start=d == 0, stop=d == HC - 1)
                    ins.then_inc(p_bc.h, 1)
                p_bc.add(1)
                mark(f"g_mm{nu}", p_bc)
                if em:
                    ins = None
                    for d in range(HC):
                        ins = E.matmul(BT["ps_u"][:, par * 512:par * 512 + 512],
                                       BT["wu_T"][:, d * DI_ + icx * 128:d * DI_ + (icx + 1) * 128],
                                       BT["qx_fm"][:, d * T_ + tc * 512:d * T_ + (tc + 1) * 512],
                                       start=d == 0, stop=d == HC - 1)
                    ins.then_inc(p_bc.h, 1)
                p_bc.add(1)
                mark(f"u_mm{nu}", p_bc)
                nu += 1

    def B_act(E):
        em = E is not None
        for d in range(HC):
            if em:
                if d == 0:
                    wt(E, "wsp_all0")
                    wt(E, "wsp_all1")
                E.dma_start(out=BT["wg_T"][:, d * DI_:(d + 1) * DI_],
                            in_=wq_sp[0].ap()[:, d * 128:(d + 1) * 128],
                            transpose=True).then_inc(s_wgu.h, 16)
            s_wgu.add(16)
        for d in range(HC):
            if em:
                E.dma_start(out=BT["wu_T"][:, d * DI_:(d + 1) * DI_],
                            in_=wq_sp[1].ap()[:, d * 128:(d + 1) * 128],
                            transpose=True).then_inc(s_wgu.h, 16)
            s_wgu.add(16)
        mark("wguT", s_wgu)
        if em:
            wt(E, "bx_tr")
            wt(E, "cfg")
            E.activation(out=sgl[:, 16:17], in_=clipx[:], func=AF.Sigmoid).then_inc(a_m.h, 1)
        a_m.add(1)
        mark("sigx", a_m)
        for n in range(NQT):
            if em:
                wt(E, f"qxt{n}")
                E.activation(
                    out=BT["qx_fm"][:].rearrange("p (h s k) -> p h s k", h=HC, s=NQT)[:, :, n, :],
                    in_=BT["ps_xtb"][:, (n % 2) * 1024:(n % 2) * 1024 + 1024].rearrange(
                        "p (h k) -> p h k", h=HC),
                    func=AF.Copy).then_inc(a_m.h, 1)
            a_m.add(1)
            mark(f"qxev{n}", a_m)
        for nu in range(4 * IC):
            par = nu % 2
            if em:
                wt(E, f"g_mm{nu}")
                if nu >= 2:
                    wt(E, f"hdone{nu - 2}")
                E.activation(out=BT["silu"][par][:],
                             in_=BT["ps_g"][:, par * 512:par * 512 + 512],
                             func=AF.Silu).then_inc(a_m.h, 1)
            a_m.add(1)
            mark(f"silu{nu}", a_m)

    def B_dve(E):
        em = E is not None
        if em:
            wt(E, "arx_ld")
            E.tensor_scalar(out=junk[:, 0:128], in0=zeros[:], scalar1=arx_sb[:],
                            scalar2=None, op0=ALU.add)
            E.engine_nop().then_inc(d_bc.h, 1)
        d_bc.add(1)
        mark("bx_rep", d_bc)
        if em:
            wt(E, "bx_tr")
            E.tensor_reduce(sgl[:, 17:18], BT["ps_t"][:, 0:128], AX.X, ALU.max)
            wt(E, "sigx")
            E.drain()
            E.tensor_tensor(sgl[:, 18:19], sgl[:, 16:17], sgl[:, 17:18], ALU.mult)
            E.drain()
            E.tensor_scalar(out=sgl[:, 18:19], in0=sgl[:, 18:19],
                            scalar1=1.0 / 127.0, scalar2=None, op0=ALU.mult)
            E.drain()
            E.reciprocal(sgl[:, 19:20], sgl[:, 18:19])
            E.engine_nop().then_inc(d_bc.h, 1)
        d_bc.add(1)
        mark("sx", d_bc)

        for n in range(NQT):
            if em:
                if n == 0:
                    wt(E, "sx")
                wt(E, f"lnrd{n}")
                if n >= 2:
                    wt(E, f"qxt{n - 2}")
                E.tensor_scalar(out=junk[:, 0:D_], in0=BT["lnrd"][n % 4][:],
                                scalar1=sgl[:, 19:20], scalar2=127.49,
                                op0=ALU.mult, op1=ALU.min)
                E.drain()
                E.tensor_scalar(out=wtmpi[:, 0:D_], in0=junk[:, 0:D_],
                                scalar1=-128.49, scalar2=None, op0=ALU.max)
                E.drain()
                E.tensor_scalar(out=BT["qxst"][n % 2][:], in0=wtmpi[:, 0:D_],
                                scalar1=sgl[:, 18:19], scalar2=None,
                                op0=ALU.mult).then_inc(d_bc.h, 1)
            d_bc.add(1)
            mark(f"qx{n}q", d_bc)

        for nu in range(4 * IC):
            par = nu % 2
            if em:
                wt(E, f"silu{nu}")
                wt(E, f"u_mm{nu}")
                if nu >= 2:
                    wt(E, f"hsp{nu - 2}")
                E.tensor_tensor(BT["hst"][par][:], BT["silu"][par][:],
                                BT["ps_u"][:, par * 512:par * 512 + 512], ALU.mult)
                E.drain()
                E.tensor_reduce(stat[:, 14:15], BT["hst"][par][:], AX.X, ALU.max,
                                apply_absolute_value=True)
                E.drain()
                E.tensor_tensor(mx_h[:], mx_h[:], stat[:, 14:15], ALU.max)
                E.engine_nop().then_inc(d_bc.h, 1)
            d_bc.add(1)
            mark(f"hdone{nu}", d_bc)

    def B_gpsimd(E):
        em = E is not None
        for nu in range(4 * IC):
            icx, tc = nu // 4, nu % 4
            par = nu % 2
            if em:
                wt(E, f"hdone{nu}")
                E.dma_start(out=h_spd.ap()[:, icx * T_ + tc * 512:icx * T_ + (tc + 1) * 512],
                            in_=BT["hst"][par][:]).then_inc(g_hsp[par].h, 16)
            g_hsp[par].add(16)
            mark(f"hsp{nu}", g_hsp[par])
        mark("hsp_all0", g_hsp[0])
        mark("hsp_all1", g_hsp[1])
        if em:
            wt(E, f"hdone{60}")
            E.collective_compute("AllReduce", ALU.max, replica_groups=RG,
                                 ins=[ar_i[2].ap().opt()],
                                 outs=[ar_o[2].ap().opt()]).then_inc(g_cc.h, 1)
        g_cc.add(1)
        if em:
            wt(E, f"hdone{4 * IC - 1}")
            E.dma_start(out=ar_i[1].ap(), in_=mx_h[:]).then_inc(g_ar.h, 16)
        g_ar.add(16)
        if em:
            E.wait_ge(g_ar.h, g_ar.n)
            E.collective_compute("AllReduce", ALU.max, replica_groups=RG,
                                 ins=[ar_i[1].ap().opt()],
                                 outs=[ar_o[1].ap().opt()]).then_inc(g_cc.h, 1)
        g_cc.add(1)
        mark("cc_arh", g_cc)

    def C_sync(E):
        em = E is not None

        def hld_load(n, EE):
            if EE is not None:
                if n < 2:
                    wt(EE, "hsp_all0")
                    wt(EE, "hsp_all1")
                else:
                    wt(EE, f"hq{n - 2}")
                EE.dma_start(
                    out=CT["hbuf"][n % 2][:].rearrange("p (i k) -> p i k", i=IC),
                    in_=h_spd.ap()[:, :].rearrange("p (i t) -> p i t", i=IC)[:, :, n * 128:(n + 1) * 128],
                ).then_inc(s_hld[n % 2].h, 16)
            s_hld[n % 2].add(16)
            mark(f"hld{n}", s_hld[n % 2])

        def lnr_load(m, EE):
            if EE is not None:
                if m == 0:
                    wt(EE, "lnsp_all0")
                    wt(EE, "lnsp_all1")
                if m >= 4:
                    wt(EE, f"r2d{m - 4}")
                EE.dma_start(out=CT["lnr"][m % 4][:],
                             in_=ln1_sp.ap()[m * 128:(m + 1) * 128, :]).then_inc(s_clnr[m % 4].h, 16)
            s_clnr[m % 4].add(16)
            mark(f"lnr{m}", s_clnr[m % 4])

        C_sync.hld_load = hld_load
        C_sync.lnr_load = lnr_load
        hld_load(0, E)
        if em:
            wt(E, "cc_arh")
            E.dma_start(out=arh_sb[:], in_=ar_o[1].ap()).then_inc(s_arh.h, 16)
        s_arh.add(16)
        mark("arh_ld", s_arh)
        for n in range(1, NQT):
            if n % 2 == 0:
                hld_load(n, E)
            else:
                lnr_load(n, E)
        if em:
            E.wait_ge(s_out[0].h, 16 * NQT // 2)
            E.wait_ge(s_out[1].h, 16 * NQT // 2)

    def C_pe(E):
        em = E is not None
        if em:
            wt(E, "ch_rep")
            E.transpose(CT["ps_ct"][:, 0:128], junk[:, 0:128], identf[:]).then_inc(p_bc.h, 1)
        p_bc.add(1)
        mark("ch_tr", p_bc)
        for n in range(NQT):
            par = n % 2
            if em:
                if n == 0:
                    wt(E, "qwdT")
                wt(E, f"hq{n}")
                if n >= 2:
                    wt(E, f"r2d{n - 2}")
                ins = None
                for icx in range(IC):
                    for hoh in range(2):
                        ins = E.matmul(
                            CT["ps_d"][:, par * 1024 + hoh * 512:par * 1024 + (hoh + 1) * 512],
                            CT["hq"][par][:, icx * 128:(icx + 1) * 128],
                            CT["qwd_T"][:, icx * D_ + hoh * 512:icx * D_ + (hoh + 1) * 512],
                            start=icx == 0, stop=icx == IC - 1)
                ins.then_inc(p_bc.h, 1)
            p_bc.add(1)
            mark(f"dn{n}", p_bc)

    def C_act(E):
        em = E is not None
        for icx in range(IC):
            if em:
                if icx == 0:
                    wt(E, "wsp_all0")
                    wt(E, "wsp_all1")
                E.dma_start(out=CT["qwd_T"][:, icx * D_:(icx + 1) * D_],
                            in_=wq_sp[2].ap()[:, icx * 128:(icx + 1) * 128],
                            transpose=True).then_inc(s_qwd.h, 16)
            s_qwd.add(16)
        mark("qwdT", s_qwd)
        if em:
            wt(E, "ch_tr")
            E.activation(out=sgl[:, 20:21], in_=clipd[:], func=AF.Sigmoid).then_inc(a_m.h, 1)
        a_m.add(1)
        mark("sigd", a_m)

        def act_load(m, EE):
            if m % 2 == 1:
                C_sync.hld_load(m, EE)
            else:
                C_sync.lnr_load(m, EE)

        act_load(0, E)
        act_load(1, E)
        for n in range(NQT + 3):
            if n + 2 < NQT:
                act_load(n + 2, E)
            if 3 <= n <= NQT + 2:
                m = n - 3
                p = m % 2
                if em:
                    wt(E, f"bias{m}")
                    if m >= 2:
                        wt(E, f"out{m - 2}")
                    E.drain()
                    E.activation(out=CT["ot"][p][:], in_=CT["r2"][p][:],
                                 func=AF.Identity, bias=stat[:, 14 + p:15 + p],
                                 scale=stat[:, 6 + p:7 + p]).then_inc(a_m.h, 1)
                a_m.add(1)
                mark(f"cote{m}", a_m)
            if 1 <= n <= NQT:
                m = n - 1
                p = m % 2
                if em:
                    wt(E, f"r2d{m}")
                    E.drain()
                    E.activation(out=junk[:, 0:D_], in_=CT["r2"][p][:],
                                 func=AF.Square,
                                 accum_out=stat[:, 10 + p:11 + p]).then_inc(a_m.h, 1)
                a_m.add(1)
                mark(f"sqa{m}", a_m)
            if 2 <= n <= NQT + 1:
                m = n - 2
                p = m % 2
                if em:
                    wt(E, f"vard{m}")
                    E.activation(out=stat[:, 4 + p:5 + p], in_=stat[:, 2 + p:3 + p],
                                 func=AF.Ln, bias=eps_sb[:], scale=1.0)
                    E.drain()
                    E.activation(out=stat[:, 6 + p:7 + p], in_=stat[:, 4 + p:5 + p],
                                 func=AF.Exp, bias=0.0, scale=-0.5).then_inc(a_m.h, 1)
                a_m.add(1)
                mark(f"crstd{m}", a_m)

    def C_dve(E):
        em = E is not None
        if em:
            wt(E, "arh_ld")
            E.tensor_scalar(out=junk[:, 0:128], in0=zeros[:], scalar1=arh_sb[:],
                            scalar2=None, op0=ALU.add)
            E.engine_nop().then_inc(d_bc.h, 1)
        d_bc.add(1)
        mark("ch_rep", d_bc)
        if em:
            wt(E, "ch_tr")
            E.tensor_reduce(sgl[:, 21:22], CT["ps_ct"][:, 0:128], AX.X, ALU.max)
            wt(E, "sigd")
            E.drain()
            E.tensor_tensor(sgl[:, 22:23], sgl[:, 20:21], sgl[:, 21:22], ALU.mult)
            E.drain()
            E.tensor_scalar(out=sgl[:, 22:23], in0=sgl[:, 22:23],
                            scalar1=1.0 / 127.0, scalar2=None, op0=ALU.mult)
            E.drain()
            E.reciprocal(sgl[:, 23:24], sgl[:, 22:23])
            E.engine_nop().then_inc(d_bc.h, 1)
        d_bc.add(1)
        mark("sh", d_bc)

        for n in range(NQT + 3):
            if n < NQT:
                par = n % 2
                if em:
                    if n == 0:
                        wt(E, "sh")
                    wt(E, f"hld{n}")
                    E.tensor_scalar(out=CT["ctmp"][:], in0=CT["hbuf"][par][:],
                                    scalar1=sgl[:, 23:24], scalar2=127.49,
                                    op0=ALU.mult, op1=ALU.min)
                    E.drain()
                    E.tensor_scalar(out=CT["ctmpi"][:], in0=CT["ctmp"][:],
                                    scalar1=-128.49, scalar2=None, op0=ALU.max)
                    E.drain()
                    E.tensor_scalar(out=CT["hq"][par][:], in0=CT["ctmpi"][:],
                                    scalar1=sgl[:, 22:23], scalar2=None,
                                    op0=ALU.mult).then_inc(d_bc.h, 1)
                d_bc.add(1)
                mark(f"hq{n}", d_bc)
            if 3 <= n <= NQT + 2:
                m = n - 3
                p = m % 2
                if em:
                    wt(E, f"crstd{m}")
                    E.tensor_scalar(out=stat[:, 14 + p:15 + p], in0=stat[:, 0 + p:1 + p],
                                    scalar1=stat[:, 6 + p:7 + p], scalar2=-1.0,
                                    op0=ALU.mult, op1=ALU.mult).then_inc(d_bc.h, 1)
                d_bc.add(1)
                mark(f"bias{m}", d_bc)
            if 1 <= n <= NQT:
                m = n - 1
                p = m % 2
                if em:
                    wt(E, f"dn{m}")
                    wt(E, f"lnr{m}")
                    if m >= 2:
                        wt(E, f"cote{m - 2}")
                    E.scalar_tensor_tensor(
                        out=CT["r2"][p][:], in0=CT["ps_d"][:, p * 1024:p * 1024 + 1024],
                        scalar=1.0, in1=CT["lnr"][m % 4][:],
                        op0=ALU.mult, op1=ALU.add, accum_out=stat[:, 8 + p:9 + p])
                    E.drain()
                    E.tensor_scalar(out=stat[:, 0 + p:1 + p], in0=stat[:, 8 + p:9 + p],
                                    scalar1=1.0 / D_, scalar2=None,
                                    op0=ALU.mult).then_inc(d_bc.h, 1)
                d_bc.add(1)
                mark(f"r2d{m}", d_bc)
            if 2 <= n <= NQT + 1:
                m = n - 2
                p = m % 2
                if em:
                    wt(E, f"sqa{m}")
                    E.drain()
                    E.tensor_scalar(out=stat[:, 2 + p:3 + p], in0=stat[:, 10 + p:11 + p],
                                    scalar1=1.0 / D_, scalar2=None, op0=ALU.mult)
                    E.tensor_tensor(stat[:, 12 + p:13 + p], stat[:, 0 + p:1 + p],
                                    stat[:, 0 + p:1 + p], ALU.mult)
                    E.drain()
                    E.tensor_sub(stat[:, 2 + p:3 + p], stat[:, 2 + p:3 + p],
                                 stat[:, 12 + p:13 + p])
                    E.engine_nop().then_inc(d_bc.h, 1)
                d_bc.add(1)
                mark(f"vard{m}", d_bc)

    def C_gpsimd(E):
        em = E is not None
        for n in range(NQT):
            if em:
                wt(E, f"cote{n}")
                E.dma_start(out=out_t.ap()[n * 128:(n + 1) * 128, :],
                            in_=CT["ot"][n % 2][:]).then_inc(s_out[n % 2].h, 16)
            s_out[n % 2].add(16)
            mark(f"out{n}", s_out[n % 2])

    for _k in ["kbuf", "vbuf", "pseg", "osb", "rsb", "resid", "lnst",
               "ps_s", "ps_ot", "ps_r"]:
        AT[_k] = None

    A_gpsimd(None)
    A_pe(None)
    A_act(None)
    A_dve(None)
    A_sync(None)
    B_sync(None)
    B_pe(None)
    B_act(None)
    B_dve(None)
    B_gpsimd(None)
    C_sync(None)
    C_pe(None)
    C_act(None)
    C_dve(None)
    C_gpsimd(None)
    dry = {id(c): c.n for c in ctrs}
    for c in ctrs:
        c.n = 0

    es_pr.close()
    es_a2 = ExitStack()
    AT["kbuf"] = [es_a2.enter_context(nc.sbuf_tensor(f"kbuf{i}", [128, HC * 512], BF16)) for i in range(3)]
    AT["vbuf"] = [es_a2.enter_context(nc.sbuf_tensor(f"vbuf{i}", [128, 4 * D_], BF16)) for i in range(3)]
    AT["pseg"] = es_a2.enter_context(nc.sbuf_tensor("pseg", [128, 8 * 1024], BF16))
    AT["osb"] = es_a2.enter_context(nc.sbuf_tensor("osb", [128, 8 * D_], F32))
    AT["rsb"] = es_a2.enter_context(nc.sbuf_tensor("rsb", [128, 8], F32))
    AT["resid"] = es_a2.enter_context(nc.sbuf_tensor("resid", [128, D_], F32))
    AT["lnst"] = es_a2.enter_context(nc.sbuf_tensor("lnst", [128, 2 * D_], F32))
    AT["ps_s"] = es_a2.enter_context(nc.psum_tensor("ps_s", [128, 1024], F32))
    AT["ps_ot"] = es_a2.enter_context(nc.psum_tensor("ps_ot", [128, 2048], F32))
    AT["ps_r"] = es_a2.enter_context(nc.psum_tensor("ps_r", [128, 1024], F32))

    with nc.Block("A") as blk:
        @blk.gpsimd
        def _(E):
            A_gpsimd(E)

        @blk.tensor
        def _(E):
            A_pe(E)

        @blk.scalar
        def _(E):
            A_act(E)

        @blk.vector
        def _(E):
            A_dve(E)

        @blk.sync
        def _(E):
            A_sync(E)

    es_a2.close()
    es_a1.close()
    es_b = ExitStack()
    BT["qx_fm"] = es_b.enter_context(nc.sbuf_tensor("qx_fm", [128, HC * T_], BF16))
    BT["wg_T"] = es_b.enter_context(nc.sbuf_tensor("wg_T", [128, HC * DI_], BF16))
    BT["wu_T"] = es_b.enter_context(nc.sbuf_tensor("wu_T", [128, HC * DI_], BF16))
    BT["silu"] = [es_b.enter_context(nc.sbuf_tensor(f"silu{i}", [128, 512], F32)) for i in range(2)]
    BT["hst"] = [es_b.enter_context(nc.sbuf_tensor(f"hst{i}", [128, 512], F32)) for i in range(2)]
    BT["qxst"] = [es_b.enter_context(nc.sbuf_tensor(f"qxst{i}", [128, D_], BF16)) for i in range(2)]
    BT["lnrd"] = [es_b.enter_context(nc.sbuf_tensor(f"lnrd{i}", [128, D_], F32)) for i in range(4)]
    BT["ps_xtb"] = es_b.enter_context(nc.psum_tensor("ps_xtb", [128, 2048], BF16))
    BT["ps_g"] = es_b.enter_context(nc.psum_tensor("ps_g", [128, 1024], F32))
    BT["ps_u"] = es_b.enter_context(nc.psum_tensor("ps_u", [128, 1024], F32))
    BT["ps_t"] = es_b.enter_context(nc.psum_tensor("ps_t", [128, 512], F32))

    with nc.Block("B") as blk:
        @blk.gpsimd
        def _(E):
            B_gpsimd(E)

        @blk.tensor
        def _(E):
            B_pe(E)

        @blk.scalar
        def _(E):
            B_act(E)

        @blk.vector
        def _(E):
            B_dve(E)

        @blk.sync
        def _(E):
            B_sync(E)

    es_b.close()
    es_c = ExitStack()
    CT["qwd_T"] = es_c.enter_context(nc.sbuf_tensor("qwd_T", [128, IC * D_], BF16))
    CT["hbuf"] = [es_c.enter_context(nc.sbuf_tensor(f"hbuf{i}", [128, DI_], F32)) for i in range(2)]
    CT["hq"] = [es_c.enter_context(nc.sbuf_tensor(f"hq{i}", [128, DI_], BF16)) for i in range(2)]
    CT["lnr"] = [es_c.enter_context(nc.sbuf_tensor(f"lnr{i}", [128, D_], F32)) for i in range(4)]
    CT["r2"] = [es_c.enter_context(nc.sbuf_tensor(f"r2_{i}", [128, D_], F32)) for i in range(2)]
    CT["ot"] = [es_c.enter_context(nc.sbuf_tensor(f"ot{i}", [128, D_], F32)) for i in range(2)]
    CT["ctmp"] = es_c.enter_context(nc.sbuf_tensor("ctmp", [128, DI_], F32))
    CT["ctmpi"] = es_c.enter_context(nc.sbuf_tensor("ctmpi", [128, DI_], I32))
    CT["ps_d"] = es_c.enter_context(nc.psum_tensor("ps_d", [128, 2048], F32))
    CT["ps_ct"] = es_c.enter_context(nc.psum_tensor("ps_ct", [128, 512], F32))

    with nc.Block("C") as blk:
        @blk.gpsimd
        def _(E):
            C_gpsimd(E)

        @blk.tensor
        def _(E):
            C_pe(E)

        @blk.scalar
        def _(E):
            C_act(E)

        @blk.vector
        def _(E):
            C_dve(E)

        @blk.sync
        def _(E):
            C_sync(E)

    for c in ctrs:
        assert c.n == dry[id(c)], (c.name, c.n, dry[id(c)])

    es_c.close()
    es.close()
    return nc


_NC_CACHE = {}


def _get_nc():
    if "nc" not in _NC_CACHE:
        _NC_CACHE["nc"] = build_nc()
    return _NC_CACHE["nc"]


def kernel(L0_attn_embeddings, L0_attn_W_Q, L0_attn_W_K, L0_attn_W_V,
           L0_ffn_clip_x, L0_ffn_clip_down, L0_ffn_w_gate, L0_ffn_w_up,
           L0_ffn_w_down):
    x = np.asarray(L0_attn_embeddings, dtype=np.float32).reshape(S_TOK, D_)
    wq = np.ascontiguousarray(np.asarray(L0_attn_W_Q, dtype=np.float32))
    wk = np.ascontiguousarray(np.asarray(L0_attn_W_K, dtype=np.float32))
    wv = np.ascontiguousarray(np.asarray(L0_attn_W_V, dtype=np.float32))
    wg = np.ascontiguousarray(np.asarray(L0_ffn_w_gate, dtype=np.float32))
    wu = np.ascontiguousarray(np.asarray(L0_ffn_w_up, dtype=np.float32))
    wd = np.ascontiguousarray(np.asarray(L0_ffn_w_down, dtype=np.float32))
    cx = float(np.asarray(L0_ffn_clip_x).reshape(-1)[0])
    cd = float(np.asarray(L0_ffn_clip_down).reshape(-1)[0])
    cx_t = np.full((128, 1), cx, dtype=np.float32)
    cd_t = np.full((128, 1), cd, dtype=np.float32)
    identf = np.eye(128, dtype=np.float32)
    identb = np.eye(128, dtype=ml_dtypes.bfloat16)

    in_maps = []
    tri = np.triu(np.ones((128, 128), dtype=np.float32)).astype(ml_dtypes.bfloat16)
    one_m = np.ones((128, 128), dtype=ml_dtypes.bfloat16)
    zero_m = np.zeros((128, 128), dtype=ml_dtypes.bfloat16)
    for c in range(NCORES):
        qrows = np.concatenate(
            [x[(8 * s + c) * 128:(8 * s + c + 1) * 128] for s in range(NQT)], axis=0)
        kvrows = np.concatenate(
            [x[(8 * j + c) * 512:(8 * j + c + 1) * 512] for j in range(NKV)], axis=0)
        mk = np.empty((8, 128, 128), dtype=ml_dtypes.bfloat16)
        for w in range(8):
            mk[w] = one_m if w < c else (tri if w == c else zero_m)
        in_maps.append({
            "xq": np.ascontiguousarray(qrows),
            "xkv": np.ascontiguousarray(kvrows),
            "wq": wq, "wk": wk, "wv": wv,
            "wg": wg, "wu": wu, "wd": wd,
            "clipx": cx_t, "clipd": cd_t,
            "masks": np.ascontiguousarray(mk.reshape(8 * 128, 128)),
            "identf": identf, "identb": identb,
        })

    nc = _get_nc()
    res = run_bass_kernel_spmd(nc, in_maps, core_ids=list(range(NCORES)))

    out = np.empty((S_TOK, D_), dtype=np.float32)
    for c in range(NCORES):
        oc = res.results[c]["out"]
        for s in range(NQT):
            out[(8 * s + c) * 128:(8 * s + c + 1) * 128] = oc[s * 128:(s + 1) * 128]
    return out.reshape(1, S_TOK, D_)


# version 13
# speedup vs baseline: 1.0921x; 1.0163x over previous
from contextlib import ExitStack

import numpy as np
import ml_dtypes
import concourse.bass as bass
import concourse.mybir as mybir
from concourse.bass_utils import run_bass_kernel_spmd

F32 = mybir.dt.float32
BF16 = mybir.dt.bfloat16
I32 = mybir.dt.int32
AF = mybir.ActivationFunctionType
ALU = mybir.AluOpType
AX = mybir.AxisListType

NCORES = 8
S_TOK = 16384
D_ = 1024
DI_ = 2816
T_ = 2048
NQT = 16
NPAIR = 8
NKV = 4
NCH = 4
HC = 8
IC = 22
EPS = 1e-5
NWU = 2 * IC + 8

PAIR_T = [16 * (u + 1) for u in range(NPAIR)]
CUM_T = [0]
for _u in range(NPAIR):
    CUM_T.append(CUM_T[-1] + PAIR_T[_u])
PAIR_G = [4 * (u + 1) for u in range(NPAIR)]
CUM_G = [0]
for _u in range(NPAIR):
    CUM_G.append(CUM_G[-1] + PAIR_G[_u])

WQ_PROD = 0
PROD_STAGES = []
for _j in range(4):
    PROD_STAGES.append(("kv", _j))
    PROD_STAGES.append(("q", _j))
PROD_TILES = []
for _st, _i in PROD_STAGES:
    for _k in range(4):
        PROD_TILES.append((_st, 4 * _i + _k))
PT_I = {t: i for i, t in enumerate(PROD_TILES)}
WD_SUBW = [1024, 1024, 768]
WU_KIND = ([(0, i, 0, 1) for i in range(IC)] + [(1, i, 0, 1) for i in range(IC)]
           + [(2, rb, s, p) for rb in range(8) for p in (0, 1) for s in range(3)])
NWU = len(WU_KIND)

CHUNKS = 2
SC_S0 = [0, 8]
SC_NSEG = [8, 16]
SEG_LIST = [(v, g) for v in range(CHUNKS) for g in range(SC_NSEG[v])]
SEG_I = {key: i for i, key in enumerate(SEG_LIST)}


def seg_qlo(v, g):
    return max(SC_S0[v], g)


def seg_ns(v, g):
    return SC_S0[v] + 8 - seg_qlo(v, g)


def seg_subs(v, g):
    W = seg_ns(v, g) * 128
    subs = []
    off = 0
    while off < W:
        w = min(512, W - off)
        subs.append((off, w))
        off += w
    return subs


STILE_LIST = [(v, g, t2, si2) for (v, g) in SEG_LIST for t2 in range(8)
              for si2 in range(len(seg_subs(v, g)))]
STILE_I = {key: i for i, key in enumerate(STILE_LIST)}
SWEEP_LIST = [(v, g, si) for (v, g) in SEG_LIST for si in range(seg_ns(v, g))]
SWEEP_I = {key: i for i, key in enumerate(SWEEP_LIST)}
LOAD_LIST = [(v, gi2) for v in range(CHUNKS) for gi2 in range(2 * SC_NSEG[v])]
LOAD_I = {key: i for i, key in enumerate(LOAD_LIST)}

WQ_SEG = {}
_k = WQ_PROD
_segs = list(SEG_LIST)
_i = 0
while _k < NWU:
    _sg = _segs[2 + _i % (len(_segs) - 2)]
    WQ_SEG[_sg] = WQ_SEG.get(_sg, 0) + 1
    _k += 1
    _i += 1


def _wu_w(k):
    kind, chunk, sub, phase = WU_KIND[k]
    return D_ if kind < 2 else WD_SUBW[sub]


def _wu_spills(k):
    kind, chunk, sub, phase = WU_KIND[k]
    return kind < 2 or phase == 1


def _prev_spill(k):
    for k2 in range(k - 1, -1, -1):
        if _wu_spills(k2) and k2 % 2 == k % 2:
            return k2
    return None


def ti(u, t):
    return CUM_T[u] + t


def gi(u, g):
    return CUM_G[u] + g


INV_G = {}
for _u in range(NPAIR):
    for _g in range(PAIR_G[_u]):
        INV_G[gi(_u, _g)] = (_u, _g)


def mki(u, t):
    return 16 * u + (t - 16 * u)


class Ctr:
    def __init__(self, es, nc, name):
        self.h = es.enter_context(nc.semaphore(name))
        self.name = name
        self.n = 0

    def add(self, k=1):
        self.n += k
        return self.n


def build_nc():
    nc = bass.Bass("TRN2", target_bir_lowering=False, debug=False)

    xq_in = nc.dram_tensor("xq", [T_, D_], F32, kind="ExternalInput").ap()
    xkv_in = nc.dram_tensor("xkv", [T_, D_], F32, kind="ExternalInput").ap()
    wq_in = nc.dram_tensor("wq", [D_, D_], F32, kind="ExternalInput").ap()
    wk_in = nc.dram_tensor("wk", [D_, D_], F32, kind="ExternalInput").ap()
    wv_in = nc.dram_tensor("wv", [D_, D_], F32, kind="ExternalInput").ap()
    wg_in = nc.dram_tensor("wg", [DI_, D_], F32, kind="ExternalInput").ap()
    wu_in = nc.dram_tensor("wu", [DI_, D_], F32, kind="ExternalInput").ap()
    wd_in = nc.dram_tensor("wd", [D_, DI_], F32, kind="ExternalInput").ap()
    clipx_in = nc.dram_tensor("clipx", [128, 1], F32, kind="ExternalInput").ap()
    clipd_in = nc.dram_tensor("clipd", [128, 1], F32, kind="ExternalInput").ap()
    masks_in = nc.dram_tensor("masks", [8 * 128, 128], BF16, kind="ExternalInput").ap()
    identf_in = nc.dram_tensor("identf", [128, 128], F32, kind="ExternalInput").ap()
    identb_in = nc.dram_tensor("identb", [128, 128], BF16, kind="ExternalInput").ap()
    out_t = nc.dram_tensor("out", [T_, D_], F32, kind="ExternalOutput")

    ksrc = [nc.dram_tensor(f"ksrc{j}", [128, HC * 512], BF16) for j in range(NCH)]
    vsrc = [nc.dram_tensor(f"vsrc{j}", [128, 4 * D_], BF16) for j in range(NCH)]
    kall = [nc.dram_tensor(f"kall{j}", [NCORES * 128, HC * 512], BF16,
                           addr_space="Shared") for j in range(NCH)]
    vall = [nc.dram_tensor(f"vall{j}", [NCORES * 128, 4 * D_], BF16,
                           addr_space="Shared") for j in range(NCH)]
    wq_sp = [nc.dram_tensor("wgq_s", [DI_, D_], BF16),
             nc.dram_tensor("wuq_s", [DI_, D_], BF16),
             nc.dram_tensor("wdq_s", [D_, DI_], BF16)]
    ln1_sp = nc.dram_tensor("ln1_s", [T_, D_], F32)
    h_spd = nc.dram_tensor("h_s", [128, IC * T_], F32)
    ar_i = [nc.dram_tensor(f"ar_i{i}", [128, 1], F32) for i in range(3)]
    ar_o = [nc.dram_tensor(f"ar_o{i}", [128, 1], F32) for i in range(3)]

    RG = [list(range(NCORES))]
    es = ExitStack()

    def sbg(name, shape, dt):
        return es.enter_context(nc.sbuf_tensor(name, shape, dt))

    identf = sbg("identf_sb", [128, 128], F32)
    identb = sbg("identb_sb", [128, 128], BF16)
    zeros = sbg("zeros_sb", [128, 128], F32)
    ones_bf = sbg("ones_bf", [128, 1], BF16)
    clipx = sbg("clipx_sb", [128, 1], F32)
    clipd = sbg("clipd_sb", [128, 1], F32)
    sgl = sbg("sgl", [128, 32], F32)
    stat = sbg("stat", [128, 32], F32)
    mx_x = sbg("mx_x", [128, 1], F32)
    mx_h = sbg("mx_h", [128, 1], F32)
    junk = sbg("junk", [128, D_], F32)
    wtmp = sbg("wtmp", [128, D_], F32)
    wtmpi = sbg("wtmpi", [128, D_], I32)
    wqbf = [sbg(f"wqbf{i}", [128, D_], BF16) for i in range(2)]
    negm = sbg("negm_sb", [128, 1], F32)
    eps_sb = sbg("eps_sb", [128, 1], F32)
    arx_sb = sbg("arx_sb", [128, 1], F32)
    arh_sb = sbg("arh_sb", [128, 1], F32)

    es_a1 = ExitStack()

    def sba1(name, shape, dt):
        return es_a1.enter_context(nc.sbuf_tensor(name, shape, dt))

    masks = sba1("masks_sb", [128, 8 * 128], BF16)
    q_fm = sba1("q_fm", [128, HC * T_], BF16)
    ksamp = sba1("ksamp", [128, HC * 128], BF16)
    xqr = [sba1(f"xqr{i}", [128, D_], F32) for i in range(2)]
    wraw = [sba1(f"wraw{i}", [128, D_], F32) for i in range(3)]
    kstage = sba1("kstage", [128, HC * 512], BF16)
    vstage = sba1("vstage", [128, 4 * D_], BF16)

    es_pr = ExitStack()
    xstage = [es_pr.enter_context(nc.sbuf_tensor(f"xstage{i}", [128, D_], BF16)) for i in range(2)]
    xfm = [es_pr.enter_context(nc.sbuf_tensor(f"xfm{i}", [128, HC * 512], BF16)) for i in range(2)]
    wkv_bf = [es_pr.enter_context(nc.sbuf_tensor(f"wkv{i}", [128, HC * D_], BF16)) for i in range(2)]
    wq_bf = es_pr.enter_context(nc.sbuf_tensor("wq_bf", [128, HC * D_], BF16))
    ps_xt = es_pr.enter_context(nc.psum_tensor("ps_xt", [128, 2048], BF16))
    ps_pa = es_pr.enter_context(nc.psum_tensor("ps_pa", [128, 512], F32))
    ps_pb = es_pr.enter_context(nc.psum_tensor("ps_pb", [128, 512], F32))
    ps_v = es_pr.enter_context(nc.psum_tensor("ps_v", [128, 2048], F32))

    AT = {}

    BT = {}
    CT = {}

    gx = [Ctr(es, nc, f"gx{i}") for i in range(2)]
    g_id = Ctr(es, nc, "g_id")
    gw_k = Ctr(es, nc, "gw_k")
    gw_v = Ctr(es, nc, "gw_v")
    gw_q = Ctr(es, nc, "gw_q")
    g_ksp = Ctr(es, nc, "g_ksp")
    g_vsp = Ctr(es, nc, "g_vsp")
    g_wsp = [Ctr(es, nc, f"g_wsp{i}") for i in range(2)]
    g_lnsp = [Ctr(es, nc, f"g_lnsp{i}") for i in range(2)]
    g_hsp = [Ctr(es, nc, f"g_hsp{i}") for i in range(2)]
    g_ar = Ctr(es, nc, "g_ar")
    g_cc = Ctr(es, nc, "g_cc")
    s_cfg = Ctr(es, nc, "s_cfg")
    kg = [Ctr(es, nc, f"kg{i}") for i in range(3)]
    vg = [Ctr(es, nc, f"vg{i}") for i in range(3)]
    d_oa = Ctr(es, nc, "d_oa")
    sxq = [Ctr(es, nc, f"sxq{i}") for i in range(2)]
    sw = [Ctr(es, nc, f"sw{i}") for i in range(3)]
    s_arx = Ctr(es, nc, "s_arx")
    s_lnr = [Ctr(es, nc, f"s_lnr{i}") for i in range(4)]
    s_wgu = Ctr(es, nc, "s_wgu")
    s_arh = Ctr(es, nc, "s_arh")
    s_qwd = Ctr(es, nc, "s_qwd")
    s_hld = [Ctr(es, nc, f"s_hld{i}") for i in range(2)]
    s_clnr = [Ctr(es, nc, f"s_clnr{i}") for i in range(4)]
    s_out = [Ctr(es, nc, f"s_out{i}") for i in range(2)]
    p_pr = Ctr(es, nc, "p_pr")
    p_s = Ctr(es, nc, "p_s")
    p_pv = Ctr(es, nc, "p_pv")
    p_m = Ctr(es, nc, "p_m")
    p_bc = Ctr(es, nc, "p_bc")
    a_ev = Ctr(es, nc, "a_ev")
    a_exp = Ctr(es, nc, "a_exp")
    a_m = Ctr(es, nc, "a_m")
    d_mk = Ctr(es, nc, "d_mk")
    d_ep = Ctr(es, nc, "d_ep")
    d_q = Ctr(es, nc, "d_q")
    d_m = Ctr(es, nc, "d_m")
    d_bc = Ctr(es, nc, "d_bc")
    ctrs = (gx + [g_id, gw_k, gw_v, gw_q, g_ksp, g_vsp] + g_wsp + g_lnsp
            + g_hsp + [g_ar, g_cc, s_cfg] + kg + vg + sxq + sw
            + [s_arx] + s_lnr + [s_wgu, s_arh, s_qwd] + s_hld + s_clnr
            + s_out + [p_pr, p_s, p_pv, p_m, p_bc, a_ev, a_exp, a_m, d_mk,
                       d_ep, d_q, d_m, d_bc, d_oa])

    CK = {}

    def mark(name, ctr):
        if name in CK:
            assert CK[name] == (ctr, ctr.n), (name, CK[name][1], ctr.n)
        CK[name] = (ctr, ctr.n)

    def wt(eng, name):
        ctr, val = CK[name]
        if val > 0:
            eng.wait_ge(ctr.h, val)


    def A_gpsimd(E):
        em = E is not None
        if em:
            E.memset(sgl[:, 31:32], 0.0)
            E.drain()
            E.dma_start(out=ar_i[2].ap(), in_=sgl[:, 31:32]).then_inc(g_ar.h, 16)
        g_ar.add(16)
        if em:
            E.wait_ge(g_ar.h, g_ar.n)

        if em:
            E.dma_start(out=identb[:], in_=identb_in).then_inc(g_id.h, 16)
        g_id.add(16)
        mark("identb", g_id)

        def xload(i):
            st, m = PROD_TILES[i]
            srcten = xkv_in if st == "kv" else xq_in
            if em:
                if i >= 2:
                    wt(E, f"pt_t{i - 2}")
                E.dma_start(out=xstage[i % 2][:],
                            in_=srcten[m * 128:(m + 1) * 128, :]).then_inc(gx[i % 2].h, 16)
            gx[i % 2].add(16)
            mark(f"ldt{i}", gx[i % 2])

        for i in range(4):
            xload(i)
        for d in range(HC):
            if em:
                E.dma_start(out=wkv_bf[0][:, d * D_:(d + 1) * D_],
                            in_=wk_in[d * 128:(d + 1) * 128, :]).then_inc(gw_k.h, 16)
            gw_k.add(16)
        mark("wk_all", gw_k)
        for d in range(HC):
            if em:
                E.dma_start(out=wkv_bf[1][:, d * D_:(d + 1) * D_],
                            in_=wv_in[d * 128:(d + 1) * 128, :]).then_inc(gw_v.h, 16)
            gw_v.add(16)
        mark("wv_all", gw_v)
        for d in range(HC):
            if em:
                E.dma_start(out=wq_bf[:, d * D_:(d + 1) * D_],
                            in_=wq_in[d * 128:(d + 1) * 128, :]).then_inc(gw_q.h, 16)
            gw_q.add(16)
        mark("wq_all", gw_q)

        def wraw_load(k):
            kind, chunk, sub, phase = WU_KIND[k]
            w = _wu_w(k)
            b = k % 3
            if em:
                if k >= 3:
                    wt(E, f"wq_u{k - 3}")
                if kind < 2:
                    ap_src = [wg_in, wu_in][kind][chunk * 128:(chunk + 1) * 128, :]
                else:
                    colo = sum(WD_SUBW[:sub])
                    ap_src = wd_in[chunk * 128:(chunk + 1) * 128, colo:colo + w]
                E.dma_start(out=wraw[b][:, 0:w], in_=ap_src).then_inc(sw[b].h, 16)
            sw[b].add(16)
            mark(f"wl{k}", sw[b])

        def spill_k(j):
            if em:
                wt(E, f"kev{j}")
                E.dma_start(out=ksrc[j].ap(), in_=kstage[:]).then_inc(g_ksp.h, 16)
            g_ksp.add(16)
            mark(f"ksp{j}", g_ksp)
            if em:
                E.wait_ge(g_ksp.h, g_ksp.n)
                E.collective_compute("AllGather", ALU.bypass, replica_groups=RG,
                                     ins=[ksrc[j].ap().opt()],
                                     outs=[kall[j].ap().opt()]).then_inc(g_cc.h, 1)
            g_cc.add(1)
            mark(f"cc_k{j}", g_cc)

        def spill_v(j):
            if em:
                wt(E, f"vev{j}")
                E.dma_start(out=vsrc[j].ap(), in_=vstage[:]).then_inc(g_vsp.h, 16)
            g_vsp.add(16)
            mark(f"vsp{j}", g_vsp)
            if em:
                E.wait_ge(g_vsp.h, g_vsp.n)
                E.collective_compute("AllGather", ALU.bypass, replica_groups=RG,
                                     ins=[vsrc[j].ap().opt()],
                                     outs=[vall[j].ap().opt()]).then_inc(g_cc.h, 1)
            g_cc.add(1)
            mark(f"cc_ch{j}", g_cc)

        for i in range(4, 8):
            xload(i)
        spill_k(0)
        for i in range(8, 12):
            xload(i)
        spill_v(0)
        for i in range(12, 16):
            xload(i)
        spill_k(1)
        spill_v(1)
        for i in range(16, 20):
            xload(i)
        for i in range(20, 24):
            xload(i)
        spill_k(2)
        spill_v(2)
        for i in range(24, 28):
            xload(i)
        for i in range(28, 32):
            xload(i)
        spill_k(3)
        spill_v(3)

        def wspill(k):
            if not _wu_spills(k):
                return
            kind, chunk, sub, phase = WU_KIND[k]
            w = _wu_w(k)
            b = k % 2
            if em:
                wt(E, f"wq_u{k}")
                if kind < 2:
                    dst = wq_sp[kind].ap()[chunk * 128:(chunk + 1) * 128, :]
                else:
                    colo = sum(WD_SUBW[:sub])
                    dst = wq_sp[2].ap()[chunk * 128:(chunk + 1) * 128, colo:colo + w]
                E.dma_start(out=dst, in_=wqbf[b][:, 0:w]).then_inc(g_wsp[b].h, 16)
            g_wsp[b].add(16)
            mark(f"wsp{k}", g_wsp[b])

        kld_done = 0
        ksp_done = 0

        def wq_step(n):
            nonlocal kld_done, ksp_done
            for _ in range(n):
                if kld_done < NWU:
                    wraw_load(kld_done)
                    kld_done += 1
                if kld_done >= 3 and ksp_done < kld_done - 3:
                    wspill(ksp_done)
                    ksp_done += 1

        wq_step(WQ_PROD)
        for v in range(CHUNKS):
            for g in range(SC_NSEG[v]):
                wq_step(WQ_SEG.get((v, g), 0))
                if (v, g) in ((1, 0), (1, 5), (1, 10)):
                    if em:
                        wt(E, f"segdone{v}_{g}")
                        E.collective_compute("AllReduce", ALU.max, replica_groups=RG,
                                             ins=[ar_i[2].ap().opt()],
                                             outs=[ar_o[2].ap().opt()]).then_inc(g_cc.h, 1)
                    g_cc.add(1)
                if g >= SC_S0[v]:
                    while ksp_done < kld_done:
                        wspill(ksp_done)
                        ksp_done += 1
                    s = g
                    if em:
                        wt(E, f"epi_s{s}")
                        E.dma_start(out=ln1_sp.ap()[s * 128:(s + 1) * 128, :],
                                    in_=AT["lnst"][:, (s % 2) * D_:(s % 2 + 1) * D_]).then_inc(g_lnsp[s % 2].h, 16)
                    g_lnsp[s % 2].add(16)
                    mark(f"lnsp_s{s}", g_lnsp[s % 2])
        while ksp_done < NWU:
            wspill(ksp_done)
            ksp_done += 1
        assert ksp_done == NWU and kld_done == NWU
        mark("lnsp_all0", g_lnsp[0])
        mark("lnsp_all1", g_lnsp[1])
        mark("wsp_all0", g_wsp[0])
        mark("wsp_all1", g_wsp[1])
        if em:
            wt(E, "epic1")
            E.dma_start(out=ar_i[0].ap(), in_=mx_x[:]).then_inc(g_ar.h, 16)
        g_ar.add(16)
        if em:
            E.wait_ge(g_ar.h, g_ar.n)
            E.collective_compute("AllReduce", ALU.max, replica_groups=RG,
                                 ins=[ar_i[0].ap().opt()],
                                 outs=[ar_o[0].ap().opt()]).then_inc(g_cc.h, 1)
        g_cc.add(1)
        mark("cc_arx", g_cc)

    def A_pe(E):
        em = E is not None

        def xtile(i):
            if em:
                wt(E, f"ldt{i}")
                if i < 2:
                    wt(E, "identb")
                else:
                    wt(E, f"ptev{i - 2}")
                ins = None
                for d in range(HC):
                    ins = E.transpose(
                        ps_xt[:, (i % 2) * 1024 + d * 128:(i % 2) * 1024 + (d + 1) * 128],
                        xstage[i % 2][:, d * 128:(d + 1) * 128], identb[:])
                ins.then_inc(p_pr.h, 1)
            p_pr.add(1)
            mark(f"pt_t{i}", p_pr)

        for st, idx in PROD_STAGES:
            base = PT_I[(st, 4 * idx)]
            for k in range(4):
                xtile(base + k)
            if st == "kv":
                j = idx
                if em:
                    wt(E, "wk_all")
                    wt(E, f"xfmkv{j}")
                for d in range(HC):
                    if em:
                        if d >= 2:
                            wt(E, f"kevd{j}_{d - 2}")
                        if j == 1 and d < 2:
                            wt(E, ["mrep", "negm"][d])
                        pb = [ps_pa, ps_pb][d % 2]
                        ins = None
                        for hch in range(HC):
                            ins = E.matmul(pb[:, 0:512],
                                           wkv_bf[0][:, hch * D_ + d * 128:hch * D_ + (d + 1) * 128],
                                           xfm[0][:, hch * 512:(hch + 1) * 512],
                                           start=hch == 0, stop=hch == HC - 1)
                        ins.then_inc(p_pr.h, 1)
                    p_pr.add(1)
                    mark(f"kmm{j}_{d}", p_pr)
                if em:
                    wt(E, "wv_all")
                for ksub in range(4):
                    if em:
                        if ksub >= 2:
                            wt(E, f"vevk{j}_{ksub - 2}")
                        vb0 = (ksub % 2) * 1024
                        ins = None
                        for hch in range(HC):
                            for dvh in range(2):
                                ins = E.matmul(
                                    ps_v[:, vb0 + dvh * 512:vb0 + (dvh + 1) * 512],
                                    xfm[0][:, hch * 512 + ksub * 128:hch * 512 + ksub * 128 + 128],
                                    wkv_bf[1][:, hch * D_ + dvh * 512:hch * D_ + (dvh + 1) * 512],
                                    start=hch == 0, stop=hch == HC - 1)
                        ins.then_inc(p_pr.h, 1)
                    p_pr.add(1)
                    mark(f"vmm{j}_{ksub}", p_pr)
            else:
                r = idx
                if em:
                    wt(E, "wq_all")
                    wt(E, f"xfmq{r}")
                for d in range(HC):
                    if em:
                        if d >= 2:
                            wt(E, f"qevd{r}_{d - 2}")
                        pb = [ps_pa, ps_pb][d % 2]
                        ins = None
                        for hch in range(HC):
                            ins = E.matmul(pb[:, 0:512],
                                           wq_bf[:, hch * D_ + d * 128:hch * D_ + (d + 1) * 128],
                                           xfm[1][:, hch * 512:(hch + 1) * 512],
                                           start=hch == 0, stop=hch == HC - 1)
                        ins.then_inc(p_pr.h, 1)
                    p_pr.add(1)
                    mark(f"qmm{r}_{d}", p_pr)
                if r == 0:
                    if em:
                        wt(E, "ksampc")
                        wt(E, "qevd0_7")
                        ins = None
                        for d in range(HC):
                            ins = E.matmul(ps_pa[:, 0:128], ksamp[:, d * 128:(d + 1) * 128],
                                           q_fm[:, d * T_:d * T_ + 128],
                                           start=d == 0, stop=d == HC - 1)
                        ins.then_inc(p_m.h, 1)
                    p_m.add(1)
                    mark("samp_mm", p_m)
                    if em:
                        wt(E, "mrep")
                        wt(E, "cfg")
                        E.transpose(ps_pb[:, 0:128], junk[:, 0:128], identf[:]).then_inc(p_m.h, 1)
                    p_m.add(1)
                    mark("mrep_t", p_m)

        ps_s, ps_ot, ps_r = AT["ps_s"], AT["ps_ot"], AT["ps_r"]

        for v in range(CHUNKS):
            for g in range(SC_NSEG[v]):
                qlo = seg_qlo(v, g)
                subs = seg_subs(v, g)
                for t2 in range(8):
                    t = 8 * g + t2
                    L = LOAD_I[(v, 2 * g + t2 // 4)]
                    buf = L % 3
                    for si2, (off, w) in enumerate(subs):
                        STI = STILE_I[(v, g, t2, si2)]
                        if em:
                            wt(E, f"kgl{L}")
                            if STI == 0:
                                wt(E, "prod_ev")
                                wt(E, "negm")
                            if STI >= 2:
                                E.wait_ge(a_exp.h, STI - 1)
                            slot = ps_s[:, (STI % 2) * 512:(STI % 2) * 512 + w]
                            ins = None
                            for d in range(HC):
                                ins = E.matmul(
                                    slot,
                                    AT["kbuf"][buf][:, d * 512 + (t % 4) * 128:d * 512 + (t % 4) * 128 + 128],
                                    q_fm[:, d * T_ + 128 * qlo + off:d * T_ + 128 * qlo + off + w],
                                    start=d == 0, stop=d == HC - 1)
                            ins.then_inc(p_s.h, 1)
                        p_s.add(1)
                        assert p_s.n == STI + 1
                for si in range(seg_ns(v, g)):
                    SW = SWEEP_I[(v, g, si)]
                    if em:
                        if SW == 0:
                            wt(E, "prod_ev")
                        if si == 0:
                            wt(E, f"vgl{LOAD_I[(v, 2 * g)]}")
                            wt(E, f"vgl{LOAD_I[(v, 2 * g + 1)]}")
                        E.wait_ge(a_exp.h, STILE_I[(v, g, 7, len(subs) - 1)] + 1)
                        if g >= SC_S0[v]:
                            wt(E, f"mseg{v}_{g}")
                        if SW >= 2:
                            E.wait_ge(d_oa.h, SW - 1)
                        par = SW % 2
                        ins = None
                        for t2 in range(8):
                            buf = LOAD_I[(v, 2 * g + t2 // 4)] % 3
                            lh = AT["pseg"][:, t2 * 1024 + si * 128:t2 * 1024 + si * 128 + 128]
                            for dvh in range(2):
                                ins = E.matmul(
                                    ps_ot[:, par * 1024 + dvh * 512:par * 1024 + (dvh + 1) * 512],
                                    lh,
                                    AT["vbuf"][buf][:, (8 * g + t2) % 4 * 1024 + dvh * 512:(8 * g + t2) % 4 * 1024 + (dvh + 1) * 512],
                                    start=t2 == 0, stop=t2 == 7)
                            ins = E.matmul(ps_r[:, par * 512:par * 512 + 1], lh, ones_bf[:],
                                           start=t2 == 0, stop=t2 == 7)
                        ins.then_inc(p_pv.h, 1)
                    p_pv.add(1)
                    assert p_pv.n == SW + 1
                mark(f"segdone{v}_{g}", p_pv)

    def A_act(E):
        em = E is not None

        def evac(waits, out_fn, in_fn, name):
            if em:
                for w in waits:
                    wt(E, w)
                E.activation(out=out_fn(), in_=in_fn(), func=AF.Copy).then_inc(a_ev.h, 1)
            a_ev.add(1)
            mark(name, a_ev)

        for st, idx in PROD_STAGES:
            base = PT_I[(st, 4 * idx)]
            buf = 0 if st == "kv" else 1
            for k in range(4):
                i = base + k
                extra = []
                if k == 0:
                    if st == "kv" and idx >= 1:
                        extra = [f"vmm{idx - 1}_3"]
                    if st == "q" and idx >= 1:
                        extra = [f"qmm{idx - 1}_7"]
                evac([f"pt_t{i}"] + extra,
                     lambda buf=buf, k=k: xfm[buf][:].rearrange(
                         "p (h s k) -> p h s k", h=HC, s=4)[:, :, k, :],
                     lambda i=i: ps_xt[:, (i % 2) * 1024:(i % 2) * 1024 + 1024].rearrange(
                         "p (h k) -> p h k", h=HC),
                     f"ptev{i}")
            if st == "kv":
                j = idx
                mark(f"xfmkv{j}", a_ev)
                for d in range(HC):
                    extra = []
                    if d == 0 and j >= 1:
                        extra = [f"ksp{j - 1}"]
                        if j == 1:
                            extra.append("ksampc")
                    evac([f"kmm{j}_{d}"] + extra,
                         lambda d=d: kstage[:, d * 512:(d + 1) * 512],
                         lambda d=d: [ps_pa, ps_pb][d % 2][:, 0:512],
                         f"kevd{j}_{d}")
                mark(f"kev{j}", a_ev)
                for ksub in range(4):
                    extra = [f"vsp{j - 1}"] if (ksub == 0 and j >= 1) else []
                    evac([f"vmm{j}_{ksub}"] + extra,
                         lambda k=ksub: vstage[:, k * 1024:(k + 1) * 1024],
                         lambda k=ksub: ps_v[:, (k % 2) * 1024:(k % 2) * 1024 + 1024],
                         f"vevk{j}_{ksub}")
                mark(f"vev{j}", a_ev)
            else:
                r = idx
                mark(f"xfmq{r}", a_ev)
                for d in range(HC):
                    evac([f"qmm{r}_{d}"],
                         lambda r=r, d=d: q_fm[:, d * T_ + r * 512:d * T_ + (r + 1) * 512],
                         lambda d=d: [ps_pa, ps_pb][d % 2][:, 0:512],
                         f"qevd{r}_{d}")
        mark("prod_ev", a_ev)

        ps_s = AT["ps_s"]
        for v in range(CHUNKS):
            for g in range(SC_NSEG[v]):
                subs = seg_subs(v, g)
                SI = SEG_I[(v, g)]
                for gi2 in (2 * g, 2 * g + 1):
                    L = LOAD_I[(v, gi2)]
                    j = gi2 // 8
                    r = gi2 % 8
                    if em:
                        wt(E, f"cc_ch{j}")
                        if L < 3:
                            wt(E, "qmm3_7")
                            wt(E, "prod_ev")
                            wt(E, "ksp3")
                            wt(E, "vsp3")
                        else:
                            vp, gp = LOAD_LIST[L - 3]
                            wt(E, f"segdone{vp}_{gp // 2}")
                        E.dma_start(out=AT["vbuf"][L % 3][:],
                                    in_=vall[j].ap()[r * 128:(r + 1) * 128, :]).then_inc(vg[L % 3].h, 16)
                    vg[L % 3].add(16)
                    mark(f"vgl{L}", vg[L % 3])
                if g >= SC_S0[v]:
                    s = g
                    if em:
                        if s >= 2:
                            wt(E, f"vstt{s - 2}")
                        E.dma_start(out=xqr[s % 2][:],
                                    in_=xq_in[s * 128:(s + 1) * 128, :]).then_inc(sxq[s % 2].h, 16)
                    sxq[s % 2].add(16)
                    mark(f"xq{s}", sxq[s % 2])
                for t2 in range(8):
                    for si2, (off, w) in enumerate(subs):
                        STI = STILE_I[(v, g, t2, si2)]
                        if em:
                            E.wait_ge(p_s.h, STI + 1)
                            if STI == 0:
                                wt(E, "negm")
                            if SI >= 1:
                                vp, gp = SEG_LIST[SI - 1]
                                wt(E, f"segdone{vp}_{gp}")
                            E.activation(out=AT["pseg"][:, t2 * 1024 + off:t2 * 1024 + off + w],
                                         in_=ps_s[:, (STI % 2) * 512:(STI % 2) * 512 + w],
                                         func=AF.Exp, bias=negm[:],
                                         scale=1.0 / 32.0).then_inc(a_exp.h, 1)
                        a_exp.add(1)
                if g >= SC_S0[v]:
                    si = g - SC_S0[v]
                    s = g
                    if em:
                        wt(E, f"var_s{s}")
                        E.activation(out=sgl[:, 24:25], in_=stat[:, 8 + si:9 + si],
                                     func=AF.Ln, bias=eps_sb[:], scale=1.0)
                        E.drain()
                        E.activation(out=stat[:, 16 + si:17 + si], in_=sgl[:, 24:25],
                                     func=AF.Exp, bias=0.0, scale=-0.5).then_inc(a_m.h, 1)
                    a_m.add(1)
                    mark(f"rstd_s{s}", a_m)

    def A_dve(E):
        em = E is not None
        if em:
            E.memset(zeros[:], 0.0)
            E.memset(ones_bf[:], 1.0)
            E.memset(mx_x[:], 0.0)
            E.memset(mx_h[:], 0.0)
            E.memset(eps_sb[:], EPS)
            E.engine_nop().then_inc(d_m.h, 1)
        d_m.add(1)
        mark("dve_init", d_m)

        if em:
            wt(E, "kev0")
            E.tensor_copy(
                ksamp[:].rearrange("p (d k) -> p d k", d=HC),
                kstage[:].rearrange("p (d k) -> p d k", d=HC)[:, :, 0:128],
            ).then_inc(d_m.h, 1)
        d_m.add(1)
        mark("ksampc", d_m)

        def wq_unit(k):
            kind, chunk, sub, phase = WU_KIND[k]
            w = _wu_w(k)
            b = k % 2
            if em:
                wt(E, f"wl{k}")
                buf = wraw[k % 3]
                if kind < 2:
                    c0, c1 = 8 + b * 2, 9 + b * 2
                    kp = _prev_spill(k)
                    if kp is not None:
                        wt(E, f"wsp{kp}")
                    E.tensor_reduce(sgl[:, c0:c0 + 1], buf[:, 0:w], AX.X, ALU.max,
                                    apply_absolute_value=True)
                    E.drain()
                    E.reciprocal(sgl[:, c1:c1 + 1], sgl[:, c0:c0 + 1])
                    E.drain()
                    E.tensor_scalar(out=sgl[:, c1:c1 + 1], in0=sgl[:, c1:c1 + 1],
                                    scalar1=127.0, scalar2=None, op0=ALU.mult)
                    E.tensor_scalar(out=sgl[:, c0:c0 + 1], in0=sgl[:, c0:c0 + 1],
                                    scalar1=1.0 / 127.0, scalar2=None, op0=ALU.mult)
                    E.drain()
                    E.tensor_scalar(out=wtmp[:, 0:w], in0=buf[:, 0:w],
                                    scalar1=sgl[:, c1:c1 + 1], scalar2=127.49,
                                    op0=ALU.mult, op1=ALU.min)
                    E.drain()
                    E.tensor_scalar(out=wtmpi[:, 0:w], in0=wtmp[:, 0:w],
                                    scalar1=-128.49, scalar2=None, op0=ALU.max)
                    E.drain()
                    E.tensor_scalar(out=wqbf[b][:, 0:w], in0=wtmpi[:, 0:w],
                                    scalar1=sgl[:, c0:c0 + 1], scalar2=None,
                                    op0=ALU.mult).then_inc(d_q.h, 1)
                elif phase == 0:
                    E.tensor_reduce(sgl[:, 13:14], buf[:, 0:w], AX.X, ALU.max,
                                    apply_absolute_value=True)
                    E.drain()
                    if sub == 0:
                        E.tensor_copy(sgl[:, 12:13], sgl[:, 13:14])
                    else:
                        E.tensor_max(sgl[:, 12:13], sgl[:, 12:13], sgl[:, 13:14])
                    E.drain()
                    if sub == 2:
                        E.reciprocal(sgl[:, 14:15], sgl[:, 12:13])
                        E.drain()
                        E.tensor_scalar(out=sgl[:, 14:15], in0=sgl[:, 14:15],
                                        scalar1=127.0, scalar2=None, op0=ALU.mult)
                        E.tensor_scalar(out=sgl[:, 12:13], in0=sgl[:, 12:13],
                                        scalar1=1.0 / 127.0, scalar2=None, op0=ALU.mult)
                        E.drain()
                    E.engine_nop().then_inc(d_q.h, 1)
                else:
                    kp = _prev_spill(k)
                    if kp is not None:
                        wt(E, f"wsp{kp}")
                    E.tensor_scalar(out=wtmp[:, 0:w], in0=buf[:, 0:w],
                                    scalar1=sgl[:, 14:15], scalar2=127.49,
                                    op0=ALU.mult, op1=ALU.min)
                    E.drain()
                    E.tensor_scalar(out=wtmpi[:, 0:w], in0=wtmp[:, 0:w],
                                    scalar1=-128.49, scalar2=None, op0=ALU.max)
                    E.drain()
                    E.tensor_scalar(out=wqbf[b][:, 0:w], in0=wtmpi[:, 0:w],
                                    scalar1=sgl[:, 12:13], scalar2=None,
                                    op0=ALU.mult).then_inc(d_q.h, 1)
            d_q.add(1)
            mark(f"wq_u{k}", d_q)

        kdone = 0
        for k in range(WQ_PROD):
            wq_unit(k)
            kdone += 1

        if em:
            wt(E, "samp_mm")
            E.tensor_reduce(sgl[:, 0:1], ps_pa[:, 0:128], AX.X, ALU.max)
            E.drain()
            E.tensor_scalar(out=junk[:, 0:128], in0=zeros[:],
                            scalar1=sgl[:, 0:1], scalar2=None, op0=ALU.add)
            E.engine_nop().then_inc(d_m.h, 1)
        d_m.add(1)
        mark("mrep", d_m)
        if em:
            wt(E, "mrep_t")
            E.tensor_reduce(sgl[:, 1:2], ps_pb[:, 0:128], AX.X, ALU.max)
            E.drain()
            E.tensor_scalar(out=negm[:], in0=sgl[:, 1:2], scalar1=-1.0 / 32.0,
                            scalar2=-2.0, op0=ALU.mult, op1=ALU.add).then_inc(d_m.h, 1)
        d_m.add(1)
        mark("negm", d_m)

        ps_ot, ps_r = AT.get("ps_ot"), AT.get("ps_r")
        for v in range(CHUNKS):
            for g in range(SC_NSEG[v]):
                qlo = seg_qlo(v, g)
                subs = seg_subs(v, g)
                if g >= SC_S0[v]:
                    for t2 in range(8):
                        STI = STILE_I[(v, g, t2, 0)]
                        if em:
                            if d_mk.n == 0:
                                wt(E, "cfg")
                            E.wait_ge(a_exp.h, STI + 1)
                            E.tensor_tensor(AT["pseg"][:, t2 * 1024:t2 * 1024 + 128],
                                            AT["pseg"][:, t2 * 1024:t2 * 1024 + 128],
                                            masks[:, t2 * 128:(t2 + 1) * 128],
                                            ALU.mult).then_inc(d_mk.h, 1)
                        d_mk.add(1)
                    mark(f"mseg{v}_{g}", d_mk)
                for si in range(seg_ns(v, g)):
                    SW = SWEEP_I[(v, g, si)]
                    par = SW % 2
                    osl = qlo + si - SC_S0[v]
                    first = g == 0
                    if em:
                        E.wait_ge(p_pv.h, SW + 1)
                        if first:
                            E.tensor_copy(AT["osb"][:, osl * D_:(osl + 1) * D_],
                                          ps_ot[:, par * 1024:par * 1024 + 1024])
                            E.drain()
                            E.tensor_copy(AT["rsb"][:, osl:osl + 1],
                                          ps_r[:, par * 512:par * 512 + 1])
                            E.drain()
                        else:
                            E.tensor_add(AT["osb"][:, osl * D_:(osl + 1) * D_],
                                         AT["osb"][:, osl * D_:(osl + 1) * D_],
                                         ps_ot[:, par * 1024:par * 1024 + 1024])
                            E.drain()
                            E.tensor_add(AT["rsb"][:, osl:osl + 1],
                                         AT["rsb"][:, osl:osl + 1],
                                         ps_r[:, par * 512:par * 512 + 1])
                            E.drain()
                        E.engine_nop().then_inc(d_oa.h, 1)
                    d_oa.add(1)
                    assert d_oa.n == SW + 1
                for _ in range(WQ_SEG.get((v, g), 0)):
                    if kdone < NWU:
                        wq_unit(kdone)
                        kdone += 1
                if g >= SC_S0[v]:
                    si = g - SC_S0[v]
                    s = g
                    if em:
                        wt(E, f"xq{s}")
                        if s >= 2:
                            wt(E, f"lnsp_s{s - 2}")
                        E.reciprocal(sgl[:, 2:3], AT["rsb"][:, si:si + 1])
                        E.drain()
                        E.scalar_tensor_tensor(
                            out=AT["resid"][:], in0=AT["osb"][:, si * D_:(si + 1) * D_],
                            scalar=sgl[:, 2:3], in1=xqr[s % 2][:],
                            op0=ALU.mult, op1=ALU.add, accum_out=stat[:, 24:25])
                        E.engine_nop().then_inc(d_ep.h, 1)
                    d_ep.add(1)
                    mark(f"vstt{s}", d_ep)
                    if em:
                        E.drain()
                        E.scalar_tensor_tensor(
                            out=junk[:, 0:D_], in0=AT["resid"][:], scalar=1.0,
                            in1=AT["resid"][:], op0=ALU.mult, op1=ALU.mult,
                            accum_out=stat[:, 25:26])
                        E.drain()
                        E.tensor_scalar(out=stat[:, si:si + 1], in0=stat[:, 24:25],
                                        scalar1=1.0 / D_, scalar2=None, op0=ALU.mult)
                        E.drain()
                        E.tensor_tensor(stat[:, 26:27], stat[:, si:si + 1],
                                        stat[:, si:si + 1], ALU.mult)
                        E.tensor_scalar(out=stat[:, 8 + si:9 + si], in0=stat[:, 25:26],
                                        scalar1=1.0 / D_, scalar2=None, op0=ALU.mult)
                        E.drain()
                        E.tensor_sub(stat[:, 8 + si:9 + si], stat[:, 8 + si:9 + si],
                                     stat[:, 26:27])
                        E.engine_nop().then_inc(d_ep.h, 1)
                    d_ep.add(1)
                    mark(f"var_s{s}", d_ep)
                    if em:
                        wt(E, f"rstd_s{s}")
                        E.tensor_scalar(out=AT["lnst"][:, (s % 2) * D_:(s % 2 + 1) * D_],
                                        in0=AT["resid"][:],
                                        scalar1=stat[:, si:si + 1],
                                        scalar2=stat[:, 16 + si:17 + si],
                                        op0=ALU.subtract, op1=ALU.mult)
                        E.drain()
                        E.tensor_reduce(stat[:, 27:28],
                                        AT["lnst"][:, (s % 2) * D_:(s % 2 + 1) * D_],
                                        AX.X, ALU.max, apply_absolute_value=True)
                        E.drain()
                        E.tensor_tensor(mx_x[:], mx_x[:], stat[:, 27:28], ALU.max)
                        E.engine_nop().then_inc(d_ep.h, 1)
                    d_ep.add(1)
                    mark(f"epi_s{s}", d_ep)
            mark(f"epic{v}", d_ep)
        assert kdone == NWU

    def A_sync(E):
        em = E is not None
        if em:
            E.dma_start(out=identf[:], in_=identf_in).then_inc(s_cfg.h, 16)
            E.dma_start(out=clipx[:], in_=clipx_in).then_inc(s_cfg.h, 16)
            E.dma_start(out=clipd[:], in_=clipd_in).then_inc(s_cfg.h, 16)
            for tt in range(8):
                E.dma_start(out=masks[:, tt * 128:(tt + 1) * 128],
                            in_=masks_in[tt * 128:(tt + 1) * 128, :]).then_inc(s_cfg.h, 16)
        s_cfg.add(16 * 11)
        mark("cfg", s_cfg)


        for v in range(CHUNKS):
            for g in range(SC_NSEG[v]):
                for gi2 in (2 * g, 2 * g + 1):
                    L = LOAD_I[(v, gi2)]
                    j = gi2 // 8
                    r = gi2 % 8
                    if em:
                        wt(E, f"cc_ch{j}")
                        if L < 3:
                            wt(E, "qmm3_7")
                            wt(E, "prod_ev")
                            wt(E, "ksp3")
                            wt(E, "vsp3")
                        else:
                            vp, gp = LOAD_LIST[L - 3]
                            wt(E, f"segdone{vp}_{gp // 2}")
                        E.dma_start(out=AT["kbuf"][L % 3][:],
                                    in_=kall[j].ap()[r * 128:(r + 1) * 128, :]).then_inc(kg[L % 3].h, 16)
                    kg[L % 3].add(16)
                    mark(f"kgl{L}", kg[L % 3])

    def B_sync(E):
        em = E is not None

        def lnrd_load(n):
            if em:
                if n == 0:
                    wt(E, "lnsp_all0")
                    wt(E, "lnsp_all1")
                if n >= 4:
                    wt(E, f"qx{n - 4}q")
                E.dma_start(out=BT["lnrd"][n % 4][:],
                            in_=ln1_sp.ap()[n * 128:(n + 1) * 128, :]).then_inc(s_lnr[n % 4].h, 16)
            s_lnr[n % 4].add(16)
            mark(f"lnrd{n}", s_lnr[n % 4])

        for n in range(4):
            lnrd_load(n)
        if em:
            wt(E, "cc_arx")
            E.dma_start(out=arx_sb[:], in_=ar_o[0].ap()).then_inc(s_arx.h, 16)
        s_arx.add(16)
        mark("arx_ld", s_arx)
        for n in range(4, NQT):
            lnrd_load(n)

    def B_pe(E):
        em = E is not None
        if em:
            wt(E, "bx_rep")
            E.transpose(BT["ps_t"][:, 0:128], junk[:, 0:128], identf[:]).then_inc(p_bc.h, 1)
        p_bc.add(1)
        mark("bx_tr", p_bc)
        for n in range(NQT):
            if em:
                wt(E, f"qx{n}q")
                if n >= 2:
                    wt(E, f"qxev{n - 2}")
                ins = None
                for d in range(HC):
                    ins = E.transpose(
                        BT["ps_xtb"][:, (n % 2) * 1024 + d * 128:(n % 2) * 1024 + (d + 1) * 128],
                        BT["qxst"][n % 2][:, d * 128:(d + 1) * 128], identb[:])
                ins.then_inc(p_bc.h, 1)
            p_bc.add(1)
            mark(f"qxt{n}", p_bc)
        nu = 0
        for icx in range(IC):
            for tc in range(4):
                par = nu % 2
                if em:
                    if nu == 0:
                        wt(E, "wguT")
                    if icx == 0:
                        wt(E, f"qxev{4 * tc + 3}")
                    if nu >= 2:
                        wt(E, f"hdone{nu - 2}")
                    ins = None
                    for d in range(HC):
                        ins = E.matmul(BT["ps_g"][:, par * 512:par * 512 + 512],
                                       BT["wg_T"][:, d * DI_ + icx * 128:d * DI_ + (icx + 1) * 128],
                                       BT["qx_fm"][:, d * T_ + tc * 512:d * T_ + (tc + 1) * 512],
                                       start=d == 0, stop=d == HC - 1)
                    ins.then_inc(p_bc.h, 1)
                p_bc.add(1)
                mark(f"g_mm{nu}", p_bc)
                if em:
                    ins = None
                    for d in range(HC):
                        ins = E.matmul(BT["ps_u"][:, par * 512:par * 512 + 512],
                                       BT["wu_T"][:, d * DI_ + icx * 128:d * DI_ + (icx + 1) * 128],
                                       BT["qx_fm"][:, d * T_ + tc * 512:d * T_ + (tc + 1) * 512],
                                       start=d == 0, stop=d == HC - 1)
                    ins.then_inc(p_bc.h, 1)
                p_bc.add(1)
                mark(f"u_mm{nu}", p_bc)
                nu += 1

    def B_act(E):
        em = E is not None
        for d in range(HC):
            if em:
                if d == 0:
                    wt(E, "wsp_all0")
                    wt(E, "wsp_all1")
                E.dma_start(out=BT["wg_T"][:, d * DI_:(d + 1) * DI_],
                            in_=wq_sp[0].ap()[:, d * 128:(d + 1) * 128],
                            transpose=True).then_inc(s_wgu.h, 16)
            s_wgu.add(16)
        for d in range(HC):
            if em:
                E.dma_start(out=BT["wu_T"][:, d * DI_:(d + 1) * DI_],
                            in_=wq_sp[1].ap()[:, d * 128:(d + 1) * 128],
                            transpose=True).then_inc(s_wgu.h, 16)
            s_wgu.add(16)
        mark("wguT", s_wgu)
        if em:
            wt(E, "bx_tr")
            wt(E, "cfg")
            E.activation(out=sgl[:, 16:17], in_=clipx[:], func=AF.Sigmoid).then_inc(a_m.h, 1)
        a_m.add(1)
        mark("sigx", a_m)
        for n in range(NQT):
            if em:
                wt(E, f"qxt{n}")
                E.activation(
                    out=BT["qx_fm"][:].rearrange("p (h s k) -> p h s k", h=HC, s=NQT)[:, :, n, :],
                    in_=BT["ps_xtb"][:, (n % 2) * 1024:(n % 2) * 1024 + 1024].rearrange(
                        "p (h k) -> p h k", h=HC),
                    func=AF.Copy).then_inc(a_m.h, 1)
            a_m.add(1)
            mark(f"qxev{n}", a_m)
        for nu in range(4 * IC):
            par = nu % 2
            if em:
                wt(E, f"g_mm{nu}")
                if nu >= 2:
                    wt(E, f"hdone{nu - 2}")
                E.activation(out=BT["silu"][par][:],
                             in_=BT["ps_g"][:, par * 512:par * 512 + 512],
                             func=AF.Silu).then_inc(a_m.h, 1)
            a_m.add(1)
            mark(f"silu{nu}", a_m)

    def B_dve(E):
        em = E is not None
        if em:
            wt(E, "arx_ld")
            E.tensor_scalar(out=junk[:, 0:128], in0=zeros[:], scalar1=arx_sb[:],
                            scalar2=None, op0=ALU.add)
            E.engine_nop().then_inc(d_bc.h, 1)
        d_bc.add(1)
        mark("bx_rep", d_bc)
        if em:
            wt(E, "bx_tr")
            E.tensor_reduce(sgl[:, 17:18], BT["ps_t"][:, 0:128], AX.X, ALU.max)
            wt(E, "sigx")
            E.drain()
            E.tensor_tensor(sgl[:, 18:19], sgl[:, 16:17], sgl[:, 17:18], ALU.mult)
            E.drain()
            E.tensor_scalar(out=sgl[:, 18:19], in0=sgl[:, 18:19],
                            scalar1=1.0 / 127.0, scalar2=None, op0=ALU.mult)
            E.drain()
            E.reciprocal(sgl[:, 19:20], sgl[:, 18:19])
            E.engine_nop().then_inc(d_bc.h, 1)
        d_bc.add(1)
        mark("sx", d_bc)

        for n in range(NQT):
            if em:
                if n == 0:
                    wt(E, "sx")
                wt(E, f"lnrd{n}")
                if n >= 2:
                    wt(E, f"qxt{n - 2}")
                E.tensor_scalar(out=junk[:, 0:D_], in0=BT["lnrd"][n % 4][:],
                                scalar1=sgl[:, 19:20], scalar2=127.49,
                                op0=ALU.mult, op1=ALU.min)
                E.drain()
                E.tensor_scalar(out=wtmpi[:, 0:D_], in0=junk[:, 0:D_],
                                scalar1=-128.49, scalar2=None, op0=ALU.max)
                E.drain()
                E.tensor_scalar(out=BT["qxst"][n % 2][:], in0=wtmpi[:, 0:D_],
                                scalar1=sgl[:, 18:19], scalar2=None,
                                op0=ALU.mult).then_inc(d_bc.h, 1)
            d_bc.add(1)
            mark(f"qx{n}q", d_bc)

        for nu in range(4 * IC):
            par = nu % 2
            if em:
                wt(E, f"silu{nu}")
                wt(E, f"u_mm{nu}")
                if nu >= 2:
                    wt(E, f"hsp{nu - 2}")
                E.tensor_tensor(BT["hst"][par][:], BT["silu"][par][:],
                                BT["ps_u"][:, par * 512:par * 512 + 512], ALU.mult)
                E.drain()
                E.tensor_reduce(stat[:, 14:15], BT["hst"][par][:], AX.X, ALU.max,
                                apply_absolute_value=True)
                E.drain()
                E.tensor_tensor(mx_h[:], mx_h[:], stat[:, 14:15], ALU.max)
                E.engine_nop().then_inc(d_bc.h, 1)
            d_bc.add(1)
            mark(f"hdone{nu}", d_bc)

    def B_gpsimd(E):
        em = E is not None
        for nu in range(4 * IC):
            icx, tc = nu // 4, nu % 4
            par = nu % 2
            if em:
                wt(E, f"hdone{nu}")
                E.dma_start(out=h_spd.ap()[:, icx * T_ + tc * 512:icx * T_ + (tc + 1) * 512],
                            in_=BT["hst"][par][:]).then_inc(g_hsp[par].h, 16)
            g_hsp[par].add(16)
            mark(f"hsp{nu}", g_hsp[par])
        mark("hsp_all0", g_hsp[0])
        mark("hsp_all1", g_hsp[1])
        if em:
            wt(E, f"hdone{60}")
            E.collective_compute("AllReduce", ALU.max, replica_groups=RG,
                                 ins=[ar_i[2].ap().opt()],
                                 outs=[ar_o[2].ap().opt()]).then_inc(g_cc.h, 1)
        g_cc.add(1)
        if em:
            wt(E, f"hdone{4 * IC - 1}")
            E.dma_start(out=ar_i[1].ap(), in_=mx_h[:]).then_inc(g_ar.h, 16)
        g_ar.add(16)
        if em:
            E.wait_ge(g_ar.h, g_ar.n)
            E.collective_compute("AllReduce", ALU.max, replica_groups=RG,
                                 ins=[ar_i[1].ap().opt()],
                                 outs=[ar_o[1].ap().opt()]).then_inc(g_cc.h, 1)
        g_cc.add(1)
        mark("cc_arh", g_cc)

    def C_sync(E):
        em = E is not None

        def hld_load(n, EE):
            if EE is not None:
                if n < 2:
                    wt(EE, "hsp_all0")
                    wt(EE, "hsp_all1")
                else:
                    wt(EE, f"hq{n - 2}")
                EE.dma_start(
                    out=CT["hbuf"][n % 2][:].rearrange("p (i k) -> p i k", i=IC),
                    in_=h_spd.ap()[:, :].rearrange("p (i t) -> p i t", i=IC)[:, :, n * 128:(n + 1) * 128],
                ).then_inc(s_hld[n % 2].h, 16)
            s_hld[n % 2].add(16)
            mark(f"hld{n}", s_hld[n % 2])

        def lnr_load(m, EE):
            if EE is not None:
                if m == 0:
                    wt(EE, "lnsp_all0")
                    wt(EE, "lnsp_all1")
                if m >= 4:
                    wt(EE, f"r2d{m - 4}")
                EE.dma_start(out=CT["lnr"][m % 4][:],
                             in_=ln1_sp.ap()[m * 128:(m + 1) * 128, :]).then_inc(s_clnr[m % 4].h, 16)
            s_clnr[m % 4].add(16)
            mark(f"lnr{m}", s_clnr[m % 4])

        C_sync.hld_load = hld_load
        C_sync.lnr_load = lnr_load
        hld_load(0, E)
        if em:
            wt(E, "cc_arh")
            E.dma_start(out=arh_sb[:], in_=ar_o[1].ap()).then_inc(s_arh.h, 16)
        s_arh.add(16)
        mark("arh_ld", s_arh)
        for n in range(1, NQT):
            if n % 2 == 0:
                hld_load(n, E)
            else:
                lnr_load(n, E)
        if em:
            E.wait_ge(s_out[0].h, 16 * NQT // 2)
            E.wait_ge(s_out[1].h, 16 * NQT // 2)

    def C_pe(E):
        em = E is not None
        if em:
            wt(E, "ch_rep")
            E.transpose(CT["ps_ct"][:, 0:128], junk[:, 0:128], identf[:]).then_inc(p_bc.h, 1)
        p_bc.add(1)
        mark("ch_tr", p_bc)
        for n in range(NQT):
            par = n % 2
            if em:
                if n == 0:
                    wt(E, "qwdT")
                wt(E, f"hq{n}")
                if n >= 2:
                    wt(E, f"r2d{n - 2}")
                ins = None
                for icx in range(IC):
                    for hoh in range(2):
                        ins = E.matmul(
                            CT["ps_d"][:, par * 1024 + hoh * 512:par * 1024 + (hoh + 1) * 512],
                            CT["hq"][par][:, icx * 128:(icx + 1) * 128],
                            CT["qwd_T"][:, icx * D_ + hoh * 512:icx * D_ + (hoh + 1) * 512],
                            start=icx == 0, stop=icx == IC - 1)
                ins.then_inc(p_bc.h, 1)
            p_bc.add(1)
            mark(f"dn{n}", p_bc)

    def C_act(E):
        em = E is not None
        for icx in range(IC):
            if em:
                if icx == 0:
                    wt(E, "wsp_all0")
                    wt(E, "wsp_all1")
                E.dma_start(out=CT["qwd_T"][:, icx * D_:(icx + 1) * D_],
                            in_=wq_sp[2].ap()[:, icx * 128:(icx + 1) * 128],
                            transpose=True).then_inc(s_qwd.h, 16)
            s_qwd.add(16)
        mark("qwdT", s_qwd)
        if em:
            wt(E, "ch_tr")
            E.activation(out=sgl[:, 20:21], in_=clipd[:], func=AF.Sigmoid).then_inc(a_m.h, 1)
        a_m.add(1)
        mark("sigd", a_m)

        def act_load(m, EE):
            if m % 2 == 1:
                C_sync.hld_load(m, EE)
            else:
                C_sync.lnr_load(m, EE)

        act_load(0, E)
        act_load(1, E)
        for n in range(NQT + 3):
            if n + 2 < NQT:
                act_load(n + 2, E)
            if 3 <= n <= NQT + 2:
                m = n - 3
                p = m % 2
                if em:
                    wt(E, f"bias{m}")
                    if m >= 2:
                        wt(E, f"out{m - 2}")
                    E.drain()
                    E.activation(out=CT["ot"][p][:], in_=CT["r2"][p][:],
                                 func=AF.Identity, bias=stat[:, 14 + p:15 + p],
                                 scale=stat[:, 6 + p:7 + p]).then_inc(a_m.h, 1)
                a_m.add(1)
                mark(f"cote{m}", a_m)
            if 1 <= n <= NQT:
                m = n - 1
                p = m % 2
                if em:
                    wt(E, f"r2d{m}")
                    E.drain()
                    E.activation(out=junk[:, 0:D_], in_=CT["r2"][p][:],
                                 func=AF.Square,
                                 accum_out=stat[:, 10 + p:11 + p]).then_inc(a_m.h, 1)
                a_m.add(1)
                mark(f"sqa{m}", a_m)
            if 2 <= n <= NQT + 1:
                m = n - 2
                p = m % 2
                if em:
                    wt(E, f"vard{m}")
                    E.activation(out=stat[:, 4 + p:5 + p], in_=stat[:, 2 + p:3 + p],
                                 func=AF.Ln, bias=eps_sb[:], scale=1.0)
                    E.drain()
                    E.activation(out=stat[:, 6 + p:7 + p], in_=stat[:, 4 + p:5 + p],
                                 func=AF.Exp, bias=0.0, scale=-0.5).then_inc(a_m.h, 1)
                a_m.add(1)
                mark(f"crstd{m}", a_m)

    def C_dve(E):
        em = E is not None
        if em:
            wt(E, "arh_ld")
            E.tensor_scalar(out=junk[:, 0:128], in0=zeros[:], scalar1=arh_sb[:],
                            scalar2=None, op0=ALU.add)
            E.engine_nop().then_inc(d_bc.h, 1)
        d_bc.add(1)
        mark("ch_rep", d_bc)
        if em:
            wt(E, "ch_tr")
            E.tensor_reduce(sgl[:, 21:22], CT["ps_ct"][:, 0:128], AX.X, ALU.max)
            wt(E, "sigd")
            E.drain()
            E.tensor_tensor(sgl[:, 22:23], sgl[:, 20:21], sgl[:, 21:22], ALU.mult)
            E.drain()
            E.tensor_scalar(out=sgl[:, 22:23], in0=sgl[:, 22:23],
                            scalar1=1.0 / 127.0, scalar2=None, op0=ALU.mult)
            E.drain()
            E.reciprocal(sgl[:, 23:24], sgl[:, 22:23])
            E.engine_nop().then_inc(d_bc.h, 1)
        d_bc.add(1)
        mark("sh", d_bc)

        for n in range(NQT + 3):
            if n < NQT:
                par = n % 2
                if em:
                    if n == 0:
                        wt(E, "sh")
                    wt(E, f"hld{n}")
                    E.tensor_scalar(out=CT["ctmp"][:], in0=CT["hbuf"][par][:],
                                    scalar1=sgl[:, 23:24], scalar2=127.49,
                                    op0=ALU.mult, op1=ALU.min)
                    E.drain()
                    E.tensor_scalar(out=CT["ctmpi"][:], in0=CT["ctmp"][:],
                                    scalar1=-128.49, scalar2=None, op0=ALU.max)
                    E.drain()
                    E.tensor_scalar(out=CT["hq"][par][:], in0=CT["ctmpi"][:],
                                    scalar1=sgl[:, 22:23], scalar2=None,
                                    op0=ALU.mult).then_inc(d_bc.h, 1)
                d_bc.add(1)
                mark(f"hq{n}", d_bc)
            if 3 <= n <= NQT + 2:
                m = n - 3
                p = m % 2
                if em:
                    wt(E, f"crstd{m}")
                    E.tensor_scalar(out=stat[:, 14 + p:15 + p], in0=stat[:, 0 + p:1 + p],
                                    scalar1=stat[:, 6 + p:7 + p], scalar2=-1.0,
                                    op0=ALU.mult, op1=ALU.mult).then_inc(d_bc.h, 1)
                d_bc.add(1)
                mark(f"bias{m}", d_bc)
            if 1 <= n <= NQT:
                m = n - 1
                p = m % 2
                if em:
                    wt(E, f"dn{m}")
                    wt(E, f"lnr{m}")
                    if m >= 2:
                        wt(E, f"cote{m - 2}")
                    E.scalar_tensor_tensor(
                        out=CT["r2"][p][:], in0=CT["ps_d"][:, p * 1024:p * 1024 + 1024],
                        scalar=1.0, in1=CT["lnr"][m % 4][:],
                        op0=ALU.mult, op1=ALU.add, accum_out=stat[:, 8 + p:9 + p])
                    E.drain()
                    E.tensor_scalar(out=stat[:, 0 + p:1 + p], in0=stat[:, 8 + p:9 + p],
                                    scalar1=1.0 / D_, scalar2=None,
                                    op0=ALU.mult).then_inc(d_bc.h, 1)
                d_bc.add(1)
                mark(f"r2d{m}", d_bc)
            if 2 <= n <= NQT + 1:
                m = n - 2
                p = m % 2
                if em:
                    wt(E, f"sqa{m}")
                    E.drain()
                    E.tensor_scalar(out=stat[:, 2 + p:3 + p], in0=stat[:, 10 + p:11 + p],
                                    scalar1=1.0 / D_, scalar2=None, op0=ALU.mult)
                    E.tensor_tensor(stat[:, 12 + p:13 + p], stat[:, 0 + p:1 + p],
                                    stat[:, 0 + p:1 + p], ALU.mult)
                    E.drain()
                    E.tensor_sub(stat[:, 2 + p:3 + p], stat[:, 2 + p:3 + p],
                                 stat[:, 12 + p:13 + p])
                    E.engine_nop().then_inc(d_bc.h, 1)
                d_bc.add(1)
                mark(f"vard{m}", d_bc)

    def C_gpsimd(E):
        em = E is not None
        for n in range(NQT):
            if em:
                wt(E, f"cote{n}")
                E.dma_start(out=out_t.ap()[n * 128:(n + 1) * 128, :],
                            in_=CT["ot"][n % 2][:]).then_inc(s_out[n % 2].h, 16)
            s_out[n % 2].add(16)
            mark(f"out{n}", s_out[n % 2])

    for _k in ["kbuf", "vbuf", "pseg", "osb", "rsb", "resid", "lnst",
               "ps_s", "ps_ot", "ps_r"]:
        AT[_k] = None

    A_gpsimd(None)
    A_pe(None)
    A_act(None)
    A_dve(None)
    A_sync(None)
    B_sync(None)
    B_pe(None)
    B_act(None)
    B_dve(None)
    B_gpsimd(None)
    C_sync(None)
    C_pe(None)
    C_act(None)
    C_dve(None)
    C_gpsimd(None)
    dry = {id(c): c.n for c in ctrs}
    for c in ctrs:
        c.n = 0

    es_pr.close()
    es_a2 = ExitStack()
    AT["kbuf"] = [es_a2.enter_context(nc.sbuf_tensor(f"kbuf{i}", [128, HC * 512], BF16)) for i in range(3)]
    AT["vbuf"] = [es_a2.enter_context(nc.sbuf_tensor(f"vbuf{i}", [128, 4 * D_], BF16)) for i in range(3)]
    AT["pseg"] = es_a2.enter_context(nc.sbuf_tensor("pseg", [128, 8 * 1024], BF16))
    AT["osb"] = es_a2.enter_context(nc.sbuf_tensor("osb", [128, 8 * D_], F32))
    AT["rsb"] = es_a2.enter_context(nc.sbuf_tensor("rsb", [128, 8], F32))
    AT["resid"] = es_a2.enter_context(nc.sbuf_tensor("resid", [128, D_], F32))
    AT["lnst"] = es_a2.enter_context(nc.sbuf_tensor("lnst", [128, 2 * D_], F32))
    AT["ps_s"] = es_a2.enter_context(nc.psum_tensor("ps_s", [128, 1024], F32))
    AT["ps_ot"] = es_a2.enter_context(nc.psum_tensor("ps_ot", [128, 2048], F32))
    AT["ps_r"] = es_a2.enter_context(nc.psum_tensor("ps_r", [128, 1024], F32))

    with nc.Block("A") as blk:
        @blk.gpsimd
        def _(E):
            A_gpsimd(E)

        @blk.tensor
        def _(E):
            A_pe(E)

        @blk.scalar
        def _(E):
            A_act(E)

        @blk.vector
        def _(E):
            A_dve(E)

        @blk.sync
        def _(E):
            A_sync(E)

    es_a2.close()
    es_a1.close()
    es_b = ExitStack()
    BT["qx_fm"] = es_b.enter_context(nc.sbuf_tensor("qx_fm", [128, HC * T_], BF16))
    BT["wg_T"] = es_b.enter_context(nc.sbuf_tensor("wg_T", [128, HC * DI_], BF16))
    BT["wu_T"] = es_b.enter_context(nc.sbuf_tensor("wu_T", [128, HC * DI_], BF16))
    BT["silu"] = [es_b.enter_context(nc.sbuf_tensor(f"silu{i}", [128, 512], F32)) for i in range(2)]
    BT["hst"] = [es_b.enter_context(nc.sbuf_tensor(f"hst{i}", [128, 512], F32)) for i in range(2)]
    BT["qxst"] = [es_b.enter_context(nc.sbuf_tensor(f"qxst{i}", [128, D_], BF16)) for i in range(2)]
    BT["lnrd"] = [es_b.enter_context(nc.sbuf_tensor(f"lnrd{i}", [128, D_], F32)) for i in range(4)]
    BT["ps_xtb"] = es_b.enter_context(nc.psum_tensor("ps_xtb", [128, 2048], BF16))
    BT["ps_g"] = es_b.enter_context(nc.psum_tensor("ps_g", [128, 1024], F32))
    BT["ps_u"] = es_b.enter_context(nc.psum_tensor("ps_u", [128, 1024], F32))
    BT["ps_t"] = es_b.enter_context(nc.psum_tensor("ps_t", [128, 512], F32))

    with nc.Block("B") as blk:
        @blk.gpsimd
        def _(E):
            B_gpsimd(E)

        @blk.tensor
        def _(E):
            B_pe(E)

        @blk.scalar
        def _(E):
            B_act(E)

        @blk.vector
        def _(E):
            B_dve(E)

        @blk.sync
        def _(E):
            B_sync(E)

    es_b.close()
    es_c = ExitStack()
    CT["qwd_T"] = es_c.enter_context(nc.sbuf_tensor("qwd_T", [128, IC * D_], BF16))
    CT["hbuf"] = [es_c.enter_context(nc.sbuf_tensor(f"hbuf{i}", [128, DI_], F32)) for i in range(2)]
    CT["hq"] = [es_c.enter_context(nc.sbuf_tensor(f"hq{i}", [128, DI_], BF16)) for i in range(2)]
    CT["lnr"] = [es_c.enter_context(nc.sbuf_tensor(f"lnr{i}", [128, D_], F32)) for i in range(4)]
    CT["r2"] = [es_c.enter_context(nc.sbuf_tensor(f"r2_{i}", [128, D_], F32)) for i in range(2)]
    CT["ot"] = [es_c.enter_context(nc.sbuf_tensor(f"ot{i}", [128, D_], F32)) for i in range(2)]
    CT["ctmp"] = es_c.enter_context(nc.sbuf_tensor("ctmp", [128, DI_], F32))
    CT["ctmpi"] = es_c.enter_context(nc.sbuf_tensor("ctmpi", [128, DI_], I32))
    CT["ps_d"] = es_c.enter_context(nc.psum_tensor("ps_d", [128, 2048], F32))
    CT["ps_ct"] = es_c.enter_context(nc.psum_tensor("ps_ct", [128, 512], F32))

    with nc.Block("C") as blk:
        @blk.gpsimd
        def _(E):
            C_gpsimd(E)

        @blk.tensor
        def _(E):
            C_pe(E)

        @blk.scalar
        def _(E):
            C_act(E)

        @blk.vector
        def _(E):
            C_dve(E)

        @blk.sync
        def _(E):
            C_sync(E)

    for c in ctrs:
        assert c.n == dry[id(c)], (c.name, c.n, dry[id(c)])

    es_c.close()
    es.close()
    return nc


_NC_CACHE = {}


def _get_nc():
    if "nc" not in _NC_CACHE:
        _NC_CACHE["nc"] = build_nc()
    return _NC_CACHE["nc"]


def kernel(L0_attn_embeddings, L0_attn_W_Q, L0_attn_W_K, L0_attn_W_V,
           L0_ffn_clip_x, L0_ffn_clip_down, L0_ffn_w_gate, L0_ffn_w_up,
           L0_ffn_w_down):
    x = np.asarray(L0_attn_embeddings, dtype=np.float32).reshape(S_TOK, D_)
    wq = np.ascontiguousarray(np.asarray(L0_attn_W_Q, dtype=np.float32))
    wk = np.ascontiguousarray(np.asarray(L0_attn_W_K, dtype=np.float32))
    wv = np.ascontiguousarray(np.asarray(L0_attn_W_V, dtype=np.float32))
    wg = np.ascontiguousarray(np.asarray(L0_ffn_w_gate, dtype=np.float32))
    wu = np.ascontiguousarray(np.asarray(L0_ffn_w_up, dtype=np.float32))
    wd = np.ascontiguousarray(np.asarray(L0_ffn_w_down, dtype=np.float32))
    cx = float(np.asarray(L0_ffn_clip_x).reshape(-1)[0])
    cd = float(np.asarray(L0_ffn_clip_down).reshape(-1)[0])
    cx_t = np.full((128, 1), cx, dtype=np.float32)
    cd_t = np.full((128, 1), cd, dtype=np.float32)
    identf = np.eye(128, dtype=np.float32)
    identb = np.eye(128, dtype=ml_dtypes.bfloat16)

    in_maps = []
    tri = np.triu(np.ones((128, 128), dtype=np.float32)).astype(ml_dtypes.bfloat16)
    one_m = np.ones((128, 128), dtype=ml_dtypes.bfloat16)
    zero_m = np.zeros((128, 128), dtype=ml_dtypes.bfloat16)
    for c in range(NCORES):
        qrows = np.concatenate(
            [x[(8 * s + c) * 128:(8 * s + c + 1) * 128] for s in range(NQT)], axis=0)
        kvrows = np.concatenate(
            [x[(8 * j + c) * 512:(8 * j + c + 1) * 512] for j in range(NKV)], axis=0)
        mk = np.empty((8, 128, 128), dtype=ml_dtypes.bfloat16)
        for w in range(8):
            mk[w] = one_m if w < c else (tri if w == c else zero_m)
        in_maps.append({
            "xq": np.ascontiguousarray(qrows),
            "xkv": np.ascontiguousarray(kvrows),
            "wq": wq, "wk": wk, "wv": wv,
            "wg": wg, "wu": wu, "wd": wd,
            "clipx": cx_t, "clipd": cd_t,
            "masks": np.ascontiguousarray(mk.reshape(8 * 128, 128)),
            "identf": identf, "identb": identb,
        })

    nc = _get_nc()
    res = run_bass_kernel_spmd(nc, in_maps, core_ids=list(range(NCORES)))

    out = np.empty((S_TOK, D_), dtype=np.float32)
    for c in range(NCORES):
        oc = res.results[c]["out"]
        for s in range(NQT):
            out[(8 * s + c) * 128:(8 * s + c + 1) * 128] = oc[s * 128:(s + 1) * 128]
    return out.reshape(1, S_TOK, D_)


# version 15
# speedup vs baseline: 1.1240x; 1.0292x over previous
from contextlib import ExitStack

import numpy as np
import ml_dtypes
import concourse.bass as bass
import concourse.mybir as mybir
from concourse.bass_utils import run_bass_kernel_spmd

F32 = mybir.dt.float32
BF16 = mybir.dt.bfloat16
I32 = mybir.dt.int32
AF = mybir.ActivationFunctionType
ALU = mybir.AluOpType
AX = mybir.AxisListType

NCORES = 8
S_TOK = 16384
D_ = 1024
DI_ = 2816
T_ = 2048
NQT = 16
NPAIR = 8
NKV = 4
NCH = 4
HC = 8
IC = 22
EPS = 1e-5
NWU = 2 * IC + 8

PAIR_T = [16 * (u + 1) for u in range(NPAIR)]
CUM_T = [0]
for _u in range(NPAIR):
    CUM_T.append(CUM_T[-1] + PAIR_T[_u])
PAIR_G = [4 * (u + 1) for u in range(NPAIR)]
CUM_G = [0]
for _u in range(NPAIR):
    CUM_G.append(CUM_G[-1] + PAIR_G[_u])

WQ_PROD = 0
PROD_STAGES = []
for _j in range(4):
    PROD_STAGES.append(("kv", _j))
    PROD_STAGES.append(("q", _j))
PROD_TILES = []
for _st, _i in PROD_STAGES:
    for _k in range(4):
        PROD_TILES.append((_st, 4 * _i + _k))
PT_I = {t: i for i, t in enumerate(PROD_TILES)}
WD_SUBW = [1024, 1024, 768]
WU_KIND = ([(0, i, 0, 1) for i in range(IC)] + [(1, i, 0, 1) for i in range(IC)]
           + [(2, rb, s, p) for rb in range(8) for p in (0, 1) for s in range(3)])
NWU = len(WU_KIND)

CHUNKS = 2
SC_S0 = [0, 8]
SC_NSEG = [8, 16]
SEG_LIST = [(v, g) for v in range(CHUNKS) for g in range(SC_NSEG[v])]
SEG_I = {key: i for i, key in enumerate(SEG_LIST)}


def seg_qlo(v, g):
    return max(SC_S0[v], g)


def seg_ns(v, g):
    return SC_S0[v] + 8 - seg_qlo(v, g)


def seg_subs(v, g):
    W = seg_ns(v, g) * 128
    subs = []
    off = 0
    while off < W:
        w = min(512, W - off)
        subs.append((off, w))
        off += w
    return subs


STILE_LIST = [(v, g, t2, si2) for (v, g) in SEG_LIST for t2 in range(8)
              for si2 in range(len(seg_subs(v, g)))]
STILE_I = {key: i for i, key in enumerate(STILE_LIST)}
SWEEP_LIST = [(v, g, si) for (v, g) in SEG_LIST for si in range(seg_ns(v, g))]
SWEEP_I = {key: i for i, key in enumerate(SWEEP_LIST)}
LOAD_LIST = [(v, gi2) for v in range(CHUNKS) for gi2 in range(2 * SC_NSEG[v])]
LOAD_I = {key: i for i, key in enumerate(LOAD_LIST)}

WQ_SEG = {}
_k = WQ_PROD
_skip = {(0, 5), (0, 6), (0, 7), (1, 11), (1, 12), (1, 13), (1, 14), (1, 15)}
_segs = [s for s in SEG_LIST[2:] if s not in _skip]
_i = 0
while _k < NWU:
    _sg = _segs[_i % len(_segs)]
    WQ_SEG[_sg] = WQ_SEG.get(_sg, 0) + 1
    _k += 1
    _i += 1


def _wu_w(k):
    kind, chunk, sub, phase = WU_KIND[k]
    return D_ if kind < 2 else WD_SUBW[sub]


def _wu_spills(k):
    kind, chunk, sub, phase = WU_KIND[k]
    return kind < 2 or phase == 1


def _prev_spill(k):
    for k2 in range(k - 1, -1, -1):
        if _wu_spills(k2) and k2 % 2 == k % 2:
            return k2
    return None


def ti(u, t):
    return CUM_T[u] + t


def gi(u, g):
    return CUM_G[u] + g


INV_G = {}
for _u in range(NPAIR):
    for _g in range(PAIR_G[_u]):
        INV_G[gi(_u, _g)] = (_u, _g)


def mki(u, t):
    return 16 * u + (t - 16 * u)


class Ctr:
    def __init__(self, es, nc, name):
        self.h = es.enter_context(nc.semaphore(name))
        self.name = name
        self.n = 0

    def add(self, k=1):
        self.n += k
        return self.n


def build_nc():
    nc = bass.Bass("TRN2", target_bir_lowering=False, debug=False)

    xq_in = nc.dram_tensor("xq", [T_, D_], F32, kind="ExternalInput").ap()
    xkv_in = nc.dram_tensor("xkv", [T_, D_], F32, kind="ExternalInput").ap()
    wq_in = nc.dram_tensor("wq", [D_, D_], F32, kind="ExternalInput").ap()
    wk_in = nc.dram_tensor("wk", [D_, D_], F32, kind="ExternalInput").ap()
    wv_in = nc.dram_tensor("wv", [D_, D_], F32, kind="ExternalInput").ap()
    wg_in = nc.dram_tensor("wg", [DI_, D_], F32, kind="ExternalInput").ap()
    wu_in = nc.dram_tensor("wu", [DI_, D_], F32, kind="ExternalInput").ap()
    wd_in = nc.dram_tensor("wd", [D_, DI_], F32, kind="ExternalInput").ap()
    clipx_in = nc.dram_tensor("clipx", [128, 1], F32, kind="ExternalInput").ap()
    clipd_in = nc.dram_tensor("clipd", [128, 1], F32, kind="ExternalInput").ap()
    masks_in = nc.dram_tensor("masks", [8 * 128, 128], BF16, kind="ExternalInput").ap()
    identf_in = nc.dram_tensor("identf", [128, 128], F32, kind="ExternalInput").ap()
    identb_in = nc.dram_tensor("identb", [128, 128], BF16, kind="ExternalInput").ap()
    out_t = nc.dram_tensor("out", [T_, D_], F32, kind="ExternalOutput")

    ksrc = [nc.dram_tensor(f"ksrc{j}", [128, HC * 512], BF16) for j in range(NCH)]
    vsrc = [nc.dram_tensor(f"vsrc{j}", [128, 4 * D_], BF16) for j in range(NCH)]
    kall = [nc.dram_tensor(f"kall{j}", [NCORES * 128, HC * 512], BF16,
                           addr_space="Shared") for j in range(NCH)]
    vall = [nc.dram_tensor(f"vall{j}", [NCORES * 128, 4 * D_], BF16,
                           addr_space="Shared") for j in range(NCH)]
    wq_sp = [nc.dram_tensor("wgq_s", [DI_, D_], BF16),
             nc.dram_tensor("wuq_s", [DI_, D_], BF16),
             nc.dram_tensor("wdq_s", [D_, DI_], BF16)]
    ln1_sp = nc.dram_tensor("ln1_s", [T_, D_], F32)
    h_spd = nc.dram_tensor("h_s", [128, IC * T_], F32)
    ar_i = [nc.dram_tensor(f"ar_i{i}", [128, 1], F32) for i in range(3)]
    ar_o = [nc.dram_tensor(f"ar_o{i}", [128, 1], F32) for i in range(3)]

    RG = [list(range(NCORES))]
    es = ExitStack()

    def sbg(name, shape, dt):
        return es.enter_context(nc.sbuf_tensor(name, shape, dt))

    identf = sbg("identf_sb", [128, 128], F32)
    identb = sbg("identb_sb", [128, 128], BF16)
    zeros = sbg("zeros_sb", [128, 128], F32)
    ones_bf = sbg("ones_bf", [128, 1], BF16)
    clipx = sbg("clipx_sb", [128, 1], F32)
    clipd = sbg("clipd_sb", [128, 1], F32)
    sgl = sbg("sgl", [128, 32], F32)
    stat = sbg("stat", [128, 32], F32)
    mx_x = sbg("mx_x", [128, 1], F32)
    mx_h = sbg("mx_h", [128, 1], F32)
    junk = sbg("junk", [128, D_], F32)
    wtmp = sbg("wtmp", [128, D_], F32)
    wtmpi = sbg("wtmpi", [128, D_], I32)
    wqbf = [sbg(f"wqbf{i}", [128, D_], BF16) for i in range(2)]
    negm = sbg("negm_sb", [128, 1], F32)
    eps_sb = sbg("eps_sb", [128, 1], F32)
    arx_sb = sbg("arx_sb", [128, 1], F32)
    arh_sb = sbg("arh_sb", [128, 1], F32)

    es_a1 = ExitStack()

    def sba1(name, shape, dt):
        return es_a1.enter_context(nc.sbuf_tensor(name, shape, dt))

    masks = sba1("masks_sb", [128, 8 * 128], BF16)
    q_fm = sba1("q_fm", [128, HC * T_], BF16)
    ksamp = sba1("ksamp", [128, HC * 128], BF16)
    xqr = [sba1(f"xqr{i}", [128, D_], F32) for i in range(2)]
    wraw = [sba1(f"wraw{i}", [128, D_], F32) for i in range(4)]
    kstage = sba1("kstage", [128, HC * 512], BF16)
    vstage = sba1("vstage", [128, 4 * D_], BF16)

    es_pr = ExitStack()
    xstage = [es_pr.enter_context(nc.sbuf_tensor(f"xstage{i}", [128, D_], BF16)) for i in range(2)]
    xfm = [es_pr.enter_context(nc.sbuf_tensor(f"xfm{i}", [128, HC * 512], BF16)) for i in range(2)]
    wkv_bf = [es_pr.enter_context(nc.sbuf_tensor(f"wkv{i}", [128, HC * D_], BF16)) for i in range(2)]
    wq_bf = es_pr.enter_context(nc.sbuf_tensor("wq_bf", [128, HC * D_], BF16))
    ps_xt = es_pr.enter_context(nc.psum_tensor("ps_xt", [128, 2048], BF16))
    ps_pa = es_pr.enter_context(nc.psum_tensor("ps_pa", [128, 512], F32))
    ps_pb = es_pr.enter_context(nc.psum_tensor("ps_pb", [128, 512], F32))
    ps_v = es_pr.enter_context(nc.psum_tensor("ps_v", [128, 2048], F32))

    AT = {}

    BT = {}
    CT = {}

    gx = [Ctr(es, nc, f"gx{i}") for i in range(2)]
    g_id = Ctr(es, nc, "g_id")
    gw_k = Ctr(es, nc, "gw_k")
    gw_v = Ctr(es, nc, "gw_v")
    gw_q = Ctr(es, nc, "gw_q")
    g_ksp = Ctr(es, nc, "g_ksp")
    g_vsp = Ctr(es, nc, "g_vsp")
    g_wsp = [Ctr(es, nc, f"g_wsp{i}") for i in range(2)]
    g_lnsp = [Ctr(es, nc, f"g_lnsp{i}") for i in range(2)]
    g_hsp = [Ctr(es, nc, f"g_hsp{i}") for i in range(2)]
    g_ar = Ctr(es, nc, "g_ar")
    g_cc = Ctr(es, nc, "g_cc")
    s_cfg = Ctr(es, nc, "s_cfg")
    kg = [Ctr(es, nc, f"kg{i}") for i in range(3)]
    vg = [Ctr(es, nc, f"vg{i}") for i in range(3)]
    d_oa = Ctr(es, nc, "d_oa")
    sxq = [Ctr(es, nc, f"sxq{i}") for i in range(2)]
    sw = [Ctr(es, nc, f"sw{i}") for i in range(4)]
    s_arx = Ctr(es, nc, "s_arx")
    s_lnr = [Ctr(es, nc, f"s_lnr{i}") for i in range(4)]
    s_wgu = Ctr(es, nc, "s_wgu")
    s_arh = Ctr(es, nc, "s_arh")
    s_qwd = Ctr(es, nc, "s_qwd")
    s_hld = [Ctr(es, nc, f"s_hld{i}") for i in range(2)]
    s_clnr = [Ctr(es, nc, f"s_clnr{i}") for i in range(4)]
    s_out = [Ctr(es, nc, f"s_out{i}") for i in range(2)]
    p_pr = Ctr(es, nc, "p_pr")
    p_s = Ctr(es, nc, "p_s")
    p_pv = Ctr(es, nc, "p_pv")
    p_m = Ctr(es, nc, "p_m")
    p_bc = Ctr(es, nc, "p_bc")
    a_ev = Ctr(es, nc, "a_ev")
    a_exp = Ctr(es, nc, "a_exp")
    a_m = Ctr(es, nc, "a_m")
    d_mk = Ctr(es, nc, "d_mk")
    d_ep = Ctr(es, nc, "d_ep")
    d_q = Ctr(es, nc, "d_q")
    d_m = Ctr(es, nc, "d_m")
    d_bc = Ctr(es, nc, "d_bc")
    ctrs = (gx + [g_id, gw_k, gw_v, gw_q, g_ksp, g_vsp] + g_wsp + g_lnsp
            + g_hsp + [g_ar, g_cc, s_cfg] + kg + vg + sxq + sw
            + [s_arx] + s_lnr + [s_wgu, s_arh, s_qwd] + s_hld + s_clnr
            + s_out + [p_pr, p_s, p_pv, p_m, p_bc, a_ev, a_exp, a_m, d_mk,
                       d_ep, d_q, d_m, d_bc, d_oa])

    CK = {}

    def mark(name, ctr):
        if name in CK:
            assert CK[name] == (ctr, ctr.n), (name, CK[name][1], ctr.n)
        CK[name] = (ctr, ctr.n)

    def wt(eng, name):
        ctr, val = CK[name]
        if val > 0:
            eng.wait_ge(ctr.h, val)


    def A_gpsimd(E):
        em = E is not None
        if em:
            E.memset(sgl[:, 31:32], 0.0)
            E.drain()
            E.dma_start(out=ar_i[2].ap(), in_=sgl[:, 31:32]).then_inc(g_ar.h, 16)
        g_ar.add(16)
        if em:
            E.wait_ge(g_ar.h, g_ar.n)

        if em:
            E.dma_start(out=identb[:], in_=identb_in).then_inc(g_id.h, 16)
        g_id.add(16)
        mark("identb", g_id)

        def xload(i):
            st, m = PROD_TILES[i]
            srcten = xkv_in if st == "kv" else xq_in
            if em:
                if i >= 2:
                    wt(E, f"pt_t{i - 2}")
                E.dma_start(out=xstage[i % 2][:],
                            in_=srcten[m * 128:(m + 1) * 128, :]).then_inc(gx[i % 2].h, 16)
            gx[i % 2].add(16)
            mark(f"ldt{i}", gx[i % 2])

        for i in range(4):
            xload(i)
        for d in range(HC):
            if em:
                E.dma_start(out=wkv_bf[0][:, d * D_:(d + 1) * D_],
                            in_=wk_in[d * 128:(d + 1) * 128, :]).then_inc(gw_k.h, 16)
            gw_k.add(16)
        mark("wk_all", gw_k)
        for d in range(HC):
            if em:
                E.dma_start(out=wkv_bf[1][:, d * D_:(d + 1) * D_],
                            in_=wv_in[d * 128:(d + 1) * 128, :]).then_inc(gw_v.h, 16)
            gw_v.add(16)
        mark("wv_all", gw_v)
        for d in range(HC):
            if em:
                E.dma_start(out=wq_bf[:, d * D_:(d + 1) * D_],
                            in_=wq_in[d * 128:(d + 1) * 128, :]).then_inc(gw_q.h, 16)
            gw_q.add(16)
        mark("wq_all", gw_q)

        def wraw_load(k):
            kind, chunk, sub, phase = WU_KIND[k]
            w = _wu_w(k)
            b = k % 4
            if em:
                if k >= 4:
                    wt(E, f"wq_u{k - 4}")
                if kind < 2:
                    ap_src = [wg_in, wu_in][kind][chunk * 128:(chunk + 1) * 128, :]
                else:
                    colo = sum(WD_SUBW[:sub])
                    ap_src = wd_in[chunk * 128:(chunk + 1) * 128, colo:colo + w]
                E.dma_start(out=wraw[b][:, 0:w], in_=ap_src).then_inc(sw[b].h, 16)
            sw[b].add(16)
            mark(f"wl{k}", sw[b])

        def spill_k(j):
            if em:
                wt(E, f"kev{j}")
                E.dma_start(out=ksrc[j].ap(), in_=kstage[:]).then_inc(g_ksp.h, 16)
            g_ksp.add(16)
            mark(f"ksp{j}", g_ksp)
            if em:
                E.wait_ge(g_ksp.h, g_ksp.n)
                E.collective_compute("AllGather", ALU.bypass, replica_groups=RG,
                                     ins=[ksrc[j].ap().opt()],
                                     outs=[kall[j].ap().opt()]).then_inc(g_cc.h, 1)
            g_cc.add(1)
            mark(f"cc_k{j}", g_cc)

        def spill_v(j):
            if em:
                wt(E, f"vev{j}")
                E.dma_start(out=vsrc[j].ap(), in_=vstage[:]).then_inc(g_vsp.h, 16)
            g_vsp.add(16)
            mark(f"vsp{j}", g_vsp)
            if em:
                E.wait_ge(g_vsp.h, g_vsp.n)
                E.collective_compute("AllGather", ALU.bypass, replica_groups=RG,
                                     ins=[vsrc[j].ap().opt()],
                                     outs=[vall[j].ap().opt()]).then_inc(g_cc.h, 1)
            g_cc.add(1)
            mark(f"cc_ch{j}", g_cc)

        for i in range(4, 8):
            xload(i)
        spill_k(0)
        for i in range(8, 12):
            xload(i)
        spill_v(0)
        for i in range(12, 16):
            xload(i)
        spill_k(1)
        spill_v(1)
        for i in range(16, 20):
            xload(i)
        for i in range(20, 24):
            xload(i)
        spill_k(2)
        spill_v(2)
        for i in range(24, 28):
            xload(i)
        for i in range(28, 32):
            xload(i)
        spill_k(3)
        spill_v(3)

        def wspill(k):
            if not _wu_spills(k):
                return
            kind, chunk, sub, phase = WU_KIND[k]
            w = _wu_w(k)
            b = k % 2
            if em:
                wt(E, f"wq_u{k}")
                if kind < 2:
                    dst = wq_sp[kind].ap()[chunk * 128:(chunk + 1) * 128, :]
                else:
                    colo = sum(WD_SUBW[:sub])
                    dst = wq_sp[2].ap()[chunk * 128:(chunk + 1) * 128, colo:colo + w]
                E.dma_start(out=dst, in_=wqbf[b][:, 0:w]).then_inc(g_wsp[b].h, 16)
            g_wsp[b].add(16)
            mark(f"wsp{k}", g_wsp[b])

        kld_done = 0
        ksp_done = 0

        def wq_step(n):
            nonlocal kld_done, ksp_done
            for _ in range(n):
                if kld_done < NWU:
                    wraw_load(kld_done)
                    kld_done += 1
                if kld_done >= 3 and ksp_done < kld_done - 3:
                    wspill(ksp_done)
                    ksp_done += 1

        wq_step(WQ_PROD)
        for v in range(CHUNKS):
            for g in range(SC_NSEG[v]):
                wq_step(WQ_SEG.get((v, g), 0))
                if (v, g) in ((1, 0), (1, 5), (1, 10)):
                    if em:
                        wt(E, f"segdone{v}_{g}")
                        E.collective_compute("AllReduce", ALU.max, replica_groups=RG,
                                             ins=[ar_i[2].ap().opt()],
                                             outs=[ar_o[2].ap().opt()]).then_inc(g_cc.h, 1)
                    g_cc.add(1)
                if g >= SC_S0[v]:
                    while ksp_done < kld_done:
                        wspill(ksp_done)
                        ksp_done += 1
                    s = g
                    if em:
                        wt(E, f"epi_s{s}")
                        E.dma_start(out=ln1_sp.ap()[s * 128:(s + 1) * 128, :],
                                    in_=AT["lnst"][:, (s % 2) * D_:(s % 2 + 1) * D_]).then_inc(g_lnsp[s % 2].h, 16)
                    g_lnsp[s % 2].add(16)
                    mark(f"lnsp_s{s}", g_lnsp[s % 2])
        while ksp_done < NWU:
            wspill(ksp_done)
            ksp_done += 1
        assert ksp_done == NWU and kld_done == NWU
        mark("lnsp_all0", g_lnsp[0])
        mark("lnsp_all1", g_lnsp[1])
        mark("wsp_all0", g_wsp[0])
        mark("wsp_all1", g_wsp[1])
        if em:
            wt(E, "epic1")
            E.dma_start(out=ar_i[0].ap(), in_=mx_x[:]).then_inc(g_ar.h, 16)
        g_ar.add(16)
        if em:
            E.wait_ge(g_ar.h, g_ar.n)
            E.collective_compute("AllReduce", ALU.max, replica_groups=RG,
                                 ins=[ar_i[0].ap().opt()],
                                 outs=[ar_o[0].ap().opt()]).then_inc(g_cc.h, 1)
        g_cc.add(1)
        mark("cc_arx", g_cc)

    def A_pe(E):
        em = E is not None

        def xtile(i):
            if em:
                wt(E, f"ldt{i}")
                if i < 2:
                    wt(E, "identb")
                else:
                    wt(E, f"ptev{i - 2}")
                ins = None
                for d in range(HC):
                    ins = E.transpose(
                        ps_xt[:, (i % 2) * 1024 + d * 128:(i % 2) * 1024 + (d + 1) * 128],
                        xstage[i % 2][:, d * 128:(d + 1) * 128], identb[:])
                ins.then_inc(p_pr.h, 1)
            p_pr.add(1)
            mark(f"pt_t{i}", p_pr)

        for st, idx in PROD_STAGES:
            base = PT_I[(st, 4 * idx)]
            for k in range(4):
                xtile(base + k)
            if st == "kv":
                j = idx
                if em:
                    wt(E, "wk_all")
                    wt(E, f"xfmkv{j}")
                for d in range(HC):
                    if em:
                        if d >= 2:
                            wt(E, f"kevd{j}_{d - 2}")
                        if j == 1 and d < 2:
                            wt(E, ["mrep", "negm"][d])
                        pb = [ps_pa, ps_pb][d % 2]
                        ins = None
                        for hch in range(HC):
                            ins = E.matmul(pb[:, 0:512],
                                           wkv_bf[0][:, hch * D_ + d * 128:hch * D_ + (d + 1) * 128],
                                           xfm[0][:, hch * 512:(hch + 1) * 512],
                                           start=hch == 0, stop=hch == HC - 1)
                        ins.then_inc(p_pr.h, 1)
                    p_pr.add(1)
                    mark(f"kmm{j}_{d}", p_pr)
                if em:
                    wt(E, "wv_all")
                for ksub in range(4):
                    if em:
                        if ksub >= 2:
                            wt(E, f"vevk{j}_{ksub - 2}")
                        vb0 = (ksub % 2) * 1024
                        ins = None
                        for hch in range(HC):
                            for dvh in range(2):
                                ins = E.matmul(
                                    ps_v[:, vb0 + dvh * 512:vb0 + (dvh + 1) * 512],
                                    xfm[0][:, hch * 512 + ksub * 128:hch * 512 + ksub * 128 + 128],
                                    wkv_bf[1][:, hch * D_ + dvh * 512:hch * D_ + (dvh + 1) * 512],
                                    start=hch == 0, stop=hch == HC - 1)
                        ins.then_inc(p_pr.h, 1)
                    p_pr.add(1)
                    mark(f"vmm{j}_{ksub}", p_pr)
            else:
                r = idx
                if em:
                    wt(E, "wq_all")
                    wt(E, f"xfmq{r}")
                for d in range(HC):
                    if em:
                        if d >= 2:
                            wt(E, f"qevd{r}_{d - 2}")
                        pb = [ps_pa, ps_pb][d % 2]
                        ins = None
                        for hch in range(HC):
                            ins = E.matmul(pb[:, 0:512],
                                           wq_bf[:, hch * D_ + d * 128:hch * D_ + (d + 1) * 128],
                                           xfm[1][:, hch * 512:(hch + 1) * 512],
                                           start=hch == 0, stop=hch == HC - 1)
                        ins.then_inc(p_pr.h, 1)
                    p_pr.add(1)
                    mark(f"qmm{r}_{d}", p_pr)
                if r == 0:
                    if em:
                        wt(E, "ksampc")
                        wt(E, "qevd0_7")
                        ins = None
                        for d in range(HC):
                            ins = E.matmul(ps_pa[:, 0:128], ksamp[:, d * 128:(d + 1) * 128],
                                           q_fm[:, d * T_:d * T_ + 128],
                                           start=d == 0, stop=d == HC - 1)
                        ins.then_inc(p_m.h, 1)
                    p_m.add(1)
                    mark("samp_mm", p_m)
                    if em:
                        wt(E, "mrep")
                        wt(E, "cfg")
                        E.transpose(ps_pb[:, 0:128], junk[:, 0:128], identf[:]).then_inc(p_m.h, 1)
                    p_m.add(1)
                    mark("mrep_t", p_m)

        ps_s, ps_ot, ps_r = AT["ps_s"], AT["ps_ot"], AT["ps_r"]

        for v in range(CHUNKS):
            for g in range(SC_NSEG[v]):
                qlo = seg_qlo(v, g)
                subs = seg_subs(v, g)
                for t2 in range(8):
                    t = 8 * g + t2
                    L = LOAD_I[(v, 2 * g + t2 // 4)]
                    buf = L % 3
                    for si2, (off, w) in enumerate(subs):
                        STI = STILE_I[(v, g, t2, si2)]
                        if em:
                            wt(E, f"kgl{L}")
                            if STI == 0:
                                wt(E, "prod_ev")
                                wt(E, "negm")
                            if STI >= 2:
                                E.wait_ge(a_exp.h, STI - 1)
                            slot = ps_s[:, (STI % 2) * 512:(STI % 2) * 512 + w]
                            ins = None
                            for d in range(HC):
                                ins = E.matmul(
                                    slot,
                                    AT["kbuf"][buf][:, d * 512 + (t % 4) * 128:d * 512 + (t % 4) * 128 + 128],
                                    q_fm[:, d * T_ + 128 * qlo + off:d * T_ + 128 * qlo + off + w],
                                    start=d == 0, stop=d == HC - 1)
                            ins.then_inc(p_s.h, 1)
                        p_s.add(1)
                        assert p_s.n == STI + 1
                for si in range(seg_ns(v, g)):
                    SW = SWEEP_I[(v, g, si)]
                    if em:
                        if SW == 0:
                            wt(E, "prod_ev")
                        if si == 0:
                            wt(E, f"vgl{LOAD_I[(v, 2 * g)]}")
                            wt(E, f"vgl{LOAD_I[(v, 2 * g + 1)]}")
                        E.wait_ge(a_exp.h, STILE_I[(v, g, 7, len(subs) - 1)] + 1)
                        if g >= SC_S0[v]:
                            wt(E, f"mseg{v}_{g}")
                        if SW >= 2:
                            E.wait_ge(d_oa.h, SW - 1)
                        par = SW % 2
                        ins = None
                        for t2 in range(8):
                            buf = LOAD_I[(v, 2 * g + t2 // 4)] % 3
                            lh = AT["pseg"][:, t2 * 1024 + si * 128:t2 * 1024 + si * 128 + 128]
                            for dvh in range(2):
                                ins = E.matmul(
                                    ps_ot[:, par * 1024 + dvh * 512:par * 1024 + (dvh + 1) * 512],
                                    lh,
                                    AT["vbuf"][buf][:, (8 * g + t2) % 4 * 1024 + dvh * 512:(8 * g + t2) % 4 * 1024 + (dvh + 1) * 512],
                                    start=t2 == 0, stop=t2 == 7)
                            ins = E.matmul(ps_r[:, par * 512:par * 512 + 1], lh, ones_bf[:],
                                           start=t2 == 0, stop=t2 == 7)
                        ins.then_inc(p_pv.h, 1)
                    p_pv.add(1)
                    assert p_pv.n == SW + 1
                mark(f"segdone{v}_{g}", p_pv)

    def A_act(E):
        em = E is not None

        def evac(waits, out_fn, in_fn, name):
            if em:
                for w in waits:
                    wt(E, w)
                E.activation(out=out_fn(), in_=in_fn(), func=AF.Copy).then_inc(a_ev.h, 1)
            a_ev.add(1)
            mark(name, a_ev)

        for st, idx in PROD_STAGES:
            base = PT_I[(st, 4 * idx)]
            buf = 0 if st == "kv" else 1
            for k in range(4):
                i = base + k
                extra = []
                if k == 0:
                    if st == "kv" and idx >= 1:
                        extra = [f"vmm{idx - 1}_3"]
                    if st == "q" and idx >= 1:
                        extra = [f"qmm{idx - 1}_7"]
                evac([f"pt_t{i}"] + extra,
                     lambda buf=buf, k=k: xfm[buf][:].rearrange(
                         "p (h s k) -> p h s k", h=HC, s=4)[:, :, k, :],
                     lambda i=i: ps_xt[:, (i % 2) * 1024:(i % 2) * 1024 + 1024].rearrange(
                         "p (h k) -> p h k", h=HC),
                     f"ptev{i}")
            if st == "kv":
                j = idx
                mark(f"xfmkv{j}", a_ev)
                for d in range(HC):
                    extra = []
                    if d == 0 and j >= 1:
                        extra = [f"ksp{j - 1}"]
                        if j == 1:
                            extra.append("ksampc")
                    evac([f"kmm{j}_{d}"] + extra,
                         lambda d=d: kstage[:, d * 512:(d + 1) * 512],
                         lambda d=d: [ps_pa, ps_pb][d % 2][:, 0:512],
                         f"kevd{j}_{d}")
                mark(f"kev{j}", a_ev)
                for ksub in range(4):
                    extra = [f"vsp{j - 1}"] if (ksub == 0 and j >= 1) else []
                    evac([f"vmm{j}_{ksub}"] + extra,
                         lambda k=ksub: vstage[:, k * 1024:(k + 1) * 1024],
                         lambda k=ksub: ps_v[:, (k % 2) * 1024:(k % 2) * 1024 + 1024],
                         f"vevk{j}_{ksub}")
                mark(f"vev{j}", a_ev)
            else:
                r = idx
                mark(f"xfmq{r}", a_ev)
                for d in range(HC):
                    evac([f"qmm{r}_{d}"],
                         lambda r=r, d=d: q_fm[:, d * T_ + r * 512:d * T_ + (r + 1) * 512],
                         lambda d=d: [ps_pa, ps_pb][d % 2][:, 0:512],
                         f"qevd{r}_{d}")
        mark("prod_ev", a_ev)

        ps_s = AT["ps_s"]
        for v in range(CHUNKS):
            for g in range(SC_NSEG[v]):
                subs = seg_subs(v, g)
                SI = SEG_I[(v, g)]
                for gi2 in (2 * g, 2 * g + 1):
                    L = LOAD_I[(v, gi2)]
                    j = gi2 // 8
                    r = gi2 % 8
                    if em:
                        wt(E, f"cc_ch{j}")
                        if L < 3:
                            wt(E, "qmm3_7")
                            wt(E, "prod_ev")
                            wt(E, "ksp3")
                            wt(E, "vsp3")
                        else:
                            vp, gp = LOAD_LIST[L - 3]
                            wt(E, f"segdone{vp}_{gp // 2}")
                        E.dma_start(out=AT["vbuf"][L % 3][:],
                                    in_=vall[j].ap()[r * 128:(r + 1) * 128, :]).then_inc(vg[L % 3].h, 16)
                    vg[L % 3].add(16)
                    mark(f"vgl{L}", vg[L % 3])
                if g >= SC_S0[v]:
                    s = g
                    if em:
                        if s >= 2:
                            wt(E, f"vstt{s - 2}")
                        E.dma_start(out=xqr[s % 2][:],
                                    in_=xq_in[s * 128:(s + 1) * 128, :]).then_inc(sxq[s % 2].h, 16)
                    sxq[s % 2].add(16)
                    mark(f"xq{s}", sxq[s % 2])
                for t2 in range(8):
                    for si2, (off, w) in enumerate(subs):
                        STI = STILE_I[(v, g, t2, si2)]
                        if em:
                            E.wait_ge(p_s.h, STI + 1)
                            if STI == 0:
                                wt(E, "negm")
                            if SI >= 1:
                                vp, gp = SEG_LIST[SI - 1]
                                wt(E, f"segdone{vp}_{gp}")
                            E.activation(out=AT["pseg"][:, t2 * 1024 + off:t2 * 1024 + off + w],
                                         in_=ps_s[:, (STI % 2) * 512:(STI % 2) * 512 + w],
                                         func=AF.Exp, bias=negm[:],
                                         scale=1.0 / 32.0).then_inc(a_exp.h, 1)
                        a_exp.add(1)
                if g >= SC_S0[v]:
                    si = g - SC_S0[v]
                    s = g
                    if em:
                        wt(E, f"var_s{s}")
                        E.activation(out=sgl[:, 24:25], in_=stat[:, 8 + si:9 + si],
                                     func=AF.Ln, bias=eps_sb[:], scale=1.0)
                        E.drain()
                        E.activation(out=stat[:, 16 + si:17 + si], in_=sgl[:, 24:25],
                                     func=AF.Exp, bias=0.0, scale=-0.5).then_inc(a_m.h, 1)
                    a_m.add(1)
                    mark(f"rstd_s{s}", a_m)

    def A_dve(E):
        em = E is not None
        if em:
            E.memset(zeros[:], 0.0)
            E.memset(ones_bf[:], 1.0)
            E.memset(mx_x[:], 0.0)
            E.memset(mx_h[:], 0.0)
            E.memset(eps_sb[:], EPS)
            E.engine_nop().then_inc(d_m.h, 1)
        d_m.add(1)
        mark("dve_init", d_m)

        if em:
            wt(E, "kev0")
            E.tensor_copy(
                ksamp[:].rearrange("p (d k) -> p d k", d=HC),
                kstage[:].rearrange("p (d k) -> p d k", d=HC)[:, :, 0:128],
            ).then_inc(d_m.h, 1)
        d_m.add(1)
        mark("ksampc", d_m)

        def wq_unit(k):
            kind, chunk, sub, phase = WU_KIND[k]
            w = _wu_w(k)
            b = k % 2
            if em:
                wt(E, f"wl{k}")
                buf = wraw[k % 4]
                if kind < 2:
                    c0, c1 = 8 + b * 2, 9 + b * 2
                    kp = _prev_spill(k)
                    if kp is not None:
                        wt(E, f"wsp{kp}")
                    E.tensor_reduce(sgl[:, c0:c0 + 1], buf[:, 0:w], AX.X, ALU.max,
                                    apply_absolute_value=True)
                    E.drain()
                    E.reciprocal(sgl[:, c1:c1 + 1], sgl[:, c0:c0 + 1])
                    E.drain()
                    E.tensor_scalar(out=sgl[:, c1:c1 + 1], in0=sgl[:, c1:c1 + 1],
                                    scalar1=127.0, scalar2=None, op0=ALU.mult)
                    E.tensor_scalar(out=sgl[:, c0:c0 + 1], in0=sgl[:, c0:c0 + 1],
                                    scalar1=1.0 / 127.0, scalar2=None, op0=ALU.mult)
                    E.drain()
                    E.tensor_scalar(out=wtmp[:, 0:w], in0=buf[:, 0:w],
                                    scalar1=sgl[:, c1:c1 + 1], scalar2=127.49,
                                    op0=ALU.mult, op1=ALU.min)
                    E.drain()
                    E.tensor_scalar(out=wtmpi[:, 0:w], in0=wtmp[:, 0:w],
                                    scalar1=-128.49, scalar2=None, op0=ALU.max)
                    E.drain()
                    E.tensor_scalar(out=wqbf[b][:, 0:w], in0=wtmpi[:, 0:w],
                                    scalar1=sgl[:, c0:c0 + 1], scalar2=None,
                                    op0=ALU.mult).then_inc(d_q.h, 1)
                elif phase == 0:
                    E.tensor_reduce(sgl[:, 13:14], buf[:, 0:w], AX.X, ALU.max,
                                    apply_absolute_value=True)
                    E.drain()
                    if sub == 0:
                        E.tensor_copy(sgl[:, 12:13], sgl[:, 13:14])
                    else:
                        E.tensor_max(sgl[:, 12:13], sgl[:, 12:13], sgl[:, 13:14])
                    E.drain()
                    if sub == 2:
                        E.reciprocal(sgl[:, 14:15], sgl[:, 12:13])
                        E.drain()
                        E.tensor_scalar(out=sgl[:, 14:15], in0=sgl[:, 14:15],
                                        scalar1=127.0, scalar2=None, op0=ALU.mult)
                        E.tensor_scalar(out=sgl[:, 12:13], in0=sgl[:, 12:13],
                                        scalar1=1.0 / 127.0, scalar2=None, op0=ALU.mult)
                        E.drain()
                    E.engine_nop().then_inc(d_q.h, 1)
                else:
                    kp = _prev_spill(k)
                    if kp is not None:
                        wt(E, f"wsp{kp}")
                    E.tensor_scalar(out=wtmp[:, 0:w], in0=buf[:, 0:w],
                                    scalar1=sgl[:, 14:15], scalar2=127.49,
                                    op0=ALU.mult, op1=ALU.min)
                    E.drain()
                    E.tensor_scalar(out=wtmpi[:, 0:w], in0=wtmp[:, 0:w],
                                    scalar1=-128.49, scalar2=None, op0=ALU.max)
                    E.drain()
                    E.tensor_scalar(out=wqbf[b][:, 0:w], in0=wtmpi[:, 0:w],
                                    scalar1=sgl[:, 12:13], scalar2=None,
                                    op0=ALU.mult).then_inc(d_q.h, 1)
            d_q.add(1)
            mark(f"wq_u{k}", d_q)

        kdone = 0
        for k in range(WQ_PROD):
            wq_unit(k)
            kdone += 1

        if em:
            wt(E, "samp_mm")
            E.tensor_reduce(sgl[:, 0:1], ps_pa[:, 0:128], AX.X, ALU.max)
            E.drain()
            E.tensor_scalar(out=junk[:, 0:128], in0=zeros[:],
                            scalar1=sgl[:, 0:1], scalar2=None, op0=ALU.add)
            E.engine_nop().then_inc(d_m.h, 1)
        d_m.add(1)
        mark("mrep", d_m)
        if em:
            wt(E, "mrep_t")
            E.tensor_reduce(sgl[:, 1:2], ps_pb[:, 0:128], AX.X, ALU.max)
            E.drain()
            E.tensor_scalar(out=negm[:], in0=sgl[:, 1:2], scalar1=-1.0 / 32.0,
                            scalar2=-2.0, op0=ALU.mult, op1=ALU.add).then_inc(d_m.h, 1)
        d_m.add(1)
        mark("negm", d_m)

        ps_ot, ps_r = AT.get("ps_ot"), AT.get("ps_r")
        for v in range(CHUNKS):
            for g in range(SC_NSEG[v]):
                qlo = seg_qlo(v, g)
                subs = seg_subs(v, g)
                if g >= SC_S0[v]:
                    for t2 in range(8):
                        STI = STILE_I[(v, g, t2, 0)]
                        if em:
                            if d_mk.n == 0:
                                wt(E, "cfg")
                            E.wait_ge(a_exp.h, STI + 1)
                            E.tensor_tensor(AT["pseg"][:, t2 * 1024:t2 * 1024 + 128],
                                            AT["pseg"][:, t2 * 1024:t2 * 1024 + 128],
                                            masks[:, t2 * 128:(t2 + 1) * 128],
                                            ALU.mult).then_inc(d_mk.h, 1)
                        d_mk.add(1)
                    mark(f"mseg{v}_{g}", d_mk)
                for si in range(seg_ns(v, g)):
                    SW = SWEEP_I[(v, g, si)]
                    par = SW % 2
                    osl = qlo + si - SC_S0[v]
                    first = g == 0
                    if em:
                        E.wait_ge(p_pv.h, SW + 1)
                        if first:
                            E.tensor_copy(AT["osb"][:, osl * D_:(osl + 1) * D_],
                                          ps_ot[:, par * 1024:par * 1024 + 1024])
                            E.drain()
                            E.tensor_copy(AT["rsb"][:, osl:osl + 1],
                                          ps_r[:, par * 512:par * 512 + 1])
                            E.drain()
                        else:
                            E.tensor_add(AT["osb"][:, osl * D_:(osl + 1) * D_],
                                         AT["osb"][:, osl * D_:(osl + 1) * D_],
                                         ps_ot[:, par * 1024:par * 1024 + 1024])
                            E.drain()
                            E.tensor_add(AT["rsb"][:, osl:osl + 1],
                                         AT["rsb"][:, osl:osl + 1],
                                         ps_r[:, par * 512:par * 512 + 1])
                            E.drain()
                        E.engine_nop().then_inc(d_oa.h, 1)
                    d_oa.add(1)
                    assert d_oa.n == SW + 1
                for _ in range(WQ_SEG.get((v, g), 0)):
                    if kdone < NWU:
                        wq_unit(kdone)
                        kdone += 1
                if g >= SC_S0[v]:
                    si = g - SC_S0[v]
                    s = g
                    if em:
                        wt(E, f"xq{s}")
                        if s >= 2:
                            wt(E, f"lnsp_s{s - 2}")
                        E.reciprocal(sgl[:, 2:3], AT["rsb"][:, si:si + 1])
                        E.drain()
                        E.scalar_tensor_tensor(
                            out=AT["resid"][:], in0=AT["osb"][:, si * D_:(si + 1) * D_],
                            scalar=sgl[:, 2:3], in1=xqr[s % 2][:],
                            op0=ALU.mult, op1=ALU.add, accum_out=stat[:, 24:25])
                        E.engine_nop().then_inc(d_ep.h, 1)
                    d_ep.add(1)
                    mark(f"vstt{s}", d_ep)
                    if em:
                        E.drain()
                        E.scalar_tensor_tensor(
                            out=junk[:, 0:D_], in0=AT["resid"][:], scalar=1.0,
                            in1=AT["resid"][:], op0=ALU.mult, op1=ALU.mult,
                            accum_out=stat[:, 25:26])
                        E.drain()
                        E.tensor_scalar(out=stat[:, si:si + 1], in0=stat[:, 24:25],
                                        scalar1=1.0 / D_, scalar2=None, op0=ALU.mult)
                        E.drain()
                        E.tensor_tensor(stat[:, 26:27], stat[:, si:si + 1],
                                        stat[:, si:si + 1], ALU.mult)
                        E.tensor_scalar(out=stat[:, 8 + si:9 + si], in0=stat[:, 25:26],
                                        scalar1=1.0 / D_, scalar2=None, op0=ALU.mult)
                        E.drain()
                        E.tensor_sub(stat[:, 8 + si:9 + si], stat[:, 8 + si:9 + si],
                                     stat[:, 26:27])
                        E.engine_nop().then_inc(d_ep.h, 1)
                    d_ep.add(1)
                    mark(f"var_s{s}", d_ep)
                    if em:
                        wt(E, f"rstd_s{s}")
                        E.tensor_scalar(out=AT["lnst"][:, (s % 2) * D_:(s % 2 + 1) * D_],
                                        in0=AT["resid"][:],
                                        scalar1=stat[:, si:si + 1],
                                        scalar2=stat[:, 16 + si:17 + si],
                                        op0=ALU.subtract, op1=ALU.mult)
                        E.drain()
                        E.tensor_reduce(stat[:, 27:28],
                                        AT["lnst"][:, (s % 2) * D_:(s % 2 + 1) * D_],
                                        AX.X, ALU.max, apply_absolute_value=True)
                        E.drain()
                        E.tensor_tensor(mx_x[:], mx_x[:], stat[:, 27:28], ALU.max)
                        E.engine_nop().then_inc(d_ep.h, 1)
                    d_ep.add(1)
                    mark(f"epi_s{s}", d_ep)
            mark(f"epic{v}", d_ep)
        assert kdone == NWU

    def A_sync(E):
        em = E is not None
        if em:
            E.dma_start(out=identf[:], in_=identf_in).then_inc(s_cfg.h, 16)
            E.dma_start(out=clipx[:], in_=clipx_in).then_inc(s_cfg.h, 16)
            E.dma_start(out=clipd[:], in_=clipd_in).then_inc(s_cfg.h, 16)
            for tt in range(8):
                E.dma_start(out=masks[:, tt * 128:(tt + 1) * 128],
                            in_=masks_in[tt * 128:(tt + 1) * 128, :]).then_inc(s_cfg.h, 16)
        s_cfg.add(16 * 11)
        mark("cfg", s_cfg)


        for v in range(CHUNKS):
            for g in range(SC_NSEG[v]):
                for gi2 in (2 * g, 2 * g + 1):
                    L = LOAD_I[(v, gi2)]
                    j = gi2 // 8
                    r = gi2 % 8
                    if em:
                        wt(E, f"cc_ch{j}")
                        if L < 3:
                            wt(E, "qmm3_7")
                            wt(E, "prod_ev")
                            wt(E, "ksp3")
                            wt(E, "vsp3")
                        else:
                            vp, gp = LOAD_LIST[L - 3]
                            wt(E, f"segdone{vp}_{gp // 2}")
                        E.dma_start(out=AT["kbuf"][L % 3][:],
                                    in_=kall[j].ap()[r * 128:(r + 1) * 128, :]).then_inc(kg[L % 3].h, 16)
                    kg[L % 3].add(16)
                    mark(f"kgl{L}", kg[L % 3])

    def B_sync(E):
        em = E is not None

        def lnrd_load(n):
            if em:
                if n == 0:
                    wt(E, "lnsp_all0")
                    wt(E, "lnsp_all1")
                if n >= 4:
                    wt(E, f"qx{n - 4}q")
                E.dma_start(out=BT["lnrd"][n % 4][:],
                            in_=ln1_sp.ap()[n * 128:(n + 1) * 128, :]).then_inc(s_lnr[n % 4].h, 16)
            s_lnr[n % 4].add(16)
            mark(f"lnrd{n}", s_lnr[n % 4])

        for n in range(4):
            lnrd_load(n)
        if em:
            wt(E, "cc_arx")
            E.dma_start(out=arx_sb[:], in_=ar_o[0].ap()).then_inc(s_arx.h, 16)
        s_arx.add(16)
        mark("arx_ld", s_arx)
        for n in range(4, NQT):
            lnrd_load(n)

    def B_pe(E):
        em = E is not None
        if em:
            wt(E, "bx_rep")
            E.transpose(BT["ps_t"][:, 0:128], junk[:, 0:128], identf[:]).then_inc(p_bc.h, 1)
        p_bc.add(1)
        mark("bx_tr", p_bc)
        for n in range(NQT):
            if em:
                wt(E, f"qx{n}q")
                if n >= 2:
                    wt(E, f"qxev{n - 2}")
                ins = None
                for d in range(HC):
                    ins = E.transpose(
                        BT["ps_xtb"][:, (n % 2) * 1024 + d * 128:(n % 2) * 1024 + (d + 1) * 128],
                        BT["qxst"][n % 2][:, d * 128:(d + 1) * 128], identb[:])
                ins.then_inc(p_bc.h, 1)
            p_bc.add(1)
            mark(f"qxt{n}", p_bc)
        nu = 0
        for icx in range(IC):
            for tc in range(4):
                par = nu % 2
                if em:
                    if nu == 0:
                        wt(E, "wguT")
                    if icx == 0:
                        wt(E, f"qxev{4 * tc + 3}")
                    if nu >= 2:
                        wt(E, f"hdone{nu - 2}")
                    ins = None
                    for d in range(HC):
                        ins = E.matmul(BT["ps_g"][:, par * 512:par * 512 + 512],
                                       BT["wg_T"][:, d * DI_ + icx * 128:d * DI_ + (icx + 1) * 128],
                                       BT["qx_fm"][:, d * T_ + tc * 512:d * T_ + (tc + 1) * 512],
                                       start=d == 0, stop=d == HC - 1)
                    ins.then_inc(p_bc.h, 1)
                p_bc.add(1)
                mark(f"g_mm{nu}", p_bc)
                if em:
                    ins = None
                    for d in range(HC):
                        ins = E.matmul(BT["ps_u"][:, par * 512:par * 512 + 512],
                                       BT["wu_T"][:, d * DI_ + icx * 128:d * DI_ + (icx + 1) * 128],
                                       BT["qx_fm"][:, d * T_ + tc * 512:d * T_ + (tc + 1) * 512],
                                       start=d == 0, stop=d == HC - 1)
                    ins.then_inc(p_bc.h, 1)
                p_bc.add(1)
                mark(f"u_mm{nu}", p_bc)
                nu += 1

    def B_act(E):
        em = E is not None
        for d in range(HC):
            if em:
                if d == 0:
                    wt(E, "wsp_all0")
                    wt(E, "wsp_all1")
                E.dma_start(out=BT["wg_T"][:, d * DI_:(d + 1) * DI_],
                            in_=wq_sp[0].ap()[:, d * 128:(d + 1) * 128],
                            transpose=True).then_inc(s_wgu.h, 16)
            s_wgu.add(16)
        for d in range(HC):
            if em:
                E.dma_start(out=BT["wu_T"][:, d * DI_:(d + 1) * DI_],
                            in_=wq_sp[1].ap()[:, d * 128:(d + 1) * 128],
                            transpose=True).then_inc(s_wgu.h, 16)
            s_wgu.add(16)
        mark("wguT", s_wgu)
        if em:
            wt(E, "bx_tr")
            wt(E, "cfg")
            E.activation(out=sgl[:, 16:17], in_=clipx[:], func=AF.Sigmoid).then_inc(a_m.h, 1)
        a_m.add(1)
        mark("sigx", a_m)
        for n in range(NQT):
            if em:
                wt(E, f"qxt{n}")
                E.activation(
                    out=BT["qx_fm"][:].rearrange("p (h s k) -> p h s k", h=HC, s=NQT)[:, :, n, :],
                    in_=BT["ps_xtb"][:, (n % 2) * 1024:(n % 2) * 1024 + 1024].rearrange(
                        "p (h k) -> p h k", h=HC),
                    func=AF.Copy).then_inc(a_m.h, 1)
            a_m.add(1)
            mark(f"qxev{n}", a_m)
        for nu in range(4 * IC):
            par = nu % 2
            if em:
                wt(E, f"g_mm{nu}")
                if nu >= 2:
                    wt(E, f"hdone{nu - 2}")
                E.activation(out=BT["silu"][par][:],
                             in_=BT["ps_g"][:, par * 512:par * 512 + 512],
                             func=AF.Silu).then_inc(a_m.h, 1)
            a_m.add(1)
            mark(f"silu{nu}", a_m)

    def B_dve(E):
        em = E is not None
        if em:
            wt(E, "arx_ld")
            E.tensor_scalar(out=junk[:, 0:128], in0=zeros[:], scalar1=arx_sb[:],
                            scalar2=None, op0=ALU.add)
            E.engine_nop().then_inc(d_bc.h, 1)
        d_bc.add(1)
        mark("bx_rep", d_bc)
        if em:
            wt(E, "bx_tr")
            E.tensor_reduce(sgl[:, 17:18], BT["ps_t"][:, 0:128], AX.X, ALU.max)
            wt(E, "sigx")
            E.drain()
            E.tensor_tensor(sgl[:, 18:19], sgl[:, 16:17], sgl[:, 17:18], ALU.mult)
            E.drain()
            E.tensor_scalar(out=sgl[:, 18:19], in0=sgl[:, 18:19],
                            scalar1=1.0 / 127.0, scalar2=None, op0=ALU.mult)
            E.drain()
            E.reciprocal(sgl[:, 19:20], sgl[:, 18:19])
            E.engine_nop().then_inc(d_bc.h, 1)
        d_bc.add(1)
        mark("sx", d_bc)

        for n in range(NQT):
            if em:
                if n == 0:
                    wt(E, "sx")
                wt(E, f"lnrd{n}")
                if n >= 2:
                    wt(E, f"qxt{n - 2}")
                E.tensor_scalar(out=junk[:, 0:D_], in0=BT["lnrd"][n % 4][:],
                                scalar1=sgl[:, 19:20], scalar2=127.49,
                                op0=ALU.mult, op1=ALU.min)
                E.drain()
                E.tensor_scalar(out=wtmpi[:, 0:D_], in0=junk[:, 0:D_],
                                scalar1=-128.49, scalar2=None, op0=ALU.max)
                E.drain()
                E.tensor_scalar(out=BT["qxst"][n % 2][:], in0=wtmpi[:, 0:D_],
                                scalar1=sgl[:, 18:19], scalar2=None,
                                op0=ALU.mult).then_inc(d_bc.h, 1)
            d_bc.add(1)
            mark(f"qx{n}q", d_bc)

        for nu in range(4 * IC):
            par = nu % 2
            if em:
                wt(E, f"silu{nu}")
                wt(E, f"u_mm{nu}")
                if nu >= 2:
                    wt(E, f"hsp{nu - 2}")
                E.tensor_tensor(BT["hst"][par][:], BT["silu"][par][:],
                                BT["ps_u"][:, par * 512:par * 512 + 512], ALU.mult)
                E.drain()
                E.tensor_reduce(stat[:, 14:15], BT["hst"][par][:], AX.X, ALU.max,
                                apply_absolute_value=True)
                E.drain()
                E.tensor_tensor(mx_h[:], mx_h[:], stat[:, 14:15], ALU.max)
                E.engine_nop().then_inc(d_bc.h, 1)
            d_bc.add(1)
            mark(f"hdone{nu}", d_bc)

    def B_gpsimd(E):
        em = E is not None
        for nu in range(4 * IC):
            icx, tc = nu // 4, nu % 4
            par = nu % 2
            if em:
                wt(E, f"hdone{nu}")
                E.dma_start(out=h_spd.ap()[:, icx * T_ + tc * 512:icx * T_ + (tc + 1) * 512],
                            in_=BT["hst"][par][:]).then_inc(g_hsp[par].h, 16)
            g_hsp[par].add(16)
            mark(f"hsp{nu}", g_hsp[par])
        mark("hsp_all0", g_hsp[0])
        mark("hsp_all1", g_hsp[1])
        if em:
            wt(E, f"hdone{60}")
            E.collective_compute("AllReduce", ALU.max, replica_groups=RG,
                                 ins=[ar_i[2].ap().opt()],
                                 outs=[ar_o[2].ap().opt()]).then_inc(g_cc.h, 1)
        g_cc.add(1)
        if em:
            wt(E, f"hdone{4 * IC - 1}")
            E.dma_start(out=ar_i[1].ap(), in_=mx_h[:]).then_inc(g_ar.h, 16)
        g_ar.add(16)
        if em:
            E.wait_ge(g_ar.h, g_ar.n)
            E.collective_compute("AllReduce", ALU.max, replica_groups=RG,
                                 ins=[ar_i[1].ap().opt()],
                                 outs=[ar_o[1].ap().opt()]).then_inc(g_cc.h, 1)
        g_cc.add(1)
        mark("cc_arh", g_cc)

    def C_sync(E):
        em = E is not None

        def hld_load(n, EE):
            if EE is not None:
                if n < 2:
                    wt(EE, "hsp_all0")
                    wt(EE, "hsp_all1")
                else:
                    wt(EE, f"hq{n - 2}")
                EE.dma_start(
                    out=CT["hbuf"][n % 2][:].rearrange("p (i k) -> p i k", i=IC),
                    in_=h_spd.ap()[:, :].rearrange("p (i t) -> p i t", i=IC)[:, :, n * 128:(n + 1) * 128],
                ).then_inc(s_hld[n % 2].h, 16)
            s_hld[n % 2].add(16)
            mark(f"hld{n}", s_hld[n % 2])

        def lnr_load(m, EE):
            if EE is not None:
                if m == 0:
                    wt(EE, "lnsp_all0")
                    wt(EE, "lnsp_all1")
                if m >= 4:
                    wt(EE, f"r2d{m - 4}")
                EE.dma_start(out=CT["lnr"][m % 4][:],
                             in_=ln1_sp.ap()[m * 128:(m + 1) * 128, :]).then_inc(s_clnr[m % 4].h, 16)
            s_clnr[m % 4].add(16)
            mark(f"lnr{m}", s_clnr[m % 4])

        C_sync.hld_load = hld_load
        C_sync.lnr_load = lnr_load
        hld_load(0, E)
        if em:
            wt(E, "cc_arh")
            E.dma_start(out=arh_sb[:], in_=ar_o[1].ap()).then_inc(s_arh.h, 16)
        s_arh.add(16)
        mark("arh_ld", s_arh)
        for n in range(1, NQT):
            if n % 2 == 0:
                hld_load(n, E)
            else:
                lnr_load(n, E)
        if em:
            E.wait_ge(s_out[0].h, 16 * NQT // 2)
            E.wait_ge(s_out[1].h, 16 * NQT // 2)

    def C_pe(E):
        em = E is not None
        if em:
            wt(E, "ch_rep")
            E.transpose(CT["ps_ct"][:, 0:128], junk[:, 0:128], identf[:]).then_inc(p_bc.h, 1)
        p_bc.add(1)
        mark("ch_tr", p_bc)
        for n in range(NQT):
            par = n % 2
            if em:
                if n == 0:
                    wt(E, "qwdT")
                wt(E, f"hq{n}")
                if n >= 2:
                    wt(E, f"r2d{n - 2}")
                ins = None
                for icx in range(IC):
                    for hoh in range(2):
                        ins = E.matmul(
                            CT["ps_d"][:, par * 1024 + hoh * 512:par * 1024 + (hoh + 1) * 512],
                            CT["hq"][par][:, icx * 128:(icx + 1) * 128],
                            CT["qwd_T"][:, icx * D_ + hoh * 512:icx * D_ + (hoh + 1) * 512],
                            start=icx == 0, stop=icx == IC - 1)
                ins.then_inc(p_bc.h, 1)
            p_bc.add(1)
            mark(f"dn{n}", p_bc)

    def C_act(E):
        em = E is not None
        for icx in range(IC):
            if em:
                if icx == 0:
                    wt(E, "wsp_all0")
                    wt(E, "wsp_all1")
                E.dma_start(out=CT["qwd_T"][:, icx * D_:(icx + 1) * D_],
                            in_=wq_sp[2].ap()[:, icx * 128:(icx + 1) * 128],
                            transpose=True).then_inc(s_qwd.h, 16)
            s_qwd.add(16)
        mark("qwdT", s_qwd)
        if em:
            wt(E, "ch_tr")
            E.activation(out=sgl[:, 20:21], in_=clipd[:], func=AF.Sigmoid).then_inc(a_m.h, 1)
        a_m.add(1)
        mark("sigd", a_m)

        def act_load(m, EE):
            if m % 2 == 1:
                C_sync.hld_load(m, EE)
            else:
                C_sync.lnr_load(m, EE)

        act_load(0, E)
        act_load(1, E)
        for n in range(NQT + 3):
            if n + 2 < NQT:
                act_load(n + 2, E)
            if 3 <= n <= NQT + 2:
                m = n - 3
                p = m % 2
                if em:
                    wt(E, f"bias{m}")
                    if m >= 2:
                        wt(E, f"out{m - 2}")
                    E.drain()
                    E.activation(out=CT["ot"][p][:], in_=CT["r2"][p][:],
                                 func=AF.Identity, bias=stat[:, 14 + p:15 + p],
                                 scale=stat[:, 6 + p:7 + p]).then_inc(a_m.h, 1)
                a_m.add(1)
                mark(f"cote{m}", a_m)
            if 1 <= n <= NQT:
                m = n - 1
                p = m % 2
                if em:
                    wt(E, f"r2d{m}")
                    E.drain()
                    E.activation(out=junk[:, 0:D_], in_=CT["r2"][p][:],
                                 func=AF.Square,
                                 accum_out=stat[:, 10 + p:11 + p]).then_inc(a_m.h, 1)
                a_m.add(1)
                mark(f"sqa{m}", a_m)
            if 2 <= n <= NQT + 1:
                m = n - 2
                p = m % 2
                if em:
                    wt(E, f"vard{m}")
                    E.activation(out=stat[:, 4 + p:5 + p], in_=stat[:, 2 + p:3 + p],
                                 func=AF.Ln, bias=eps_sb[:], scale=1.0)
                    E.drain()
                    E.activation(out=stat[:, 6 + p:7 + p], in_=stat[:, 4 + p:5 + p],
                                 func=AF.Exp, bias=0.0, scale=-0.5).then_inc(a_m.h, 1)
                a_m.add(1)
                mark(f"crstd{m}", a_m)

    def C_dve(E):
        em = E is not None
        if em:
            wt(E, "arh_ld")
            E.tensor_scalar(out=junk[:, 0:128], in0=zeros[:], scalar1=arh_sb[:],
                            scalar2=None, op0=ALU.add)
            E.engine_nop().then_inc(d_bc.h, 1)
        d_bc.add(1)
        mark("ch_rep", d_bc)
        if em:
            wt(E, "ch_tr")
            E.tensor_reduce(sgl[:, 21:22], CT["ps_ct"][:, 0:128], AX.X, ALU.max)
            wt(E, "sigd")
            E.drain()
            E.tensor_tensor(sgl[:, 22:23], sgl[:, 20:21], sgl[:, 21:22], ALU.mult)
            E.drain()
            E.tensor_scalar(out=sgl[:, 22:23], in0=sgl[:, 22:23],
                            scalar1=1.0 / 127.0, scalar2=None, op0=ALU.mult)
            E.drain()
            E.reciprocal(sgl[:, 23:24], sgl[:, 22:23])
            E.engine_nop().then_inc(d_bc.h, 1)
        d_bc.add(1)
        mark("sh", d_bc)

        for n in range(NQT + 3):
            if n < NQT:
                par = n % 2
                if em:
                    if n == 0:
                        wt(E, "sh")
                    wt(E, f"hld{n}")
                    E.tensor_scalar(out=CT["ctmp"][:], in0=CT["hbuf"][par][:],
                                    scalar1=sgl[:, 23:24], scalar2=127.49,
                                    op0=ALU.mult, op1=ALU.min)
                    E.drain()
                    E.tensor_scalar(out=CT["ctmpi"][:], in0=CT["ctmp"][:],
                                    scalar1=-128.49, scalar2=None, op0=ALU.max)
                    E.drain()
                    E.tensor_scalar(out=CT["hq"][par][:], in0=CT["ctmpi"][:],
                                    scalar1=sgl[:, 22:23], scalar2=None,
                                    op0=ALU.mult).then_inc(d_bc.h, 1)
                d_bc.add(1)
                mark(f"hq{n}", d_bc)
            if 3 <= n <= NQT + 2:
                m = n - 3
                p = m % 2
                if em:
                    wt(E, f"crstd{m}")
                    E.tensor_scalar(out=stat[:, 14 + p:15 + p], in0=stat[:, 0 + p:1 + p],
                                    scalar1=stat[:, 6 + p:7 + p], scalar2=-1.0,
                                    op0=ALU.mult, op1=ALU.mult).then_inc(d_bc.h, 1)
                d_bc.add(1)
                mark(f"bias{m}", d_bc)
            if 1 <= n <= NQT:
                m = n - 1
                p = m % 2
                if em:
                    wt(E, f"dn{m}")
                    wt(E, f"lnr{m}")
                    if m >= 2:
                        wt(E, f"cote{m - 2}")
                    E.scalar_tensor_tensor(
                        out=CT["r2"][p][:], in0=CT["ps_d"][:, p * 1024:p * 1024 + 1024],
                        scalar=1.0, in1=CT["lnr"][m % 4][:],
                        op0=ALU.mult, op1=ALU.add, accum_out=stat[:, 8 + p:9 + p])
                    E.drain()
                    E.tensor_scalar(out=stat[:, 0 + p:1 + p], in0=stat[:, 8 + p:9 + p],
                                    scalar1=1.0 / D_, scalar2=None,
                                    op0=ALU.mult).then_inc(d_bc.h, 1)
                d_bc.add(1)
                mark(f"r2d{m}", d_bc)
            if 2 <= n <= NQT + 1:
                m = n - 2
                p = m % 2
                if em:
                    wt(E, f"sqa{m}")
                    E.drain()
                    E.tensor_scalar(out=stat[:, 2 + p:3 + p], in0=stat[:, 10 + p:11 + p],
                                    scalar1=1.0 / D_, scalar2=None, op0=ALU.mult)
                    E.tensor_tensor(stat[:, 12 + p:13 + p], stat[:, 0 + p:1 + p],
                                    stat[:, 0 + p:1 + p], ALU.mult)
                    E.drain()
                    E.tensor_sub(stat[:, 2 + p:3 + p], stat[:, 2 + p:3 + p],
                                 stat[:, 12 + p:13 + p])
                    E.engine_nop().then_inc(d_bc.h, 1)
                d_bc.add(1)
                mark(f"vard{m}", d_bc)

    def C_gpsimd(E):
        em = E is not None
        for n in range(NQT):
            if em:
                wt(E, f"cote{n}")
                E.dma_start(out=out_t.ap()[n * 128:(n + 1) * 128, :],
                            in_=CT["ot"][n % 2][:]).then_inc(s_out[n % 2].h, 16)
            s_out[n % 2].add(16)
            mark(f"out{n}", s_out[n % 2])

    for _k in ["kbuf", "vbuf", "pseg", "osb", "rsb", "resid", "lnst",
               "ps_s", "ps_ot", "ps_r"]:
        AT[_k] = None

    A_gpsimd(None)
    A_pe(None)
    A_act(None)
    A_dve(None)
    A_sync(None)
    B_sync(None)
    B_pe(None)
    B_act(None)
    B_dve(None)
    B_gpsimd(None)
    C_sync(None)
    C_pe(None)
    C_act(None)
    C_dve(None)
    C_gpsimd(None)
    dry = {id(c): c.n for c in ctrs}
    for c in ctrs:
        c.n = 0

    es_pr.close()
    es_a2 = ExitStack()
    AT["kbuf"] = [es_a2.enter_context(nc.sbuf_tensor(f"kbuf{i}", [128, HC * 512], BF16)) for i in range(3)]
    AT["vbuf"] = [es_a2.enter_context(nc.sbuf_tensor(f"vbuf{i}", [128, 4 * D_], BF16)) for i in range(3)]
    AT["pseg"] = es_a2.enter_context(nc.sbuf_tensor("pseg", [128, 8 * 1024], BF16))
    AT["osb"] = es_a2.enter_context(nc.sbuf_tensor("osb", [128, 8 * D_], F32))
    AT["rsb"] = es_a2.enter_context(nc.sbuf_tensor("rsb", [128, 8], F32))
    AT["resid"] = es_a2.enter_context(nc.sbuf_tensor("resid", [128, D_], F32))
    AT["lnst"] = es_a2.enter_context(nc.sbuf_tensor("lnst", [128, 2 * D_], F32))
    AT["ps_s"] = es_a2.enter_context(nc.psum_tensor("ps_s", [128, 1024], F32))
    AT["ps_ot"] = es_a2.enter_context(nc.psum_tensor("ps_ot", [128, 2048], F32))
    AT["ps_r"] = es_a2.enter_context(nc.psum_tensor("ps_r", [128, 1024], F32))

    with nc.Block("A") as blk:
        @blk.gpsimd
        def _(E):
            A_gpsimd(E)

        @blk.tensor
        def _(E):
            A_pe(E)

        @blk.scalar
        def _(E):
            A_act(E)

        @blk.vector
        def _(E):
            A_dve(E)

        @blk.sync
        def _(E):
            A_sync(E)

    es_a2.close()
    es_a1.close()
    es_b = ExitStack()
    BT["qx_fm"] = es_b.enter_context(nc.sbuf_tensor("qx_fm", [128, HC * T_], BF16))
    BT["wg_T"] = es_b.enter_context(nc.sbuf_tensor("wg_T", [128, HC * DI_], BF16))
    BT["wu_T"] = es_b.enter_context(nc.sbuf_tensor("wu_T", [128, HC * DI_], BF16))
    BT["silu"] = [es_b.enter_context(nc.sbuf_tensor(f"silu{i}", [128, 512], F32)) for i in range(2)]
    BT["hst"] = [es_b.enter_context(nc.sbuf_tensor(f"hst{i}", [128, 512], F32)) for i in range(2)]
    BT["qxst"] = [es_b.enter_context(nc.sbuf_tensor(f"qxst{i}", [128, D_], BF16)) for i in range(2)]
    BT["lnrd"] = [es_b.enter_context(nc.sbuf_tensor(f"lnrd{i}", [128, D_], F32)) for i in range(4)]
    BT["ps_xtb"] = es_b.enter_context(nc.psum_tensor("ps_xtb", [128, 2048], BF16))
    BT["ps_g"] = es_b.enter_context(nc.psum_tensor("ps_g", [128, 1024], F32))
    BT["ps_u"] = es_b.enter_context(nc.psum_tensor("ps_u", [128, 1024], F32))
    BT["ps_t"] = es_b.enter_context(nc.psum_tensor("ps_t", [128, 512], F32))

    with nc.Block("B") as blk:
        @blk.gpsimd
        def _(E):
            B_gpsimd(E)

        @blk.tensor
        def _(E):
            B_pe(E)

        @blk.scalar
        def _(E):
            B_act(E)

        @blk.vector
        def _(E):
            B_dve(E)

        @blk.sync
        def _(E):
            B_sync(E)

    es_b.close()
    es_c = ExitStack()
    CT["qwd_T"] = es_c.enter_context(nc.sbuf_tensor("qwd_T", [128, IC * D_], BF16))
    CT["hbuf"] = [es_c.enter_context(nc.sbuf_tensor(f"hbuf{i}", [128, DI_], F32)) for i in range(2)]
    CT["hq"] = [es_c.enter_context(nc.sbuf_tensor(f"hq{i}", [128, DI_], BF16)) for i in range(2)]
    CT["lnr"] = [es_c.enter_context(nc.sbuf_tensor(f"lnr{i}", [128, D_], F32)) for i in range(4)]
    CT["r2"] = [es_c.enter_context(nc.sbuf_tensor(f"r2_{i}", [128, D_], F32)) for i in range(2)]
    CT["ot"] = [es_c.enter_context(nc.sbuf_tensor(f"ot{i}", [128, D_], F32)) for i in range(2)]
    CT["ctmp"] = es_c.enter_context(nc.sbuf_tensor("ctmp", [128, DI_], F32))
    CT["ctmpi"] = es_c.enter_context(nc.sbuf_tensor("ctmpi", [128, DI_], I32))
    CT["ps_d"] = es_c.enter_context(nc.psum_tensor("ps_d", [128, 2048], F32))
    CT["ps_ct"] = es_c.enter_context(nc.psum_tensor("ps_ct", [128, 512], F32))

    with nc.Block("C") as blk:
        @blk.gpsimd
        def _(E):
            C_gpsimd(E)

        @blk.tensor
        def _(E):
            C_pe(E)

        @blk.scalar
        def _(E):
            C_act(E)

        @blk.vector
        def _(E):
            C_dve(E)

        @blk.sync
        def _(E):
            C_sync(E)

    for c in ctrs:
        assert c.n == dry[id(c)], (c.name, c.n, dry[id(c)])

    es_c.close()
    es.close()
    return nc


_NC_CACHE = {}


def _get_nc():
    if "nc" not in _NC_CACHE:
        _NC_CACHE["nc"] = build_nc()
    return _NC_CACHE["nc"]


def kernel(L0_attn_embeddings, L0_attn_W_Q, L0_attn_W_K, L0_attn_W_V,
           L0_ffn_clip_x, L0_ffn_clip_down, L0_ffn_w_gate, L0_ffn_w_up,
           L0_ffn_w_down):
    x = np.asarray(L0_attn_embeddings, dtype=np.float32).reshape(S_TOK, D_)
    wq = np.ascontiguousarray(np.asarray(L0_attn_W_Q, dtype=np.float32))
    wk = np.ascontiguousarray(np.asarray(L0_attn_W_K, dtype=np.float32))
    wv = np.ascontiguousarray(np.asarray(L0_attn_W_V, dtype=np.float32))
    wg = np.ascontiguousarray(np.asarray(L0_ffn_w_gate, dtype=np.float32))
    wu = np.ascontiguousarray(np.asarray(L0_ffn_w_up, dtype=np.float32))
    wd = np.ascontiguousarray(np.asarray(L0_ffn_w_down, dtype=np.float32))
    cx = float(np.asarray(L0_ffn_clip_x).reshape(-1)[0])
    cd = float(np.asarray(L0_ffn_clip_down).reshape(-1)[0])
    cx_t = np.full((128, 1), cx, dtype=np.float32)
    cd_t = np.full((128, 1), cd, dtype=np.float32)
    identf = np.eye(128, dtype=np.float32)
    identb = np.eye(128, dtype=ml_dtypes.bfloat16)

    in_maps = []
    tri = np.triu(np.ones((128, 128), dtype=np.float32)).astype(ml_dtypes.bfloat16)
    one_m = np.ones((128, 128), dtype=ml_dtypes.bfloat16)
    zero_m = np.zeros((128, 128), dtype=ml_dtypes.bfloat16)
    for c in range(NCORES):
        qrows = np.concatenate(
            [x[(8 * s + c) * 128:(8 * s + c + 1) * 128] for s in range(NQT)], axis=0)
        kvrows = np.concatenate(
            [x[(8 * j + c) * 512:(8 * j + c + 1) * 512] for j in range(NKV)], axis=0)
        mk = np.empty((8, 128, 128), dtype=ml_dtypes.bfloat16)
        for w in range(8):
            mk[w] = one_m if w < c else (tri if w == c else zero_m)
        in_maps.append({
            "xq": np.ascontiguousarray(qrows),
            "xkv": np.ascontiguousarray(kvrows),
            "wq": wq, "wk": wk, "wv": wv,
            "wg": wg, "wu": wu, "wd": wd,
            "clipx": cx_t, "clipd": cd_t,
            "masks": np.ascontiguousarray(mk.reshape(8 * 128, 128)),
            "identf": identf, "identb": identb,
        })

    nc = _get_nc()
    res = run_bass_kernel_spmd(nc, in_maps, core_ids=list(range(NCORES)))

    out = np.empty((S_TOK, D_), dtype=np.float32)
    for c in range(NCORES):
        oc = res.results[c]["out"]
        for s in range(NQT):
            out[(8 * s + c) * 128:(8 * s + c + 1) * 128] = oc[s * 128:(s + 1) * 128]
    return out.reshape(1, S_TOK, D_)


# version 16 (final)
# speedup vs baseline: 1.1260x; 1.0018x over previous
from contextlib import ExitStack

import numpy as np
import ml_dtypes
import concourse.bass as bass
import concourse.mybir as mybir
from concourse.bass_utils import run_bass_kernel_spmd

F32 = mybir.dt.float32
BF16 = mybir.dt.bfloat16
I32 = mybir.dt.int32
AF = mybir.ActivationFunctionType
ALU = mybir.AluOpType
AX = mybir.AxisListType

NCORES = 8
S_TOK = 16384
D_ = 1024
DI_ = 2816
T_ = 2048
NQT = 16
NPAIR = 8
NKV = 4
NCH = 4
HC = 8
IC = 22
EPS = 1e-5
NWU = 2 * IC + 8

PAIR_T = [16 * (u + 1) for u in range(NPAIR)]
CUM_T = [0]
for _u in range(NPAIR):
    CUM_T.append(CUM_T[-1] + PAIR_T[_u])
PAIR_G = [4 * (u + 1) for u in range(NPAIR)]
CUM_G = [0]
for _u in range(NPAIR):
    CUM_G.append(CUM_G[-1] + PAIR_G[_u])

WQ_PROD = 0
PROD_STAGES = []
for _j in range(4):
    PROD_STAGES.append(("kv", _j))
    PROD_STAGES.append(("q", _j))
PROD_TILES = []
for _st, _i in PROD_STAGES:
    for _k in range(4):
        PROD_TILES.append((_st, 4 * _i + _k))
PT_I = {t: i for i, t in enumerate(PROD_TILES)}
WD_SUBW = [1024, 1024, 768]
WU_KIND = ([(0, i, 0, 1) for i in range(IC)] + [(1, i, 0, 1) for i in range(IC)]
           + [(2, rb, s, p) for rb in range(8) for p in (0, 1) for s in range(3)])
NWU = len(WU_KIND)

CHUNKS = 2
SC_S0 = [0, 8]
SC_NSEG = [8, 16]
SEG_LIST = [(v, g) for v in range(CHUNKS) for g in range(SC_NSEG[v])]
SEG_I = {key: i for i, key in enumerate(SEG_LIST)}


def seg_qlo(v, g):
    return max(SC_S0[v], g)


def seg_ns(v, g):
    return SC_S0[v] + 8 - seg_qlo(v, g)


def seg_subs(v, g):
    W = seg_ns(v, g) * 128
    subs = []
    off = 0
    while off < W:
        w = min(512, W - off)
        subs.append((off, w))
        off += w
    return subs


STILE_LIST = [(v, g, t2, si2) for (v, g) in SEG_LIST for t2 in range(8)
              for si2 in range(len(seg_subs(v, g)))]
STILE_I = {key: i for i, key in enumerate(STILE_LIST)}
SWEEP_LIST = [(v, g, si) for (v, g) in SEG_LIST for si in range(seg_ns(v, g))]
SWEEP_I = {key: i for i, key in enumerate(SWEEP_LIST)}
LOAD_LIST = [(v, gi2) for v in range(CHUNKS) for gi2 in range(2 * SC_NSEG[v])]
LOAD_I = {key: i for i, key in enumerate(LOAD_LIST)}

WQ_SEG = {}
_k = WQ_PROD
_skip = {(0, 5), (0, 6), (0, 7), (1, 11), (1, 12), (1, 13), (1, 14), (1, 15)}
_segs = [s for s in SEG_LIST[2:] if s not in _skip]
_i = 0
while _k < NWU:
    _sg = _segs[_i % len(_segs)]
    WQ_SEG[_sg] = WQ_SEG.get(_sg, 0) + 1
    _k += 1
    _i += 1


def _wu_w(k):
    kind, chunk, sub, phase = WU_KIND[k]
    return D_ if kind < 2 else WD_SUBW[sub]


def _wu_spills(k):
    kind, chunk, sub, phase = WU_KIND[k]
    return kind < 2 or phase == 1


def _prev_spill(k):
    for k2 in range(k - 1, -1, -1):
        if _wu_spills(k2) and k2 % 2 == k % 2:
            return k2
    return None


def ti(u, t):
    return CUM_T[u] + t


def gi(u, g):
    return CUM_G[u] + g


INV_G = {}
for _u in range(NPAIR):
    for _g in range(PAIR_G[_u]):
        INV_G[gi(_u, _g)] = (_u, _g)


def mki(u, t):
    return 16 * u + (t - 16 * u)


class Ctr:
    def __init__(self, es, nc, name):
        self.h = es.enter_context(nc.semaphore(name))
        self.name = name
        self.n = 0

    def add(self, k=1):
        self.n += k
        return self.n


def build_nc():
    nc = bass.Bass("TRN2", target_bir_lowering=False, debug=False)

    xq_in = nc.dram_tensor("xq", [T_, D_], F32, kind="ExternalInput").ap()
    xkv_in = nc.dram_tensor("xkv", [T_, D_], F32, kind="ExternalInput").ap()
    wq_in = nc.dram_tensor("wq", [D_, D_], F32, kind="ExternalInput").ap()
    wk_in = nc.dram_tensor("wk", [D_, D_], F32, kind="ExternalInput").ap()
    wv_in = nc.dram_tensor("wv", [D_, D_], F32, kind="ExternalInput").ap()
    wg_in = nc.dram_tensor("wg", [DI_, D_], F32, kind="ExternalInput").ap()
    wu_in = nc.dram_tensor("wu", [DI_, D_], F32, kind="ExternalInput").ap()
    wd_in = nc.dram_tensor("wd", [D_, DI_], F32, kind="ExternalInput").ap()
    clipx_in = nc.dram_tensor("clipx", [128, 1], F32, kind="ExternalInput").ap()
    clipd_in = nc.dram_tensor("clipd", [128, 1], F32, kind="ExternalInput").ap()
    masks_in = nc.dram_tensor("masks", [8 * 128, 128], BF16, kind="ExternalInput").ap()
    identf_in = nc.dram_tensor("identf", [128, 128], F32, kind="ExternalInput").ap()
    identb_in = nc.dram_tensor("identb", [128, 128], BF16, kind="ExternalInput").ap()
    out_t = nc.dram_tensor("out", [T_, D_], F32, kind="ExternalOutput")

    ksrc = [nc.dram_tensor(f"ksrc{j}", [128, HC * 512], BF16) for j in range(NCH)]
    vsrc = [nc.dram_tensor(f"vsrc{j}", [128, 4 * D_], BF16) for j in range(NCH)]
    kall = [nc.dram_tensor(f"kall{j}", [NCORES * 128, HC * 512], BF16,
                           addr_space="Shared") for j in range(NCH)]
    vall = [nc.dram_tensor(f"vall{j}", [NCORES * 128, 4 * D_], BF16,
                           addr_space="Shared") for j in range(NCH)]
    wq_sp = [nc.dram_tensor("wgq_s", [DI_, D_], BF16),
             nc.dram_tensor("wuq_s", [DI_, D_], BF16),
             nc.dram_tensor("wdq_s", [D_, DI_], BF16)]
    ln1_sp = nc.dram_tensor("ln1_s", [T_, D_], F32)
    h_spd = nc.dram_tensor("h_s", [128, IC * T_], F32)
    ar_i = [nc.dram_tensor(f"ar_i{i}", [128, 1], F32) for i in range(3)]
    ar_o = [nc.dram_tensor(f"ar_o{i}", [128, 1], F32) for i in range(3)]

    RG = [list(range(NCORES))]
    es = ExitStack()

    def sbg(name, shape, dt):
        return es.enter_context(nc.sbuf_tensor(name, shape, dt))

    identf = sbg("identf_sb", [128, 128], F32)
    identb = sbg("identb_sb", [128, 128], BF16)
    zeros = sbg("zeros_sb", [128, 128], F32)
    ones_bf = sbg("ones_bf", [128, 1], BF16)
    clipx = sbg("clipx_sb", [128, 1], F32)
    clipd = sbg("clipd_sb", [128, 1], F32)
    sgl = sbg("sgl", [128, 32], F32)
    stat = sbg("stat", [128, 32], F32)
    mx_x = sbg("mx_x", [128, 1], F32)
    mx_h = sbg("mx_h", [128, 1], F32)
    junk = sbg("junk", [128, D_], F32)
    wtmp = sbg("wtmp", [128, D_], F32)
    wtmpi = sbg("wtmpi", [128, D_], I32)
    wqbf = [sbg(f"wqbf{i}", [128, D_], BF16) for i in range(2)]
    negm = sbg("negm_sb", [128, 1], F32)
    eps_sb = sbg("eps_sb", [128, 1], F32)
    arx_sb = sbg("arx_sb", [128, 1], F32)
    arh_sb = sbg("arh_sb", [128, 1], F32)

    es_a1 = ExitStack()

    def sba1(name, shape, dt):
        return es_a1.enter_context(nc.sbuf_tensor(name, shape, dt))

    masks = sba1("masks_sb", [128, 8 * 128], BF16)
    q_fm = sba1("q_fm", [128, HC * T_], BF16)
    ksamp = sba1("ksamp", [128, HC * 128], BF16)
    xqr = [sba1(f"xqr{i}", [128, D_], F32) for i in range(2)]
    wraw = [sba1(f"wraw{i}", [128, D_], F32) for i in range(5)]
    kstage = sba1("kstage", [128, HC * 512], BF16)
    vstage = sba1("vstage", [128, 4 * D_], BF16)

    es_pr = ExitStack()
    xstage = [es_pr.enter_context(nc.sbuf_tensor(f"xstage{i}", [128, D_], BF16)) for i in range(2)]
    xfm = [es_pr.enter_context(nc.sbuf_tensor(f"xfm{i}", [128, HC * 512], BF16)) for i in range(2)]
    wkv_bf = [es_pr.enter_context(nc.sbuf_tensor(f"wkv{i}", [128, HC * D_], BF16)) for i in range(2)]
    wq_bf = es_pr.enter_context(nc.sbuf_tensor("wq_bf", [128, HC * D_], BF16))
    ps_xt = es_pr.enter_context(nc.psum_tensor("ps_xt", [128, 2048], BF16))
    ps_pa = es_pr.enter_context(nc.psum_tensor("ps_pa", [128, 512], F32))
    ps_pb = es_pr.enter_context(nc.psum_tensor("ps_pb", [128, 512], F32))
    ps_v = es_pr.enter_context(nc.psum_tensor("ps_v", [128, 2048], F32))

    AT = {}

    BT = {}
    CT = {}

    gx = [Ctr(es, nc, f"gx{i}") for i in range(2)]
    g_id = Ctr(es, nc, "g_id")
    gw_k = Ctr(es, nc, "gw_k")
    gw_v = Ctr(es, nc, "gw_v")
    gw_q = Ctr(es, nc, "gw_q")
    g_ksp = Ctr(es, nc, "g_ksp")
    g_vsp = Ctr(es, nc, "g_vsp")
    g_wsp = [Ctr(es, nc, f"g_wsp{i}") for i in range(2)]
    g_lnsp = [Ctr(es, nc, f"g_lnsp{i}") for i in range(2)]
    g_hsp = [Ctr(es, nc, f"g_hsp{i}") for i in range(2)]
    g_ar = Ctr(es, nc, "g_ar")
    g_cc = Ctr(es, nc, "g_cc")
    s_cfg = Ctr(es, nc, "s_cfg")
    kg = [Ctr(es, nc, f"kg{i}") for i in range(3)]
    vg = [Ctr(es, nc, f"vg{i}") for i in range(3)]
    d_oa = Ctr(es, nc, "d_oa")
    sxq = [Ctr(es, nc, f"sxq{i}") for i in range(2)]
    sw = [Ctr(es, nc, f"sw{i}") for i in range(5)]
    s_arx = Ctr(es, nc, "s_arx")
    s_lnr = [Ctr(es, nc, f"s_lnr{i}") for i in range(4)]
    s_wgu = Ctr(es, nc, "s_wgu")
    s_arh = Ctr(es, nc, "s_arh")
    s_qwd = Ctr(es, nc, "s_qwd")
    s_hld = [Ctr(es, nc, f"s_hld{i}") for i in range(2)]
    s_clnr = [Ctr(es, nc, f"s_clnr{i}") for i in range(4)]
    s_out = [Ctr(es, nc, f"s_out{i}") for i in range(2)]
    p_pr = Ctr(es, nc, "p_pr")
    p_s = Ctr(es, nc, "p_s")
    p_pv = Ctr(es, nc, "p_pv")
    p_m = Ctr(es, nc, "p_m")
    p_bc = Ctr(es, nc, "p_bc")
    a_ev = Ctr(es, nc, "a_ev")
    a_exp = Ctr(es, nc, "a_exp")
    a_m = Ctr(es, nc, "a_m")
    d_mk = Ctr(es, nc, "d_mk")
    d_ep = Ctr(es, nc, "d_ep")
    d_q = Ctr(es, nc, "d_q")
    d_m = Ctr(es, nc, "d_m")
    d_bc = Ctr(es, nc, "d_bc")
    ctrs = (gx + [g_id, gw_k, gw_v, gw_q, g_ksp, g_vsp] + g_wsp + g_lnsp
            + g_hsp + [g_ar, g_cc, s_cfg] + kg + vg + sxq + sw
            + [s_arx] + s_lnr + [s_wgu, s_arh, s_qwd] + s_hld + s_clnr
            + s_out + [p_pr, p_s, p_pv, p_m, p_bc, a_ev, a_exp, a_m, d_mk,
                       d_ep, d_q, d_m, d_bc, d_oa])

    CK = {}

    def mark(name, ctr):
        if name in CK:
            assert CK[name] == (ctr, ctr.n), (name, CK[name][1], ctr.n)
        CK[name] = (ctr, ctr.n)

    def wt(eng, name):
        ctr, val = CK[name]
        if val > 0:
            eng.wait_ge(ctr.h, val)


    def A_gpsimd(E):
        em = E is not None
        if em:
            E.memset(sgl[:, 31:32], 0.0)
            E.drain()
            E.dma_start(out=ar_i[2].ap(), in_=sgl[:, 31:32]).then_inc(g_ar.h, 16)
        g_ar.add(16)
        if em:
            E.wait_ge(g_ar.h, g_ar.n)

        if em:
            E.dma_start(out=identb[:], in_=identb_in).then_inc(g_id.h, 16)
        g_id.add(16)
        mark("identb", g_id)

        def xload(i):
            st, m = PROD_TILES[i]
            srcten = xkv_in if st == "kv" else xq_in
            if em:
                if i >= 2:
                    wt(E, f"pt_t{i - 2}")
                E.dma_start(out=xstage[i % 2][:],
                            in_=srcten[m * 128:(m + 1) * 128, :]).then_inc(gx[i % 2].h, 16)
            gx[i % 2].add(16)
            mark(f"ldt{i}", gx[i % 2])

        for i in range(4):
            xload(i)
        for d in range(HC):
            if em:
                E.dma_start(out=wkv_bf[0][:, d * D_:(d + 1) * D_],
                            in_=wk_in[d * 128:(d + 1) * 128, :]).then_inc(gw_k.h, 16)
            gw_k.add(16)
        mark("wk_all", gw_k)
        for d in range(HC):
            if em:
                E.dma_start(out=wkv_bf[1][:, d * D_:(d + 1) * D_],
                            in_=wv_in[d * 128:(d + 1) * 128, :]).then_inc(gw_v.h, 16)
            gw_v.add(16)
        mark("wv_all", gw_v)
        for d in range(HC):
            if em:
                E.dma_start(out=wq_bf[:, d * D_:(d + 1) * D_],
                            in_=wq_in[d * 128:(d + 1) * 128, :]).then_inc(gw_q.h, 16)
            gw_q.add(16)
        mark("wq_all", gw_q)

        def wraw_load(k):
            kind, chunk, sub, phase = WU_KIND[k]
            w = _wu_w(k)
            b = k % 5
            if em:
                if k >= 5:
                    wt(E, f"wq_u{k - 5}")
                if kind < 2:
                    ap_src = [wg_in, wu_in][kind][chunk * 128:(chunk + 1) * 128, :]
                else:
                    colo = sum(WD_SUBW[:sub])
                    ap_src = wd_in[chunk * 128:(chunk + 1) * 128, colo:colo + w]
                E.dma_start(out=wraw[b][:, 0:w], in_=ap_src).then_inc(sw[b].h, 16)
            sw[b].add(16)
            mark(f"wl{k}", sw[b])

        def spill_k(j):
            if em:
                wt(E, f"kev{j}")
                E.dma_start(out=ksrc[j].ap(), in_=kstage[:]).then_inc(g_ksp.h, 16)
            g_ksp.add(16)
            mark(f"ksp{j}", g_ksp)
            if em:
                E.wait_ge(g_ksp.h, g_ksp.n)
                E.collective_compute("AllGather", ALU.bypass, replica_groups=RG,
                                     ins=[ksrc[j].ap().opt()],
                                     outs=[kall[j].ap().opt()]).then_inc(g_cc.h, 1)
            g_cc.add(1)
            mark(f"cc_k{j}", g_cc)

        def spill_v(j):
            if em:
                wt(E, f"vev{j}")
                E.dma_start(out=vsrc[j].ap(), in_=vstage[:]).then_inc(g_vsp.h, 16)
            g_vsp.add(16)
            mark(f"vsp{j}", g_vsp)
            if em:
                E.wait_ge(g_vsp.h, g_vsp.n)
                E.collective_compute("AllGather", ALU.bypass, replica_groups=RG,
                                     ins=[vsrc[j].ap().opt()],
                                     outs=[vall[j].ap().opt()]).then_inc(g_cc.h, 1)
            g_cc.add(1)
            mark(f"cc_ch{j}", g_cc)

        for i in range(4, 8):
            xload(i)
        spill_k(0)
        for i in range(8, 12):
            xload(i)
        spill_v(0)
        for i in range(12, 16):
            xload(i)
        spill_k(1)
        spill_v(1)
        for i in range(16, 20):
            xload(i)
        for i in range(20, 24):
            xload(i)
        spill_k(2)
        spill_v(2)
        for i in range(24, 28):
            xload(i)
        for i in range(28, 32):
            xload(i)
        spill_k(3)
        spill_v(3)

        def wspill(k):
            if not _wu_spills(k):
                return
            kind, chunk, sub, phase = WU_KIND[k]
            w = _wu_w(k)
            b = k % 2
            if em:
                wt(E, f"wq_u{k}")
                if kind < 2:
                    dst = wq_sp[kind].ap()[chunk * 128:(chunk + 1) * 128, :]
                else:
                    colo = sum(WD_SUBW[:sub])
                    dst = wq_sp[2].ap()[chunk * 128:(chunk + 1) * 128, colo:colo + w]
                E.dma_start(out=dst, in_=wqbf[b][:, 0:w]).then_inc(g_wsp[b].h, 16)
            g_wsp[b].add(16)
            mark(f"wsp{k}", g_wsp[b])

        kld_done = 0
        ksp_done = 0

        def wq_step(n):
            nonlocal kld_done, ksp_done
            for _ in range(n):
                if kld_done < NWU:
                    wraw_load(kld_done)
                    kld_done += 1
                if kld_done >= 3 and ksp_done < kld_done - 3:
                    wspill(ksp_done)
                    ksp_done += 1

        wq_step(WQ_PROD)
        for v in range(CHUNKS):
            for g in range(SC_NSEG[v]):
                wq_step(WQ_SEG.get((v, g), 0))
                if (v, g) in ((1, 0), (1, 5), (1, 10)):
                    if em:
                        wt(E, f"segdone{v}_{g}")
                        E.collective_compute("AllReduce", ALU.max, replica_groups=RG,
                                             ins=[ar_i[2].ap().opt()],
                                             outs=[ar_o[2].ap().opt()]).then_inc(g_cc.h, 1)
                    g_cc.add(1)
                if g >= SC_S0[v]:
                    while ksp_done < kld_done:
                        wspill(ksp_done)
                        ksp_done += 1
                    s = g
                    if em:
                        wt(E, f"epi_s{s}")
                        E.dma_start(out=ln1_sp.ap()[s * 128:(s + 1) * 128, :],
                                    in_=AT["lnst"][:, (s % 2) * D_:(s % 2 + 1) * D_]).then_inc(g_lnsp[s % 2].h, 16)
                    g_lnsp[s % 2].add(16)
                    mark(f"lnsp_s{s}", g_lnsp[s % 2])
        while ksp_done < NWU:
            wspill(ksp_done)
            ksp_done += 1
        assert ksp_done == NWU and kld_done == NWU
        mark("lnsp_all0", g_lnsp[0])
        mark("lnsp_all1", g_lnsp[1])
        mark("wsp_all0", g_wsp[0])
        mark("wsp_all1", g_wsp[1])
        if em:
            wt(E, "epic1")
            E.dma_start(out=ar_i[0].ap(), in_=mx_x[:]).then_inc(g_ar.h, 16)
        g_ar.add(16)
        if em:
            E.wait_ge(g_ar.h, g_ar.n)
            E.collective_compute("AllReduce", ALU.max, replica_groups=RG,
                                 ins=[ar_i[0].ap().opt()],
                                 outs=[ar_o[0].ap().opt()]).then_inc(g_cc.h, 1)
        g_cc.add(1)
        mark("cc_arx", g_cc)

    def A_pe(E):
        em = E is not None

        def xtile(i):
            if em:
                wt(E, f"ldt{i}")
                if i < 2:
                    wt(E, "identb")
                else:
                    wt(E, f"ptev{i - 2}")
                ins = None
                for d in range(HC):
                    ins = E.transpose(
                        ps_xt[:, (i % 2) * 1024 + d * 128:(i % 2) * 1024 + (d + 1) * 128],
                        xstage[i % 2][:, d * 128:(d + 1) * 128], identb[:])
                ins.then_inc(p_pr.h, 1)
            p_pr.add(1)
            mark(f"pt_t{i}", p_pr)

        for st, idx in PROD_STAGES:
            base = PT_I[(st, 4 * idx)]
            for k in range(4):
                xtile(base + k)
            if st == "kv":
                j = idx
                if em:
                    wt(E, "wk_all")
                    wt(E, f"xfmkv{j}")
                for d in range(HC):
                    if em:
                        if d >= 2:
                            wt(E, f"kevd{j}_{d - 2}")
                        if j == 1 and d < 2:
                            wt(E, ["mrep", "negm"][d])
                        pb = [ps_pa, ps_pb][d % 2]
                        ins = None
                        for hch in range(HC):
                            ins = E.matmul(pb[:, 0:512],
                                           wkv_bf[0][:, hch * D_ + d * 128:hch * D_ + (d + 1) * 128],
                                           xfm[0][:, hch * 512:(hch + 1) * 512],
                                           start=hch == 0, stop=hch == HC - 1)
                        ins.then_inc(p_pr.h, 1)
                    p_pr.add(1)
                    mark(f"kmm{j}_{d}", p_pr)
                if em:
                    wt(E, "wv_all")
                for ksub in range(4):
                    if em:
                        if ksub >= 2:
                            wt(E, f"vevk{j}_{ksub - 2}")
                        vb0 = (ksub % 2) * 1024
                        ins = None
                        for hch in range(HC):
                            for dvh in range(2):
                                ins = E.matmul(
                                    ps_v[:, vb0 + dvh * 512:vb0 + (dvh + 1) * 512],
                                    xfm[0][:, hch * 512 + ksub * 128:hch * 512 + ksub * 128 + 128],
                                    wkv_bf[1][:, hch * D_ + dvh * 512:hch * D_ + (dvh + 1) * 512],
                                    start=hch == 0, stop=hch == HC - 1)
                        ins.then_inc(p_pr.h, 1)
                    p_pr.add(1)
                    mark(f"vmm{j}_{ksub}", p_pr)
            else:
                r = idx
                if em:
                    wt(E, "wq_all")
                    wt(E, f"xfmq{r}")
                for d in range(HC):
                    if em:
                        if d >= 2:
                            wt(E, f"qevd{r}_{d - 2}")
                        pb = [ps_pa, ps_pb][d % 2]
                        ins = None
                        for hch in range(HC):
                            ins = E.matmul(pb[:, 0:512],
                                           wq_bf[:, hch * D_ + d * 128:hch * D_ + (d + 1) * 128],
                                           xfm[1][:, hch * 512:(hch + 1) * 512],
                                           start=hch == 0, stop=hch == HC - 1)
                        ins.then_inc(p_pr.h, 1)
                    p_pr.add(1)
                    mark(f"qmm{r}_{d}", p_pr)
                if r == 0:
                    if em:
                        wt(E, "ksampc")
                        wt(E, "qevd0_7")
                        ins = None
                        for d in range(HC):
                            ins = E.matmul(ps_pa[:, 0:128], ksamp[:, d * 128:(d + 1) * 128],
                                           q_fm[:, d * T_:d * T_ + 128],
                                           start=d == 0, stop=d == HC - 1)
                        ins.then_inc(p_m.h, 1)
                    p_m.add(1)
                    mark("samp_mm", p_m)
                    if em:
                        wt(E, "mrep")
                        wt(E, "cfg")
                        E.transpose(ps_pb[:, 0:128], junk[:, 0:128], identf[:]).then_inc(p_m.h, 1)
                    p_m.add(1)
                    mark("mrep_t", p_m)

        ps_s, ps_ot, ps_r = AT["ps_s"], AT["ps_ot"], AT["ps_r"]

        for v in range(CHUNKS):
            for g in range(SC_NSEG[v]):
                qlo = seg_qlo(v, g)
                subs = seg_subs(v, g)
                for t2 in range(8):
                    t = 8 * g + t2
                    L = LOAD_I[(v, 2 * g + t2 // 4)]
                    buf = L % 3
                    for si2, (off, w) in enumerate(subs):
                        STI = STILE_I[(v, g, t2, si2)]
                        if em:
                            wt(E, f"kgl{L}")
                            if STI == 0:
                                wt(E, "prod_ev")
                                wt(E, "negm")
                            if STI >= 2:
                                E.wait_ge(a_exp.h, STI - 1)
                            slot = ps_s[:, (STI % 2) * 512:(STI % 2) * 512 + w]
                            ins = None
                            for d in range(HC):
                                ins = E.matmul(
                                    slot,
                                    AT["kbuf"][buf][:, d * 512 + (t % 4) * 128:d * 512 + (t % 4) * 128 + 128],
                                    q_fm[:, d * T_ + 128 * qlo + off:d * T_ + 128 * qlo + off + w],
                                    start=d == 0, stop=d == HC - 1)
                            ins.then_inc(p_s.h, 1)
                        p_s.add(1)
                        assert p_s.n == STI + 1
                for si in range(seg_ns(v, g)):
                    SW = SWEEP_I[(v, g, si)]
                    if em:
                        if SW == 0:
                            wt(E, "prod_ev")
                        if si == 0:
                            wt(E, f"vgl{LOAD_I[(v, 2 * g)]}")
                            wt(E, f"vgl{LOAD_I[(v, 2 * g + 1)]}")
                        E.wait_ge(a_exp.h, STILE_I[(v, g, 7, len(subs) - 1)] + 1)
                        if g >= SC_S0[v]:
                            wt(E, f"mseg{v}_{g}")
                        if SW >= 2:
                            E.wait_ge(d_oa.h, SW - 1)
                        par = SW % 2
                        ins = None
                        for t2 in range(8):
                            buf = LOAD_I[(v, 2 * g + t2 // 4)] % 3
                            lh = AT["pseg"][:, t2 * 1024 + si * 128:t2 * 1024 + si * 128 + 128]
                            for dvh in range(2):
                                ins = E.matmul(
                                    ps_ot[:, par * 1024 + dvh * 512:par * 1024 + (dvh + 1) * 512],
                                    lh,
                                    AT["vbuf"][buf][:, (8 * g + t2) % 4 * 1024 + dvh * 512:(8 * g + t2) % 4 * 1024 + (dvh + 1) * 512],
                                    start=t2 == 0, stop=t2 == 7)
                            ins = E.matmul(ps_r[:, par * 512:par * 512 + 1], lh, ones_bf[:],
                                           start=t2 == 0, stop=t2 == 7)
                        ins.then_inc(p_pv.h, 1)
                    p_pv.add(1)
                    assert p_pv.n == SW + 1
                mark(f"segdone{v}_{g}", p_pv)

    def A_act(E):
        em = E is not None

        def evac(waits, out_fn, in_fn, name):
            if em:
                for w in waits:
                    wt(E, w)
                E.activation(out=out_fn(), in_=in_fn(), func=AF.Copy).then_inc(a_ev.h, 1)
            a_ev.add(1)
            mark(name, a_ev)

        for st, idx in PROD_STAGES:
            base = PT_I[(st, 4 * idx)]
            buf = 0 if st == "kv" else 1
            for k in range(4):
                i = base + k
                extra = []
                if k == 0:
                    if st == "kv" and idx >= 1:
                        extra = [f"vmm{idx - 1}_3"]
                    if st == "q" and idx >= 1:
                        extra = [f"qmm{idx - 1}_7"]
                evac([f"pt_t{i}"] + extra,
                     lambda buf=buf, k=k: xfm[buf][:].rearrange(
                         "p (h s k) -> p h s k", h=HC, s=4)[:, :, k, :],
                     lambda i=i: ps_xt[:, (i % 2) * 1024:(i % 2) * 1024 + 1024].rearrange(
                         "p (h k) -> p h k", h=HC),
                     f"ptev{i}")
            if st == "kv":
                j = idx
                mark(f"xfmkv{j}", a_ev)
                for d in range(HC):
                    extra = []
                    if d == 0 and j >= 1:
                        extra = [f"ksp{j - 1}"]
                        if j == 1:
                            extra.append("ksampc")
                    evac([f"kmm{j}_{d}"] + extra,
                         lambda d=d: kstage[:, d * 512:(d + 1) * 512],
                         lambda d=d: [ps_pa, ps_pb][d % 2][:, 0:512],
                         f"kevd{j}_{d}")
                mark(f"kev{j}", a_ev)
                for ksub in range(4):
                    extra = [f"vsp{j - 1}"] if (ksub == 0 and j >= 1) else []
                    evac([f"vmm{j}_{ksub}"] + extra,
                         lambda k=ksub: vstage[:, k * 1024:(k + 1) * 1024],
                         lambda k=ksub: ps_v[:, (k % 2) * 1024:(k % 2) * 1024 + 1024],
                         f"vevk{j}_{ksub}")
                mark(f"vev{j}", a_ev)
            else:
                r = idx
                mark(f"xfmq{r}", a_ev)
                for d in range(HC):
                    evac([f"qmm{r}_{d}"],
                         lambda r=r, d=d: q_fm[:, d * T_ + r * 512:d * T_ + (r + 1) * 512],
                         lambda d=d: [ps_pa, ps_pb][d % 2][:, 0:512],
                         f"qevd{r}_{d}")
        mark("prod_ev", a_ev)

        ps_s = AT["ps_s"]
        for v in range(CHUNKS):
            for g in range(SC_NSEG[v]):
                subs = seg_subs(v, g)
                SI = SEG_I[(v, g)]
                for gi2 in (2 * g, 2 * g + 1):
                    L = LOAD_I[(v, gi2)]
                    j = gi2 // 8
                    r = gi2 % 8
                    if em:
                        wt(E, f"cc_ch{j}")
                        if L < 3:
                            wt(E, "qmm3_7")
                            wt(E, "prod_ev")
                            wt(E, "ksp3")
                            wt(E, "vsp3")
                        else:
                            vp, gp = LOAD_LIST[L - 3]
                            wt(E, f"segdone{vp}_{gp // 2}")
                        E.dma_start(out=AT["vbuf"][L % 3][:],
                                    in_=vall[j].ap()[r * 128:(r + 1) * 128, :]).then_inc(vg[L % 3].h, 16)
                    vg[L % 3].add(16)
                    mark(f"vgl{L}", vg[L % 3])
                if g >= SC_S0[v]:
                    s = g
                    if em:
                        if s >= 2:
                            wt(E, f"vstt{s - 2}")
                        E.dma_start(out=xqr[s % 2][:],
                                    in_=xq_in[s * 128:(s + 1) * 128, :]).then_inc(sxq[s % 2].h, 16)
                    sxq[s % 2].add(16)
                    mark(f"xq{s}", sxq[s % 2])
                for t2 in range(8):
                    for si2, (off, w) in enumerate(subs):
                        STI = STILE_I[(v, g, t2, si2)]
                        if em:
                            E.wait_ge(p_s.h, STI + 1)
                            if STI == 0:
                                wt(E, "negm")
                            if SI >= 1:
                                vp, gp = SEG_LIST[SI - 1]
                                wt(E, f"segdone{vp}_{gp}")
                            E.activation(out=AT["pseg"][:, t2 * 1024 + off:t2 * 1024 + off + w],
                                         in_=ps_s[:, (STI % 2) * 512:(STI % 2) * 512 + w],
                                         func=AF.Exp, bias=negm[:],
                                         scale=1.0 / 32.0).then_inc(a_exp.h, 1)
                        a_exp.add(1)
                if g >= SC_S0[v]:
                    si = g - SC_S0[v]
                    s = g
                    if em:
                        wt(E, f"var_s{s}")
                        E.activation(out=sgl[:, 24:25], in_=stat[:, 8 + si:9 + si],
                                     func=AF.Ln, bias=eps_sb[:], scale=1.0)
                        E.drain()
                        E.activation(out=stat[:, 16 + si:17 + si], in_=sgl[:, 24:25],
                                     func=AF.Exp, bias=0.0, scale=-0.5).then_inc(a_m.h, 1)
                    a_m.add(1)
                    mark(f"rstd_s{s}", a_m)

    def A_dve(E):
        em = E is not None
        if em:
            E.memset(zeros[:], 0.0)
            E.memset(ones_bf[:], 1.0)
            E.memset(mx_x[:], 0.0)
            E.memset(mx_h[:], 0.0)
            E.memset(eps_sb[:], EPS)
            E.engine_nop().then_inc(d_m.h, 1)
        d_m.add(1)
        mark("dve_init", d_m)

        if em:
            wt(E, "kev0")
            E.tensor_copy(
                ksamp[:].rearrange("p (d k) -> p d k", d=HC),
                kstage[:].rearrange("p (d k) -> p d k", d=HC)[:, :, 0:128],
            ).then_inc(d_m.h, 1)
        d_m.add(1)
        mark("ksampc", d_m)

        def wq_unit(k):
            kind, chunk, sub, phase = WU_KIND[k]
            w = _wu_w(k)
            b = k % 2
            if em:
                wt(E, f"wl{k}")
                buf = wraw[k % 5]
                if kind < 2:
                    c0, c1 = 8 + b * 2, 9 + b * 2
                    kp = _prev_spill(k)
                    if kp is not None:
                        wt(E, f"wsp{kp}")
                    E.tensor_reduce(sgl[:, c0:c0 + 1], buf[:, 0:w], AX.X, ALU.max,
                                    apply_absolute_value=True)
                    E.drain()
                    E.reciprocal(sgl[:, c1:c1 + 1], sgl[:, c0:c0 + 1])
                    E.drain()
                    E.tensor_scalar(out=sgl[:, c1:c1 + 1], in0=sgl[:, c1:c1 + 1],
                                    scalar1=127.0, scalar2=None, op0=ALU.mult)
                    E.tensor_scalar(out=sgl[:, c0:c0 + 1], in0=sgl[:, c0:c0 + 1],
                                    scalar1=1.0 / 127.0, scalar2=None, op0=ALU.mult)
                    E.drain()
                    E.tensor_scalar(out=wtmp[:, 0:w], in0=buf[:, 0:w],
                                    scalar1=sgl[:, c1:c1 + 1], scalar2=127.49,
                                    op0=ALU.mult, op1=ALU.min)
                    E.drain()
                    E.tensor_scalar(out=wtmpi[:, 0:w], in0=wtmp[:, 0:w],
                                    scalar1=-128.49, scalar2=None, op0=ALU.max)
                    E.drain()
                    E.tensor_scalar(out=wqbf[b][:, 0:w], in0=wtmpi[:, 0:w],
                                    scalar1=sgl[:, c0:c0 + 1], scalar2=None,
                                    op0=ALU.mult).then_inc(d_q.h, 1)
                elif phase == 0:
                    E.tensor_reduce(sgl[:, 13:14], buf[:, 0:w], AX.X, ALU.max,
                                    apply_absolute_value=True)
                    E.drain()
                    if sub == 0:
                        E.tensor_copy(sgl[:, 12:13], sgl[:, 13:14])
                    else:
                        E.tensor_max(sgl[:, 12:13], sgl[:, 12:13], sgl[:, 13:14])
                    E.drain()
                    if sub == 2:
                        E.reciprocal(sgl[:, 14:15], sgl[:, 12:13])
                        E.drain()
                        E.tensor_scalar(out=sgl[:, 14:15], in0=sgl[:, 14:15],
                                        scalar1=127.0, scalar2=None, op0=ALU.mult)
                        E.tensor_scalar(out=sgl[:, 12:13], in0=sgl[:, 12:13],
                                        scalar1=1.0 / 127.0, scalar2=None, op0=ALU.mult)
                        E.drain()
                    E.engine_nop().then_inc(d_q.h, 1)
                else:
                    kp = _prev_spill(k)
                    if kp is not None:
                        wt(E, f"wsp{kp}")
                    E.tensor_scalar(out=wtmp[:, 0:w], in0=buf[:, 0:w],
                                    scalar1=sgl[:, 14:15], scalar2=127.49,
                                    op0=ALU.mult, op1=ALU.min)
                    E.drain()
                    E.tensor_scalar(out=wtmpi[:, 0:w], in0=wtmp[:, 0:w],
                                    scalar1=-128.49, scalar2=None, op0=ALU.max)
                    E.drain()
                    E.tensor_scalar(out=wqbf[b][:, 0:w], in0=wtmpi[:, 0:w],
                                    scalar1=sgl[:, 12:13], scalar2=None,
                                    op0=ALU.mult).then_inc(d_q.h, 1)
            d_q.add(1)
            mark(f"wq_u{k}", d_q)

        kdone = 0
        for k in range(WQ_PROD):
            wq_unit(k)
            kdone += 1

        if em:
            wt(E, "samp_mm")
            E.tensor_reduce(sgl[:, 0:1], ps_pa[:, 0:128], AX.X, ALU.max)
            E.drain()
            E.tensor_scalar(out=junk[:, 0:128], in0=zeros[:],
                            scalar1=sgl[:, 0:1], scalar2=None, op0=ALU.add)
            E.engine_nop().then_inc(d_m.h, 1)
        d_m.add(1)
        mark("mrep", d_m)
        if em:
            wt(E, "mrep_t")
            E.tensor_reduce(sgl[:, 1:2], ps_pb[:, 0:128], AX.X, ALU.max)
            E.drain()
            E.tensor_scalar(out=negm[:], in0=sgl[:, 1:2], scalar1=-1.0 / 32.0,
                            scalar2=-2.0, op0=ALU.mult, op1=ALU.add).then_inc(d_m.h, 1)
        d_m.add(1)
        mark("negm", d_m)

        ps_ot, ps_r = AT.get("ps_ot"), AT.get("ps_r")
        for v in range(CHUNKS):
            for g in range(SC_NSEG[v]):
                qlo = seg_qlo(v, g)
                subs = seg_subs(v, g)
                if g >= SC_S0[v]:
                    for t2 in range(8):
                        STI = STILE_I[(v, g, t2, 0)]
                        if em:
                            if d_mk.n == 0:
                                wt(E, "cfg")
                            E.wait_ge(a_exp.h, STI + 1)
                            E.tensor_tensor(AT["pseg"][:, t2 * 1024:t2 * 1024 + 128],
                                            AT["pseg"][:, t2 * 1024:t2 * 1024 + 128],
                                            masks[:, t2 * 128:(t2 + 1) * 128],
                                            ALU.mult).then_inc(d_mk.h, 1)
                        d_mk.add(1)
                    mark(f"mseg{v}_{g}", d_mk)
                for si in range(seg_ns(v, g)):
                    SW = SWEEP_I[(v, g, si)]
                    par = SW % 2
                    osl = qlo + si - SC_S0[v]
                    first = g == 0
                    if em:
                        E.wait_ge(p_pv.h, SW + 1)
                        if first:
                            E.tensor_copy(AT["osb"][:, osl * D_:(osl + 1) * D_],
                                          ps_ot[:, par * 1024:par * 1024 + 1024])
                            E.drain()
                            E.tensor_copy(AT["rsb"][:, osl:osl + 1],
                                          ps_r[:, par * 512:par * 512 + 1])
                            E.drain()
                        else:
                            E.tensor_add(AT["osb"][:, osl * D_:(osl + 1) * D_],
                                         AT["osb"][:, osl * D_:(osl + 1) * D_],
                                         ps_ot[:, par * 1024:par * 1024 + 1024])
                            E.drain()
                            E.tensor_add(AT["rsb"][:, osl:osl + 1],
                                         AT["rsb"][:, osl:osl + 1],
                                         ps_r[:, par * 512:par * 512 + 1])
                            E.drain()
                        E.engine_nop().then_inc(d_oa.h, 1)
                    d_oa.add(1)
                    assert d_oa.n == SW + 1
                for _ in range(WQ_SEG.get((v, g), 0)):
                    if kdone < NWU:
                        wq_unit(kdone)
                        kdone += 1
                if g >= SC_S0[v]:
                    si = g - SC_S0[v]
                    s = g
                    if em:
                        wt(E, f"xq{s}")
                        if s >= 2:
                            wt(E, f"lnsp_s{s - 2}")
                        E.reciprocal(sgl[:, 2:3], AT["rsb"][:, si:si + 1])
                        E.drain()
                        E.scalar_tensor_tensor(
                            out=AT["resid"][:], in0=AT["osb"][:, si * D_:(si + 1) * D_],
                            scalar=sgl[:, 2:3], in1=xqr[s % 2][:],
                            op0=ALU.mult, op1=ALU.add, accum_out=stat[:, 24:25])
                        E.engine_nop().then_inc(d_ep.h, 1)
                    d_ep.add(1)
                    mark(f"vstt{s}", d_ep)
                    if em:
                        E.drain()
                        E.scalar_tensor_tensor(
                            out=junk[:, 0:D_], in0=AT["resid"][:], scalar=1.0,
                            in1=AT["resid"][:], op0=ALU.mult, op1=ALU.mult,
                            accum_out=stat[:, 25:26])
                        E.drain()
                        E.tensor_scalar(out=stat[:, si:si + 1], in0=stat[:, 24:25],
                                        scalar1=1.0 / D_, scalar2=None, op0=ALU.mult)
                        E.drain()
                        E.tensor_tensor(stat[:, 26:27], stat[:, si:si + 1],
                                        stat[:, si:si + 1], ALU.mult)
                        E.tensor_scalar(out=stat[:, 8 + si:9 + si], in0=stat[:, 25:26],
                                        scalar1=1.0 / D_, scalar2=None, op0=ALU.mult)
                        E.drain()
                        E.tensor_sub(stat[:, 8 + si:9 + si], stat[:, 8 + si:9 + si],
                                     stat[:, 26:27])
                        E.engine_nop().then_inc(d_ep.h, 1)
                    d_ep.add(1)
                    mark(f"var_s{s}", d_ep)
                    if em:
                        wt(E, f"rstd_s{s}")
                        E.tensor_scalar(out=AT["lnst"][:, (s % 2) * D_:(s % 2 + 1) * D_],
                                        in0=AT["resid"][:],
                                        scalar1=stat[:, si:si + 1],
                                        scalar2=stat[:, 16 + si:17 + si],
                                        op0=ALU.subtract, op1=ALU.mult)
                        E.drain()
                        E.tensor_reduce(stat[:, 27:28],
                                        AT["lnst"][:, (s % 2) * D_:(s % 2 + 1) * D_],
                                        AX.X, ALU.max, apply_absolute_value=True)
                        E.drain()
                        E.tensor_tensor(mx_x[:], mx_x[:], stat[:, 27:28], ALU.max)
                        E.engine_nop().then_inc(d_ep.h, 1)
                    d_ep.add(1)
                    mark(f"epi_s{s}", d_ep)
            mark(f"epic{v}", d_ep)
        assert kdone == NWU

    def A_sync(E):
        em = E is not None
        if em:
            E.dma_start(out=identf[:], in_=identf_in).then_inc(s_cfg.h, 16)
            E.dma_start(out=clipx[:], in_=clipx_in).then_inc(s_cfg.h, 16)
            E.dma_start(out=clipd[:], in_=clipd_in).then_inc(s_cfg.h, 16)
            for tt in range(8):
                E.dma_start(out=masks[:, tt * 128:(tt + 1) * 128],
                            in_=masks_in[tt * 128:(tt + 1) * 128, :]).then_inc(s_cfg.h, 16)
        s_cfg.add(16 * 11)
        mark("cfg", s_cfg)


        for v in range(CHUNKS):
            for g in range(SC_NSEG[v]):
                for gi2 in (2 * g, 2 * g + 1):
                    L = LOAD_I[(v, gi2)]
                    j = gi2 // 8
                    r = gi2 % 8
                    if em:
                        wt(E, f"cc_ch{j}")
                        if L < 3:
                            wt(E, "qmm3_7")
                            wt(E, "prod_ev")
                            wt(E, "ksp3")
                            wt(E, "vsp3")
                        else:
                            vp, gp = LOAD_LIST[L - 3]
                            wt(E, f"segdone{vp}_{gp // 2}")
                        E.dma_start(out=AT["kbuf"][L % 3][:],
                                    in_=kall[j].ap()[r * 128:(r + 1) * 128, :]).then_inc(kg[L % 3].h, 16)
                    kg[L % 3].add(16)
                    mark(f"kgl{L}", kg[L % 3])

    def B_sync(E):
        em = E is not None

        def lnrd_load(n):
            if em:
                if n == 0:
                    wt(E, "lnsp_all0")
                    wt(E, "lnsp_all1")
                if n >= 4:
                    wt(E, f"qx{n - 4}q")
                E.dma_start(out=BT["lnrd"][n % 4][:],
                            in_=ln1_sp.ap()[n * 128:(n + 1) * 128, :]).then_inc(s_lnr[n % 4].h, 16)
            s_lnr[n % 4].add(16)
            mark(f"lnrd{n}", s_lnr[n % 4])

        for n in range(4):
            lnrd_load(n)
        if em:
            wt(E, "cc_arx")
            E.dma_start(out=arx_sb[:], in_=ar_o[0].ap()).then_inc(s_arx.h, 16)
        s_arx.add(16)
        mark("arx_ld", s_arx)
        for n in range(4, NQT):
            lnrd_load(n)

    def B_pe(E):
        em = E is not None
        if em:
            wt(E, "bx_rep")
            E.transpose(BT["ps_t"][:, 0:128], junk[:, 0:128], identf[:]).then_inc(p_bc.h, 1)
        p_bc.add(1)
        mark("bx_tr", p_bc)
        for n in range(NQT):
            if em:
                wt(E, f"qx{n}q")
                if n >= 2:
                    wt(E, f"qxev{n - 2}")
                ins = None
                for d in range(HC):
                    ins = E.transpose(
                        BT["ps_xtb"][:, (n % 2) * 1024 + d * 128:(n % 2) * 1024 + (d + 1) * 128],
                        BT["qxst"][n % 2][:, d * 128:(d + 1) * 128], identb[:])
                ins.then_inc(p_bc.h, 1)
            p_bc.add(1)
            mark(f"qxt{n}", p_bc)
        nu = 0
        for icx in range(IC):
            for tc in range(4):
                par = nu % 2
                if em:
                    if nu == 0:
                        wt(E, "wguT")
                    if icx == 0:
                        wt(E, f"qxev{4 * tc + 3}")
                    if nu >= 2:
                        wt(E, f"hdone{nu - 2}")
                    ins = None
                    for d in range(HC):
                        ins = E.matmul(BT["ps_g"][:, par * 512:par * 512 + 512],
                                       BT["wg_T"][:, d * DI_ + icx * 128:d * DI_ + (icx + 1) * 128],
                                       BT["qx_fm"][:, d * T_ + tc * 512:d * T_ + (tc + 1) * 512],
                                       start=d == 0, stop=d == HC - 1)
                    ins.then_inc(p_bc.h, 1)
                p_bc.add(1)
                mark(f"g_mm{nu}", p_bc)
                if em:
                    ins = None
                    for d in range(HC):
                        ins = E.matmul(BT["ps_u"][:, par * 512:par * 512 + 512],
                                       BT["wu_T"][:, d * DI_ + icx * 128:d * DI_ + (icx + 1) * 128],
                                       BT["qx_fm"][:, d * T_ + tc * 512:d * T_ + (tc + 1) * 512],
                                       start=d == 0, stop=d == HC - 1)
                    ins.then_inc(p_bc.h, 1)
                p_bc.add(1)
                mark(f"u_mm{nu}", p_bc)
                nu += 1

    def B_act(E):
        em = E is not None
        for d in range(HC):
            if em:
                if d == 0:
                    wt(E, "wsp_all0")
                    wt(E, "wsp_all1")
                E.dma_start(out=BT["wg_T"][:, d * DI_:(d + 1) * DI_],
                            in_=wq_sp[0].ap()[:, d * 128:(d + 1) * 128],
                            transpose=True).then_inc(s_wgu.h, 16)
            s_wgu.add(16)
        for d in range(HC):
            if em:
                E.dma_start(out=BT["wu_T"][:, d * DI_:(d + 1) * DI_],
                            in_=wq_sp[1].ap()[:, d * 128:(d + 1) * 128],
                            transpose=True).then_inc(s_wgu.h, 16)
            s_wgu.add(16)
        mark("wguT", s_wgu)
        if em:
            wt(E, "bx_tr")
            wt(E, "cfg")
            E.activation(out=sgl[:, 16:17], in_=clipx[:], func=AF.Sigmoid).then_inc(a_m.h, 1)
        a_m.add(1)
        mark("sigx", a_m)
        for n in range(NQT):
            if em:
                wt(E, f"qxt{n}")
                E.activation(
                    out=BT["qx_fm"][:].rearrange("p (h s k) -> p h s k", h=HC, s=NQT)[:, :, n, :],
                    in_=BT["ps_xtb"][:, (n % 2) * 1024:(n % 2) * 1024 + 1024].rearrange(
                        "p (h k) -> p h k", h=HC),
                    func=AF.Copy).then_inc(a_m.h, 1)
            a_m.add(1)
            mark(f"qxev{n}", a_m)
        for nu in range(4 * IC):
            par = nu % 2
            if em:
                wt(E, f"g_mm{nu}")
                if nu >= 2:
                    wt(E, f"hdone{nu - 2}")
                E.activation(out=BT["silu"][par][:],
                             in_=BT["ps_g"][:, par * 512:par * 512 + 512],
                             func=AF.Silu).then_inc(a_m.h, 1)
            a_m.add(1)
            mark(f"silu{nu}", a_m)

    def B_dve(E):
        em = E is not None
        if em:
            wt(E, "arx_ld")
            E.tensor_scalar(out=junk[:, 0:128], in0=zeros[:], scalar1=arx_sb[:],
                            scalar2=None, op0=ALU.add)
            E.engine_nop().then_inc(d_bc.h, 1)
        d_bc.add(1)
        mark("bx_rep", d_bc)
        if em:
            wt(E, "bx_tr")
            E.tensor_reduce(sgl[:, 17:18], BT["ps_t"][:, 0:128], AX.X, ALU.max)
            wt(E, "sigx")
            E.drain()
            E.tensor_tensor(sgl[:, 18:19], sgl[:, 16:17], sgl[:, 17:18], ALU.mult)
            E.drain()
            E.tensor_scalar(out=sgl[:, 18:19], in0=sgl[:, 18:19],
                            scalar1=1.0 / 127.0, scalar2=None, op0=ALU.mult)
            E.drain()
            E.reciprocal(sgl[:, 19:20], sgl[:, 18:19])
            E.engine_nop().then_inc(d_bc.h, 1)
        d_bc.add(1)
        mark("sx", d_bc)

        for n in range(NQT):
            if em:
                if n == 0:
                    wt(E, "sx")
                wt(E, f"lnrd{n}")
                if n >= 2:
                    wt(E, f"qxt{n - 2}")
                E.tensor_scalar(out=junk[:, 0:D_], in0=BT["lnrd"][n % 4][:],
                                scalar1=sgl[:, 19:20], scalar2=127.49,
                                op0=ALU.mult, op1=ALU.min)
                E.drain()
                E.tensor_scalar(out=wtmpi[:, 0:D_], in0=junk[:, 0:D_],
                                scalar1=-128.49, scalar2=None, op0=ALU.max)
                E.drain()
                E.tensor_scalar(out=BT["qxst"][n % 2][:], in0=wtmpi[:, 0:D_],
                                scalar1=sgl[:, 18:19], scalar2=None,
                                op0=ALU.mult).then_inc(d_bc.h, 1)
            d_bc.add(1)
            mark(f"qx{n}q", d_bc)

        for nu in range(4 * IC):
            par = nu % 2
            if em:
                wt(E, f"silu{nu}")
                wt(E, f"u_mm{nu}")
                if nu >= 2:
                    wt(E, f"hsp{nu - 2}")
                E.tensor_tensor(BT["hst"][par][:], BT["silu"][par][:],
                                BT["ps_u"][:, par * 512:par * 512 + 512], ALU.mult)
                E.drain()
                E.tensor_reduce(stat[:, 14:15], BT["hst"][par][:], AX.X, ALU.max,
                                apply_absolute_value=True)
                E.drain()
                E.tensor_tensor(mx_h[:], mx_h[:], stat[:, 14:15], ALU.max)
                E.engine_nop().then_inc(d_bc.h, 1)
            d_bc.add(1)
            mark(f"hdone{nu}", d_bc)

    def B_gpsimd(E):
        em = E is not None
        for nu in range(4 * IC):
            icx, tc = nu // 4, nu % 4
            par = nu % 2
            if em:
                wt(E, f"hdone{nu}")
                E.dma_start(out=h_spd.ap()[:, icx * T_ + tc * 512:icx * T_ + (tc + 1) * 512],
                            in_=BT["hst"][par][:]).then_inc(g_hsp[par].h, 16)
            g_hsp[par].add(16)
            mark(f"hsp{nu}", g_hsp[par])
        mark("hsp_all0", g_hsp[0])
        mark("hsp_all1", g_hsp[1])
        if em:
            wt(E, f"hdone{60}")
            E.collective_compute("AllReduce", ALU.max, replica_groups=RG,
                                 ins=[ar_i[2].ap().opt()],
                                 outs=[ar_o[2].ap().opt()]).then_inc(g_cc.h, 1)
        g_cc.add(1)
        if em:
            wt(E, f"hdone{4 * IC - 1}")
            E.dma_start(out=ar_i[1].ap(), in_=mx_h[:]).then_inc(g_ar.h, 16)
        g_ar.add(16)
        if em:
            E.wait_ge(g_ar.h, g_ar.n)
            E.collective_compute("AllReduce", ALU.max, replica_groups=RG,
                                 ins=[ar_i[1].ap().opt()],
                                 outs=[ar_o[1].ap().opt()]).then_inc(g_cc.h, 1)
        g_cc.add(1)
        mark("cc_arh", g_cc)

    def C_sync(E):
        em = E is not None

        def hld_load(n, EE):
            if EE is not None:
                if n < 2:
                    wt(EE, "hsp_all0")
                    wt(EE, "hsp_all1")
                else:
                    wt(EE, f"hq{n - 2}")
                EE.dma_start(
                    out=CT["hbuf"][n % 2][:].rearrange("p (i k) -> p i k", i=IC),
                    in_=h_spd.ap()[:, :].rearrange("p (i t) -> p i t", i=IC)[:, :, n * 128:(n + 1) * 128],
                ).then_inc(s_hld[n % 2].h, 16)
            s_hld[n % 2].add(16)
            mark(f"hld{n}", s_hld[n % 2])

        def lnr_load(m, EE):
            if EE is not None:
                if m == 0:
                    wt(EE, "lnsp_all0")
                    wt(EE, "lnsp_all1")
                if m >= 4:
                    wt(EE, f"r2d{m - 4}")
                EE.dma_start(out=CT["lnr"][m % 4][:],
                             in_=ln1_sp.ap()[m * 128:(m + 1) * 128, :]).then_inc(s_clnr[m % 4].h, 16)
            s_clnr[m % 4].add(16)
            mark(f"lnr{m}", s_clnr[m % 4])

        C_sync.hld_load = hld_load
        C_sync.lnr_load = lnr_load
        hld_load(0, E)
        if em:
            wt(E, "cc_arh")
            E.dma_start(out=arh_sb[:], in_=ar_o[1].ap()).then_inc(s_arh.h, 16)
        s_arh.add(16)
        mark("arh_ld", s_arh)
        for n in range(1, NQT):
            if n % 2 == 0:
                hld_load(n, E)
            else:
                lnr_load(n, E)
        if em:
            E.wait_ge(s_out[0].h, 16 * NQT // 2)
            E.wait_ge(s_out[1].h, 16 * NQT // 2)

    def C_pe(E):
        em = E is not None
        if em:
            wt(E, "ch_rep")
            E.transpose(CT["ps_ct"][:, 0:128], junk[:, 0:128], identf[:]).then_inc(p_bc.h, 1)
        p_bc.add(1)
        mark("ch_tr", p_bc)
        for n in range(NQT):
            par = n % 2
            if em:
                if n == 0:
                    wt(E, "qwdT")
                wt(E, f"hq{n}")
                if n >= 2:
                    wt(E, f"r2d{n - 2}")
                ins = None
                for icx in range(IC):
                    for hoh in range(2):
                        ins = E.matmul(
                            CT["ps_d"][:, par * 1024 + hoh * 512:par * 1024 + (hoh + 1) * 512],
                            CT["hq"][par][:, icx * 128:(icx + 1) * 128],
                            CT["qwd_T"][:, icx * D_ + hoh * 512:icx * D_ + (hoh + 1) * 512],
                            start=icx == 0, stop=icx == IC - 1)
                ins.then_inc(p_bc.h, 1)
            p_bc.add(1)
            mark(f"dn{n}", p_bc)

    def C_act(E):
        em = E is not None
        for icx in range(IC):
            if em:
                if icx == 0:
                    wt(E, "wsp_all0")
                    wt(E, "wsp_all1")
                E.dma_start(out=CT["qwd_T"][:, icx * D_:(icx + 1) * D_],
                            in_=wq_sp[2].ap()[:, icx * 128:(icx + 1) * 128],
                            transpose=True).then_inc(s_qwd.h, 16)
            s_qwd.add(16)
        mark("qwdT", s_qwd)
        if em:
            wt(E, "ch_tr")
            E.activation(out=sgl[:, 20:21], in_=clipd[:], func=AF.Sigmoid).then_inc(a_m.h, 1)
        a_m.add(1)
        mark("sigd", a_m)

        def act_load(m, EE):
            if m % 2 == 1:
                C_sync.hld_load(m, EE)
            else:
                C_sync.lnr_load(m, EE)

        act_load(0, E)
        act_load(1, E)
        for n in range(NQT + 3):
            if n + 2 < NQT:
                act_load(n + 2, E)
            if 3 <= n <= NQT + 2:
                m = n - 3
                p = m % 2
                if em:
                    wt(E, f"bias{m}")
                    if m >= 2:
                        wt(E, f"out{m - 2}")
                    E.drain()
                    E.activation(out=CT["ot"][p][:], in_=CT["r2"][p][:],
                                 func=AF.Identity, bias=stat[:, 14 + p:15 + p],
                                 scale=stat[:, 6 + p:7 + p]).then_inc(a_m.h, 1)
                a_m.add(1)
                mark(f"cote{m}", a_m)
            if 1 <= n <= NQT:
                m = n - 1
                p = m % 2
                if em:
                    wt(E, f"r2d{m}")
                    E.drain()
                    E.activation(out=junk[:, 0:D_], in_=CT["r2"][p][:],
                                 func=AF.Square,
                                 accum_out=stat[:, 10 + p:11 + p]).then_inc(a_m.h, 1)
                a_m.add(1)
                mark(f"sqa{m}", a_m)
            if 2 <= n <= NQT + 1:
                m = n - 2
                p = m % 2
                if em:
                    wt(E, f"vard{m}")
                    E.activation(out=stat[:, 4 + p:5 + p], in_=stat[:, 2 + p:3 + p],
                                 func=AF.Ln, bias=eps_sb[:], scale=1.0)
                    E.drain()
                    E.activation(out=stat[:, 6 + p:7 + p], in_=stat[:, 4 + p:5 + p],
                                 func=AF.Exp, bias=0.0, scale=-0.5).then_inc(a_m.h, 1)
                a_m.add(1)
                mark(f"crstd{m}", a_m)

    def C_dve(E):
        em = E is not None
        if em:
            wt(E, "arh_ld")
            E.tensor_scalar(out=junk[:, 0:128], in0=zeros[:], scalar1=arh_sb[:],
                            scalar2=None, op0=ALU.add)
            E.engine_nop().then_inc(d_bc.h, 1)
        d_bc.add(1)
        mark("ch_rep", d_bc)
        if em:
            wt(E, "ch_tr")
            E.tensor_reduce(sgl[:, 21:22], CT["ps_ct"][:, 0:128], AX.X, ALU.max)
            wt(E, "sigd")
            E.drain()
            E.tensor_tensor(sgl[:, 22:23], sgl[:, 20:21], sgl[:, 21:22], ALU.mult)
            E.drain()
            E.tensor_scalar(out=sgl[:, 22:23], in0=sgl[:, 22:23],
                            scalar1=1.0 / 127.0, scalar2=None, op0=ALU.mult)
            E.drain()
            E.reciprocal(sgl[:, 23:24], sgl[:, 22:23])
            E.engine_nop().then_inc(d_bc.h, 1)
        d_bc.add(1)
        mark("sh", d_bc)

        for n in range(NQT + 3):
            if n < NQT:
                par = n % 2
                if em:
                    if n == 0:
                        wt(E, "sh")
                    wt(E, f"hld{n}")
                    E.tensor_scalar(out=CT["ctmp"][:], in0=CT["hbuf"][par][:],
                                    scalar1=sgl[:, 23:24], scalar2=127.49,
                                    op0=ALU.mult, op1=ALU.min)
                    E.drain()
                    E.tensor_scalar(out=CT["ctmpi"][:], in0=CT["ctmp"][:],
                                    scalar1=-128.49, scalar2=None, op0=ALU.max)
                    E.drain()
                    E.tensor_scalar(out=CT["hq"][par][:], in0=CT["ctmpi"][:],
                                    scalar1=sgl[:, 22:23], scalar2=None,
                                    op0=ALU.mult).then_inc(d_bc.h, 1)
                d_bc.add(1)
                mark(f"hq{n}", d_bc)
            if 3 <= n <= NQT + 2:
                m = n - 3
                p = m % 2
                if em:
                    wt(E, f"crstd{m}")
                    E.tensor_scalar(out=stat[:, 14 + p:15 + p], in0=stat[:, 0 + p:1 + p],
                                    scalar1=stat[:, 6 + p:7 + p], scalar2=-1.0,
                                    op0=ALU.mult, op1=ALU.mult).then_inc(d_bc.h, 1)
                d_bc.add(1)
                mark(f"bias{m}", d_bc)
            if 1 <= n <= NQT:
                m = n - 1
                p = m % 2
                if em:
                    wt(E, f"dn{m}")
                    wt(E, f"lnr{m}")
                    if m >= 2:
                        wt(E, f"cote{m - 2}")
                    E.scalar_tensor_tensor(
                        out=CT["r2"][p][:], in0=CT["ps_d"][:, p * 1024:p * 1024 + 1024],
                        scalar=1.0, in1=CT["lnr"][m % 4][:],
                        op0=ALU.mult, op1=ALU.add, accum_out=stat[:, 8 + p:9 + p])
                    E.drain()
                    E.tensor_scalar(out=stat[:, 0 + p:1 + p], in0=stat[:, 8 + p:9 + p],
                                    scalar1=1.0 / D_, scalar2=None,
                                    op0=ALU.mult).then_inc(d_bc.h, 1)
                d_bc.add(1)
                mark(f"r2d{m}", d_bc)
            if 2 <= n <= NQT + 1:
                m = n - 2
                p = m % 2
                if em:
                    wt(E, f"sqa{m}")
                    E.drain()
                    E.tensor_scalar(out=stat[:, 2 + p:3 + p], in0=stat[:, 10 + p:11 + p],
                                    scalar1=1.0 / D_, scalar2=None, op0=ALU.mult)
                    E.tensor_tensor(stat[:, 12 + p:13 + p], stat[:, 0 + p:1 + p],
                                    stat[:, 0 + p:1 + p], ALU.mult)
                    E.drain()
                    E.tensor_sub(stat[:, 2 + p:3 + p], stat[:, 2 + p:3 + p],
                                 stat[:, 12 + p:13 + p])
                    E.engine_nop().then_inc(d_bc.h, 1)
                d_bc.add(1)
                mark(f"vard{m}", d_bc)

    def C_gpsimd(E):
        em = E is not None
        for n in range(NQT):
            if em:
                wt(E, f"cote{n}")
                E.dma_start(out=out_t.ap()[n * 128:(n + 1) * 128, :],
                            in_=CT["ot"][n % 2][:]).then_inc(s_out[n % 2].h, 16)
            s_out[n % 2].add(16)
            mark(f"out{n}", s_out[n % 2])

    for _k in ["kbuf", "vbuf", "pseg", "osb", "rsb", "resid", "lnst",
               "ps_s", "ps_ot", "ps_r"]:
        AT[_k] = None

    A_gpsimd(None)
    A_pe(None)
    A_act(None)
    A_dve(None)
    A_sync(None)
    B_sync(None)
    B_pe(None)
    B_act(None)
    B_dve(None)
    B_gpsimd(None)
    C_sync(None)
    C_pe(None)
    C_act(None)
    C_dve(None)
    C_gpsimd(None)
    dry = {id(c): c.n for c in ctrs}
    for c in ctrs:
        c.n = 0

    es_pr.close()
    es_a2 = ExitStack()
    AT["kbuf"] = [es_a2.enter_context(nc.sbuf_tensor(f"kbuf{i}", [128, HC * 512], BF16)) for i in range(3)]
    AT["vbuf"] = [es_a2.enter_context(nc.sbuf_tensor(f"vbuf{i}", [128, 4 * D_], BF16)) for i in range(3)]
    AT["pseg"] = es_a2.enter_context(nc.sbuf_tensor("pseg", [128, 8 * 1024], BF16))
    AT["osb"] = es_a2.enter_context(nc.sbuf_tensor("osb", [128, 8 * D_], F32))
    AT["rsb"] = es_a2.enter_context(nc.sbuf_tensor("rsb", [128, 8], F32))
    AT["resid"] = es_a2.enter_context(nc.sbuf_tensor("resid", [128, D_], F32))
    AT["lnst"] = es_a2.enter_context(nc.sbuf_tensor("lnst", [128, 2 * D_], F32))
    AT["ps_s"] = es_a2.enter_context(nc.psum_tensor("ps_s", [128, 1024], F32))
    AT["ps_ot"] = es_a2.enter_context(nc.psum_tensor("ps_ot", [128, 2048], F32))
    AT["ps_r"] = es_a2.enter_context(nc.psum_tensor("ps_r", [128, 1024], F32))

    with nc.Block("A") as blk:
        @blk.gpsimd
        def _(E):
            A_gpsimd(E)

        @blk.tensor
        def _(E):
            A_pe(E)

        @blk.scalar
        def _(E):
            A_act(E)

        @blk.vector
        def _(E):
            A_dve(E)

        @blk.sync
        def _(E):
            A_sync(E)

    es_a2.close()
    es_a1.close()
    es_b = ExitStack()
    BT["qx_fm"] = es_b.enter_context(nc.sbuf_tensor("qx_fm", [128, HC * T_], BF16))
    BT["wg_T"] = es_b.enter_context(nc.sbuf_tensor("wg_T", [128, HC * DI_], BF16))
    BT["wu_T"] = es_b.enter_context(nc.sbuf_tensor("wu_T", [128, HC * DI_], BF16))
    BT["silu"] = [es_b.enter_context(nc.sbuf_tensor(f"silu{i}", [128, 512], F32)) for i in range(2)]
    BT["hst"] = [es_b.enter_context(nc.sbuf_tensor(f"hst{i}", [128, 512], F32)) for i in range(2)]
    BT["qxst"] = [es_b.enter_context(nc.sbuf_tensor(f"qxst{i}", [128, D_], BF16)) for i in range(2)]
    BT["lnrd"] = [es_b.enter_context(nc.sbuf_tensor(f"lnrd{i}", [128, D_], F32)) for i in range(4)]
    BT["ps_xtb"] = es_b.enter_context(nc.psum_tensor("ps_xtb", [128, 2048], BF16))
    BT["ps_g"] = es_b.enter_context(nc.psum_tensor("ps_g", [128, 1024], F32))
    BT["ps_u"] = es_b.enter_context(nc.psum_tensor("ps_u", [128, 1024], F32))
    BT["ps_t"] = es_b.enter_context(nc.psum_tensor("ps_t", [128, 512], F32))

    with nc.Block("B") as blk:
        @blk.gpsimd
        def _(E):
            B_gpsimd(E)

        @blk.tensor
        def _(E):
            B_pe(E)

        @blk.scalar
        def _(E):
            B_act(E)

        @blk.vector
        def _(E):
            B_dve(E)

        @blk.sync
        def _(E):
            B_sync(E)

    es_b.close()
    es_c = ExitStack()
    CT["qwd_T"] = es_c.enter_context(nc.sbuf_tensor("qwd_T", [128, IC * D_], BF16))
    CT["hbuf"] = [es_c.enter_context(nc.sbuf_tensor(f"hbuf{i}", [128, DI_], F32)) for i in range(2)]
    CT["hq"] = [es_c.enter_context(nc.sbuf_tensor(f"hq{i}", [128, DI_], BF16)) for i in range(2)]
    CT["lnr"] = [es_c.enter_context(nc.sbuf_tensor(f"lnr{i}", [128, D_], F32)) for i in range(4)]
    CT["r2"] = [es_c.enter_context(nc.sbuf_tensor(f"r2_{i}", [128, D_], F32)) for i in range(2)]
    CT["ot"] = [es_c.enter_context(nc.sbuf_tensor(f"ot{i}", [128, D_], F32)) for i in range(2)]
    CT["ctmp"] = es_c.enter_context(nc.sbuf_tensor("ctmp", [128, DI_], F32))
    CT["ctmpi"] = es_c.enter_context(nc.sbuf_tensor("ctmpi", [128, DI_], I32))
    CT["ps_d"] = es_c.enter_context(nc.psum_tensor("ps_d", [128, 2048], F32))
    CT["ps_ct"] = es_c.enter_context(nc.psum_tensor("ps_ct", [128, 512], F32))

    with nc.Block("C") as blk:
        @blk.gpsimd
        def _(E):
            C_gpsimd(E)

        @blk.tensor
        def _(E):
            C_pe(E)

        @blk.scalar
        def _(E):
            C_act(E)

        @blk.vector
        def _(E):
            C_dve(E)

        @blk.sync
        def _(E):
            C_sync(E)

    for c in ctrs:
        assert c.n == dry[id(c)], (c.name, c.n, dry[id(c)])

    es_c.close()
    es.close()
    return nc


_NC_CACHE = {}


def _get_nc():
    if "nc" not in _NC_CACHE:
        _NC_CACHE["nc"] = build_nc()
    return _NC_CACHE["nc"]


def kernel(L0_attn_embeddings, L0_attn_W_Q, L0_attn_W_K, L0_attn_W_V,
           L0_ffn_clip_x, L0_ffn_clip_down, L0_ffn_w_gate, L0_ffn_w_up,
           L0_ffn_w_down):
    x = np.asarray(L0_attn_embeddings, dtype=np.float32).reshape(S_TOK, D_)
    wq = np.ascontiguousarray(np.asarray(L0_attn_W_Q, dtype=np.float32))
    wk = np.ascontiguousarray(np.asarray(L0_attn_W_K, dtype=np.float32))
    wv = np.ascontiguousarray(np.asarray(L0_attn_W_V, dtype=np.float32))
    wg = np.ascontiguousarray(np.asarray(L0_ffn_w_gate, dtype=np.float32))
    wu = np.ascontiguousarray(np.asarray(L0_ffn_w_up, dtype=np.float32))
    wd = np.ascontiguousarray(np.asarray(L0_ffn_w_down, dtype=np.float32))
    cx = float(np.asarray(L0_ffn_clip_x).reshape(-1)[0])
    cd = float(np.asarray(L0_ffn_clip_down).reshape(-1)[0])
    cx_t = np.full((128, 1), cx, dtype=np.float32)
    cd_t = np.full((128, 1), cd, dtype=np.float32)
    identf = np.eye(128, dtype=np.float32)
    identb = np.eye(128, dtype=ml_dtypes.bfloat16)

    in_maps = []
    tri = np.triu(np.ones((128, 128), dtype=np.float32)).astype(ml_dtypes.bfloat16)
    one_m = np.ones((128, 128), dtype=ml_dtypes.bfloat16)
    zero_m = np.zeros((128, 128), dtype=ml_dtypes.bfloat16)
    for c in range(NCORES):
        qrows = np.concatenate(
            [x[(8 * s + c) * 128:(8 * s + c + 1) * 128] for s in range(NQT)], axis=0)
        kvrows = np.concatenate(
            [x[(8 * j + c) * 512:(8 * j + c + 1) * 512] for j in range(NKV)], axis=0)
        mk = np.empty((8, 128, 128), dtype=ml_dtypes.bfloat16)
        for w in range(8):
            mk[w] = one_m if w < c else (tri if w == c else zero_m)
        in_maps.append({
            "xq": np.ascontiguousarray(qrows),
            "xkv": np.ascontiguousarray(kvrows),
            "wq": wq, "wk": wk, "wv": wv,
            "wg": wg, "wu": wu, "wd": wd,
            "clipx": cx_t, "clipd": cd_t,
            "masks": np.ascontiguousarray(mk.reshape(8 * 128, 128)),
            "identf": identf, "identb": identb,
        })

    nc = _get_nc()
    res = run_bass_kernel_spmd(nc, in_maps, core_ids=list(range(NCORES)))

    out = np.empty((S_TOK, D_), dtype=np.float32)
    for c in range(NCORES):
        oc = res.results[c]["out"]
        for s in range(NQT):
            out[(8 * s + c) * 128:(8 * s + c + 1) * 128] = oc[s * 128:(s + 1) * 128]
    return out.reshape(1, S_TOK, D_)
